# version 05
# baseline (speedup 1.0000x reference)
import numpy as np
import contextlib
import concourse.bass as bass
import concourse.mybir as mybir
from concourse.bass_utils import run_bass_kernel_spmd

F32 = mybir.dt.float32
BF16 = mybir.dt.bfloat16
AF = mybir.ActivationFunctionType
ALU = mybir.AluOpType
AX = mybir.AxisListType

NCORES = 8
TOK = 2048
NT = TOK // 128
DM = 1024
C = 128
EPS = 1e-5
CONSTVALS = (1e-5, 1.0 / 64, -0.5, 1.0 / 1024, 0.125, -1.0, 1.0 / 32, 1.0)


def register_consts(nc, cvals):
    if getattr(nc, "_cv_registered", False):
        return
    nc._cv_registered = True
    for i, v in enumerate(CONSTVALS):
        nc.const_aps.register(F32, float(v), cvals[:, i:i + 1])

ENGS = ("sync", "scalar", "vector", "tensor", "gpsimd")
DBG_L0F = False
N_CC = 3
DBG_L1F = False
DBG_FUSED = False


class Sem:
    __slots__ = ("h", "n", "max_wait", "fenced")

    def __init__(self, h):
        self.h = h
        self.n = 0
        self.max_wait = 0
        self.fenced = 0


class Plan:

    def __init__(self):
        self.q = {e: [] for e in ENGS}
        self.waited = {}

    def op(self, eng, f, sem=None, k=1):
        if sem is not None and k == 16:
            if sem.max_wait > sem.fenced:
                self.q[eng].append(("wait", sem, sem.max_wait))
                sem.fenced = sem.max_wait
            sem.n += 16
            self.q[eng].append(("dma", f, sem, sem.n))
        elif sem is not None:
            sem.n += k
            self.q[eng].append(("op", f, sem, sem.n))
        else:
            self.q[eng].append(("op", f, None, 0))

    def wait(self, eng, sem, n=None):
        n = sem.n if n is None else n
        if n <= 0:
            return
        self.waited.setdefault(id(sem), set()).add(n)
        if n > sem.max_wait:
            sem.max_wait = n
        self.q[eng].append(("wait", sem, n))

    def emit(self, nc):
        emitted = {}

        def runner(items):
            def run(e):
                for it in items:
                    if it[0] == "wait":
                        _, sem, n = it
                        e.wait_ge(sem.h, n)
                    elif it[0] == "dma":
                        _, f, sem, cnt = it
                        f(e).then_inc(sem.h, 16)
                    else:
                        _, f, sem, cnt = it
                        ins = f(e)
                        if sem is not None and cnt in self.waited.get(id(sem), ()):
                            last = emitted.get(id(sem), 0)
                            ins.then_inc(sem.h, cnt - last)
                            emitted[id(sem)] = cnt
            return run

        with nc.Block() as block:
            block.sync(runner(self.q["sync"]))
            block.scalar(runner(self.q["scalar"]))
            block.vector(runner(self.q["vector"]))
            block.tensor(runner(self.q["tensor"]))
            block.gpsimd(runner(self.q["gpsimd"]))


class Sems:
    def __init__(self, nc, es):
        mk = lambda name: Sem(es.enter_context(nc.semaphore(name)))
        self.ldh = mk("ldh")
        self.lds = mk("lds")
        self.sth = mk("sth")
        self.cT = mk("cT")
        self.cV = mk("cV")
        self.cS = mk("cS")
        self.cG = mk("cG")



def host_constants():
    c = {}
    c["ident"] = np.eye(128, dtype=np.float32)
    c["t1u"] = (-1.0 / 16.0) * np.triu(np.ones((128, 128), np.float32))
    c["mask_incl"] = np.triu(np.ones((128, 128), np.float32))
    c["ones_col"] = np.ones((128, 1), np.float32)
    c["ones_row"] = np.ones((1, 128), np.float32)
    c["ones_row2048"] = np.ones((1, 2048), np.float32)
    from kernel import CONSTVALS as _cv
    c["constvals"] = np.tile(np.array(list(_cv) + [0.0] * (8 - len(_cv)),
                                      np.float32), (128, 1))
    dm = np.zeros((128, 2048), np.float32)
    for j in range(4):
        for i in range(128):
            dm[i, j * 512:(j + 1) * 512][np.arange(512) >= j * 128 + i] = 1.0
    c["diag_mask"] = dm
    return c


def core_layout(core):
    s, g = core // 4, core % 4
    return s, g, [g, 7 - g]


def per_core_arrays(inputs, core):
    s, g, octs = core_layout(core)
    x = np.asarray(inputs["L0_attn_x"])
    xs = np.concatenate([x[s, 1024 * j:1024 * (j + 1)] for j in octs], axis=0)
    d = {"x_in": np.ascontiguousarray(xs, dtype=np.float32)}
    theta = np.zeros((64, 128), np.float32)
    cbias = np.full((64, 16), -1e9, np.float32)
    for oi, j in enumerate(octs):
        for jp in range(8):
            if jp < j:
                cbias[:, oi * 8 + jp] = 0.0
                for jpp in range(jp + 1, j):
                    theta[:, oi * 64 + jp * 8 + jpp] = 1.0
    d["theta"] = theta
    d["cbias"] = cbias
    return d


def assemble_output(shards):
    out = np.empty((2, 8192, 1024), np.float32)
    for core in range(NCORES):
        s, g, octs = core_layout(core)
        sh = shards[core]
        for oi, j in enumerate(octs):
            out[s, 1024 * j:1024 * (j + 1)] = sh[1024 * oi:1024 * (oi + 1)]
    return out



def emit_l0_attn(nc, P, S, es, W, CN, CI, x_in, x_out, tag="", upto=99, p2cut=99, dbg=False):
    sb = lambda name, shape, dt: es.enter_context(nc.sbuf_tensor(name + tag, shape, dt))
    psa = lambda name, shape, dt: es.enter_context(nc.psum_tensor(name + tag, shape, dt))

    q_fm = sb("q_fm", [128, 16384], BF16)
    k_tm = sb("k_tm", [128, 16384], BF16)
    v_tm = sb("v_tm", [128, 16384], BF16)
    gs_tm = sb("gs_tm", [128, 16384], BF16)
    arena = sb("arena", [128, 16384], BF16)
    wbuf = sb("wbuf", [128, 8192], BF16)
    u_sb = sb("u_sb", [17, 2048], F32)
    wg2b = sb("wg2b", [17, 1024], F32)
    wg1 = sb("wg1", [128, 128], BF16)
    xres = sb("xres", [128, 1024], F32)
    S_sb = sb("S_sb", [64, 1024], F32)
    S_bf = sb("S_bf", [64, 1024], BF16)
    a_all = sb("a_all", [64, 128], F32)
    a64 = sb("a64", [64, 16], F32)
    aoct = sb("aoct", [128, 8], F32)
    e3 = sb("e3", [128, 8], F32)
    acml = sb("acml", [64, 16], F32)
    e3h = sb("e3h", [64, 16], F32)
    ccoef = sb("ccoef", [64, 16], F32)
    stat = sb("stat", [128, 16], F32)
    stat2 = sb("stat2", [128, 16], F32)
    col1 = sb("col1", [128, 1], F32)
    col2 = sb("col2", [128, 1], F32)
    col3 = sb("col3", [128, 1], F32)
    junkS = sb("junkS", [128, 1], F32)
    junkV = sb("junkV", [128, 1], F32)
    gnw_bc = sb("gnw_bc", [128, 1024], BF16)
    id_bf = sb("id_bf", [128, 128], BF16)
    t1u = sb("t1u", [128, 128], F32)
    mask_bf = sb("mask_bf", [128, 128], BF16)
    onescol = sb("onescol", [128, 1], BF16)
    onesrow = sb("onesrow", [1, 128], BF16)
    murow = sb("murow", [1, 128], F32)
    rsrow = sb("rsrow", [1, 128], F32)
    varrow = sb("varrow", [1, 128], F32)
    theta_sb = sb("theta_sb", [64, 128], F32)
    cbias_sb = sb("cbias_sb", [64, 16], F32)
    cvals = sb("cvals", [128, 8], F32)
    register_consts(nc, cvals)

    def av(c0, c1, dt=F32):
        v = arena[:, c0:c1]
        return v.bitcast(dt) if dt != BF16 else v
    scr1 = av(0, 2048)
    scr2 = av(2048, 4096)
    scr3 = av(4096, 6144)
    qt64 = arena[0:64, 6144:8192]
    ktf64 = arena[0:64, 8192:10240]
    kt_bf = av(10240, 11264, BF16)
    pt_bf = av(11264, 13312, BF16)
    og_bf = av(13312, 14336, BF16)
    ogf_bf = av(14336, 15360, BF16)
    Stmp = scr3[0:64, :]
    r_sb = scr2
    stage2 = gs_tm[:, 0:1024]
    proj_scr = xres[:, 0:512]
    sloc = xres[0:64, :]

    ps_big = psa("ps_big", [128, 2048], F32)
    ps_o = psa("ps_o", [128, 1024], F32)
    ps_mkv = ps_o
    ps_t = psa("ps_t", [128, 1024], BF16)

    sin_d = nc.dram_tensor("sin_l0" + tag, [64, 2048], F32).ap()
    summ = nc.dram_tensor("summ_l0" + tag, [64, 2080], F32).ap()
    summ_g = nc.dram_tensor("summ_l0_g" + tag, [256, 2080], F32).ap()

    for dst, srcap in [(id_bf[:], CN["ident"]), (mask_bf[:], CN["mask_incl"]),
                       (onescol[:], CN["ones_col"]), (onesrow[:], CN["ones_row"]),
                       (gnw_bc[:], W["gnw_tiled"]),
                       (wg1[:].rearrange("p (kc n) -> p kc n", n=16),
                        W["Wg1"].rearrange("(kc p) n -> p kc n", p=128))]:
        P.op("gpsimd", lambda e, d=dst, s=srcap: e.dma_start(out=d, in_=s), S.lds, 16)
    for dst, srcap in [(t1u[:], CN["t1u"]), (theta_sb[:], CI["theta"]),
                       (cbias_sb[:], CI["cbias"]), (wg2b[0:16, :], W["Wg2"]),
                       (wg2b[16:17, :], W["bg2"]),
                       (u_sb[16:17, :], CN["ones_row2048"]),
                       (varrow[:], CN["ones_row"]),
                       (cvals[:], CN["constvals"])]:
        P.op("sync", lambda e, d=dst, s=srcap: e.dma_start(out=d, in_=s), S.ldh, 16)
    P.wait("tensor", S.lds)
    P.wait("tensor", S.ldh)

    P.wait("scalar", S.ldh)
    P.op("scalar", lambda e: e.activation(stat2[:, 0:8], cvals[:, 0:8], AF.Exp), S.cS)
    P.op("scalar", lambda e: e.activation(stat2[:, 0:1], cvals[:, 7:8], AF.Ln), S.cS)
    P.op("scalar", lambda e: e.activation(stat2[:, 0:8], cvals[:, 0:8], AF.Square), S.cS)
    P.op("scalar", lambda e: e.activation(stat2[:, 0:8], cvals[:, 0:8], AF.Silu), S.cS)

    x_fm = arena
    for t in range(NT):
        P.wait("sync", S.cV)
        P.op("sync", lambda e, t=t: e.dma_start(
            out=xres[:], in_=x_in[128 * t:128 * (t + 1), :]), S.ldh, 16)
        P.wait("vector", S.ldh)
        P.wait("vector", S.cT)
        P.op("vector", lambda e: e.tensor_copy(stage2, xres[:]), S.cV)
        P.wait("tensor", S.cV)
        for c in range(8):
            P.op("tensor", lambda e, c=c: e.transpose(
                ps_t[:, 128 * c:128 * (c + 1)], stage2[:, 128 * c:128 * (c + 1)],
                id_bf[:]), S.cT)
        P.wait("vector", S.cT)
        dst = x_fm[:].rearrange("p (fc n) -> p fc n", n=2048)[:, :, t * 128:(t + 1) * 128]
        P.op("vector", lambda e, d=dst: e.tensor_copy(d, ps_t[:].rearrange(
            "p (fc n) -> p fc n", n=128)), S.cV)

    if upto <= 1:
        return

    def stream_w(name):
        P.wait("gpsimd", S.cT)
        P.op("gpsimd", lambda e, s=W[name].rearrange("(kc p) n -> p kc n", p=128):
             e.dma_start(out=wbuf[:].rearrange("p (kc n) -> p kc n", n=1024), in_=s),
             S.lds, 16)
        P.wait("tensor", S.lds)

    stream_w("Wq")
    P.wait("tensor", S.cV)
    for oc in range(8):
        for tg in range(4):
            slot = (oc * 4 + tg) % 4
            po = ps_big[:, 512 * slot:512 * (slot + 1)]
            P.wait("tensor", S.cS)
            P.wait("tensor", S.cV)
            for kc in range(8):
                P.op("tensor", lambda e, po=po, kc=kc, oc=oc, tg=tg: e.matmul(
                    po, wbuf[:, kc * 1024 + oc * 128:kc * 1024 + (oc + 1) * 128],
                    x_fm[:, kc * 2048 + tg * 512:kc * 2048 + (tg + 1) * 512],
                    start=(kc == 0), stop=(kc == 7)), S.cT if kc == 7 else None)
            P.wait("scalar", S.cT)
            P.op("scalar", lambda e, po=po, oc=oc, tg=tg: e.activation(
                q_fm[:, oc * 2048 + tg * 512:oc * 2048 + (tg + 1) * 512], po,
                AF.Copy, scale=0.125), S.cS)

    for name, dst in [("Wk", k_tm), ("Wv", v_tm)]:
        stream_w(name)
        for t in range(NT):
            for fg in range(2):
                slot = (t * 2 + fg) % 4
                po = ps_big[:, 512 * slot:512 * (slot + 1)]
                P.wait("tensor", S.cV)
                P.wait("tensor", S.cS)
                for kc in range(8):
                    P.op("tensor", lambda e, po=po, kc=kc, t=t, fg=fg: e.matmul(
                        po, x_fm[:, kc * 2048 + t * 128:kc * 2048 + (t + 1) * 128],
                        wbuf[:, kc * 1024 + fg * 512:kc * 1024 + (fg + 1) * 512],
                        start=(kc == 0), stop=(kc == 7)), S.cT if kc == 7 else None)
                P.wait("vector", S.cT)
                P.op("vector", lambda e, po=po, t=t, fg=fg, dst=dst: e.tensor_copy(
                    dst[:, t * 1024 + fg * 512:t * 1024 + (fg + 1) * 512], po), S.cV)

    stream_w("Wg")
    for t in range(NT):
        for fg in range(2):
            slot = (t * 2 + fg) % 4
            po = ps_big[:, 512 * slot:512 * (slot + 1)]
            P.wait("tensor", S.cS)
            P.wait("tensor", S.cV)
            for kc in range(8):
                P.op("tensor", lambda e, po=po, kc=kc, t=t, fg=fg: e.matmul(
                    po, x_fm[:, kc * 2048 + t * 128:kc * 2048 + (t + 1) * 128],
                    wbuf[:, kc * 1024 + fg * 512:kc * 1024 + (fg + 1) * 512],
                    start=(kc == 0), stop=(kc == 7)), S.cT if kc == 7 else None)
            P.wait("scalar", S.cT)
            P.wait("scalar", S.cV)
            P.op("scalar", lambda e, po=po: e.activation(proj_scr, po, AF.Silu), S.cS)
            P.wait("vector", S.cS)
            P.op("vector", lambda e, t=t, fg=fg: e.tensor_mul(
                gs_tm[:, t * 1024 + fg * 512:t * 1024 + (fg + 1) * 512],
                proj_scr, gnw_bc[:, fg * 512:(fg + 1) * 512]), S.cV)

    for tg in range(4):
        P.wait("tensor", S.cS)
        P.wait("tensor", S.cV)
        for kc in range(8):
            P.op("tensor", lambda e, kc=kc, tg=tg: e.matmul(
                ps_mkv[0:16, 0:512], wg1[:, kc * 16:(kc + 1) * 16],
                x_fm[:, kc * 2048 + tg * 512:kc * 2048 + (tg + 1) * 512],
                start=(kc == 0), stop=(kc == 7)), S.cT if kc == 7 else None)
        P.wait("scalar", S.cT)
        P.op("scalar", lambda e, tg=tg: e.activation(
            u_sb[0:16, tg * 512:(tg + 1) * 512], ps_mkv[0:16, 0:512], AF.Copy), S.cS)

    if dbg:
        for nm, buf in [("dq", q_fm), ("dk", k_tm), ("dv", v_tm), ("dgs", gs_tm)]:
            dap = nc.dram_tensor(nm + tag, [128, 16384], F32, kind="ExternalOutput").ap()
            P.wait("gpsimd", S.cS)
            P.wait("gpsimd", S.cV)
            P.op("gpsimd", lambda e, d=dap, b=buf: e.dma_start(out=d, in_=b[:]), S.lds, 16)
        dup = nc.dram_tensor("du" + tag, [17, 2048], F32, kind="ExternalOutput").ap()
        P.op("gpsimd", lambda e, d=dup: e.dma_start(out=d, in_=u_sb[:]), S.lds, 16)

    if upto <= 2:
        return

    def chunk_preamble(tile, with_e1):
        P.wait("tensor", S.cS)
        P.wait("tensor", S.cV)
        for fg in range(2):
            P.op("tensor", lambda e, fg=fg, tile=tile: e.matmul(
                ps_big[:, fg * 512:(fg + 1) * 512],
                u_sb[0:17, tile * 128:(tile + 1) * 128],
                wg2b[0:17, fg * 512:(fg + 1) * 512],
                start=True, stop=True), S.cT)
        P.wait("scalar", S.cT)
        P.wait("scalar", S.cV)
        P.wait("scalar", S.sth)
        P.op("scalar", lambda e: e.activation(scr1, ps_big[:, 0:1024], AF.Exp, scale=-1.0), S.cS)
        P.op("scalar", lambda e: e.activation(scr2, scr1, AF.Ln, bias=1.0), S.cS)
        P.wait("tensor", S.cS)
        for fg in range(2):
            P.op("tensor", lambda e, fg=fg: e.matmul(
                ps_big[:, 1024 + fg * 512:1024 + (fg + 1) * 512], t1u[:],
                scr2[:, fg * 512:(fg + 1) * 512], start=True, stop=True), S.cT)
        for c in range(8):
            P.op("tensor", lambda e, c=c: e.matmul(
                ps_big[:, c * 128:(c + 1) * 128],
                scr2[:, c * 128:(c + 1) * 128], t1u[:],
                start=True, stop=True), S.cT)
        P.wait("scalar", S.cT)
        afm3 = ps_big[:, 0:1024].rearrange("p (c t) -> p c t", t=128)
        P.op("scalar", lambda e: e.activation(
            e3[:].rearrange("p (c o) -> p c o", o=1), afm3[:, :, 127:128], AF.Exp), S.cS)
        P.op("scalar", lambda e: e.activation(
            stat2[:, 0:8].rearrange("p (c o) -> p c o", o=1), afm3[:, :, 127:128],
            AF.Copy), S.cS)
        P.op("scalar", lambda e: e.activation(scr1, ps_big[:, 1024:2048], AF.Exp, scale=-1.0), S.cS)
        if with_e1:
            P.op("scalar", lambda e: e.activation(scr3, ps_big[:, 0:1024], AF.Exp), S.cS)
        P.wait("vector", S.cS)
        P.op("vector", lambda e, tile=tile: e.tensor_mul(
            kt_bf, k_tm[:, tile * 1024:(tile + 1) * 1024], scr1), S.cV)

    def mkv_and_update(tile):
        P.wait("tensor", S.cV)
        for h in range(16):
            P.op("tensor", lambda e, h=h, tile=tile: e.matmul(
                ps_mkv[0:64, 64 * h:64 * (h + 1)], kt_bf[:, h * 64:(h + 1) * 64],
                v_tm[:, tile * 1024 + h * 64:tile * 1024 + (h + 1) * 64],
                start=True, stop=True), S.cT if h == 15 else None)
        P.wait("vector", S.cT)
        e3hv = e3h[:].rearrange("p (h o) -> p h o", o=1)
        P.op("vector", lambda e, e3hv=e3hv: e.tensor_copy(
            e3hv[:, 0:16:2, :], e3[0:64, :].unsqueeze(2)), S.cV)
        P.op("vector", lambda e, e3hv=e3hv: e.tensor_copy(
            e3hv[:, 1:16:2, :], e3[64:128, :].unsqueeze(2)), S.cV)
        P.op("vector", lambda e: e.tensor_add(Stmp, S_sb[:], ps_mkv[0:64, :]), S.cV)
        P.op("vector", lambda e: e.tensor_mul(
            S_sb[:].rearrange("p (h v) -> p h v", v=64),
            Stmp.rearrange("p (h v) -> p h v", v=64),
            e3h[:].unsqueeze(2).broadcast_to([64, 16, 64])), S.cV)

    for oi in range(2):
        P.wait("vector", S.cT)
        P.wait("vector", S.sth)
        P.op("vector", lambda e: e.memset(S_sb[:], 0.0), S.cV)
        P.op("vector", lambda e: e.memset(aoct[:], 0.0), S.cV)
        for ci in range(8):
            tile = oi * 8 + ci
            chunk_preamble(tile, with_e1=False)
            if dbg and tile == 0:
                dsp = nc.dram_tensor("dspl" + tag, [128, 1024], F32, kind="ExternalOutput").ap()
                de2 = nc.dram_tensor("de2" + tag, [128, 1024], F32, kind="ExternalOutput").ap()
                de3 = nc.dram_tensor("de3" + tag, [128, 8], F32, kind="ExternalOutput").ap()
                dkt = nc.dram_tensor("dkt" + tag, [128, 1024], F32, kind="ExternalOutput").ap()
                P.wait("gpsimd", S.cS)
                P.wait("gpsimd", S.cV)
                P.op("gpsimd", lambda e, d=dsp: e.dma_start(out=d, in_=scr2), S.lds, 16)
                P.op("gpsimd", lambda e, d=de2: e.dma_start(out=d, in_=scr1), S.lds, 16)
                P.op("gpsimd", lambda e, d=de3: e.dma_start(out=d, in_=e3[:]), S.lds, 16)
                P.op("gpsimd", lambda e, d=dkt: e.dma_start(out=d, in_=kt_bf), S.lds, 16)
                P.wait("scalar", S.lds)
                P.wait("vector", S.lds)
            P.wait("vector", S.cS)
            P.op("vector", lambda e: e.tensor_add(aoct[:], aoct[:], stat2[:, 0:8]), S.cV)
            mkv_and_update(tile)
            if dbg and tile == 0:
                ds1 = nc.dram_tensor("ds1" + tag, [64, 1024], F32, kind="ExternalOutput").ap()
                P.wait("sync", S.cV)
                P.op("sync", lambda e, d=ds1: e.dma_start(out=d, in_=S_sb[:]), S.sth, 16)
                P.wait("vector", S.sth)
        a64v = a64[:].rearrange("p (h o) -> p h o", o=1)
        P.op("vector", lambda e, a64v=a64v: e.tensor_copy(
            a64v[:, 0:16:2, :], aoct[0:64, :].unsqueeze(2)), S.cV)
        P.op("vector", lambda e, a64v=a64v: e.tensor_copy(
            a64v[:, 1:16:2, :], aoct[64:128, :].unsqueeze(2)), S.cV)
        P.wait("sync", S.cV)
        P.op("sync", lambda e, oi=oi: e.dma_start(
            out=summ[:, oi * 1040:oi * 1040 + 16], in_=a64[:]), S.sth, 16)
        P.op("sync", lambda e, oi=oi: e.dma_start(
            out=summ[:, oi * 1040 + 16:oi * 1040 + 1040], in_=S_sb[:]), S.sth, 16)

    if upto <= 3:
        return

    P.wait("gpsimd", S.sth)
    P.op("gpsimd", lambda e: e.collective_compute(
        "AllGather", mybir.AluOpType.bypass,
        replica_groups=[[0, 1, 2, 3], [4, 5, 6, 7]],
        ins=[summ.opt()], outs=[summ_g.opt()]), S.cG, 1)
    P.wait("sync", S.cG)
    for j in range(8):
        rank, half = (j, 0) if j < 4 else (7 - j, 1)
        P.op("sync", lambda e, j=j, rank=rank, half=half: e.dma_start(
            out=a_all[:, 16 * j:16 * (j + 1)],
            in_=summ_g[64 * rank:64 * (rank + 1), half * 1040:half * 1040 + 16]),
            S.ldh, 16)
    P.wait("vector", S.ldh)
    for oi in range(2):
        P.wait("vector", S.sth)
        P.op("vector", lambda e: e.memset(S_sb[:], 0.0), S.cV)
        for jp in range(8):
            rank, half = (jp, 0) if jp < 4 else (7 - jp, 1)
            P.op("vector", lambda e: e.memset(acml[:], 0.0), S.cV)
            for jpp in range(8):
                tc = theta_sb[:, oi * 64 + jp * 8 + jpp:oi * 64 + jp * 8 + jpp + 1]
                P.op("vector", lambda e, jpp=jpp, tc=tc: e.scalar_tensor_tensor(
                    acml[:], a_all[:, 16 * jpp:16 * (jpp + 1)], tc, acml[:],
                    ALU.mult, ALU.add), S.cV)
            P.wait("scalar", S.cV)
            P.op("scalar", lambda e, oi=oi, jp=jp: e.activation(
                ccoef[:], acml[:], AF.Exp,
                bias=cbias_sb[:, oi * 8 + jp:oi * 8 + jp + 1]), S.cS)
            P.wait("sync", S.cV, S.cV.n)
            P.op("sync", lambda e, rank=rank, half=half: e.dma_start(
                out=sloc, in_=summ_g[64 * rank:64 * (rank + 1),
                                     half * 1040 + 16:half * 1040 + 1040]), S.ldh, 16)
            P.wait("vector", S.ldh)
            P.wait("vector", S.cS)
            P.op("vector", lambda e: e.tensor_mul(
                Stmp.rearrange("p (h v) -> p h v", v=64),
                sloc.rearrange("p (h v) -> p h v", v=64),
                ccoef[:].unsqueeze(2).broadcast_to([64, 16, 64])), S.cV)
            P.op("vector", lambda e: e.tensor_add(S_sb[:], S_sb[:], Stmp), S.cV)
        P.wait("sync", S.cV)
        P.op("sync", lambda e, oi=oi: e.dma_start(
            out=sin_d[:, 1024 * oi:1024 * (oi + 1)], in_=S_sb[:]), S.sth, 16)

    if upto <= 4:
        return

    stream_w("Wo")

    for oi in range(2):
        P.wait("sync", S.cV)
        P.wait("sync", S.sth)
        P.op("sync", lambda e, oi=oi: e.dma_start(
            out=S_sb[:], in_=sin_d[:, 1024 * oi:1024 * (oi + 1)]), S.ldh, 16)
        P.wait("vector", S.ldh)
        P.wait("vector", S.cT)
        for ci in range(8):
            tile = oi * 8 + ci
            P.wait("sync", S.cV)
            P.op("sync", lambda e, tile=tile: e.dma_start(
                out=xres[:], in_=x_in[128 * tile:128 * (tile + 1), :]), S.ldh, 16)
            chunk_preamble(tile, with_e1=True)
            if p2cut <= 1:
                continue
            P.op("vector", lambda e: e.tensor_copy(S_bf[:], S_sb[:]), S.cV)
            q3 = q_fm[:].rearrange("p (oc n) -> p oc n", n=2048)
            e13 = scr3.rearrange("p (c t) -> p c t", t=128)
            qt3 = qt64.rearrange("p (h t) -> p h t", t=128)
            P.op("vector", lambda e, q3=q3, e13=e13, qt3=qt3, tile=tile: e.tensor_mul(
                qt3[:, 0:16:2, :],
                q3[0:64, :, tile * 128:(tile + 1) * 128], e13[0:64, :, :]), S.cV)
            P.op("vector", lambda e, q3=q3, e13=e13, qt3=qt3, tile=tile: e.tensor_mul(
                qt3[:, 1:16:2, :],
                q3[64:128, :, tile * 128:(tile + 1) * 128], e13[64:128, :, :]), S.cV)
            P.wait("tensor", S.cV)
            for c in range(8):
                P.op("tensor", lambda e, c=c: e.transpose(
                    ps_t[:, 128 * c:128 * (c + 1)], kt_bf[:, 128 * c:128 * (c + 1)],
                    id_bf[:]), S.cT)
            P.wait("vector", S.cT)
            pt3 = ps_t[:].rearrange("p (c t) -> p c t", t=128)
            kf3 = ktf64.rearrange("p (h t) -> p h t", t=128)
            P.op("vector", lambda e, pt3=pt3, kf3=kf3: e.tensor_copy(
                kf3[:, 0:16:2, :], pt3[0:64, :, :]), S.cV)
            P.op("vector", lambda e, pt3=pt3, kf3=kf3: e.tensor_copy(
                kf3[:, 1:16:2, :], pt3[64:128, :, :]), S.cV)
            if dbg and tile == 0:
                dqt = nc.dram_tensor("dqt" + tag, [64, 2048], F32, kind="ExternalOutput").ap()
                dkf = nc.dram_tensor("dkf" + tag, [64, 2048], F32, kind="ExternalOutput").ap()
                P.wait("gpsimd", S.cV)
                P.op("gpsimd", lambda e, d=dqt: e.dma_start(out=d, in_=qt64), S.lds, 16)
                P.op("gpsimd", lambda e, d=dkf: e.dma_start(out=d, in_=ktf64), S.lds, 16)
                P.wait("tensor", S.lds)
            if p2cut <= 2:
                continue
            P.wait("tensor", S.cV)
            for h in range(16):
                P.op("tensor", lambda e, h=h: e.matmul(
                    ps_big[:, 128 * h:128 * (h + 1)],
                    ktf64[0:64, 128 * h:128 * (h + 1)],
                    qt64[0:64, 128 * h:128 * (h + 1)],
                    start=True, stop=True), S.cT if h == 15 else None)
            P.wait("vector", S.cT)
            P.op("vector", lambda e: e.tensor_mul(
                pt_bf.rearrange("p (h t) -> p h t", t=128),
                ps_big[:].rearrange("p (h t) -> p h t", t=128),
                mask_bf[:].unsqueeze(1).broadcast_to([128, 16, 128])), S.cV)
            if dbg and tile == 0:
                dpt = nc.dram_tensor("dpt" + tag, [128, 2048], F32, kind="ExternalOutput").ap()
                P.wait("gpsimd", S.cV)
                P.op("gpsimd", lambda e, d=dpt: e.dma_start(out=d, in_=pt_bf), S.lds, 16)
                P.wait("tensor", S.lds)
            if p2cut <= 3:
                continue
            P.wait("tensor", S.cV)
            for h in range(16):
                P.op("tensor", lambda e, h=h, tile=tile: e.matmul(
                    ps_o[:, 64 * h:64 * (h + 1)], pt_bf[:, 128 * h:128 * (h + 1)],
                    v_tm[:, tile * 1024 + 64 * h:tile * 1024 + 64 * (h + 1)],
                    start=True, stop=False), None)
                P.op("tensor", lambda e, h=h: e.matmul(
                    ps_o[:, 64 * h:64 * (h + 1)],
                    qt64[0:64, 128 * h:128 * (h + 1)],
                    S_bf[0:64, 64 * h:64 * (h + 1)],
                    start=False, stop=True), S.cT if h == 15 else None)
            if p2cut <= 4:
                continue
            P.wait("scalar", S.cT)
            P.op("scalar", lambda e: e.activation(scr1, ps_o[:], AF.Square,
                                                  scale=0.125), S.cS)
            P.wait("vector", S.cS)
            P.op("vector", lambda e: e.tensor_reduce(
                stat[:].rearrange("p (h o) -> p h o", o=1),
                scr1.rearrange("p (h d) -> p h d", d=64), AX.X, ALU.add), S.cV)
            P.op("vector", lambda e: e.tensor_copy(junkV[:], cvals[:, 0:1]), S.cV)
            P.wait("scalar", S.cV)
            P.op("scalar", lambda e: e.activation(stat2[:], stat[:], AF.Ln,
                                                  bias=EPS), S.cS)
            P.op("scalar", lambda e: e.activation(stat2[:], stat2[:], AF.Exp,
                                                  scale=-0.5), S.cS)
            P.wait("vector", S.cS)
            P.op("vector", lambda e: e.tensor_mul(
                scr3.rearrange("p (h d) -> p h d", d=64),
                ps_o[:].rearrange("p (h d) -> p h d", d=64),
                stat2[:].unsqueeze(2).broadcast_to([128, 16, 64])), S.cV)
            P.op("vector", lambda e, tile=tile: e.tensor_mul(
                og_bf, scr3, gs_tm[:, tile * 1024:(tile + 1) * 1024]), S.cV)
            if dbg and tile == 0:
                don = nc.dram_tensor("don" + tag, [128, 1024], F32, kind="ExternalOutput").ap()
                dog = nc.dram_tensor("dog" + tag, [128, 1024], F32, kind="ExternalOutput").ap()
                P.wait("gpsimd", S.cV)
                P.op("gpsimd", lambda e, d=don: e.dma_start(out=d, in_=scr3), S.lds, 16)
                P.op("gpsimd", lambda e, d=dog: e.dma_start(out=d, in_=og_bf), S.lds, 16)
                P.wait("tensor", S.lds)
                P.wait("scalar", S.lds)
            mkv_and_update(tile)
            P.wait("tensor", S.cV)
            for c in range(8):
                P.op("tensor", lambda e, c=c: e.transpose(
                    ps_t[:, 128 * c:128 * (c + 1)], og_bf[:, 128 * c:128 * (c + 1)],
                    id_bf[:]), S.cT)
            P.wait("vector", S.cT)
            P.op("vector", lambda e: e.tensor_copy(ogf_bf, ps_t[:]), S.cV)
            if p2cut <= 5:
                continue
            P.wait("tensor", S.cV)
            for dg in range(2):
                for kc in range(8):
                    P.op("tensor", lambda e, dg=dg, kc=kc: e.matmul(
                        ps_big[:, dg * 512:(dg + 1) * 512],
                        ogf_bf[:, kc * 128:(kc + 1) * 128],
                        wbuf[:, kc * 1024 + dg * 512:kc * 1024 + (dg + 1) * 512],
                        start=(kc == 0), stop=(kc == 7)),
                        S.cT if (dg == 1 and kc == 7) else None)
            P.wait("vector", S.cT)
            P.wait("vector", S.ldh)
            P.op("vector", lambda e: e.tensor_add(r_sb, ps_big[:, 0:1024], xres[:]), S.cV)
            P.op("vector", lambda e: e.tensor_copy(og_bf, r_sb), S.cV)
            P.wait("tensor", S.cV)
            for c in range(8):
                P.op("tensor", lambda e, c=c: e.transpose(
                    ps_t[:, 128 * c:128 * (c + 1)], og_bf[:, 128 * c:128 * (c + 1)],
                    id_bf[:]), S.cT)
            P.wait("vector", S.cT)
            P.op("vector", lambda e: e.tensor_copy(ogf_bf, ps_t[:]), S.cV)
            sq_bf = gs_tm[:, tile * 1024:(tile + 1) * 1024]
            P.wait("scalar", S.cV)
            P.op("scalar", lambda e, sq_bf=sq_bf: e.activation(
                sq_bf, ogf_bf, AF.Square, scale=1.0 / 32), S.cS)
            P.wait("tensor", S.cS)
            for c in range(8):
                P.op("tensor", lambda e, c=c: e.matmul(
                    ps_o[0:1, 0:128], onescol[:], ogf_bf[:, 128 * c:128 * (c + 1)],
                    start=(c == 0), stop=(c == 7)), None)
            for c in range(8):
                P.op("tensor", lambda e, c=c, sq_bf=sq_bf: e.matmul(
                    ps_o[0:1, 128:256], onescol[:], sq_bf[:, 128 * c:128 * (c + 1)],
                    start=(c == 0), stop=(c == 7)), S.cT if c == 7 else None)
            P.wait("scalar", S.cT)
            P.op("scalar", lambda e: e.activation(murow[:], ps_o[0:1, 0:128], AF.Copy,
                                                  scale=1.0 / 1024), S.cS)
            P.wait("vector", S.cS)
            P.op("vector", lambda e: e.tensor_mul(varrow[:], murow[:], murow[:]), S.cV)
            P.op("vector", lambda e: e.tensor_sub(varrow[:], ps_o[0:1, 128:256], varrow[:]), S.cV)
            P.wait("scalar", S.cV)
            P.op("scalar", lambda e: e.activation(varrow[:], varrow[:], AF.Ln,
                                                  bias=cvals[0:1, 0:1]), S.cS)
            P.op("scalar", lambda e: e.activation(rsrow[:], varrow[:], AF.Exp,
                                                  scale=-0.5), S.cS)
            P.wait("tensor", S.cS)
            P.op("tensor", lambda e: e.matmul(ps_o[0:128, 256:257], murow[:],
                                              cvals[0:1, 7:8],
                                              start=True, stop=True), None)
            P.op("tensor", lambda e: e.matmul(ps_o[0:128, 257:258], rsrow[:],
                                              cvals[0:1, 7:8],
                                              start=True, stop=True), S.cT)
            P.wait("vector", S.cT)
            P.op("vector", lambda e: e.tensor_copy(col1[:], ps_o[0:128, 256:257]), S.cV)
            P.op("vector", lambda e: e.tensor_copy(col2[:], ps_o[0:128, 257:258]), S.cV)
            P.op("vector", lambda e: e.tensor_scalar(
                scr2, r_sb, col1[:], col2[:], ALU.subtract, ALU.mult), S.cV)
            P.wait("sync", S.cV)
            P.op("sync", lambda e, tile=tile: e.dma_start(
                out=x_out[128 * tile:128 * (tile + 1), :], in_=scr2), S.sth, 16)


def emit_ln_tail(nc, P, S, B, r_f32, out_ap, gamma_bc=None, beta_bc=None, to_sbuf=None):
    og_bf, ogf_bf, sq_bf = B["og_bf"], B["ogf_bf"], B["sq_bf"]
    ps_t, ps_st = B["ps_t"], B["ps_st"]
    P.wait("vector", S.cT)
    P.op("vector", lambda e: e.tensor_copy(og_bf, r_f32), S.cV)
    P.wait("tensor", S.cV)
    for c in range(8):
        P.op("tensor", lambda e, c=c: e.transpose(
            ps_t[:, 128 * c:128 * (c + 1)], og_bf[:, 128 * c:128 * (c + 1)],
            B["id_bf"][:]), S.cT)
    P.wait("vector", S.cT)
    P.op("vector", lambda e: e.tensor_copy(ogf_bf, ps_t[:, 0:1024]), S.cV)
    P.wait("scalar", S.cV)
    P.op("scalar", lambda e: e.activation(sq_bf, ogf_bf, AF.Square, scale=1.0 / 32), S.cS)
    P.wait("tensor", S.cS)
    for c in range(8):
        P.op("tensor", lambda e, c=c: e.matmul(
            ps_st[0:1, 0:128], B["onescol"][:], ogf_bf[:, 128 * c:128 * (c + 1)],
            start=(c == 0), stop=(c == 7)), None)
    for c in range(8):
        P.op("tensor", lambda e, c=c: e.matmul(
            ps_st[0:1, 128:256], B["onescol"][:], sq_bf[:, 128 * c:128 * (c + 1)],
            start=(c == 0), stop=(c == 7)), S.cT if c == 7 else None)
    P.wait("scalar", S.cT)
    P.op("scalar", lambda e: e.activation(B["murow"][:], ps_st[0:1, 0:128], AF.Copy,
                                          scale=1.0 / 1024), S.cS)
    P.wait("vector", S.cS)
    P.op("vector", lambda e: e.tensor_mul(B["varrow"][:], B["murow"][:], B["murow"][:]), S.cV)
    P.op("vector", lambda e: e.tensor_sub(B["varrow"][:], ps_st[0:1, 128:256], B["varrow"][:]), S.cV)
    P.wait("scalar", S.cV)
    P.op("scalar", lambda e: e.activation(B["varrow"][:], B["varrow"][:], AF.Ln,
                                          bias=B["cvals"][0:1, 0:1]), S.cS)
    P.op("scalar", lambda e: e.activation(B["rsrow"][:], B["varrow"][:], AF.Exp,
                                          scale=-0.5), S.cS)
    P.wait("tensor", S.cS)
    onef = B["cvals"][0:1, 7:8]
    P.op("tensor", lambda e, onef=onef: e.matmul(
        ps_st[0:128, 256:257], B["murow"][:], onef, start=True, stop=True), None)
    P.op("tensor", lambda e, onef=onef: e.matmul(
        ps_st[0:128, 257:258], B["rsrow"][:], onef, start=True, stop=True), S.cT)
    P.wait("vector", S.cT)
    P.op("vector", lambda e: e.tensor_copy(B["col1"][:], ps_st[0:128, 256:257]), S.cV)
    P.op("vector", lambda e: e.tensor_copy(B["col2"][:], ps_st[0:128, 257:258]), S.cV)
    dst = B["scrO"] if to_sbuf is None else to_sbuf
    P.op("vector", lambda e, dst=dst: e.tensor_scalar(
        dst, r_f32, B["col1"][:], B["col2"][:], ALU.subtract, ALU.mult), S.cV)
    if gamma_bc is not None:
        P.op("vector", lambda e, dst=dst: e.tensor_mul(dst, dst, gamma_bc), S.cV)
        P.op("vector", lambda e, dst=dst: e.tensor_add(dst, dst, beta_bc), S.cV)
    if out_ap is not None:
        P.wait("sync", S.cV)
        P.op("sync", lambda e, dst=dst: e.dma_start(out=out_ap, in_=dst), S.sth, 16)


def emit_xfm_build(nc, P, S, B, x_in, x_fm):
    xres, stage2, ps_t, id_bf = B["xres"], B["stage2"], B["ps_t"], B["id_bf"]
    for t in range(NT):
        P.wait("sync", S.cV)
        P.op("sync", lambda e, t=t: e.dma_start(
            out=xres[:], in_=x_in[128 * t:128 * (t + 1), :]), S.ldh, 16)
        P.wait("vector", S.ldh)
        P.wait("vector", S.cT)
        P.op("vector", lambda e: e.tensor_copy(stage2, xres[:]), S.cV)
        P.wait("tensor", S.cV)
        for c in range(8):
            P.op("tensor", lambda e, c=c: e.transpose(
                ps_t[:, 128 * c:128 * (c + 1)], stage2[:, 128 * c:128 * (c + 1)],
                id_bf[:]), S.cT)
        P.wait("vector", S.cT)
        dst = x_fm[:].rearrange("p (fc n) -> p fc n", n=2048)[:, :, t * 128:(t + 1) * 128]
        P.op("vector", lambda e, d=dst: e.tensor_copy(d, ps_t[:, 0:1024].rearrange(
            "p (fc n) -> p fc n", n=128)), S.cV)


def emit_l0_ffn(nc, P, S, es, W, CN, x_in, x_out, tag="f0"):
    sb = lambda name, shape, dt: es.enter_context(nc.sbuf_tensor(name + tag, shape, dt))
    psa = lambda name, shape, dt: es.enter_context(nc.psum_tensor(name + tag, shape, dt))

    x_fm = sb("x_fm", [128, 16384], BF16)
    y_acc = sb("y_acc", [128, 16384], F32)
    g_fm = sb("g_fm", [128, 4096], BF16)
    w1b = sb("w1b", [128, 8192], BF16)
    w2b = sb("w2b", [128, 8192], BF16)
    rwf = sb("rwf", [128, 64], BF16)
    b1r = sb("b1r", [1, 4096], BF16)
    b2bc = sb("b2bc", [128, 1024], F32)
    ones512 = sb("ones512", [1, 512], BF16)
    xres = sb("xres", [128, 1024], F32)
    stage2 = sb("stage2", [128, 1024], BF16)
    scr1 = sb("scr1", [128, 1024], F32)
    scr2 = sb("scr2", [128, 1024], F32)
    rt = sb("rt", [128, 128], F32)
    fcol = sb("fcol", [128, 16], F32)
    m1 = sb("m1", [128, 16], F32)
    m2 = sb("m2", [128, 16], F32)
    tmp8 = sb("tmp8", [128, 128], F32)
    og_bf = sb("og_bf", [128, 1024], BF16)
    ogf_bf = sb("ogf_bf", [128, 1024], BF16)
    sq_bf = sb("sq_bf", [128, 1024], BF16)
    onescol = sb("onescol", [128, 1], BF16)
    murow = sb("murow", [1, 128], F32)
    rsrow = sb("rsrow", [1, 128], F32)
    varrow = sb("varrow", [1, 128], F32)
    col1 = sb("col1", [128, 1], F32)
    col2 = sb("col2", [128, 1], F32)
    id_bf = sb("id_bf", [128, 128], BF16)
    cvals = sb("cvals", [128, 8], F32)
    register_consts(nc, cvals)

    ps_h = psa("ps_h", [128, 2048], F32)
    ps_y = psa("ps_y", [128, 1024], F32)
    ps_st = psa("ps_st", [128, 512], F32)
    ps_t = psa("ps_t", [128, 1024], BF16)

    B = dict(og_bf=og_bf[:], ogf_bf=ogf_bf[:], sq_bf=sq_bf[:], ps_t=ps_t,
             ps_st=ps_st, id_bf=id_bf, onescol=onescol, murow=murow,
             rsrow=rsrow, varrow=varrow, col1=col1, col2=col2, cvals=cvals,
             scrO=scr2[:], xres=xres, stage2=stage2[:])

    for dst, srcap in [(id_bf[:], CN["ident"]), (onescol[:], CN["ones_col"]),
                       (b1r[:], W["b1"]), (ones512[:], CN["ones_row2048"][:, 0:512]),
                       (rwf[:].rearrange("p (kc e) -> p kc e", e=8),
                        W["router_w"].rearrange("(kc p) e -> p kc e", p=128))]:
        P.op("gpsimd", lambda e, d=dst, s=srcap: e.dma_start(out=d, in_=s), S.lds, 16)
    P.op("sync", lambda e: e.dma_start(out=cvals[:], in_=CN["constvals"]), S.ldh, 16)
    P.op("sync", lambda e: e.dma_start(out=b2bc[:], in_=W["b2_t"]), S.ldh, 16)
    P.wait("tensor", S.lds)
    P.wait("tensor", S.ldh)

    emit_xfm_build(nc, P, S, B, x_in, x_fm)

    P.wait("scalar", S.ldh)
    P.op("scalar", lambda e: e.activation(tmp8[:, 0:8], cvals[:, 0:8], AF.Exp), S.cS)
    P.op("scalar", lambda e: e.activation(tmp8[:, 0:1], cvals[:, 7:8], AF.Ln), S.cS)
    P.op("scalar", lambda e: e.activation(tmp8[:, 0:8], cvals[:, 0:8], AF.Gelu_apprx_tanh), S.cS)
    P.op("scalar", lambda e: e.activation(tmp8[:, 0:8], cvals[:, 0:8], AF.Square), S.cS)

    P.wait("tensor", S.cV)
    for t in range(NT):
        P.wait("tensor", S.cV)
        for kc in range(8):
            P.op("tensor", lambda e, t=t, kc=kc: e.matmul(
                ps_y[:, 8 * t:8 * (t + 1)],
                x_fm[:, kc * 2048 + t * 128:kc * 2048 + (t + 1) * 128],
                rwf[:, kc * 8:(kc + 1) * 8],
                start=(kc == 0), stop=(kc == 7)), S.cT if kc == 7 else None)
    P.wait("vector", S.cT)
    P.op("vector", lambda e: e.tensor_copy(rt[:], ps_y[:, 0:128]), S.cV)
    r3 = rt[:].rearrange("p (t e) -> p t e", e=8)
    t3 = tmp8[:].rearrange("p (t e) -> p t e", e=8)
    m1v = m1[:].rearrange("p (t o) -> p t o", o=1)
    m2v = m2[:].rearrange("p (t o) -> p t o", o=1)

    def tree(op, srcv, outv):
        P.op("vector", lambda e: e.tensor_tensor(t3[:, :, 0:4], srcv[:, :, 0:4], srcv[:, :, 4:8], op), S.cV)
        P.op("vector", lambda e: e.tensor_tensor(t3[:, :, 0:2], t3[:, :, 0:2], t3[:, :, 2:4], op), S.cV)
        P.op("vector", lambda e: e.tensor_tensor(outv, t3[:, :, 0:1], t3[:, :, 1:2], op), S.cV)

    tree(ALU.max, r3, m1v)
    P.op("vector", lambda e: e.tensor_sub(r3, r3, m1v.broadcast_to([128, 16, 8])), S.cV)
    P.wait("scalar", S.cV)
    P.op("scalar", lambda e: e.activation(rt[:], rt[:], AF.Exp), S.cS)
    P.wait("vector", S.cS)
    tree(ALU.add, r3, m2v)
    P.wait("scalar", S.cV)
    P.op("scalar", lambda e: e.activation(m2[:], m2[:], AF.Ln), S.cS)
    P.op("scalar", lambda e: e.activation(m2[:], m2[:], AF.Exp, scale=-1.0), S.cS)
    P.wait("vector", S.cS)
    P.op("vector", lambda e: e.tensor_mul(r3, r3, m2v.broadcast_to([128, 16, 8])), S.cV)
    tree(ALU.max, r3, m1v)
    P.op("vector", lambda e: e.tensor_tensor(t3, r3, m1v.broadcast_to([128, 16, 8]), ALU.is_ge), S.cV)
    P.op("vector", lambda e: e.scalar_tensor_tensor(t3, t3, -1e30, r3, ALU.mult, ALU.add), S.cV)
    P.op("vector", lambda e: e.tensor_tensor(t3[:, :, 0:4], t3[:, :, 0:4], t3[:, :, 4:8], ALU.max), S.cV)
    P.op("vector", lambda e: e.tensor_tensor(t3[:, :, 0:2], t3[:, :, 0:2], t3[:, :, 2:4], ALU.max), S.cV)
    P.op("vector", lambda e: e.tensor_tensor(m2v, t3[:, :, 0:1], t3[:, :, 1:2], ALU.max), S.cV)
    P.op("vector", lambda e: e.tensor_add(m1[:], m1[:], m2[:]), S.cV)
    P.wait("scalar", S.cV)
    P.op("scalar", lambda e: e.activation(m2[:], m1[:], AF.Ln), S.cS)
    P.op("scalar", lambda e: e.activation(m2[:], m2[:], AF.Exp, scale=-1.0), S.cS)
    P.wait("vector", S.cS)
    P.op("vector", lambda e: e.tensor_mul(fcol[:], m1[:], m2[:]), S.cV)

    for half in range(4):
        h0 = half * 1024
        P.wait("gpsimd", S.cT)
        s1 = W["w1"][:, h0:h0 + 1024].rearrange("(kc p) n -> p kc n", p=128)
        P.op("gpsimd", lambda e, s=s1: e.dma_start(
            out=w1b[:].rearrange("p (kc n) -> p kc n", n=1024), in_=s), S.lds, 16)
        s2 = W["w2"][h0:h0 + 1024, :].rearrange("(hc p) n -> p hc n", p=128)
        P.op("gpsimd", lambda e, s=s2: e.dma_start(
            out=w2b[:].rearrange("p (hc n) -> p hc n", n=1024), in_=s), S.lds, 16)
        P.wait("tensor", S.lds)
        for tg in range(4):
            for hc in range(8):
                slot = hc % 4
                po = ps_h[:, 512 * slot:512 * (slot + 1)]
                P.wait("tensor", S.cS)
                for kc in range(8):
                    P.op("tensor", lambda e, po=po, kc=kc, hc=hc, tg=tg: e.matmul(
                        po, w1b[:, kc * 1024 + hc * 128:kc * 1024 + (hc + 1) * 128],
                        x_fm[:, kc * 2048 + tg * 512:kc * 2048 + (tg + 1) * 512],
                        start=(kc == 0), stop=False), None)
                P.op("tensor", lambda e, po=po, hc=hc, h0=h0: e.matmul(
                    po, b1r[0:1, h0 + hc * 128:h0 + (hc + 1) * 128], ones512[:],
                    start=False, stop=True), S.cT)
                P.wait("scalar", S.cT)
                P.op("scalar", lambda e, po=po, hc=hc: e.activation(
                    g_fm[:, hc * 512:(hc + 1) * 512], po, AF.Gelu_apprx_tanh), S.cS)
            P.wait("tensor", S.cS)
            for tt in range(4):
                tile = tg * 4 + tt
                for dg in range(2):
                    po = ps_y[:, dg * 512:(dg + 1) * 512]
                    P.wait("tensor", S.cV)
                    for hc in range(8):
                        P.op("tensor", lambda e, po=po, hc=hc, tt=tt, dg=dg: e.matmul(
                            po, g_fm[:, hc * 512 + tt * 128:hc * 512 + tt * 128 + 128],
                            w2b[:, hc * 1024 + dg * 512:hc * 1024 + (dg + 1) * 512],
                            start=(hc == 0), stop=(hc == 7)), S.cT if hc == 7 else None)
                    P.wait("vector", S.cT)
                    ya = y_acc[:, tile * 1024 + dg * 512:tile * 1024 + (dg + 1) * 512]
                    if half == 0:
                        P.op("vector", lambda e, po=po, ya=ya: e.tensor_copy(ya, po), S.cV)
                    else:
                        P.op("vector", lambda e, po=po, ya=ya: e.tensor_add(ya, ya, po), S.cV)

    if DBG_L0F:
        dfc = nc.dram_tensor("dfc", [128, 16], F32, kind="ExternalOutput").ap()
        dya = nc.dram_tensor("dya", [128, 16384], F32, kind="ExternalOutput").ap()
        drt = nc.dram_tensor("drt", [128, 128], F32, kind="ExternalOutput").ap()
        P.wait("gpsimd", S.cV)
        P.op("gpsimd", lambda e: e.dma_start(out=dfc, in_=fcol[:]), S.lds, 16)
        P.op("gpsimd", lambda e: e.dma_start(out=dya, in_=y_acc[:]), S.lds, 16)
        P.op("gpsimd", lambda e: e.dma_start(out=drt, in_=rt[:]), S.lds, 16)
        P.wait("vector", S.lds)

    for tile in range(NT):
        P.wait("sync", S.cV)
        P.op("sync", lambda e, tile=tile: e.dma_start(
            out=xres[:], in_=x_in[128 * tile:128 * (tile + 1), :]), S.ldh, 16)
        P.wait("vector", S.ldh)
        P.wait("vector", S.sth)
        P.op("vector", lambda e, tile=tile: e.tensor_add(
            scr1[:], y_acc[:, tile * 1024:(tile + 1) * 1024], b2bc[:]), S.cV)
        P.op("vector", lambda e, tile=tile: e.tensor_single_scalar(
            scr1[:], scr1[:], fcol[:, tile:tile + 1], ALU.mult), S.cV)
        P.op("vector", lambda e: e.tensor_add(scr1[:], scr1[:], xres[:]), S.cV)
        emit_ln_tail(nc, P, S, B, scr1[:],
                     x_out[128 * tile:128 * (tile + 1), :])


def emit_l1_attn(nc, P, S, es, W, CN, CI, x_in, x_out, tag="a1", upto=99, fused=False):
    sb = lambda name, shape, dt: es.enter_context(nc.sbuf_tensor(name + tag, shape, dt))
    psa = lambda name, shape, dt: es.enter_context(nc.psum_tensor(name + tag, shape, dt))
    from concourse.bass import IndirectOffsetOnAxis

    x_fm = sb("x_fm", [128, 16384], BF16)
    big2 = sb("big2", [128, 16384], BF16)
    kf = sb("kf", [128, 4096], BF16)
    v_sb = sb("v_sb", [128, 4096], BF16)
    V65 = sb("V65", [128, 4160], BF16)
    E_sb = sb("E_sb", [128, 2048], BF16)
    obf = sb("obf", [65, 512], BF16)
    zrow = sb("zrow", [1, 512], F32)
    zr = sb("zr", [128, 4], F32)
    wbuf = sb("wbuf", [128, 8192], BF16)
    xres = sb("xres", [128, 1024], F32)
    stage2 = sb("stage2", [128, 1024], BF16)
    scr1 = sb("scr1", [128, 1024], F32)
    scr2 = sb("scr2", [128, 1024], F32)
    scr3 = sb("scr3", [128, 1024], F32)
    att_bf = sb("att_bf", [128, 1024], BF16)
    lnfm = sb("lnfm", [128, 1024], BF16)
    og_bf = sb("og_bf", [128, 1024], BF16)
    ogf_bf = sb("ogf_bf", [128, 1024], BF16)
    sq_bf = sb("sq_bf", [128, 1024], BF16)
    bqr = sb("bqr", [1, 1024], BF16)
    bkr = sb("bkr", [1, 256], BF16)
    ones512 = sb("ones512", [1, 512], BF16)
    bvr = sb("bvr", [1, 256], BF16)
    bor = sb("bor", [1, 1024], BF16)
    lng_bc = sb("lng_bc", [128, 1024], F32)
    lnb_bc = sb("lnb_bc", [128, 1024], F32)
    dmask = sb("dmask", [128, 2048], BF16)
    onescol = sb("onescol", [128, 1], BF16)
    onesrow = sb("onesrow", [1, 128], BF16)
    murow = sb("murow", [1, 128], F32)
    rsrow = sb("rsrow", [1, 128], F32)
    varrow = sb("varrow", [1, 128], F32)
    col1 = sb("col1", [128, 1], F32)
    col2 = sb("col2", [128, 1], F32)
    id_bf = sb("id_bf", [128, 128], BF16)
    kidx = sb("kidx", [64, 8], mybir.dt.int32)
    qidx = sb("qidx", [64, 32], mybir.dt.int32)
    vidx = sb("vidx", [128, 64], mybir.dt.int32)
    oidx = sb("oidx", [128, 64], mybir.dt.int32)
    cvals = sb("cvals", [128, 8], F32)
    register_consts(nc, cvals)

    K_sb = x_fm[0:64, 0:8192]
    qh_sb = x_fm[0:64, 8192:16384]

    ps_sc = psa("ps_sc", [128, 2048], F32)
    ps_o = psa("ps_o", [128, 512], F32)
    ps_st = psa("ps_st", [128, 512], F32)
    ps_t = psa("ps_t", [128, 1024], BF16)

    B = dict(og_bf=og_bf[:], ogf_bf=ogf_bf[:], sq_bf=sq_bf[:], ps_t=ps_t,
             ps_st=ps_st, id_bf=id_bf, onescol=onescol, murow=murow,
             rsrow=rsrow, varrow=varrow, col1=col1, col2=col2, cvals=cvals,
             scrO=scr2[:], xres=xres, stage2=stage2[:])

    qkv_d = nc.dram_tensor("qkv_d" + tag, [1536, 2048], BF16).ap()
    qkv_g = nc.dram_tensor("qkv_g" + tag, [6144, 2048], BF16).ap()

    for dst, srcap in [(id_bf[:], CN["ident"]), (onescol[:], CN["ones_col"]),
                       (onesrow[:], CN["ones_row"]), (dmask[:], CN["diag_mask"]),
                       (bvr[:], W["bv"]), (bor[:], W["bo"]), (bqr[:], W["bq"]),
                       (bkr[:], W["bk"]), (ones512[:], CN["ones_row2048"][:, 0:512])]:
        P.op("gpsimd", lambda e, d=dst, s=srcap: e.dma_start(out=d, in_=s), S.lds, 16)
    for dst, srcap in [(cvals[:], CN["constvals"]), (lng_bc[:], W["lng_t"]),
                       (lnb_bc[:], W["lnb_t"]), (kidx[:], CI["kidx"]),
                       (qidx[:], CI["qidx"]), (vidx[:], CI["vidx"]),
                       (oidx[:], CI["oidx"])]:
        P.op("sync", lambda e, d=dst, s=srcap: e.dma_start(out=d, in_=s), S.ldh, 16)
    P.wait("tensor", S.lds)
    P.wait("tensor", S.ldh)

    if upto != 16.5:
        emit_xfm_build(nc, P, S, B, x_in, x_fm)
    P.wait("scalar", S.ldh)
    P.op("scalar", lambda e: e.activation(scr1[:, 0:8], cvals[:, 0:8], AF.Exp), S.cS)
    P.op("scalar", lambda e: e.activation(scr1[:, 0:1], cvals[:, 7:8], AF.Ln), S.cS)
    P.op("scalar", lambda e: e.activation(scr1[:, 0:8], cvals[:, 0:8], AF.Square), S.cS)


    def stream_w(name):
        P.wait("gpsimd", S.cT)
        P.op("gpsimd", lambda e, s=W[name].rearrange("(kc p) n -> p kc n", p=128),
             w=W[name].shape[1]: e.dma_start(
                 out=wbuf[:, 0:8 * w].rearrange("p (kc n) -> p kc n", n=w), in_=s),
             S.lds, 16)
        P.wait("tensor", S.lds)

    if upto == 16.5:
        P.wait("sync", S.cS)
        P.wait("sync", S.cV)
        P.op("sync", lambda e: e.dma_start(
            out=q_d.rearrange("(oc p) t -> p oc t", p=128),
            in_=big2[:].rearrange("p (oc t) -> p oc t", t=2048)), S.sth, 16)
        P.wait("gpsimd", S.sth)
        for din, dout in [(q_d, q_g), (k_d, k_g), (v_d, v_g)]:
            P.op("gpsimd", lambda e, a=din, b=dout: e.collective_compute(
                "AllGather", mybir.AluOpType.bypass,
                replica_groups=[[0, 1, 2, 3], [4, 5, 6, 7]],
                ins=[a.opt()], outs=[b.opt()]), S.cG, 1)
        P.wait("gpsimd", S.cG)
        return
    stream_w("Wq")
    P.wait("tensor", S.cV)
    for oc in range(8):
        for tg in range(4):
            slot = (oc * 4 + tg) % 4
            po = ps_sc[:, 512 * slot:512 * (slot + 1)]
            P.wait("tensor", S.cS)
            P.wait("tensor", S.cV)
            for kc in range(8):
                P.op("tensor", lambda e, po=po, kc=kc, oc=oc, tg=tg: e.matmul(
                    po, wbuf[:, kc * 1024 + oc * 128:kc * 1024 + (oc + 1) * 128],
                    x_fm[:, kc * 2048 + tg * 512:kc * 2048 + (tg + 1) * 512],
                    start=(kc == 0), stop=False), None)
            P.op("tensor", lambda e, po=po, oc=oc: e.matmul(
                po, bqr[0:1, oc * 128:(oc + 1) * 128], ones512[:],
                start=False, stop=True), S.cT)
            P.wait("scalar", S.cT)
            P.op("scalar", lambda e, po=po, oc=oc, tg=tg: e.activation(
                big2[:, oc * 2048 + tg * 512:oc * 2048 + (tg + 1) * 512], po,
                AF.Copy, scale=0.125), S.cS)
    stream_w("Wk")
    for c in range(2):
        for tg in range(4):
            slot = (c * 4 + tg) % 4
            po = ps_sc[:, 512 * slot:512 * (slot + 1)]
            P.wait("tensor", S.cS)
            for kc in range(8):
                P.op("tensor", lambda e, po=po, kc=kc, c=c, tg=tg: e.matmul(
                    po, wbuf[:, kc * 256 + c * 128:kc * 256 + (c + 1) * 128],
                    x_fm[:, kc * 2048 + tg * 512:kc * 2048 + (tg + 1) * 512],
                    start=(kc == 0), stop=False), None)
            P.op("tensor", lambda e, po=po, c=c: e.matmul(
                po, bkr[0:1, c * 128:(c + 1) * 128], ones512[:],
                start=False, stop=True), S.cT)
            P.wait("scalar", S.cT)
            P.op("scalar", lambda e, po=po, c=c, tg=tg: e.activation(
                kf[:, c * 2048 + tg * 512:c * 2048 + (tg + 1) * 512], po,
                AF.Copy), S.cS)
    stream_w("Wv")
    for t in range(NT):
        slot = t % 4
        po = ps_sc[:, 512 * slot:512 * slot + 256]
        P.wait("tensor", S.cV)
        P.wait("tensor", S.cS)
        for kc in range(8):
            P.op("tensor", lambda e, po=po, kc=kc, t=t: e.matmul(
                po, x_fm[:, kc * 2048 + t * 128:kc * 2048 + (t + 1) * 128],
                wbuf[:, kc * 256:(kc + 1) * 256],
                start=(kc == 0), stop=False), None)
        P.op("tensor", lambda e, po=po: e.matmul(
            po, onesrow[:], bvr[:], start=False, stop=True), S.cT)
        P.wait("vector", S.cT)
        P.op("vector", lambda e, po=po, t=t: e.tensor_copy(
            v_sb[:, t * 256:(t + 1) * 256], po), S.cV)

    P.wait("sync", S.cS)
    P.wait("sync", S.cV)
    P.op("sync", lambda e: e.dma_start(
        out=qkv_d[0:1024, :].rearrange("(oc p) t -> p oc t", p=128),
        in_=big2[:].rearrange("p (oc t) -> p oc t", t=2048)), S.sth, 16)
    P.op("sync", lambda e: e.dma_start(
        out=qkv_d[1024:1280, :].rearrange("(c p) t -> p c t", p=128),
        in_=kf[:].rearrange("p (c t) -> p c t", t=2048)), S.sth, 16)
    P.op("sync", lambda e: e.dma_start(
        out=qkv_d[1280:1536, :].rearrange("r (t2 n) -> (r t2) n", n=256)
        .rearrange("(t p) n -> p t n", p=128),
        in_=v_sb[:].rearrange("p (t n) -> p t n", n=256)), S.sth, 16)
    P.wait("gpsimd", S.sth)
    P.op("gpsimd", lambda e: e.collective_compute(
        "AllGather", mybir.AluOpType.bypass,
        replica_groups=[[0, 1, 2, 3], [4, 5, 6, 7]],
        ins=[qkv_d.opt()], outs=[qkv_g.opt()]), S.cG, 1)

    P.wait("vector", S.cG, S.cG.n)
    P.wait("vector", S.sth)
    P.op("vector", lambda e: e.memset(
        V65[:].rearrange("p (kt c) -> p kt c", c=65)[:, :, 64:65], 1.0), S.cV)
    P.wait("gpsimd", S.cG)
    P.wait("gpsimd", S.cV)
    for r in range(4):
        for half in range(2):
            oct_ = r if half == 0 else 7 - r
            P.op("gpsimd", lambda e, r=r, half=half, oct_=oct_: e.indirect_dma_start(
                K_sb[:, oct_ * 1024:(oct_ + 1) * 1024], None,
                qkv_g[:, :],
                IndirectOffsetOnAxis(kidx[:, 2 * r + half:2 * r + half + 1], 1)), S.lds, 16)
    for kt in range(64):
        P.op("gpsimd", lambda e, kt=kt: e.indirect_dma_start(
            V65[:, kt * 65:kt * 65 + 64], None,
            qkv_g[:, :],
            IndirectOffsetOnAxis(vidx[:, kt:kt + 1], 1)), S.lds, 16)

    if upto <= 3:
        return
    nqb = 16
    for gp in range(4):
        P.wait("gpsimd", S.cT)
        for r in range(4):
            for half in range(2):
                oct_ = r if half == 0 else 7 - r
                P.op("gpsimd", lambda e, gp=gp, r=r, half=half, oct_=oct_:
                     e.indirect_dma_start(
                         qh_sb[:, oct_ * 1024:(oct_ + 1) * 1024], None,
                         qkv_g[:, :],
                         IndirectOffsetOnAxis(
                             qidx[:, gp * 8 + 2 * r + half:gp * 8 + 2 * r + half + 1], 1)),
                     S.lds, 16)
        P.wait("tensor", S.lds)
        for qb in range(nqb):
            for ktg in range(qb + 1):
                P.wait("tensor", S.cS)
                P.wait("tensor", S.cV)
                for j in range(4):
                    kt = ktg * 4 + j
                    P.op("tensor", lambda e, kt=kt, j=j, qb=qb: e.matmul(
                        ps_sc[:, j * 512:(j + 1) * 512],
                        K_sb[:, kt * 128:(kt + 1) * 128],
                        qh_sb[:, qb * 512:(qb + 1) * 512],
                        start=True, stop=True), S.cT if j == 3 else None)
                P.wait("scalar", S.cT)
                P.op("scalar", lambda e: e.activation(E_sb[:], ps_sc[:], AF.Exp), S.cS)
                if ktg == qb:
                    P.wait("vector", S.cS)
                    P.op("vector", lambda e: e.tensor_mul(E_sb[:], E_sb[:], dmask[:]), S.cV)
                    P.wait("tensor", S.cV)
                P.wait("tensor", S.cS)
                for j in range(4):
                    kt = ktg * 4 + j
                    P.op("tensor", lambda e, kt=kt, j=j, qb=qb, ktg=ktg: e.matmul(
                        ps_o[:, 0:512][0:65, :] if False else ps_o[0:65, 0:512],
                        V65[:, kt * 65:(kt + 1) * 65],
                        E_sb[:, j * 512:(j + 1) * 512],
                        start=(ktg == 0 and j == 0), stop=(ktg == qb and j == 3)),
                        S.cT if (ktg == qb and j == 3) else None)
            P.wait("scalar", S.cT)
            P.wait("scalar", S.cV)
            P.op("scalar", lambda e: e.activation(obf[:], ps_o[0:65, 0:512], AF.Copy), S.cS)
            P.wait("vector", S.cS)
            P.op("vector", lambda e: e.tensor_copy(zrow[:], ps_o[64:65, 0:512]), S.cV)
            P.wait("tensor", S.cV)
            for s_ in range(4):
                P.op("tensor", lambda e, s_=s_: e.matmul(
                    ps_st[0:128, s_:s_ + 1], zrow[0:1, 128 * s_:128 * (s_ + 1)],
                    cvals[0:1, 7:8], start=True, stop=True),
                    S.cT if s_ == 3 else None)
            P.wait("scalar", S.cT)
            P.op("scalar", lambda e: e.activation(zr[:, 0:4], ps_st[0:128, 0:4], AF.Ln), S.cS)
            P.op("scalar", lambda e: e.activation(zr[:, 0:4], zr[:, 0:4], AF.Exp,
                                                  scale=-1.0), S.cS)
            P.wait("tensor", S.cS)
            for s_ in range(4):
                P.op("tensor", lambda e, s_=s_: e.transpose(
                    ps_t[:, 64 * s_:64 * (s_ + 1)], obf[0:64, 128 * s_:128 * (s_ + 1)],
                    id_bf[0:64, 0:64]), S.cT if s_ == 3 else None)
            P.wait("vector", S.cT)
            for s_ in range(4):
                gt = qb * 4 + s_
                P.op("vector", lambda e, s_=s_, gt=gt, gp=gp: e.tensor_single_scalar(
                    big2[:, gt * 256 + gp * 64:gt * 256 + (gp + 1) * 64],
                    ps_t[:, 64 * s_:64 * (s_ + 1)], zr[:, s_:s_ + 1], ALU.mult), S.cV)

    if fused:
        o_d = nc.dram_tensor("o_d" + tag, [8192, 256], BF16).ap()
        o_g = nc.dram_tensor("o_g" + tag, [32768, 256], BF16).ap()
        P.wait("sync", S.cV)
        P.op("sync", lambda e: e.dma_start(
            out=o_d.rearrange("(t p) n -> p t n", p=128),
            in_=big2[:].rearrange("p (t n) -> p t n", n=256)), S.sth, 16)
        P.wait("gpsimd", S.sth)
        P.op("gpsimd", lambda e: e.collective_compute(
            "AllGather", mybir.AluOpType.bypass,
            replica_groups=[[0, 1, 2, 3], [4, 5, 6, 7]],
            ins=[o_d.opt()], outs=[o_g.opt()]), S.cG, 1)
        return o_g
    o_out = nc.dram_tensor("o_out" + tag, [8192, 256], BF16, kind="ExternalOutput").ap()
    P.wait("sync", S.cV)
    P.op("sync", lambda e: e.dma_start(
        out=o_out.rearrange("(t p) n -> p t n", p=128),
        in_=big2[:].rearrange("p (t n) -> p t n", n=256)), S.sth, 16)


def emit_l1_attn_fin(nc, P, S, es, W, CN, attn_in, x_in, x_out, tag="a1f",
                     o_gather=None):
    sb = lambda name, shape, dt: es.enter_context(nc.sbuf_tensor(name + tag, shape, dt))
    psa = lambda name, shape, dt: es.enter_context(nc.psum_tensor(name + tag, shape, dt))
    from concourse.bass import IndirectOffsetOnAxis
    wbuf = sb("wbuf", [128, 8192], BF16)
    xres = sb("xres", [128, 1024], F32)
    stage2 = sb("stage2", [128, 1024], BF16)
    scr1 = sb("scr1", [128, 1024], F32)
    scr2 = sb("scr2", [128, 1024], F32)
    scr3 = sb("scr3", [128, 1024], F32)
    att_bf = sb("att_bf", [128, 1024], BF16)
    lnfm = sb("lnfm", [128, 1024], BF16)
    og_bf = sb("og_bf", [128, 1024], BF16)
    ogf_bf = sb("ogf_bf", [128, 1024], BF16)
    sq_bf = sb("sq_bf", [128, 1024], BF16)
    if o_gather is not None:
        att4 = sb("att4", [128, 1024], BF16)
        aidx = sb("aidx", [128, 64], mybir.dt.int32)
    bor = sb("bor", [1, 1024], BF16)
    lng_bc = sb("lng_bc", [128, 1024], F32)
    lnb_bc = sb("lnb_bc", [128, 1024], F32)
    onescol = sb("onescol", [128, 1], BF16)
    onesrow = sb("onesrow", [1, 128], BF16)
    murow = sb("murow", [1, 128], F32)
    rsrow = sb("rsrow", [1, 128], F32)
    varrow = sb("varrow", [1, 128], F32)
    col1 = sb("col1", [128, 1], F32)
    col2 = sb("col2", [128, 1], F32)
    id_bf = sb("id_bf", [128, 128], BF16)
    cvals = sb("cvals", [128, 8], F32)
    register_consts(nc, cvals)
    ps_sc = psa("ps_sc", [128, 1024], F32)
    ps_st = psa("ps_st", [128, 512], F32)
    ps_t = psa("ps_t", [128, 1024], BF16)
    B = dict(og_bf=og_bf[:], ogf_bf=ogf_bf[:], sq_bf=sq_bf[:], ps_t=ps_t,
             ps_st=ps_st, id_bf=id_bf, onescol=onescol, murow=murow,
             rsrow=rsrow, varrow=varrow, col1=col1, col2=col2, cvals=cvals,
             scrO=scr2[:], xres=xres, stage2=stage2[:])
    for dst, srcap in [(id_bf[:], CN["ident"]), (onescol[:], CN["ones_col"]),
                       (onesrow[:], CN["ones_row"]), (bor[:], W["bo"])]:
        P.op("gpsimd", lambda e, d=dst, s=srcap: e.dma_start(out=d, in_=s), S.lds, 16)
    if o_gather is not None:
        P.op("gpsimd", lambda e: e.dma_start(out=aidx[:], in_=o_gather[1]), S.lds, 16)
    for dst, srcap in [(cvals[:], CN["constvals"]), (lng_bc[:], W["lng_t"]),
                       (lnb_bc[:], W["lnb_t"])]:
        P.op("sync", lambda e, d=dst, s=srcap: e.dma_start(out=d, in_=s), S.ldh, 16)
    P.op("gpsimd", lambda e: e.dma_start(
        out=wbuf[:].rearrange("p (kc n) -> p kc n", n=1024),
        in_=W["Wo"].rearrange("(kc p) n -> p kc n", p=128)), S.lds, 16)
    P.wait("tensor", S.lds)
    P.wait("tensor", S.ldh)
    P.wait("vector", S.lds)
    P.wait("vector", S.ldh)
    P.wait("scalar", S.ldh)
    P.op("scalar", lambda e: e.activation(scr1[:, 0:8], cvals[:, 0:8], AF.Exp), S.cS)
    P.op("scalar", lambda e: e.activation(scr1[:, 0:1], cvals[:, 7:8], AF.Ln), S.cS)
    P.op("scalar", lambda e: e.activation(scr1[:, 0:8], cvals[:, 0:8], AF.Square), S.cS)

    abv = att_bf[:].rearrange("p (g r c) -> p g r c", r=4, c=64)
    if o_gather is not None:
        P.wait("gpsimd", S.cG)
    for tile in range(NT):
        if o_gather is None:
            P.wait("sync", S.cV)
            P.op("sync", lambda e, tile=tile: e.dma_start(
                out=att_bf[:], in_=attn_in[128 * tile:128 * (tile + 1), :]), S.ldh, 16)
            P.wait("vector", S.ldh)
        else:
            P.wait("gpsimd", S.cV)
            for r in range(4):
                P.op("gpsimd", lambda e, tile=tile, r=r: e.indirect_dma_start(
                    att4[:, 256 * r:256 * (r + 1)], None, o_gather[0][:, :],
                    IndirectOffsetOnAxis(aidx[:, 4 * tile + r:4 * tile + r + 1], 1)),
                    S.lds, 16)
            P.wait("vector", S.lds)
            for r in range(4):
                P.op("vector", lambda e, r=r: e.tensor_copy(
                    abv[:, :, r, :],
                    att4[:, 256 * r:256 * (r + 1)].rearrange("p (g c) -> p g c", c=64)),
                    S.cV)
        P.wait("vector", S.sth)
        P.op("vector", lambda e: e.tensor_copy(scr1[:], att_bf[:]), S.cV)
        emit_ln_tail(nc, P, S, B, scr1[:], None, gamma_bc=lng_bc[:],
                     beta_bc=lnb_bc[:], to_sbuf=scr3[:])
        P.wait("vector", S.cS)
        P.op("vector", lambda e: e.tensor_copy(stage2[:], scr3[:]), S.cV)
        P.wait("tensor", S.cV)
        for c in range(8):
            P.op("tensor", lambda e, c=c: e.transpose(
                ps_t[:, 128 * c:128 * (c + 1)], stage2[:, 128 * c:128 * (c + 1)],
                id_bf[:]), S.cT)
        P.wait("vector", S.cT)
        P.op("vector", lambda e: e.tensor_copy(lnfm[:], ps_t[:, 0:1024]), S.cV)
        P.wait("tensor", S.cV)
        P.wait("sync", S.cV)
        P.op("sync", lambda e, tile=tile: e.dma_start(
            out=xres[:], in_=x_in[128 * tile:128 * (tile + 1), :]), S.ldh, 16)
        for dg in range(2):
            po = ps_sc[:, dg * 512:(dg + 1) * 512]
            for kc in range(8):
                P.op("tensor", lambda e, po=po, kc=kc, dg=dg: e.matmul(
                    po, lnfm[:, kc * 128:(kc + 1) * 128],
                    wbuf[:, kc * 1024 + dg * 512:kc * 1024 + (dg + 1) * 512],
                    start=(kc == 0), stop=False), None)
            P.op("tensor", lambda e, po=po, dg=dg: e.matmul(
                po, onesrow[:], bor[0:1, dg * 512:(dg + 1) * 512],
                start=False, stop=True), S.cT)
        P.wait("vector", S.cT)
        P.wait("vector", S.ldh)
        P.op("vector", lambda e: e.tensor_add(scr1[:], ps_sc[:, 0:1024], xres[:]), S.cV)
        emit_ln_tail(nc, P, S, B, scr1[:], x_out[128 * tile:128 * (tile + 1), :])


def emit_l1_ffn(nc, P, S, es, W, CN, x_in, x_out, tag="f1"):
    sb = lambda name, shape, dt: es.enter_context(nc.sbuf_tensor(name + tag, shape, dt))
    psa = lambda name, shape, dt: es.enter_context(nc.psum_tensor(name + tag, shape, dt))

    x_fm = sb("x_fm", [128, 16384], BF16)
    y_acc = sb("y_acc", [128, 16384], F32)
    w1t = sb("w1t", [128, 8192], BF16)
    w2t = sb("w2t", [128, 8192], BF16)
    w3t = sb("w3t", [128, 8192], BF16)
    s1_bf = sb("s1_bf", [128, 4096], F32)
    g_fm = sb("g_fm", [128, 4096], BF16)
    wgt = sb("wgt", [128, 64], BF16)
    rt = sb("rt", [128, 128], F32)
    wsb = sb("wsb", [128, 128], F32)
    m1 = sb("m1", [128, 16], F32)
    m2 = sb("m2", [128, 16], F32)
    tmp8 = sb("tmp8", [128, 128], F32)
    tmp8b = sb("tmp8b", [128, 128], F32)
    p12 = sb("p12", [128, 96], F32)
    xres = sb("xres", [128, 1024], F32)
    stage2 = sb("stage2", [128, 1024], BF16)
    scr1 = sb("scr1", [128, 1024], F32)
    scr2 = sb("scr2", [128, 1024], F32)
    og_bf = sb("og_bf", [128, 1024], BF16)
    ogf_bf = sb("ogf_bf", [128, 1024], BF16)
    sq_bf = sb("sq_bf", [128, 1024], BF16)
    onescol = sb("onescol", [128, 1], BF16)
    murow = sb("murow", [1, 128], F32)
    rsrow = sb("rsrow", [1, 128], F32)
    varrow = sb("varrow", [1, 128], F32)
    col1 = sb("col1", [128, 1], F32)
    col2 = sb("col2", [128, 1], F32)
    id_bf = sb("id_bf", [128, 128], BF16)
    cvals = sb("cvals", [128, 8], F32)
    register_consts(nc, cvals)

    ps_h = psa("ps_h", [128, 2048], F32)
    ps_y = psa("ps_y", [128, 1024], F32)
    ps_st = psa("ps_st", [128, 512], F32)
    ps_t = psa("ps_t", [128, 1024], BF16)

    B = dict(og_bf=og_bf[:], ogf_bf=ogf_bf[:], sq_bf=sq_bf[:], ps_t=ps_t,
             ps_st=ps_st, id_bf=id_bf, onescol=onescol, murow=murow,
             rsrow=rsrow, varrow=varrow, col1=col1, col2=col2, cvals=cvals,
             scrO=scr2[:], xres=xres, stage2=stage2[:])

    w1bf = nc.dram_tensor("w1bf" + tag, [16384, 1024], BF16).ap()
    w2bf = nc.dram_tensor("w2bf" + tag, [16384, 1024], BF16).ap()
    w3bf = nc.dram_tensor("w3bf" + tag, [8192, 2048], BF16).ap()

    for e_ in range(8):
        P.op("gpsimd", lambda e, e_=e_: e.dma_start(
            out=w1bf[2048 * e_:2048 * (e_ + 1), :],
            in_=W["W1"][2048 * e_:2048 * (e_ + 1), :]), S.lds, 16)
        P.op("gpsimd", lambda e, e_=e_: e.dma_start(
            out=w2bf[2048 * e_:2048 * (e_ + 1), :],
            in_=W["W2"][2048 * e_:2048 * (e_ + 1), :]), S.lds, 16)
        P.op("gpsimd", lambda e, e_=e_: e.dma_start(
            out=w3bf[1024 * e_:1024 * (e_ + 1), :],
            in_=W["W3"][1024 * e_:1024 * (e_ + 1), :]), S.lds, 16)
    for dst, srcap in [(id_bf[:], CN["ident"]), (onescol[:], CN["ones_col"]),
                       (wgt[:].rearrange("p (kc e) -> p kc e", e=8),
                        W["WgT"].rearrange("(kc p) e -> p kc e", p=128))]:
        P.op("gpsimd", lambda e, d=dst, s=srcap: e.dma_start(out=d, in_=s), S.lds, 16)
    P.op("sync", lambda e: e.dma_start(out=cvals[:], in_=CN["constvals"]), S.ldh, 16)
    P.wait("tensor", S.lds)
    P.wait("tensor", S.ldh)

    emit_xfm_build(nc, P, S, B, x_in, x_fm)

    P.wait("scalar", S.ldh)
    P.op("scalar", lambda e: e.activation(tmp8[:, 0:8], cvals[:, 0:8], AF.Exp), S.cS)
    P.op("scalar", lambda e: e.activation(tmp8[:, 0:1], cvals[:, 7:8], AF.Ln), S.cS)
    P.op("scalar", lambda e: e.activation(tmp8[:, 0:8], cvals[:, 0:8], AF.Silu), S.cS)
    P.op("scalar", lambda e: e.activation(tmp8[:, 0:8], cvals[:, 0:8], AF.Square), S.cS)

    P.wait("tensor", S.cV)
    for t in range(NT):
        P.wait("tensor", S.cV)
        for kc in range(8):
            P.op("tensor", lambda e, t=t, kc=kc: e.matmul(
                ps_y[:, 8 * t:8 * (t + 1)],
                x_fm[:, kc * 2048 + t * 128:kc * 2048 + (t + 1) * 128],
                wgt[:, kc * 8:(kc + 1) * 8],
                start=(kc == 0), stop=(kc == 7)), S.cT if kc == 7 else None)
    P.wait("vector", S.cT)
    P.op("vector", lambda e: e.tensor_copy(rt[:], ps_y[:, 0:128]), S.cV)
    r3 = rt[:].rearrange("p (t e) -> p t e", e=8)
    t3 = tmp8[:].rearrange("p (t e) -> p t e", e=8)
    u3 = tmp8b[:].rearrange("p (t e) -> p t e", e=8)
    w3 = wsb[:].rearrange("p (t e) -> p t e", e=8)
    m1v = m1[:].rearrange("p (t o) -> p t o", o=1)
    m2v = m2[:].rearrange("p (t o) -> p t o", o=1)
    jk = lambda: P.op("vector", lambda e: e.tensor_copy(scr1[:, 0:128], rt[:]), S.cV)
    jk()
    P.op("vector", lambda e: e.tensor_tensor(t3[:, :, 0:4], r3[:, :, 0:4], r3[:, :, 4:8], ALU.max), S.cV)
    jk()
    P.op("vector", lambda e: e.tensor_tensor(t3[:, :, 0:2], t3[:, :, 0:2], t3[:, :, 2:4], ALU.max), S.cV)
    jk()
    P.op("vector", lambda e: e.tensor_tensor(m1v, t3[:, :, 0:1], t3[:, :, 1:2], ALU.max), S.cV)
    jk()
    P.op("vector", lambda e: e.tensor_tensor(u3, r3, m1v.broadcast_to([128, 16, 8]), ALU.is_ge), S.cV)
    jk()
    P.op("vector", lambda e: e.scalar_tensor_tensor(t3, u3, -1e30, r3, ALU.mult, ALU.add), S.cV)
    jk()
    P.op("vector", lambda e: e.tensor_tensor(t3[:, :, 0:4], t3[:, :, 0:4], t3[:, :, 4:8], ALU.max), S.cV)
    jk()
    P.op("vector", lambda e: e.tensor_tensor(t3[:, :, 0:2], t3[:, :, 0:2], t3[:, :, 2:4], ALU.max), S.cV)
    jk()
    P.op("vector", lambda e: e.tensor_tensor(m2v, t3[:, :, 0:1], t3[:, :, 1:2], ALU.max), S.cV)
    jk()
    P.op("vector", lambda e: e.tensor_sub(p12[:, 0:16], m2[:], m1[:]), S.cV)
    jk()
    jk()
    P.wait("scalar", S.cV)
    P.op("scalar", lambda e: e.activation(p12[:, 16:32], p12[:, 0:16], AF.Exp), S.cS)
    P.op("scalar", lambda e: e.activation(p12[:, 32:48], p12[:, 16:32], AF.Ln, bias=1.0), S.cS)
    P.op("scalar", lambda e: e.activation(p12[:, 48:64], p12[:, 32:48], AF.Exp, scale=-1.0), S.cS)
    P.op("scalar", lambda e: e.copy(tmp8[:, 112:128], p12[:, 48:64]), S.cS)

    def router_part2():
        P.wait("vector", S.cS)
        P.op("vector", lambda e: e.tensor_tensor(t3, r3, m2v.broadcast_to([128, 16, 8]), ALU.is_ge), S.cV)
        P.op("vector", lambda e: e.tensor_mul(
            w3, u3, p12[:, 48:64].rearrange("p (t o) -> p t o", o=1).broadcast_to([128, 16, 8])), S.cV)
        P.op("vector", lambda e: e.tensor_mul(p12[:, 64:80], p12[:, 16:32], p12[:, 48:64]), S.cV)
        P.op("vector", lambda e: e.tensor_scalar(tmp8b[:], tmp8b[:], -1.0, 1.0, ALU.mult, ALU.add), S.cV)
        jk()
        P.op("vector", lambda e: e.tensor_mul(t3, t3, u3), S.cV)
        jk()
        P.op("vector", lambda e: e.tensor_mul(
            t3, t3, p12[:, 64:80].rearrange("p (t o) -> p t o", o=1).broadcast_to([128, 16, 8])), S.cV)
        jk()
        P.op("vector", lambda e: e.tensor_add(w3, w3, t3), S.cV)
        jk()
        jk()

    P.wait("gpsimd", S.cT)
    first = True
    P.wait("sync", S.lds)
    for e_ in range(8):
        for hf in range(2):
            P.wait("sync", S.cT)
            for dc in range(8):
                P.op("sync", lambda e, e_=e_, hf=hf, dc=dc: e.dma_start(
                    out=w1t[:, 1024 * dc:1024 * (dc + 1)],
                    in_=w1bf[2048 * e_ + 1024 * hf:2048 * e_ + 1024 * (hf + 1),
                             128 * dc:128 * (dc + 1)], transpose=True), S.ldh, 16)
                P.op("sync", lambda e, e_=e_, hf=hf, dc=dc: e.dma_start(
                    out=w2t[:, 1024 * dc:1024 * (dc + 1)],
                    in_=w2bf[2048 * e_ + 1024 * hf:2048 * e_ + 1024 * (hf + 1),
                             128 * dc:128 * (dc + 1)], transpose=True), S.ldh, 16)
                P.op("sync", lambda e, e_=e_, hf=hf, dc=dc: e.dma_start(
                    out=w3t[:, 1024 * dc:1024 * (dc + 1)],
                    in_=w3bf[1024 * e_:1024 * (e_ + 1),
                             1024 * hf + 128 * dc:1024 * hf + 128 * (dc + 1)],
                    transpose=True), S.ldh, 16)
            P.wait("tensor", S.ldh)
            if DBG_L1F and e_ == 0 and hf == 0:
                dw1 = nc.dram_tensor("dw1" + tag, [128, 8192], F32, kind="ExternalOutput").ap()
                dw3 = nc.dram_tensor("dw3" + tag, [128, 8192], F32, kind="ExternalOutput").ap()
                P.wait("gpsimd", S.ldh)
                P.op("gpsimd", lambda e: e.dma_start(out=dw1, in_=w1t[:]), S.lds, 16)
                P.op("gpsimd", lambda e: e.dma_start(out=dw3, in_=w3t[:]), S.lds, 16)
                P.wait("sync", S.lds)
            for tg in range(4):
                for hc in range(8):
                    slot = hc % 2
                    po1 = ps_h[:, 1024 * slot:1024 * slot + 512]
                    po2 = ps_h[:, 1024 * slot + 512:1024 * slot + 1024]
                    P.wait("tensor", S.cV)
                    for kc in range(8):
                        P.op("tensor", lambda e, po1=po1, kc=kc, hc=hc, tg=tg: e.matmul(
                            po1, w1t[:, kc * 1024 + hc * 128:kc * 1024 + (hc + 1) * 128],
                            x_fm[:, kc * 2048 + tg * 512:kc * 2048 + (tg + 1) * 512],
                            start=(kc == 0), stop=(kc == 7)), None)
                    for kc in range(8):
                        P.op("tensor", lambda e, po2=po2, kc=kc, hc=hc, tg=tg: e.matmul(
                            po2, w2t[:, kc * 1024 + hc * 128:kc * 1024 + (hc + 1) * 128],
                            x_fm[:, kc * 2048 + tg * 512:kc * 2048 + (tg + 1) * 512],
                            start=(kc == 0), stop=(kc == 7)),
                            S.cT if kc == 7 else None)
                    P.wait("scalar", S.cT)
                    P.op("scalar", lambda e, po1=po1, hc=hc: e.activation(
                        s1_bf[:, hc * 512:(hc + 1) * 512], po1, AF.Silu), S.cS)
                    P.wait("vector", S.cS)
                    P.op("vector", lambda e, po2=po2, hc=hc: e.tensor_mul(
                        g_fm[:, hc * 512:(hc + 1) * 512],
                        s1_bf[:, hc * 512:(hc + 1) * 512], po2), S.cV)
                if e_ == 0 and hf == 0 and tg == 0:
                    router_part2()
                P.wait("tensor", S.cV)
                for tt in range(4):
                    tile = tg * 4 + tt
                    for dg in range(2):
                        po = ps_y[:, dg * 512:(dg + 1) * 512]
                        P.wait("tensor", S.cV)
                        for hc in range(8):
                            P.op("tensor", lambda e, po=po, hc=hc, tt=tt, dg=dg: e.matmul(
                                po, g_fm[:, hc * 512 + tt * 128:hc * 512 + tt * 128 + 128],
                                w3t[:, hc * 1024 + dg * 512:hc * 1024 + (dg + 1) * 512],
                                start=(hc == 0), stop=(hc == 7)), S.cT if hc == 7 else None)
                        P.wait("vector", S.cT)
                        ya = y_acc[:, tile * 1024 + dg * 512:tile * 1024 + (dg + 1) * 512]
                        wcol = wsb[:, tile * 8 + e_:tile * 8 + e_ + 1]
                        if first:
                            P.op("vector", lambda e, po=po, ya=ya, wcol=wcol:
                                 e.tensor_single_scalar(ya, po, wcol, ALU.mult), S.cV)
                        else:
                            P.op("vector", lambda e, po=po, ya=ya, wcol=wcol:
                                 e.scalar_tensor_tensor(ya, po, wcol, ya,
                                                        ALU.mult, ALU.add), S.cV)
            first = False

    if DBG_L1F:
        dwv = nc.dram_tensor("dwv" + tag, [128, 128], F32, kind="ExternalOutput").ap()
        dya = nc.dram_tensor("dya" + tag, [128, 16384], F32, kind="ExternalOutput").ap()
        drt = nc.dram_tensor("drt" + tag, [128, 128], F32, kind="ExternalOutput").ap()
        dm1 = nc.dram_tensor("dm1" + tag, [128, 16], F32, kind="ExternalOutput").ap()
        dm2 = nc.dram_tensor("dm2" + tag, [128, 16], F32, kind="ExternalOutput").ap()
        dp12 = nc.dram_tensor("dp12" + tag, [128, 80], F32, kind="ExternalOutput").ap()
        du3 = nc.dram_tensor("du3" + tag, [128, 128], F32, kind="ExternalOutput").ap()
        P.wait("gpsimd", S.cV)
        P.op("gpsimd", lambda e: e.dma_start(out=dwv, in_=wsb[:]), S.lds, 16)
        P.op("gpsimd", lambda e: e.dma_start(out=dya, in_=y_acc[:]), S.lds, 16)
        P.op("gpsimd", lambda e: e.dma_start(out=drt, in_=rt[:]), S.lds, 16)
        P.op("gpsimd", lambda e: e.dma_start(out=dm1, in_=m1[:]), S.lds, 16)
        P.op("gpsimd", lambda e: e.dma_start(out=dm2, in_=m2[:]), S.lds, 16)
        P.op("gpsimd", lambda e: e.dma_start(out=dp12, in_=p12[:, 0:80]), S.lds, 16)
        P.op("gpsimd", lambda e: e.dma_start(out=du3, in_=tmp8b[:]), S.lds, 16)
        P.wait("vector", S.lds)

    for tile in range(NT):
        P.wait("sync", S.cV)
        P.op("sync", lambda e, tile=tile: e.dma_start(
            out=xres[:], in_=x_in[128 * tile:128 * (tile + 1), :]), S.ldh, 16)
        P.wait("vector", S.ldh)
        P.wait("vector", S.sth)
        P.op("vector", lambda e, tile=tile: e.tensor_add(
            scr1[:], y_acc[:, tile * 1024:(tile + 1) * 1024], xres[:]), S.cV)
        emit_ln_tail(nc, P, S, B, scr1[:], x_out[128 * tile:128 * (tile + 1), :])



def declare_weights_l0a(nc):
    W = {}
    for n, shp in [("Wq", [1024, 1024]), ("Wk", [1024, 1024]), ("Wv", [1024, 1024]),
                   ("Wg", [1024, 1024]), ("Wo", [1024, 1024]), ("Wg1", [1024, 16]),
                   ("Wg2", [16, 1024]), ("bg2", [1, 1024]), ("gnw_tiled", [128, 1024])]:
        W[n] = nc.dram_tensor("l0a_" + n, shp, F32, kind="ExternalInput").ap()
    return W


def declare_consts(nc, names=("ident", "t1u", "mask_incl", "ones_col", "ones_row",
                              "ones_row2048", "diag_mask", "constvals")):
    shapes = {"ident": [128, 128], "t1u": [128, 128], "mask_incl": [128, 128],
              "ones_col": [128, 1], "ones_row": [1, 128],
              "ones_row2048": [1, 2048], "diag_mask": [128, 2048],
              "constvals": [128, 8]}
    return {n: nc.dram_tensor("cn_" + n, shapes[n], F32, kind="ExternalInput").ap()
            for n in names}


def l1a_core_indices(core):
    s, g, octs = core_layout(core)
    p64 = np.arange(64, dtype=np.int32)[:, None]
    p128 = np.arange(128, dtype=np.int32)[:, None]
    kidx = np.concatenate([(1536 * r + 1024 + 64 * g + p64) * 2048 + 1024 * half
                           for r in range(4) for half in range(2)], axis=1)
    qidx = np.concatenate([(1536 * r + 64 * (4 * gp + g) + p64) * 2048 + 1024 * half
                           for gp in range(4) for r in range(4) for half in range(2)],
                          axis=1)
    vcols = []
    for kt in range(64):
        oct_ = kt // 8
        rank, off = (oct_, 0) if oct_ < 4 else (7 - oct_, 1024)
        loc = off + (kt % 8) * 128 + p128
        vcols.append((1536 * rank + 1280) * 2048 + loc * 256 + 64 * g)
    vidx = np.concatenate(vcols, axis=1).astype(np.int32)
    ocols = []
    for r in range(4):
        for tile in range(16):
            ocols.append((8192 * r + 2048 * g + tile * 128 + p128) * 256)
    oidx = np.concatenate(ocols, axis=1).astype(np.int32)
    acols = []
    for tile in range(16):
        grow0 = 1024 * octs[tile // 8] + 128 * (tile % 8)
        for r in range(4):
            acols.append((8192 * r + grow0 + p128) * 256)
    aidx = np.concatenate(acols, axis=1).astype(np.int32)
    return dict(kidx=kidx, qidx=qidx, vidx=vidx, oidx=oidx, aidx=aidx)


def stage_barrier(P, S):
    for eng in ("sync", "gpsimd"):
        for sem in (S.cT, S.cV, S.cS, S.sth, S.lds, S.ldh):
            P.wait(eng, sem)
    P.wait("tensor", S.cV)
    P.wait("tensor", S.cS)
    P.wait("vector", S.cT)
    P.wait("vector", S.cS)
    P.wait("scalar", S.cT)
    P.wait("scalar", S.cV)


def build_fused_launch():
    nc = bass.Bass("TRN2", target_bir_lowering=False, debug=False,
                   detect_race_conditions=False)
    x_in = nc.dram_tensor("x_in", [TOK, DM], F32, kind="ExternalInput").ap()
    x_out = nc.dram_tensor("x_out", [TOK, DM], F32, kind="ExternalOutput").ap()
    kd = "ExternalOutput" if DBG_FUSED else "Internal"
    x1d = nc.dram_tensor("x1d", [TOK, DM], F32, kind=kd).ap()
    x2d = nc.dram_tensor("x2d", [TOK, DM], F32, kind=kd).ap()
    x3d = nc.dram_tensor("x3d", [TOK, DM], F32, kind=kd).ap()
    W0a = declare_weights_l0a(nc)
    CN = declare_consts(nc)
    CI0 = {"theta": nc.dram_tensor("theta", [64, 128], F32, kind="ExternalInput").ap(),
           "cbias": nc.dram_tensor("cbias", [64, 16], F32, kind="ExternalInput").ap()}
    W0f = {n: nc.dram_tensor("f0_" + n, shp, F32, kind="ExternalInput").ap()
           for n, shp in [("router_w", [1024, 8]), ("w1", [1024, 4096]),
                          ("b1", [1, 4096]), ("w2", [4096, 1024]),
                          ("b2_t", [128, 1024])]}
    W1a = {n: nc.dram_tensor("a1_" + n, shp, F32, kind="ExternalInput").ap()
           for n, shp in [("Wq", [1024, 1024]), ("Wk", [1024, 256]),
                          ("Wv", [1024, 256]), ("Wo", [1024, 1024]),
                          ("bq", [1, 1024]), ("bk", [1, 256]), ("bv", [1, 256]),
                          ("bo", [1, 1024]), ("lng_t", [128, 1024]),
                          ("lnb_t", [128, 1024])]}
    CI1 = {n: nc.dram_tensor("a1_" + n, shp, mybir.dt.int32, kind="ExternalInput").ap()
           for n, shp in [("kidx", [64, 8]), ("qidx", [64, 32]),
                          ("vidx", [128, 64]), ("oidx", [128, 64]),
                          ("aidx", [128, 64])]}
    W1f = {n: nc.dram_tensor("f1_" + n, shp, F32, kind="ExternalInput").ap()
           for n, shp in [("WgT", [1024, 8]), ("W1", [16384, 1024]),
                          ("W2", [16384, 1024]), ("W3", [8192, 2048])]}
    with contextlib.ExitStack() as es:
        S = Sems(nc, es)
        P = Plan()
        cvg = es.enter_context(nc.sbuf_tensor("cvals_glob", [128, 8], F32))
        register_consts(nc, cvg)
        P.op("sync", lambda e: e.dma_start(out=cvg[:], in_=CN["constvals"]), S.ldh, 16)
        with contextlib.ExitStack() as es1:
            emit_l0_attn(nc, P, S, es1, W0a, CN, CI0, x_in, x1d)
        stage_barrier(P, S)
        with contextlib.ExitStack() as es2:
            emit_l0_ffn(nc, P, S, es2, W0f, CN, x1d, x2d)
        stage_barrier(P, S)
        with contextlib.ExitStack() as es3:
            o_g = emit_l1_attn(nc, P, S, es3, W1a, CN, CI1, x2d, None, fused=True)
        stage_barrier(P, S)
        with contextlib.ExitStack() as es4:
            emit_l1_attn_fin(nc, P, S, es4, W1a, CN, None, x2d, x3d,
                             o_gather=(o_g, CI1["aidx"]))
        stage_barrier(P, S)
        with contextlib.ExitStack() as es5:
            emit_l1_ffn(nc, P, S, es5, W1f, CN, x3d, x_out)
        P.wait("gpsimd", S.sth)
        P.wait("gpsimd", S.lds)
        P.wait("gpsimd", S.ldh)
        P.wait("sync", S.sth)
        P.wait("sync", S.lds)
        P.emit(nc)
    return nc


def build_l1_attn_launch(upto=99):
    nc = bass.Bass("TRN2", target_bir_lowering=False, debug=False,
                   detect_race_conditions=False)
    x_in = nc.dram_tensor("x_in", [TOK, DM], F32, kind="ExternalInput").ap()
    x_out = None
    W = {}
    for n, shp in [("Wq", [1024, 1024]), ("Wk", [1024, 256]), ("Wv", [1024, 256]),
                   ("Wo", [1024, 1024]), ("bq", [1, 1024]), ("bk", [1, 256]),
                   ("bv", [1, 256]), ("bo", [1, 1024]),
                   ("lng_t", [128, 1024]), ("lnb_t", [128, 1024])]:
        W[n] = nc.dram_tensor("a1_" + n, shp, F32, kind="ExternalInput").ap()
    CN = declare_consts(nc)
    CI = {n: nc.dram_tensor("a1_" + n, shp, mybir.dt.int32, kind="ExternalInput").ap()
          for n, shp in [("kidx", [64, 8]), ("qidx", [64, 32]),
                         ("vidx", [128, 64]), ("oidx", [128, 64])]}
    with contextlib.ExitStack() as es:
        S = Sems(nc, es)
        P = Plan()
        emit_l1_attn(nc, P, S, es, W, CN, CI, x_in, x_out, upto=upto)
        P.wait("gpsimd", S.sth)
        P.wait("gpsimd", S.lds)
        P.wait("sync", S.sth)
        P.emit(nc)
    return nc


def build_l1_attn_fin_launch():
    nc = bass.Bass("TRN2", target_bir_lowering=False, debug=False,
                   detect_race_conditions=False)
    attn_in = nc.dram_tensor("attn_in", [TOK, DM], BF16, kind="ExternalInput").ap()
    x_in = nc.dram_tensor("x_in", [TOK, DM], F32, kind="ExternalInput").ap()
    x_out = nc.dram_tensor("x_out", [TOK, DM], F32, kind="ExternalOutput").ap()
    W = {}
    for n, shp in [("Wo", [1024, 1024]), ("bo", [1, 1024]),
                   ("lng_t", [128, 1024]), ("lnb_t", [128, 1024])]:
        W[n] = nc.dram_tensor("a1_" + n, shp, F32, kind="ExternalInput").ap()
    CN = declare_consts(nc)
    with contextlib.ExitStack() as es:
        S = Sems(nc, es)
        P = Plan()
        emit_l1_attn_fin(nc, P, S, es, W, CN, attn_in, x_in, x_out)
        P.wait("gpsimd", S.sth)
        P.wait("gpsimd", S.lds)
        P.wait("sync", S.sth)
        P.emit(nc)
    return nc


def assemble_attn_for_core(o_results, core):
    s, g, octs = core_layout(core)
    import ml_dtypes
    attn = np.empty((TOK, DM), dtype=o_results[0].dtype)
    rows = np.concatenate([np.arange(1024 * j, 1024 * (j + 1)) for j in octs])
    for r in range(4):
        src = o_results[4 * s + r][rows]
        for gp in range(4):
            attn[:, 64 * (4 * gp + r):64 * (4 * gp + r + 1)] = src[:, 64 * gp:64 * (gp + 1)]
    return np.ascontiguousarray(attn)


def l1a_weight_arrays(inputs):
    i = {k: np.ascontiguousarray(np.asarray(v), dtype=np.float32)
         for k, v in inputs.items()}
    return {
        "a1_Wq": i["L1_attn_Wq"], "a1_Wk": i["L1_attn_Wk"], "a1_Wv": i["L1_attn_Wv"],
        "a1_Wo": i["L1_attn_Wo"],
        "a1_bq": i["L1_attn_bq"].reshape(1, 1024),
        "a1_bk": i["L1_attn_bk"].reshape(1, 256),
        "a1_bv": i["L1_attn_bv"].reshape(1, 256),
        "a1_bo": i["L1_attn_bo"].reshape(1, 1024),
        "a1_lng_t": np.ascontiguousarray(np.tile(i["L1_attn_ln_g"].reshape(1, 1024), (128, 1))),
        "a1_lnb_t": np.ascontiguousarray(np.tile(i["L1_attn_ln_b"].reshape(1, 1024), (128, 1))),
    }


def build_l1_ffn_launch():
    nc = bass.Bass("TRN2", target_bir_lowering=False, debug=False,
                   detect_race_conditions=False)
    x_in = nc.dram_tensor("x_in", [TOK, DM], F32, kind="ExternalInput").ap()
    x_out = nc.dram_tensor("x_out", [TOK, DM], F32, kind="ExternalOutput").ap()
    W = {}
    for n, shp in [("WgT", [1024, 8]), ("W1", [16384, 1024]),
                   ("W2", [16384, 1024]), ("W3", [8192, 2048])]:
        W[n] = nc.dram_tensor("f1_" + n, shp, F32, kind="ExternalInput").ap()
    CN = declare_consts(nc)
    with contextlib.ExitStack() as es:
        S = Sems(nc, es)
        P = Plan()
        emit_l1_ffn(nc, P, S, es, W, CN, x_in, x_out)
        P.wait("gpsimd", S.sth)
        P.wait("gpsimd", S.lds)
        P.wait("gpsimd", S.ldh)
        P.wait("sync", S.sth)
        P.emit(nc)
    return nc


def l1f_weight_arrays(inputs):
    i = {k: np.ascontiguousarray(np.asarray(v), dtype=np.float32)
         for k, v in inputs.items()}
    return {
        "f1_WgT": np.ascontiguousarray(i["L1_ffn_Wg"].T),
        "f1_W1": i["L1_ffn_W1"].reshape(16384, 1024),
        "f1_W2": i["L1_ffn_W2"].reshape(16384, 1024),
        "f1_W3": i["L1_ffn_W3"].reshape(8192, 2048),
    }


def build_l0_ffn_launch():
    nc = bass.Bass("TRN2", target_bir_lowering=False, debug=False,
                   detect_race_conditions=False)
    x_in = nc.dram_tensor("x_in", [TOK, DM], F32, kind="ExternalInput").ap()
    x_out = nc.dram_tensor("x_out", [TOK, DM], F32, kind="ExternalOutput").ap()
    W = {}
    for n, shp in [("router_w", [1024, 8]), ("w1", [1024, 4096]), ("b1", [1, 4096]),
                   ("w2", [4096, 1024]), ("b2_t", [128, 1024])]:
        W[n] = nc.dram_tensor("f0_" + n, shp, F32, kind="ExternalInput").ap()
    CN = declare_consts(nc)
    with contextlib.ExitStack() as es:
        S = Sems(nc, es)
        P = Plan()
        emit_l0_ffn(nc, P, S, es, W, CN, x_in, x_out)
        P.wait("gpsimd", S.sth)
        P.wait("gpsimd", S.lds)
        P.wait("sync", S.sth)
        P.emit(nc)
    return nc


def l0f_weight_arrays(inputs):
    i = {k: np.ascontiguousarray(np.asarray(v), dtype=np.float32)
         for k, v in inputs.items()}
    return {
        "f0_router_w": i["L0_ffn_router_w"], "f0_w1": i["L0_ffn_w1"],
        "f0_b1": i["L0_ffn_b1"].reshape(1, 4096), "f0_w2": i["L0_ffn_w2"],
        "f0_b2_t": np.ascontiguousarray(np.tile(i["L0_ffn_b2"].reshape(1, 1024), (128, 1))),
    }


def build_l0_attn_launch(upto=99, p2cut=99, dbg=False):
    nc = bass.Bass("TRN2", target_bir_lowering=False, debug=False,
                   detect_race_conditions=False)
    x_in = nc.dram_tensor("x_in", [TOK, DM], F32, kind="ExternalInput").ap()
    x_out = nc.dram_tensor("x_out", [TOK, DM], F32, kind="ExternalOutput").ap()
    W = declare_weights_l0a(nc)
    CN = declare_consts(nc)
    CI = {"theta": nc.dram_tensor("theta", [64, 128], F32, kind="ExternalInput").ap(),
          "cbias": nc.dram_tensor("cbias", [64, 16], F32, kind="ExternalInput").ap()}
    with contextlib.ExitStack() as es:
        S = Sems(nc, es)
        P = Plan()
        emit_l0_attn(nc, P, S, es, W, CN, CI, x_in, x_out, upto=upto, p2cut=p2cut, dbg=dbg)
        if upto < 99 or p2cut < 99:
            P.wait("sync", S.cV)
            P.wait("sync", S.cT)
            P.wait("sync", S.cS)
            P.op("sync", lambda e: e.dma_start(out=x_out[0:128, :], in_=x_in[0:128, :]),
                 S.sth, 16)
        P.wait("gpsimd", S.sth)
        P.wait("gpsimd", S.lds)
        P.wait("sync", S.sth)
        P.wait("sync", S.lds)
        P.emit(nc)
    return nc


def l0a_weight_arrays(inputs):
    i = {k: np.ascontiguousarray(np.asarray(v), dtype=np.float32)
         for k, v in inputs.items()}
    return {
        "l0a_Wq": i["L0_attn_Wq"], "l0a_Wk": i["L0_attn_Wk"], "l0a_Wv": i["L0_attn_Wv"],
        "l0a_Wg": i["L0_attn_Wg"], "l0a_Wo": i["L0_attn_Wo"], "l0a_Wg1": i["L0_attn_Wg1"],
        "l0a_Wg2": i["L0_attn_Wg2"], "l0a_bg2": i["L0_attn_bg2"].reshape(1, 1024),
        "l0a_gnw_tiled": np.ascontiguousarray(
            np.tile(np.tile(i["L0_attn_g_norm_w"], 16).reshape(1, 1024), (128, 1))),
    }


def run_l0_attn(inputs, x_full=None):
    nc = build_l0_attn_launch()
    cn = host_constants()
    wa = l0a_weight_arrays(inputs)
    in_maps = []
    base = dict(wa)
    for n, v in cn.items():
        base["cn_" + n] = v
    src = dict(inputs)
    if x_full is not None:
        src["L0_attn_x"] = x_full
    for core in range(NCORES):
        pc = per_core_arrays(src, core)
        m = dict(base)
        m["x_in"] = pc["x_in"]
        m["theta"] = pc["theta"]
        m["cbias"] = pc["cbias"]
        in_maps.append(m)
    res = run_bass_kernel_spmd(nc, in_maps, core_ids=list(range(NCORES)))
    return assemble_output([r["x_out"] for r in res.results])


FUSED = True


def kernel(**inputs):
    if not FUSED:
        return kernel_unfused(**inputs)
    cn = host_constants()
    nc = build_fused_launch()
    base = {}
    base.update(l0a_weight_arrays(inputs))
    base.update(l0f_weight_arrays(inputs))
    base.update(l1a_weight_arrays(inputs))
    base.update(l1f_weight_arrays(inputs))
    for n, v in cn.items():
        base["cn_" + n] = v
    maps = []
    for core in range(NCORES):
        pc = per_core_arrays(inputs, core)
        m = dict(base)
        m["x_in"] = pc["x_in"]
        m["theta"] = pc["theta"]
        m["cbias"] = pc["cbias"]
        for k, v in l1a_core_indices(core).items():
            m["a1_" + k] = v
        maps.append(m)
    res = run_bass_kernel_spmd(nc, maps, core_ids=list(range(NCORES)))
    return assemble_output([r["x_out"] for r in res.results])


def kernel_unfused(**inputs):
    cn = host_constants()
    base_cn = {"cn_" + n: v for n, v in cn.items()}

    def shard_x(x_full):
        return [per_core_arrays({"L0_attn_x": x_full}, core)["x_in"]
                for core in range(NCORES)]

    def run(nc, per_core_maps):
        res = run_bass_kernel_spmd(nc, per_core_maps, core_ids=list(range(NCORES)))
        return res.results

    nc = build_l0_attn_launch()
    wa = l0a_weight_arrays(inputs)
    maps = []
    for core in range(NCORES):
        pc = per_core_arrays(inputs, core)
        m = dict(wa); m.update(base_cn)
        m["x_in"] = pc["x_in"]; m["theta"] = pc["theta"]; m["cbias"] = pc["cbias"]
        maps.append(m)
    r1 = run(nc, maps)
    x1s = [r["x_out"] for r in r1]

    nc = build_l0_ffn_launch()
    wa = l0f_weight_arrays(inputs)
    maps = []
    for core in range(NCORES):
        m = dict(wa); m.update(base_cn)
        m["x_in"] = x1s[core]
        maps.append(m)
    r2 = run(nc, maps)
    x2s = [r["x_out"] for r in r2]

    nc = build_l1_attn_launch()
    wa = l1a_weight_arrays(inputs)
    maps = []
    for core in range(NCORES):
        m = {k: v for k, v in wa.items() if k in
             ("a1_Wq", "a1_Wk", "a1_Wv", "a1_bq", "a1_bk", "a1_bv", "a1_bo",
              "a1_Wo", "a1_lng_t", "a1_lnb_t")}
        m.update(base_cn)
        m["x_in"] = x2s[core]
        m.update({"a1_" + k: v for k, v in l1a_core_indices(core).items()})
        maps.append(m)
    nc_inputs = {a.memorylocations[0].name for a in nc.m.functions[0].allocations
                 if getattr(a, "kind", None) == "ExternalInput"}
    maps = [{k: v for k, v in m.items() if k in nc_inputs} for m in maps]
    r3 = run(nc, maps)
    o_res = [r["o_outa1"] for r in r3]

    nc = build_l1_attn_fin_launch()
    maps = []
    for core in range(NCORES):
        m = {k: wa[k] for k in ("a1_Wo", "a1_bo", "a1_lng_t", "a1_lnb_t")}
        m.update(base_cn)
        m["x_in"] = x2s[core]
        m["attn_in"] = assemble_attn_for_core(o_res, core)
        maps.append(m)
    r4 = run(nc, maps)
    x3s = [r["x_out"] for r in r4]

    nc = build_l1_ffn_launch()
    wa = l1f_weight_arrays(inputs)
    maps = []
    for core in range(NCORES):
        m = dict(wa); m.update(base_cn)
        m["x_in"] = x3s[core]
        maps.append(m)
    r5 = run(nc, maps)
    return assemble_output([r["x_out"] for r in r5])


# version 06
# speedup vs baseline: 1.0675x; 1.0675x over previous
import numpy as np
import contextlib
import concourse.bass as bass
import concourse.mybir as mybir
from concourse.bass_utils import run_bass_kernel_spmd

F32 = mybir.dt.float32
BF16 = mybir.dt.bfloat16
AF = mybir.ActivationFunctionType
ALU = mybir.AluOpType
AX = mybir.AxisListType

NCORES = 8
TOK = 2048
NT = TOK // 128
DM = 1024
C = 128
EPS = 1e-5
CONSTVALS = (1e-5, 1.0 / 64, -0.5, 1.0 / 1024, 0.125, -1.0, 1.0 / 32, 1.0)


def register_consts(nc, cvals):
    if getattr(nc, "_cv_registered", False):
        return
    nc._cv_registered = True
    for i, v in enumerate(CONSTVALS):
        nc.const_aps.register(F32, float(v), cvals[:, i:i + 1])

ENGS = ("sync", "scalar", "vector", "tensor", "gpsimd")
DBG_L0F = False
N_CC = 3
DBG_L1F = False
DBG_FUSED = False


class Sem:
    __slots__ = ("h", "n", "max_wait", "fenced")

    def __init__(self, h):
        self.h = h
        self.n = 0
        self.max_wait = 0
        self.fenced = 0


class Plan:

    def __init__(self):
        self.q = {e: [] for e in ENGS}
        self.waited = {}

    def op(self, eng, f, sem=None, k=1):
        if sem is not None and k == 16:
            if sem.max_wait > sem.fenced:
                self.q[eng].append(("wait", sem, sem.max_wait))
                sem.fenced = sem.max_wait
            sem.n += 16
            self.q[eng].append(("dma", f, sem, sem.n))
        elif sem is not None:
            sem.n += k
            self.q[eng].append(("op", f, sem, sem.n))
        else:
            self.q[eng].append(("op", f, None, 0))

    def wait(self, eng, sem, n=None):
        n = sem.n if n is None else n
        if n <= 0:
            return
        self.waited.setdefault(id(sem), set()).add(n)
        if n > sem.max_wait:
            sem.max_wait = n
        self.q[eng].append(("wait", sem, n))

    def emit(self, nc):
        emitted = {}

        def runner(items):
            def run(e):
                for it in items:
                    if it[0] == "wait":
                        _, sem, n = it
                        e.wait_ge(sem.h, n)
                    elif it[0] == "dma":
                        _, f, sem, cnt = it
                        f(e).then_inc(sem.h, 16)
                    else:
                        _, f, sem, cnt = it
                        ins = f(e)
                        if sem is not None and cnt in self.waited.get(id(sem), ()):
                            last = emitted.get(id(sem), 0)
                            ins.then_inc(sem.h, cnt - last)
                            emitted[id(sem)] = cnt
            return run

        with nc.Block() as block:
            block.sync(runner(self.q["sync"]))
            block.scalar(runner(self.q["scalar"]))
            block.vector(runner(self.q["vector"]))
            block.tensor(runner(self.q["tensor"]))
            block.gpsimd(runner(self.q["gpsimd"]))


class Sems:
    def __init__(self, nc, es):
        mk = lambda name: Sem(es.enter_context(nc.semaphore(name)))
        self.ldh = mk("ldh")
        self.lds = mk("lds")
        self.sth = mk("sth")
        self.cT = mk("cT")
        self.cV = mk("cV")
        self.cS = mk("cS")
        self.cG = mk("cG")
        self.bgc = mk("bgc")



def host_constants():
    c = {}
    c["ident"] = np.eye(128, dtype=np.float32)
    c["t1u"] = (-1.0 / 16.0) * np.triu(np.ones((128, 128), np.float32))
    c["mask_incl"] = np.triu(np.ones((128, 128), np.float32))
    c["ones_col"] = np.ones((128, 1), np.float32)
    c["ones_row"] = np.ones((1, 128), np.float32)
    c["ones_row2048"] = np.ones((1, 2048), np.float32)
    from kernel import CONSTVALS as _cv
    c["constvals"] = np.tile(np.array(list(_cv) + [0.0] * (8 - len(_cv)),
                                      np.float32), (128, 1))
    dm = np.zeros((128, 2048), np.float32)
    for j in range(4):
        for i in range(128):
            dm[i, j * 512:(j + 1) * 512][np.arange(512) >= j * 128 + i] = 1.0
    c["diag_mask"] = dm
    return c


def core_layout(core):
    s, g = core // 4, core % 4
    return s, g, [g, 7 - g]


def per_core_arrays(inputs, core):
    s, g, octs = core_layout(core)
    x = np.asarray(inputs["L0_attn_x"])
    xs = np.concatenate([x[s, 1024 * j:1024 * (j + 1)] for j in octs], axis=0)
    d = {"x_in": np.ascontiguousarray(xs, dtype=np.float32)}
    theta = np.zeros((64, 128), np.float32)
    cbias = np.full((64, 16), -1e9, np.float32)
    for oi, j in enumerate(octs):
        for jp in range(8):
            if jp < j:
                cbias[:, oi * 8 + jp] = 0.0
                for jpp in range(jp + 1, j):
                    theta[:, oi * 64 + jp * 8 + jpp] = 1.0
    d["theta"] = theta
    d["cbias"] = cbias
    return d


def assemble_output(shards):
    out = np.empty((2, 8192, 1024), np.float32)
    for core in range(NCORES):
        s, g, octs = core_layout(core)
        sh = shards[core]
        for oi, j in enumerate(octs):
            out[s, 1024 * j:1024 * (j + 1)] = sh[1024 * oi:1024 * (oi + 1)]
    return out



def emit_l0_attn(nc, P, S, es, W, CN, CI, x_in, x_out, tag="", upto=99, p2cut=99, dbg=False):
    sb = lambda name, shape, dt: es.enter_context(nc.sbuf_tensor(name + tag, shape, dt))
    psa = lambda name, shape, dt: es.enter_context(nc.psum_tensor(name + tag, shape, dt))

    q_fm = sb("q_fm", [128, 16384], BF16)
    k_tm = sb("k_tm", [128, 16384], BF16)
    v_tm = sb("v_tm", [128, 16384], BF16)
    gs_tm = sb("gs_tm", [128, 16384], BF16)
    arena = sb("arena", [128, 16384], BF16)
    wbuf = sb("wbuf", [128, 8192], BF16)
    u_sb = sb("u_sb", [17, 2048], F32)
    wg2b = sb("wg2b", [17, 1024], F32)
    wg1 = sb("wg1", [128, 128], BF16)
    xres = sb("xres", [128, 1024], F32)
    S_sb = sb("S_sb", [64, 1024], F32)
    S_bf = sb("S_bf", [64, 1024], BF16)
    a_all = sb("a_all", [64, 128], F32)
    a64 = sb("a64", [64, 16], F32)
    aoct = sb("aoct", [128, 8], F32)
    e3 = sb("e3", [128, 8], F32)
    acml = sb("acml", [64, 16], F32)
    e3h = sb("e3h", [64, 16], F32)
    ccoef = sb("ccoef", [64, 16], F32)
    stat = sb("stat", [128, 16], F32)
    stat2 = sb("stat2", [128, 16], F32)
    col1 = sb("col1", [128, 1], F32)
    col2 = sb("col2", [128, 1], F32)
    col3 = sb("col3", [128, 1], F32)
    junkS = sb("junkS", [128, 1], F32)
    junkV = sb("junkV", [128, 1], F32)
    gnw_bc = sb("gnw_bc", [128, 1024], BF16)
    id_bf = sb("id_bf", [128, 128], BF16)
    t1u = sb("t1u", [128, 128], F32)
    mask_bf = sb("mask_bf", [128, 128], BF16)
    onescol = sb("onescol", [128, 1], BF16)
    onesrow = sb("onesrow", [1, 128], BF16)
    murow = sb("murow", [1, 128], F32)
    rsrow = sb("rsrow", [1, 128], F32)
    varrow = sb("varrow", [1, 128], F32)
    theta_sb = sb("theta_sb", [64, 128], F32)
    cbias_sb = sb("cbias_sb", [64, 16], F32)
    cvals = sb("cvals", [128, 8], F32)
    register_consts(nc, cvals)

    def av(c0, c1, dt=F32):
        v = arena[:, c0:c1]
        return v.bitcast(dt) if dt != BF16 else v
    scr1 = av(0, 2048)
    scr2 = av(2048, 4096)
    scr3 = av(4096, 6144)
    qt64 = arena[0:64, 6144:8192]
    ktf64 = arena[0:64, 8192:10240]
    kt_bf = av(10240, 11264, BF16)
    pt_bf = av(11264, 13312, BF16)
    og_bf = av(13312, 14336, BF16)
    ogf_bf = av(14336, 15360, BF16)
    Stmp = scr3[0:64, :]
    r_sb = scr2
    stage2 = gs_tm[:, 0:1024]
    proj_scr = xres[:, 0:512]
    sloc = xres[0:64, :]

    ps_big = psa("ps_big", [128, 2048], F32)
    ps_o = psa("ps_o", [128, 1024], F32)
    ps_mkv = ps_o
    ps_t = psa("ps_t", [128, 1024], BF16)

    sin_d = nc.dram_tensor("sin_l0" + tag, [64, 2048], F32).ap()
    summ = nc.dram_tensor("summ_l0" + tag, [64, 2080], F32).ap()
    summ_g = nc.dram_tensor("summ_l0_g" + tag, [256, 2080], F32).ap()

    for dst, srcap in [(id_bf[:], CN["ident"]), (mask_bf[:], CN["mask_incl"]),
                       (onescol[:], CN["ones_col"]), (onesrow[:], CN["ones_row"]),
                       (gnw_bc[:], W["gnw_tiled"]),
                       (wg1[:].rearrange("p (kc n) -> p kc n", n=16),
                        W["Wg1"].rearrange("(kc p) n -> p kc n", p=128))]:
        P.op("gpsimd", lambda e, d=dst, s=srcap: e.dma_start(out=d, in_=s), S.lds, 16)
    for dst, srcap in [(t1u[:], CN["t1u"]), (theta_sb[:], CI["theta"]),
                       (cbias_sb[:], CI["cbias"]), (wg2b[0:16, :], W["Wg2"]),
                       (wg2b[16:17, :], W["bg2"]),
                       (u_sb[16:17, :], CN["ones_row2048"]),
                       (varrow[:], CN["ones_row"]),
                       (cvals[:], CN["constvals"])]:
        P.op("sync", lambda e, d=dst, s=srcap: e.dma_start(out=d, in_=s), S.ldh, 16)
    P.wait("tensor", S.lds)
    P.wait("tensor", S.ldh)

    P.wait("scalar", S.ldh)
    P.op("scalar", lambda e: e.activation(stat2[:, 0:8], cvals[:, 0:8], AF.Exp), S.cS)
    P.op("scalar", lambda e: e.activation(stat2[:, 0:1], cvals[:, 7:8], AF.Ln), S.cS)
    P.op("scalar", lambda e: e.activation(stat2[:, 0:8], cvals[:, 0:8], AF.Square), S.cS)
    P.op("scalar", lambda e: e.activation(stat2[:, 0:8], cvals[:, 0:8], AF.Silu), S.cS)

    x_fm = arena
    for t in range(NT):
        P.wait("sync", S.cV)
        P.op("sync", lambda e, t=t: e.dma_start(
            out=xres[:], in_=x_in[128 * t:128 * (t + 1), :]), S.ldh, 16)
        P.wait("vector", S.ldh)
        P.wait("vector", S.cT)
        P.op("vector", lambda e: e.tensor_copy(stage2, xres[:]), S.cV)
        P.wait("tensor", S.cV)
        for c in range(8):
            P.op("tensor", lambda e, c=c: e.transpose(
                ps_t[:, 128 * c:128 * (c + 1)], stage2[:, 128 * c:128 * (c + 1)],
                id_bf[:]), S.cT)
        P.wait("vector", S.cT)
        dst = x_fm[:].rearrange("p (fc n) -> p fc n", n=2048)[:, :, t * 128:(t + 1) * 128]
        P.op("vector", lambda e, d=dst: e.tensor_copy(d, ps_t[:].rearrange(
            "p (fc n) -> p fc n", n=128)), S.cV)

    if upto <= 1:
        return

    def stream_w(name):
        P.wait("gpsimd", S.cT)
        P.op("gpsimd", lambda e, s=W[name].rearrange("(kc p) n -> p kc n", p=128):
             e.dma_start(out=wbuf[:].rearrange("p (kc n) -> p kc n", n=1024), in_=s),
             S.lds, 16)
        P.wait("tensor", S.lds)

    stream_w("Wq")
    P.wait("tensor", S.cV)
    for oc in range(8):
        for tg in range(4):
            slot = (oc * 4 + tg) % 4
            po = ps_big[:, 512 * slot:512 * (slot + 1)]
            P.wait("tensor", S.cS)
            P.wait("tensor", S.cV)
            for kc in range(8):
                P.op("tensor", lambda e, po=po, kc=kc, oc=oc, tg=tg: e.matmul(
                    po, wbuf[:, kc * 1024 + oc * 128:kc * 1024 + (oc + 1) * 128],
                    x_fm[:, kc * 2048 + tg * 512:kc * 2048 + (tg + 1) * 512],
                    start=(kc == 0), stop=(kc == 7)), S.cT if kc == 7 else None)
            P.wait("scalar", S.cT)
            P.op("scalar", lambda e, po=po, oc=oc, tg=tg: e.activation(
                q_fm[:, oc * 2048 + tg * 512:oc * 2048 + (tg + 1) * 512], po,
                AF.Copy, scale=0.125), S.cS)

    for name, dst in [("Wk", k_tm), ("Wv", v_tm)]:
        stream_w(name)
        for t in range(NT):
            for fg in range(2):
                slot = (t * 2 + fg) % 4
                po = ps_big[:, 512 * slot:512 * (slot + 1)]
                P.wait("tensor", S.cV)
                P.wait("tensor", S.cS)
                for kc in range(8):
                    P.op("tensor", lambda e, po=po, kc=kc, t=t, fg=fg: e.matmul(
                        po, x_fm[:, kc * 2048 + t * 128:kc * 2048 + (t + 1) * 128],
                        wbuf[:, kc * 1024 + fg * 512:kc * 1024 + (fg + 1) * 512],
                        start=(kc == 0), stop=(kc == 7)), S.cT if kc == 7 else None)
                P.wait("vector", S.cT)
                P.op("vector", lambda e, po=po, t=t, fg=fg, dst=dst: e.tensor_copy(
                    dst[:, t * 1024 + fg * 512:t * 1024 + (fg + 1) * 512], po), S.cV)

    stream_w("Wg")
    for t in range(NT):
        for fg in range(2):
            slot = (t * 2 + fg) % 4
            po = ps_big[:, 512 * slot:512 * (slot + 1)]
            P.wait("tensor", S.cS)
            P.wait("tensor", S.cV)
            for kc in range(8):
                P.op("tensor", lambda e, po=po, kc=kc, t=t, fg=fg: e.matmul(
                    po, x_fm[:, kc * 2048 + t * 128:kc * 2048 + (t + 1) * 128],
                    wbuf[:, kc * 1024 + fg * 512:kc * 1024 + (fg + 1) * 512],
                    start=(kc == 0), stop=(kc == 7)), S.cT if kc == 7 else None)
            P.wait("scalar", S.cT)
            P.wait("scalar", S.cV)
            P.op("scalar", lambda e, po=po: e.activation(proj_scr, po, AF.Silu), S.cS)
            P.wait("vector", S.cS)
            P.op("vector", lambda e, t=t, fg=fg: e.tensor_mul(
                gs_tm[:, t * 1024 + fg * 512:t * 1024 + (fg + 1) * 512],
                proj_scr, gnw_bc[:, fg * 512:(fg + 1) * 512]), S.cV)

    for tg in range(4):
        P.wait("tensor", S.cS)
        P.wait("tensor", S.cV)
        for kc in range(8):
            P.op("tensor", lambda e, kc=kc, tg=tg: e.matmul(
                ps_mkv[0:16, 0:512], wg1[:, kc * 16:(kc + 1) * 16],
                x_fm[:, kc * 2048 + tg * 512:kc * 2048 + (tg + 1) * 512],
                start=(kc == 0), stop=(kc == 7)), S.cT if kc == 7 else None)
        P.wait("scalar", S.cT)
        P.op("scalar", lambda e, tg=tg: e.activation(
            u_sb[0:16, tg * 512:(tg + 1) * 512], ps_mkv[0:16, 0:512], AF.Copy), S.cS)

    if dbg:
        for nm, buf in [("dq", q_fm), ("dk", k_tm), ("dv", v_tm), ("dgs", gs_tm)]:
            dap = nc.dram_tensor(nm + tag, [128, 16384], F32, kind="ExternalOutput").ap()
            P.wait("gpsimd", S.cS)
            P.wait("gpsimd", S.cV)
            P.op("gpsimd", lambda e, d=dap, b=buf: e.dma_start(out=d, in_=b[:]), S.lds, 16)
        dup = nc.dram_tensor("du" + tag, [17, 2048], F32, kind="ExternalOutput").ap()
        P.op("gpsimd", lambda e, d=dup: e.dma_start(out=d, in_=u_sb[:]), S.lds, 16)

    if upto <= 2:
        return

    def chunk_preamble(tile, with_e1):
        P.wait("tensor", S.cS)
        P.wait("tensor", S.cV)
        for fg in range(2):
            P.op("tensor", lambda e, fg=fg, tile=tile: e.matmul(
                ps_big[:, fg * 512:(fg + 1) * 512],
                u_sb[0:17, tile * 128:(tile + 1) * 128],
                wg2b[0:17, fg * 512:(fg + 1) * 512],
                start=True, stop=True), S.cT)
        P.wait("scalar", S.cT)
        P.wait("scalar", S.cV)
        P.wait("scalar", S.sth)
        P.op("scalar", lambda e: e.activation(scr1, ps_big[:, 0:1024], AF.Exp, scale=-1.0), S.cS)
        P.op("scalar", lambda e: e.activation(scr2, scr1, AF.Ln, bias=1.0), S.cS)
        P.wait("tensor", S.cS)
        for fg in range(2):
            P.op("tensor", lambda e, fg=fg: e.matmul(
                ps_big[:, 1024 + fg * 512:1024 + (fg + 1) * 512], t1u[:],
                scr2[:, fg * 512:(fg + 1) * 512], start=True, stop=True), S.cT)
        for c in range(8):
            P.op("tensor", lambda e, c=c: e.matmul(
                ps_big[:, c * 128:(c + 1) * 128],
                scr2[:, c * 128:(c + 1) * 128], t1u[:],
                start=True, stop=True), S.cT)
        P.wait("scalar", S.cT)
        afm3 = ps_big[:, 0:1024].rearrange("p (c t) -> p c t", t=128)
        P.op("scalar", lambda e: e.activation(
            e3[:].rearrange("p (c o) -> p c o", o=1), afm3[:, :, 127:128], AF.Exp), S.cS)
        P.op("scalar", lambda e: e.activation(
            stat2[:, 0:8].rearrange("p (c o) -> p c o", o=1), afm3[:, :, 127:128],
            AF.Copy), S.cS)
        P.op("scalar", lambda e: e.activation(scr1, ps_big[:, 1024:2048], AF.Exp, scale=-1.0), S.cS)
        if with_e1:
            P.op("scalar", lambda e: e.activation(scr3, ps_big[:, 0:1024], AF.Exp), S.cS)
        P.wait("vector", S.cS)
        P.op("vector", lambda e, tile=tile: e.tensor_mul(
            kt_bf, k_tm[:, tile * 1024:(tile + 1) * 1024], scr1), S.cV)

    def mkv_and_update(tile):
        P.wait("tensor", S.cV)
        for h in range(16):
            P.op("tensor", lambda e, h=h, tile=tile: e.matmul(
                ps_mkv[0:64, 64 * h:64 * (h + 1)], kt_bf[:, h * 64:(h + 1) * 64],
                v_tm[:, tile * 1024 + h * 64:tile * 1024 + (h + 1) * 64],
                start=True, stop=True), S.cT if h == 15 else None)
        P.wait("vector", S.cT)
        e3hv = e3h[:].rearrange("p (h o) -> p h o", o=1)
        P.op("vector", lambda e, e3hv=e3hv: e.tensor_copy(
            e3hv[:, 0:16:2, :], e3[0:64, :].unsqueeze(2)), S.cV)
        P.op("vector", lambda e, e3hv=e3hv: e.tensor_copy(
            e3hv[:, 1:16:2, :], e3[64:128, :].unsqueeze(2)), S.cV)
        P.op("vector", lambda e: e.tensor_add(Stmp, S_sb[:], ps_mkv[0:64, :]), S.cV)
        P.op("vector", lambda e: e.tensor_mul(
            S_sb[:].rearrange("p (h v) -> p h v", v=64),
            Stmp.rearrange("p (h v) -> p h v", v=64),
            e3h[:].unsqueeze(2).broadcast_to([64, 16, 64])), S.cV)

    for oi in range(2):
        P.wait("vector", S.cT)
        P.wait("vector", S.sth)
        P.op("vector", lambda e: e.memset(S_sb[:], 0.0), S.cV)
        P.op("vector", lambda e: e.memset(aoct[:], 0.0), S.cV)
        for ci in range(8):
            tile = oi * 8 + ci
            chunk_preamble(tile, with_e1=False)
            if dbg and tile == 0:
                dsp = nc.dram_tensor("dspl" + tag, [128, 1024], F32, kind="ExternalOutput").ap()
                de2 = nc.dram_tensor("de2" + tag, [128, 1024], F32, kind="ExternalOutput").ap()
                de3 = nc.dram_tensor("de3" + tag, [128, 8], F32, kind="ExternalOutput").ap()
                dkt = nc.dram_tensor("dkt" + tag, [128, 1024], F32, kind="ExternalOutput").ap()
                P.wait("gpsimd", S.cS)
                P.wait("gpsimd", S.cV)
                P.op("gpsimd", lambda e, d=dsp: e.dma_start(out=d, in_=scr2), S.lds, 16)
                P.op("gpsimd", lambda e, d=de2: e.dma_start(out=d, in_=scr1), S.lds, 16)
                P.op("gpsimd", lambda e, d=de3: e.dma_start(out=d, in_=e3[:]), S.lds, 16)
                P.op("gpsimd", lambda e, d=dkt: e.dma_start(out=d, in_=kt_bf), S.lds, 16)
                P.wait("scalar", S.lds)
                P.wait("vector", S.lds)
            P.wait("vector", S.cS)
            P.op("vector", lambda e: e.tensor_add(aoct[:], aoct[:], stat2[:, 0:8]), S.cV)
            mkv_and_update(tile)
            if dbg and tile == 0:
                ds1 = nc.dram_tensor("ds1" + tag, [64, 1024], F32, kind="ExternalOutput").ap()
                P.wait("sync", S.cV)
                P.op("sync", lambda e, d=ds1: e.dma_start(out=d, in_=S_sb[:]), S.sth, 16)
                P.wait("vector", S.sth)
        a64v = a64[:].rearrange("p (h o) -> p h o", o=1)
        P.op("vector", lambda e, a64v=a64v: e.tensor_copy(
            a64v[:, 0:16:2, :], aoct[0:64, :].unsqueeze(2)), S.cV)
        P.op("vector", lambda e, a64v=a64v: e.tensor_copy(
            a64v[:, 1:16:2, :], aoct[64:128, :].unsqueeze(2)), S.cV)
        P.wait("sync", S.cV)
        P.op("sync", lambda e, oi=oi: e.dma_start(
            out=summ[:, oi * 1040:oi * 1040 + 16], in_=a64[:]), S.sth, 16)
        P.op("sync", lambda e, oi=oi: e.dma_start(
            out=summ[:, oi * 1040 + 16:oi * 1040 + 1040], in_=S_sb[:]), S.sth, 16)

    if upto <= 3:
        return

    P.wait("gpsimd", S.sth)
    P.op("gpsimd", lambda e: e.collective_compute(
        "AllGather", mybir.AluOpType.bypass,
        replica_groups=[[0, 1, 2, 3], [4, 5, 6, 7]],
        ins=[summ.opt()], outs=[summ_g.opt()]), S.cG, 1)
    P.wait("sync", S.cG)
    for j in range(8):
        rank, half = (j, 0) if j < 4 else (7 - j, 1)
        P.op("sync", lambda e, j=j, rank=rank, half=half: e.dma_start(
            out=a_all[:, 16 * j:16 * (j + 1)],
            in_=summ_g[64 * rank:64 * (rank + 1), half * 1040:half * 1040 + 16]),
            S.ldh, 16)
    P.wait("vector", S.ldh)
    for oi in range(2):
        P.wait("vector", S.sth)
        P.op("vector", lambda e: e.memset(S_sb[:], 0.0), S.cV)
        for jp in range(8):
            rank, half = (jp, 0) if jp < 4 else (7 - jp, 1)
            P.op("vector", lambda e: e.memset(acml[:], 0.0), S.cV)
            for jpp in range(8):
                tc = theta_sb[:, oi * 64 + jp * 8 + jpp:oi * 64 + jp * 8 + jpp + 1]
                P.op("vector", lambda e, jpp=jpp, tc=tc: e.scalar_tensor_tensor(
                    acml[:], a_all[:, 16 * jpp:16 * (jpp + 1)], tc, acml[:],
                    ALU.mult, ALU.add), S.cV)
            P.wait("scalar", S.cV)
            P.op("scalar", lambda e, oi=oi, jp=jp: e.activation(
                ccoef[:], acml[:], AF.Exp,
                bias=cbias_sb[:, oi * 8 + jp:oi * 8 + jp + 1]), S.cS)
            P.wait("sync", S.cV, S.cV.n)
            P.op("sync", lambda e, rank=rank, half=half: e.dma_start(
                out=sloc, in_=summ_g[64 * rank:64 * (rank + 1),
                                     half * 1040 + 16:half * 1040 + 1040]), S.ldh, 16)
            P.wait("vector", S.ldh)
            P.wait("vector", S.cS)
            P.op("vector", lambda e: e.tensor_mul(
                Stmp.rearrange("p (h v) -> p h v", v=64),
                sloc.rearrange("p (h v) -> p h v", v=64),
                ccoef[:].unsqueeze(2).broadcast_to([64, 16, 64])), S.cV)
            P.op("vector", lambda e: e.tensor_add(S_sb[:], S_sb[:], Stmp), S.cV)
        P.wait("sync", S.cV)
        P.op("sync", lambda e, oi=oi: e.dma_start(
            out=sin_d[:, 1024 * oi:1024 * (oi + 1)], in_=S_sb[:]), S.sth, 16)

    if upto <= 4:
        return

    stream_w("Wo")

    for oi in range(2):
        P.wait("sync", S.cV)
        P.wait("sync", S.sth)
        P.op("sync", lambda e, oi=oi: e.dma_start(
            out=S_sb[:], in_=sin_d[:, 1024 * oi:1024 * (oi + 1)]), S.ldh, 16)
        P.wait("vector", S.ldh)
        P.wait("vector", S.cT)
        for ci in range(8):
            tile = oi * 8 + ci
            P.wait("sync", S.cV)
            P.op("sync", lambda e, tile=tile: e.dma_start(
                out=xres[:], in_=x_in[128 * tile:128 * (tile + 1), :]), S.ldh, 16)
            chunk_preamble(tile, with_e1=True)
            if p2cut <= 1:
                continue
            P.op("vector", lambda e: e.tensor_copy(S_bf[:], S_sb[:]), S.cV)
            q3 = q_fm[:].rearrange("p (oc n) -> p oc n", n=2048)
            e13 = scr3.rearrange("p (c t) -> p c t", t=128)
            qt3 = qt64.rearrange("p (h t) -> p h t", t=128)
            P.op("vector", lambda e, q3=q3, e13=e13, qt3=qt3, tile=tile: e.tensor_mul(
                qt3[:, 0:16:2, :],
                q3[0:64, :, tile * 128:(tile + 1) * 128], e13[0:64, :, :]), S.cV)
            P.op("vector", lambda e, q3=q3, e13=e13, qt3=qt3, tile=tile: e.tensor_mul(
                qt3[:, 1:16:2, :],
                q3[64:128, :, tile * 128:(tile + 1) * 128], e13[64:128, :, :]), S.cV)
            P.wait("tensor", S.cV)
            for c in range(8):
                P.op("tensor", lambda e, c=c: e.transpose(
                    ps_t[:, 128 * c:128 * (c + 1)], kt_bf[:, 128 * c:128 * (c + 1)],
                    id_bf[:]), S.cT)
            P.wait("vector", S.cT)
            pt3 = ps_t[:].rearrange("p (c t) -> p c t", t=128)
            kf3 = ktf64.rearrange("p (h t) -> p h t", t=128)
            P.op("vector", lambda e, pt3=pt3, kf3=kf3: e.tensor_copy(
                kf3[:, 0:16:2, :], pt3[0:64, :, :]), S.cV)
            P.op("vector", lambda e, pt3=pt3, kf3=kf3: e.tensor_copy(
                kf3[:, 1:16:2, :], pt3[64:128, :, :]), S.cV)
            if dbg and tile == 0:
                dqt = nc.dram_tensor("dqt" + tag, [64, 2048], F32, kind="ExternalOutput").ap()
                dkf = nc.dram_tensor("dkf" + tag, [64, 2048], F32, kind="ExternalOutput").ap()
                P.wait("gpsimd", S.cV)
                P.op("gpsimd", lambda e, d=dqt: e.dma_start(out=d, in_=qt64), S.lds, 16)
                P.op("gpsimd", lambda e, d=dkf: e.dma_start(out=d, in_=ktf64), S.lds, 16)
                P.wait("tensor", S.lds)
            if p2cut <= 2:
                continue
            P.wait("tensor", S.cV)
            for h in range(16):
                P.op("tensor", lambda e, h=h: e.matmul(
                    ps_big[:, 128 * h:128 * (h + 1)],
                    ktf64[0:64, 128 * h:128 * (h + 1)],
                    qt64[0:64, 128 * h:128 * (h + 1)],
                    start=True, stop=True), S.cT if h == 15 else None)
            P.wait("vector", S.cT)
            P.op("vector", lambda e: e.tensor_mul(
                pt_bf.rearrange("p (h t) -> p h t", t=128),
                ps_big[:].rearrange("p (h t) -> p h t", t=128),
                mask_bf[:].unsqueeze(1).broadcast_to([128, 16, 128])), S.cV)
            if dbg and tile == 0:
                dpt = nc.dram_tensor("dpt" + tag, [128, 2048], F32, kind="ExternalOutput").ap()
                P.wait("gpsimd", S.cV)
                P.op("gpsimd", lambda e, d=dpt: e.dma_start(out=d, in_=pt_bf), S.lds, 16)
                P.wait("tensor", S.lds)
            if p2cut <= 3:
                continue
            P.wait("tensor", S.cV)
            for h in range(16):
                P.op("tensor", lambda e, h=h, tile=tile: e.matmul(
                    ps_o[:, 64 * h:64 * (h + 1)], pt_bf[:, 128 * h:128 * (h + 1)],
                    v_tm[:, tile * 1024 + 64 * h:tile * 1024 + 64 * (h + 1)],
                    start=True, stop=False), None)
                P.op("tensor", lambda e, h=h: e.matmul(
                    ps_o[:, 64 * h:64 * (h + 1)],
                    qt64[0:64, 128 * h:128 * (h + 1)],
                    S_bf[0:64, 64 * h:64 * (h + 1)],
                    start=False, stop=True), S.cT if h == 15 else None)
            if p2cut <= 4:
                continue
            P.wait("scalar", S.cT)
            P.op("scalar", lambda e: e.activation(scr1, ps_o[:], AF.Square,
                                                  scale=0.125), S.cS)
            P.wait("vector", S.cS)
            P.op("vector", lambda e: e.tensor_reduce(
                stat[:].rearrange("p (h o) -> p h o", o=1),
                scr1.rearrange("p (h d) -> p h d", d=64), AX.X, ALU.add), S.cV)
            P.op("vector", lambda e: e.tensor_copy(junkV[:], cvals[:, 0:1]), S.cV)
            P.wait("scalar", S.cV)
            P.op("scalar", lambda e: e.activation(stat2[:], stat[:], AF.Ln,
                                                  bias=EPS), S.cS)
            P.op("scalar", lambda e: e.activation(stat2[:], stat2[:], AF.Exp,
                                                  scale=-0.5), S.cS)
            P.wait("vector", S.cS)
            P.op("vector", lambda e: e.tensor_mul(
                scr3.rearrange("p (h d) -> p h d", d=64),
                ps_o[:].rearrange("p (h d) -> p h d", d=64),
                stat2[:].unsqueeze(2).broadcast_to([128, 16, 64])), S.cV)
            P.op("vector", lambda e, tile=tile: e.tensor_mul(
                og_bf, scr3, gs_tm[:, tile * 1024:(tile + 1) * 1024]), S.cV)
            if dbg and tile == 0:
                don = nc.dram_tensor("don" + tag, [128, 1024], F32, kind="ExternalOutput").ap()
                dog = nc.dram_tensor("dog" + tag, [128, 1024], F32, kind="ExternalOutput").ap()
                P.wait("gpsimd", S.cV)
                P.op("gpsimd", lambda e, d=don: e.dma_start(out=d, in_=scr3), S.lds, 16)
                P.op("gpsimd", lambda e, d=dog: e.dma_start(out=d, in_=og_bf), S.lds, 16)
                P.wait("tensor", S.lds)
                P.wait("scalar", S.lds)
            mkv_and_update(tile)
            P.wait("tensor", S.cV)
            for c in range(8):
                P.op("tensor", lambda e, c=c: e.transpose(
                    ps_t[:, 128 * c:128 * (c + 1)], og_bf[:, 128 * c:128 * (c + 1)],
                    id_bf[:]), S.cT)
            P.wait("vector", S.cT)
            P.op("vector", lambda e: e.tensor_copy(ogf_bf, ps_t[:]), S.cV)
            if p2cut <= 5:
                continue
            P.wait("tensor", S.cV)
            for dg in range(2):
                for kc in range(8):
                    P.op("tensor", lambda e, dg=dg, kc=kc: e.matmul(
                        ps_big[:, dg * 512:(dg + 1) * 512],
                        ogf_bf[:, kc * 128:(kc + 1) * 128],
                        wbuf[:, kc * 1024 + dg * 512:kc * 1024 + (dg + 1) * 512],
                        start=(kc == 0), stop=(kc == 7)),
                        S.cT if (dg == 1 and kc == 7) else None)
            P.wait("vector", S.cT)
            P.wait("vector", S.ldh)
            P.op("vector", lambda e: e.tensor_add(r_sb, ps_big[:, 0:1024], xres[:]), S.cV)
            P.op("vector", lambda e: e.tensor_copy(og_bf, r_sb), S.cV)
            P.wait("tensor", S.cV)
            for c in range(8):
                P.op("tensor", lambda e, c=c: e.transpose(
                    ps_t[:, 128 * c:128 * (c + 1)], og_bf[:, 128 * c:128 * (c + 1)],
                    id_bf[:]), S.cT)
            P.wait("vector", S.cT)
            P.op("vector", lambda e: e.tensor_copy(ogf_bf, ps_t[:]), S.cV)
            sq_bf = gs_tm[:, tile * 1024:(tile + 1) * 1024]
            P.wait("scalar", S.cV)
            P.op("scalar", lambda e, sq_bf=sq_bf: e.activation(
                sq_bf, ogf_bf, AF.Square, scale=1.0 / 32), S.cS)
            P.wait("tensor", S.cS)
            for c in range(8):
                P.op("tensor", lambda e, c=c: e.matmul(
                    ps_o[0:1, 0:128], onescol[:], ogf_bf[:, 128 * c:128 * (c + 1)],
                    start=(c == 0), stop=(c == 7)), None)
            for c in range(8):
                P.op("tensor", lambda e, c=c, sq_bf=sq_bf: e.matmul(
                    ps_o[0:1, 128:256], onescol[:], sq_bf[:, 128 * c:128 * (c + 1)],
                    start=(c == 0), stop=(c == 7)), S.cT if c == 7 else None)
            P.wait("scalar", S.cT)
            P.op("scalar", lambda e: e.activation(murow[:], ps_o[0:1, 0:128], AF.Copy,
                                                  scale=1.0 / 1024), S.cS)
            P.wait("vector", S.cS)
            P.op("vector", lambda e: e.tensor_mul(varrow[:], murow[:], murow[:]), S.cV)
            P.op("vector", lambda e: e.tensor_sub(varrow[:], ps_o[0:1, 128:256], varrow[:]), S.cV)
            P.wait("scalar", S.cV)
            P.op("scalar", lambda e: e.activation(varrow[:], varrow[:], AF.Ln,
                                                  bias=cvals[0:1, 0:1]), S.cS)
            P.op("scalar", lambda e: e.activation(rsrow[:], varrow[:], AF.Exp,
                                                  scale=-0.5), S.cS)
            P.wait("tensor", S.cS)
            P.op("tensor", lambda e: e.matmul(ps_o[0:128, 256:257], murow[:],
                                              cvals[0:1, 7:8],
                                              start=True, stop=True), None)
            P.op("tensor", lambda e: e.matmul(ps_o[0:128, 257:258], rsrow[:],
                                              cvals[0:1, 7:8],
                                              start=True, stop=True), S.cT)
            P.wait("vector", S.cT)
            P.op("vector", lambda e: e.tensor_copy(col1[:], ps_o[0:128, 256:257]), S.cV)
            P.op("vector", lambda e: e.tensor_copy(col2[:], ps_o[0:128, 257:258]), S.cV)
            P.op("vector", lambda e: e.tensor_scalar(
                scr2, r_sb, col1[:], col2[:], ALU.subtract, ALU.mult), S.cV)
            P.wait("sync", S.cV)
            P.op("sync", lambda e, tile=tile: e.dma_start(
                out=x_out[128 * tile:128 * (tile + 1), :], in_=scr2), S.sth, 16)


def emit_ln_tail(nc, P, S, B, r_f32, out_ap, gamma_bc=None, beta_bc=None, to_sbuf=None):
    og_bf, ogf_bf, sq_bf = B["og_bf"], B["ogf_bf"], B["sq_bf"]
    ps_t, ps_st = B["ps_t"], B["ps_st"]
    P.wait("vector", S.cT)
    P.op("vector", lambda e: e.tensor_copy(og_bf, r_f32), S.cV)
    P.wait("tensor", S.cV)
    for c in range(8):
        P.op("tensor", lambda e, c=c: e.transpose(
            ps_t[:, 128 * c:128 * (c + 1)], og_bf[:, 128 * c:128 * (c + 1)],
            B["id_bf"][:]), S.cT)
    P.wait("vector", S.cT)
    P.op("vector", lambda e: e.tensor_copy(ogf_bf, ps_t[:, 0:1024]), S.cV)
    P.wait("scalar", S.cV)
    P.op("scalar", lambda e: e.activation(sq_bf, ogf_bf, AF.Square, scale=1.0 / 32), S.cS)
    P.wait("tensor", S.cS)
    for c in range(8):
        P.op("tensor", lambda e, c=c: e.matmul(
            ps_st[0:1, 0:128], B["onescol"][:], ogf_bf[:, 128 * c:128 * (c + 1)],
            start=(c == 0), stop=(c == 7)), None)
    for c in range(8):
        P.op("tensor", lambda e, c=c: e.matmul(
            ps_st[0:1, 128:256], B["onescol"][:], sq_bf[:, 128 * c:128 * (c + 1)],
            start=(c == 0), stop=(c == 7)), S.cT if c == 7 else None)
    P.wait("scalar", S.cT)
    P.op("scalar", lambda e: e.activation(B["murow"][:], ps_st[0:1, 0:128], AF.Copy,
                                          scale=1.0 / 1024), S.cS)
    P.wait("vector", S.cS)
    P.op("vector", lambda e: e.tensor_mul(B["varrow"][:], B["murow"][:], B["murow"][:]), S.cV)
    P.op("vector", lambda e: e.tensor_sub(B["varrow"][:], ps_st[0:1, 128:256], B["varrow"][:]), S.cV)
    P.wait("scalar", S.cV)
    P.op("scalar", lambda e: e.activation(B["varrow"][:], B["varrow"][:], AF.Ln,
                                          bias=B["cvals"][0:1, 0:1]), S.cS)
    P.op("scalar", lambda e: e.activation(B["rsrow"][:], B["varrow"][:], AF.Exp,
                                          scale=-0.5), S.cS)
    P.wait("tensor", S.cS)
    onef = B["cvals"][0:1, 7:8]
    P.op("tensor", lambda e, onef=onef: e.matmul(
        ps_st[0:128, 256:257], B["murow"][:], onef, start=True, stop=True), None)
    P.op("tensor", lambda e, onef=onef: e.matmul(
        ps_st[0:128, 257:258], B["rsrow"][:], onef, start=True, stop=True), S.cT)
    P.wait("vector", S.cT)
    P.op("vector", lambda e: e.tensor_copy(B["col1"][:], ps_st[0:128, 256:257]), S.cV)
    P.op("vector", lambda e: e.tensor_copy(B["col2"][:], ps_st[0:128, 257:258]), S.cV)
    dst = B["scrO"] if to_sbuf is None else to_sbuf
    P.op("vector", lambda e, dst=dst: e.tensor_scalar(
        dst, r_f32, B["col1"][:], B["col2"][:], ALU.subtract, ALU.mult), S.cV)
    if gamma_bc is not None:
        P.op("vector", lambda e, dst=dst: e.tensor_mul(dst, dst, gamma_bc), S.cV)
        P.op("vector", lambda e, dst=dst: e.tensor_add(dst, dst, beta_bc), S.cV)
    if out_ap is not None:
        P.wait("sync", S.cV)
        P.op("sync", lambda e, dst=dst: e.dma_start(out=out_ap, in_=dst), S.sth, 16)


def emit_xfm_build(nc, P, S, B, x_in, x_fm):
    xres, stage2, ps_t, id_bf = B["xres"], B["stage2"], B["ps_t"], B["id_bf"]
    for t in range(NT):
        P.wait("sync", S.cV)
        P.op("sync", lambda e, t=t: e.dma_start(
            out=xres[:], in_=x_in[128 * t:128 * (t + 1), :]), S.ldh, 16)
        P.wait("vector", S.ldh)
        P.wait("vector", S.cT)
        P.op("vector", lambda e: e.tensor_copy(stage2, xres[:]), S.cV)
        P.wait("tensor", S.cV)
        for c in range(8):
            P.op("tensor", lambda e, c=c: e.transpose(
                ps_t[:, 128 * c:128 * (c + 1)], stage2[:, 128 * c:128 * (c + 1)],
                id_bf[:]), S.cT)
        P.wait("vector", S.cT)
        dst = x_fm[:].rearrange("p (fc n) -> p fc n", n=2048)[:, :, t * 128:(t + 1) * 128]
        P.op("vector", lambda e, d=dst: e.tensor_copy(d, ps_t[:, 0:1024].rearrange(
            "p (fc n) -> p fc n", n=128)), S.cV)


def emit_l0_ffn(nc, P, S, es, W, CN, x_in, x_out, tag="f0"):
    sb = lambda name, shape, dt: es.enter_context(nc.sbuf_tensor(name + tag, shape, dt))
    psa = lambda name, shape, dt: es.enter_context(nc.psum_tensor(name + tag, shape, dt))

    x_fm = sb("x_fm", [128, 16384], BF16)
    y_acc = sb("y_acc", [128, 16384], F32)
    g_fm = sb("g_fm", [128, 4096], BF16)
    w1b = sb("w1b", [128, 8192], BF16)
    w2b = sb("w2b", [128, 8192], BF16)
    rwf = sb("rwf", [128, 64], BF16)
    b1r = sb("b1r", [1, 4096], BF16)
    b2bc = sb("b2bc", [128, 1024], F32)
    ones512 = sb("ones512", [1, 512], BF16)
    xres = sb("xres", [128, 1024], F32)
    stage2 = sb("stage2", [128, 1024], BF16)
    scr1 = sb("scr1", [128, 1024], F32)
    scr2 = sb("scr2", [128, 1024], F32)
    rt = sb("rt", [128, 128], F32)
    fcol = sb("fcol", [128, 16], F32)
    m1 = sb("m1", [128, 16], F32)
    m2 = sb("m2", [128, 16], F32)
    tmp8 = sb("tmp8", [128, 128], F32)
    og_bf = sb("og_bf", [128, 1024], BF16)
    ogf_bf = sb("ogf_bf", [128, 1024], BF16)
    sq_bf = sb("sq_bf", [128, 1024], BF16)
    onescol = sb("onescol", [128, 1], BF16)
    murow = sb("murow", [1, 128], F32)
    rsrow = sb("rsrow", [1, 128], F32)
    varrow = sb("varrow", [1, 128], F32)
    col1 = sb("col1", [128, 1], F32)
    col2 = sb("col2", [128, 1], F32)
    id_bf = sb("id_bf", [128, 128], BF16)
    cvals = sb("cvals", [128, 8], F32)
    register_consts(nc, cvals)

    ps_h = psa("ps_h", [128, 2048], F32)
    ps_y = psa("ps_y", [128, 1024], F32)
    ps_st = psa("ps_st", [128, 512], F32)
    ps_t = psa("ps_t", [128, 1024], BF16)

    B = dict(og_bf=og_bf[:], ogf_bf=ogf_bf[:], sq_bf=sq_bf[:], ps_t=ps_t,
             ps_st=ps_st, id_bf=id_bf, onescol=onescol, murow=murow,
             rsrow=rsrow, varrow=varrow, col1=col1, col2=col2, cvals=cvals,
             scrO=scr2[:], xres=xres, stage2=stage2[:])

    for dst, srcap in [(id_bf[:], CN["ident"]), (onescol[:], CN["ones_col"]),
                       (b1r[:], W["b1"]), (ones512[:], CN["ones_row2048"][:, 0:512]),
                       (rwf[:].rearrange("p (kc e) -> p kc e", e=8),
                        W["router_w"].rearrange("(kc p) e -> p kc e", p=128))]:
        P.op("gpsimd", lambda e, d=dst, s=srcap: e.dma_start(out=d, in_=s), S.lds, 16)
    P.op("sync", lambda e: e.dma_start(out=cvals[:], in_=CN["constvals"]), S.ldh, 16)
    P.op("sync", lambda e: e.dma_start(out=b2bc[:], in_=W["b2_t"]), S.ldh, 16)
    P.wait("tensor", S.lds)
    P.wait("tensor", S.ldh)

    emit_xfm_build(nc, P, S, B, x_in, x_fm)

    P.wait("scalar", S.ldh)
    P.op("scalar", lambda e: e.activation(tmp8[:, 0:8], cvals[:, 0:8], AF.Exp), S.cS)
    P.op("scalar", lambda e: e.activation(tmp8[:, 0:1], cvals[:, 7:8], AF.Ln), S.cS)
    P.op("scalar", lambda e: e.activation(tmp8[:, 0:8], cvals[:, 0:8], AF.Gelu_apprx_tanh), S.cS)
    P.op("scalar", lambda e: e.activation(tmp8[:, 0:8], cvals[:, 0:8], AF.Square), S.cS)

    P.wait("tensor", S.cV)
    for t in range(NT):
        P.wait("tensor", S.cV)
        for kc in range(8):
            P.op("tensor", lambda e, t=t, kc=kc: e.matmul(
                ps_y[:, 8 * t:8 * (t + 1)],
                x_fm[:, kc * 2048 + t * 128:kc * 2048 + (t + 1) * 128],
                rwf[:, kc * 8:(kc + 1) * 8],
                start=(kc == 0), stop=(kc == 7)), S.cT if kc == 7 else None)
    P.wait("vector", S.cT)
    P.op("vector", lambda e: e.tensor_copy(rt[:], ps_y[:, 0:128]), S.cV)
    r3 = rt[:].rearrange("p (t e) -> p t e", e=8)
    t3 = tmp8[:].rearrange("p (t e) -> p t e", e=8)
    m1v = m1[:].rearrange("p (t o) -> p t o", o=1)
    m2v = m2[:].rearrange("p (t o) -> p t o", o=1)

    def tree(op, srcv, outv):
        P.op("vector", lambda e: e.tensor_tensor(t3[:, :, 0:4], srcv[:, :, 0:4], srcv[:, :, 4:8], op), S.cV)
        P.op("vector", lambda e: e.tensor_tensor(t3[:, :, 0:2], t3[:, :, 0:2], t3[:, :, 2:4], op), S.cV)
        P.op("vector", lambda e: e.tensor_tensor(outv, t3[:, :, 0:1], t3[:, :, 1:2], op), S.cV)

    tree(ALU.max, r3, m1v)
    P.op("vector", lambda e: e.tensor_sub(r3, r3, m1v.broadcast_to([128, 16, 8])), S.cV)
    P.wait("scalar", S.cV)
    P.op("scalar", lambda e: e.activation(rt[:], rt[:], AF.Exp), S.cS)
    P.wait("vector", S.cS)
    tree(ALU.add, r3, m2v)
    P.wait("scalar", S.cV)
    P.op("scalar", lambda e: e.activation(m2[:], m2[:], AF.Ln), S.cS)
    P.op("scalar", lambda e: e.activation(m2[:], m2[:], AF.Exp, scale=-1.0), S.cS)
    P.wait("vector", S.cS)
    P.op("vector", lambda e: e.tensor_mul(r3, r3, m2v.broadcast_to([128, 16, 8])), S.cV)
    tree(ALU.max, r3, m1v)
    P.op("vector", lambda e: e.tensor_tensor(t3, r3, m1v.broadcast_to([128, 16, 8]), ALU.is_ge), S.cV)
    P.op("vector", lambda e: e.scalar_tensor_tensor(t3, t3, -1e30, r3, ALU.mult, ALU.add), S.cV)
    P.op("vector", lambda e: e.tensor_tensor(t3[:, :, 0:4], t3[:, :, 0:4], t3[:, :, 4:8], ALU.max), S.cV)
    P.op("vector", lambda e: e.tensor_tensor(t3[:, :, 0:2], t3[:, :, 0:2], t3[:, :, 2:4], ALU.max), S.cV)
    P.op("vector", lambda e: e.tensor_tensor(m2v, t3[:, :, 0:1], t3[:, :, 1:2], ALU.max), S.cV)
    P.op("vector", lambda e: e.tensor_add(m1[:], m1[:], m2[:]), S.cV)
    P.wait("scalar", S.cV)
    P.op("scalar", lambda e: e.activation(m2[:], m1[:], AF.Ln), S.cS)
    P.op("scalar", lambda e: e.activation(m2[:], m2[:], AF.Exp, scale=-1.0), S.cS)
    P.wait("vector", S.cS)
    P.op("vector", lambda e: e.tensor_mul(fcol[:], m1[:], m2[:]), S.cV)

    for half in range(4):
        h0 = half * 1024
        P.wait("gpsimd", S.cT)
        s1 = W["w1"][:, h0:h0 + 1024].rearrange("(kc p) n -> p kc n", p=128)
        P.op("gpsimd", lambda e, s=s1: e.dma_start(
            out=w1b[:].rearrange("p (kc n) -> p kc n", n=1024), in_=s), S.lds, 16)
        s2 = W["w2"][h0:h0 + 1024, :].rearrange("(hc p) n -> p hc n", p=128)
        P.op("gpsimd", lambda e, s=s2: e.dma_start(
            out=w2b[:].rearrange("p (hc n) -> p hc n", n=1024), in_=s), S.lds, 16)
        P.wait("tensor", S.lds)
        for tg in range(4):
            for hc in range(8):
                slot = hc % 4
                po = ps_h[:, 512 * slot:512 * (slot + 1)]
                P.wait("tensor", S.cS)
                for kc in range(8):
                    P.op("tensor", lambda e, po=po, kc=kc, hc=hc, tg=tg: e.matmul(
                        po, w1b[:, kc * 1024 + hc * 128:kc * 1024 + (hc + 1) * 128],
                        x_fm[:, kc * 2048 + tg * 512:kc * 2048 + (tg + 1) * 512],
                        start=(kc == 0), stop=False), None)
                P.op("tensor", lambda e, po=po, hc=hc, h0=h0: e.matmul(
                    po, b1r[0:1, h0 + hc * 128:h0 + (hc + 1) * 128], ones512[:],
                    start=False, stop=True), S.cT)
                P.wait("scalar", S.cT)
                P.op("scalar", lambda e, po=po, hc=hc: e.activation(
                    g_fm[:, hc * 512:(hc + 1) * 512], po, AF.Gelu_apprx_tanh), S.cS)
            P.wait("tensor", S.cS)
            for tt in range(4):
                tile = tg * 4 + tt
                for dg in range(2):
                    po = ps_y[:, dg * 512:(dg + 1) * 512]
                    P.wait("tensor", S.cV)
                    for hc in range(8):
                        P.op("tensor", lambda e, po=po, hc=hc, tt=tt, dg=dg: e.matmul(
                            po, g_fm[:, hc * 512 + tt * 128:hc * 512 + tt * 128 + 128],
                            w2b[:, hc * 1024 + dg * 512:hc * 1024 + (dg + 1) * 512],
                            start=(hc == 0), stop=(hc == 7)), S.cT if hc == 7 else None)
                    P.wait("vector", S.cT)
                    ya = y_acc[:, tile * 1024 + dg * 512:tile * 1024 + (dg + 1) * 512]
                    if half == 0:
                        P.op("vector", lambda e, po=po, ya=ya: e.tensor_copy(ya, po), S.cV)
                    else:
                        P.op("vector", lambda e, po=po, ya=ya: e.tensor_add(ya, ya, po), S.cV)

    if DBG_L0F:
        dfc = nc.dram_tensor("dfc", [128, 16], F32, kind="ExternalOutput").ap()
        dya = nc.dram_tensor("dya", [128, 16384], F32, kind="ExternalOutput").ap()
        drt = nc.dram_tensor("drt", [128, 128], F32, kind="ExternalOutput").ap()
        P.wait("gpsimd", S.cV)
        P.op("gpsimd", lambda e: e.dma_start(out=dfc, in_=fcol[:]), S.lds, 16)
        P.op("gpsimd", lambda e: e.dma_start(out=dya, in_=y_acc[:]), S.lds, 16)
        P.op("gpsimd", lambda e: e.dma_start(out=drt, in_=rt[:]), S.lds, 16)
        P.wait("vector", S.lds)

    for tile in range(NT):
        P.wait("sync", S.cV)
        P.op("sync", lambda e, tile=tile: e.dma_start(
            out=xres[:], in_=x_in[128 * tile:128 * (tile + 1), :]), S.ldh, 16)
        P.wait("vector", S.ldh)
        P.wait("vector", S.sth)
        P.op("vector", lambda e, tile=tile: e.tensor_add(
            scr1[:], y_acc[:, tile * 1024:(tile + 1) * 1024], b2bc[:]), S.cV)
        P.op("vector", lambda e, tile=tile: e.tensor_single_scalar(
            scr1[:], scr1[:], fcol[:, tile:tile + 1], ALU.mult), S.cV)
        P.op("vector", lambda e: e.tensor_add(scr1[:], scr1[:], xres[:]), S.cV)
        emit_ln_tail(nc, P, S, B, scr1[:],
                     x_out[128 * tile:128 * (tile + 1), :])


def emit_l1_attn(nc, P, S, es, W, CN, CI, x_in, x_out, tag="a1", upto=99, fused=False,
                 bg_dma=None):
    sb = lambda name, shape, dt: es.enter_context(nc.sbuf_tensor(name + tag, shape, dt))
    psa = lambda name, shape, dt: es.enter_context(nc.psum_tensor(name + tag, shape, dt))
    from concourse.bass import IndirectOffsetOnAxis

    x_fm = sb("x_fm", [128, 16384], BF16)
    big2 = sb("big2", [128, 16384], BF16)
    kf = sb("kf", [128, 4096], BF16)
    v_sb = sb("v_sb", [128, 4096], BF16)
    V65 = sb("V65", [128, 4160], BF16)
    E_sb = sb("E_sb", [128, 2048], BF16)
    obf = sb("obf", [65, 512], BF16)
    zrow = sb("zrow", [1, 512], F32)
    zr = sb("zr", [128, 4], F32)
    wbuf = sb("wbuf", [128, 8192], BF16)
    xres = sb("xres", [128, 1024], F32)
    stage2 = sb("stage2", [128, 1024], BF16)
    scr1 = sb("scr1", [128, 1024], F32)
    scr2 = sb("scr2", [128, 1024], F32)
    scr3 = sb("scr3", [128, 1024], F32)
    att_bf = sb("att_bf", [128, 1024], BF16)
    lnfm = sb("lnfm", [128, 1024], BF16)
    og_bf = sb("og_bf", [128, 1024], BF16)
    ogf_bf = sb("ogf_bf", [128, 1024], BF16)
    sq_bf = sb("sq_bf", [128, 1024], BF16)
    bqr = sb("bqr", [1, 1024], BF16)
    bkr = sb("bkr", [1, 256], BF16)
    ones512 = sb("ones512", [1, 512], BF16)
    bvr = sb("bvr", [1, 256], BF16)
    bor = sb("bor", [1, 1024], BF16)
    lng_bc = sb("lng_bc", [128, 1024], F32)
    lnb_bc = sb("lnb_bc", [128, 1024], F32)
    dmask = sb("dmask", [128, 2048], BF16)
    onescol = sb("onescol", [128, 1], BF16)
    onesrow = sb("onesrow", [1, 128], BF16)
    murow = sb("murow", [1, 128], F32)
    rsrow = sb("rsrow", [1, 128], F32)
    varrow = sb("varrow", [1, 128], F32)
    col1 = sb("col1", [128, 1], F32)
    col2 = sb("col2", [128, 1], F32)
    id_bf = sb("id_bf", [128, 128], BF16)
    kidx = sb("kidx", [64, 8], mybir.dt.int32)
    qidx = sb("qidx", [64, 32], mybir.dt.int32)
    vidx = sb("vidx", [128, 64], mybir.dt.int32)
    oidx = sb("oidx", [128, 64], mybir.dt.int32)
    cvals = sb("cvals", [128, 8], F32)
    register_consts(nc, cvals)

    K_sb = x_fm[0:64, 0:8192]
    qh_sb = x_fm[0:64, 8192:16384]

    ps_sc = psa("ps_sc", [128, 2048], F32)
    ps_o = psa("ps_o", [128, 512], F32)
    ps_st = psa("ps_st", [128, 512], F32)
    ps_t = psa("ps_t", [128, 1024], BF16)

    B = dict(og_bf=og_bf[:], ogf_bf=ogf_bf[:], sq_bf=sq_bf[:], ps_t=ps_t,
             ps_st=ps_st, id_bf=id_bf, onescol=onescol, murow=murow,
             rsrow=rsrow, varrow=varrow, col1=col1, col2=col2, cvals=cvals,
             scrO=scr2[:], xres=xres, stage2=stage2[:])

    qkv_d = nc.dram_tensor("qkv_d" + tag, [1536, 2048], BF16).ap()
    qkv_g = nc.dram_tensor("qkv_g" + tag, [6144, 2048], BF16).ap()

    for dst, srcap in [(id_bf[:], CN["ident"]), (onescol[:], CN["ones_col"]),
                       (onesrow[:], CN["ones_row"]), (dmask[:], CN["diag_mask"]),
                       (bvr[:], W["bv"]), (bor[:], W["bo"]), (bqr[:], W["bq"]),
                       (bkr[:], W["bk"]), (ones512[:], CN["ones_row2048"][:, 0:512])]:
        P.op("gpsimd", lambda e, d=dst, s=srcap: e.dma_start(out=d, in_=s), S.lds, 16)
    for dst, srcap in [(cvals[:], CN["constvals"]), (lng_bc[:], W["lng_t"]),
                       (lnb_bc[:], W["lnb_t"]), (kidx[:], CI["kidx"]),
                       (qidx[:], CI["qidx"]), (vidx[:], CI["vidx"]),
                       (oidx[:], CI["oidx"])]:
        P.op("sync", lambda e, d=dst, s=srcap: e.dma_start(out=d, in_=s), S.ldh, 16)
    P.wait("tensor", S.lds)
    P.wait("tensor", S.ldh)

    if upto != 16.5:
        emit_xfm_build(nc, P, S, B, x_in, x_fm)
    P.wait("scalar", S.ldh)
    P.op("scalar", lambda e: e.activation(scr1[:, 0:8], cvals[:, 0:8], AF.Exp), S.cS)
    P.op("scalar", lambda e: e.activation(scr1[:, 0:1], cvals[:, 7:8], AF.Ln), S.cS)
    P.op("scalar", lambda e: e.activation(scr1[:, 0:8], cvals[:, 0:8], AF.Square), S.cS)


    def stream_w(name):
        P.wait("gpsimd", S.cT)
        P.op("gpsimd", lambda e, s=W[name].rearrange("(kc p) n -> p kc n", p=128),
             w=W[name].shape[1]: e.dma_start(
                 out=wbuf[:, 0:8 * w].rearrange("p (kc n) -> p kc n", n=w), in_=s),
             S.lds, 16)
        P.wait("tensor", S.lds)

    if upto == 16.5:
        P.wait("sync", S.cS)
        P.wait("sync", S.cV)
        P.op("sync", lambda e: e.dma_start(
            out=q_d.rearrange("(oc p) t -> p oc t", p=128),
            in_=big2[:].rearrange("p (oc t) -> p oc t", t=2048)), S.sth, 16)
        P.wait("gpsimd", S.sth)
        for din, dout in [(q_d, q_g), (k_d, k_g), (v_d, v_g)]:
            P.op("gpsimd", lambda e, a=din, b=dout: e.collective_compute(
                "AllGather", mybir.AluOpType.bypass,
                replica_groups=[[0, 1, 2, 3], [4, 5, 6, 7]],
                ins=[a.opt()], outs=[b.opt()]), S.cG, 1)
        P.wait("gpsimd", S.cG)
        return
    stream_w("Wq")
    P.wait("tensor", S.cV)
    for oc in range(8):
        for tg in range(4):
            slot = (oc * 4 + tg) % 4
            po = ps_sc[:, 512 * slot:512 * (slot + 1)]
            P.wait("tensor", S.cS)
            P.wait("tensor", S.cV)
            for kc in range(8):
                P.op("tensor", lambda e, po=po, kc=kc, oc=oc, tg=tg: e.matmul(
                    po, wbuf[:, kc * 1024 + oc * 128:kc * 1024 + (oc + 1) * 128],
                    x_fm[:, kc * 2048 + tg * 512:kc * 2048 + (tg + 1) * 512],
                    start=(kc == 0), stop=False), None)
            P.op("tensor", lambda e, po=po, oc=oc: e.matmul(
                po, bqr[0:1, oc * 128:(oc + 1) * 128], ones512[:],
                start=False, stop=True), S.cT)
            P.wait("scalar", S.cT)
            P.op("scalar", lambda e, po=po, oc=oc, tg=tg: e.activation(
                big2[:, oc * 2048 + tg * 512:oc * 2048 + (tg + 1) * 512], po,
                AF.Copy, scale=0.125), S.cS)
    stream_w("Wk")
    for c in range(2):
        for tg in range(4):
            slot = (c * 4 + tg) % 4
            po = ps_sc[:, 512 * slot:512 * (slot + 1)]
            P.wait("tensor", S.cS)
            for kc in range(8):
                P.op("tensor", lambda e, po=po, kc=kc, c=c, tg=tg: e.matmul(
                    po, wbuf[:, kc * 256 + c * 128:kc * 256 + (c + 1) * 128],
                    x_fm[:, kc * 2048 + tg * 512:kc * 2048 + (tg + 1) * 512],
                    start=(kc == 0), stop=False), None)
            P.op("tensor", lambda e, po=po, c=c: e.matmul(
                po, bkr[0:1, c * 128:(c + 1) * 128], ones512[:],
                start=False, stop=True), S.cT)
            P.wait("scalar", S.cT)
            P.op("scalar", lambda e, po=po, c=c, tg=tg: e.activation(
                kf[:, c * 2048 + tg * 512:c * 2048 + (tg + 1) * 512], po,
                AF.Copy), S.cS)
    stream_w("Wv")
    for t in range(NT):
        slot = t % 4
        po = ps_sc[:, 512 * slot:512 * slot + 256]
        P.wait("tensor", S.cV)
        P.wait("tensor", S.cS)
        for kc in range(8):
            P.op("tensor", lambda e, po=po, kc=kc, t=t: e.matmul(
                po, x_fm[:, kc * 2048 + t * 128:kc * 2048 + (t + 1) * 128],
                wbuf[:, kc * 256:(kc + 1) * 256],
                start=(kc == 0), stop=False), None)
        P.op("tensor", lambda e, po=po: e.matmul(
            po, onesrow[:], bvr[:], start=False, stop=True), S.cT)
        P.wait("vector", S.cT)
        P.op("vector", lambda e, po=po, t=t: e.tensor_copy(
            v_sb[:, t * 256:(t + 1) * 256], po), S.cV)

    P.wait("sync", S.cS)
    P.wait("sync", S.cV)
    P.op("sync", lambda e: e.dma_start(
        out=qkv_d[0:1024, :].rearrange("(oc p) t -> p oc t", p=128),
        in_=big2[:].rearrange("p (oc t) -> p oc t", t=2048)), S.sth, 16)
    P.op("sync", lambda e: e.dma_start(
        out=qkv_d[1024:1280, :].rearrange("(c p) t -> p c t", p=128),
        in_=kf[:].rearrange("p (c t) -> p c t", t=2048)), S.sth, 16)
    P.op("sync", lambda e: e.dma_start(
        out=qkv_d[1280:1536, :].rearrange("r (t2 n) -> (r t2) n", n=256)
        .rearrange("(t p) n -> p t n", p=128),
        in_=v_sb[:].rearrange("p (t n) -> p t n", n=256)), S.sth, 16)
    P.wait("gpsimd", S.sth)
    P.op("gpsimd", lambda e: e.collective_compute(
        "AllGather", mybir.AluOpType.bypass,
        replica_groups=[[0, 1, 2, 3], [4, 5, 6, 7]],
        ins=[qkv_d.opt()], outs=[qkv_g.opt()]), S.cG, 1)

    P.wait("vector", S.cG, S.cG.n)
    P.wait("vector", S.sth)
    P.op("vector", lambda e: e.memset(
        V65[:].rearrange("p (kt c) -> p kt c", c=65)[:, :, 64:65], 1.0), S.cV)
    P.wait("gpsimd", S.cG)
    P.wait("gpsimd", S.cV)
    for r in range(4):
        for half in range(2):
            oct_ = r if half == 0 else 7 - r
            P.op("gpsimd", lambda e, r=r, half=half, oct_=oct_: e.indirect_dma_start(
                K_sb[:, oct_ * 1024:(oct_ + 1) * 1024], None,
                qkv_g[:, :],
                IndirectOffsetOnAxis(kidx[:, 2 * r + half:2 * r + half + 1], 1)), S.lds, 16)
    for kt in range(64):
        P.op("gpsimd", lambda e, kt=kt: e.indirect_dma_start(
            V65[:, kt * 65:kt * 65 + 64], None,
            qkv_g[:, :],
            IndirectOffsetOnAxis(vidx[:, kt:kt + 1], 1)), S.lds, 16)

    if upto <= 3:
        return
    nqb = 16
    for gp in range(4):
        P.wait("gpsimd", S.cT)
        for r in range(4):
            for half in range(2):
                oct_ = r if half == 0 else 7 - r
                P.op("gpsimd", lambda e, gp=gp, r=r, half=half, oct_=oct_:
                     e.indirect_dma_start(
                         qh_sb[:, oct_ * 1024:(oct_ + 1) * 1024], None,
                         qkv_g[:, :],
                         IndirectOffsetOnAxis(
                             qidx[:, gp * 8 + 2 * r + half:gp * 8 + 2 * r + half + 1], 1)),
                     S.lds, 16)
        if bg_dma is not None:
            bg_dma(gp)
        P.wait("tensor", S.lds)
        for qb in range(nqb):
            for ktg in range(qb + 1):
                P.wait("tensor", S.cS)
                P.wait("tensor", S.cV)
                for j in range(4):
                    kt = ktg * 4 + j
                    P.op("tensor", lambda e, kt=kt, j=j, qb=qb: e.matmul(
                        ps_sc[:, j * 512:(j + 1) * 512],
                        K_sb[:, kt * 128:(kt + 1) * 128],
                        qh_sb[:, qb * 512:(qb + 1) * 512],
                        start=True, stop=True), S.cT if j == 3 else None)
                P.wait("scalar", S.cT)
                P.op("scalar", lambda e: e.activation(E_sb[:], ps_sc[:], AF.Exp), S.cS)
                if ktg == qb:
                    P.wait("vector", S.cS)
                    P.op("vector", lambda e: e.tensor_mul(E_sb[:], E_sb[:], dmask[:]), S.cV)
                    P.wait("tensor", S.cV)
                P.wait("tensor", S.cS)
                for j in range(4):
                    kt = ktg * 4 + j
                    P.op("tensor", lambda e, kt=kt, j=j, qb=qb, ktg=ktg: e.matmul(
                        ps_o[:, 0:512][0:65, :] if False else ps_o[0:65, 0:512],
                        V65[:, kt * 65:(kt + 1) * 65],
                        E_sb[:, j * 512:(j + 1) * 512],
                        start=(ktg == 0 and j == 0), stop=(ktg == qb and j == 3)),
                        S.cT if (ktg == qb and j == 3) else None)
            P.wait("scalar", S.cT)
            P.wait("scalar", S.cV)
            P.op("scalar", lambda e: e.activation(obf[:], ps_o[0:65, 0:512], AF.Copy), S.cS)
            P.wait("vector", S.cS)
            P.op("vector", lambda e: e.tensor_copy(zrow[:], ps_o[64:65, 0:512]), S.cV)
            P.wait("tensor", S.cV)
            for s_ in range(4):
                P.op("tensor", lambda e, s_=s_: e.matmul(
                    ps_st[0:128, s_:s_ + 1], zrow[0:1, 128 * s_:128 * (s_ + 1)],
                    cvals[0:1, 7:8], start=True, stop=True),
                    S.cT if s_ == 3 else None)
            P.wait("scalar", S.cT)
            P.op("scalar", lambda e: e.activation(zr[:, 0:4], ps_st[0:128, 0:4], AF.Ln), S.cS)
            P.op("scalar", lambda e: e.activation(zr[:, 0:4], zr[:, 0:4], AF.Exp,
                                                  scale=-1.0), S.cS)
            P.wait("tensor", S.cS)
            for s_ in range(4):
                P.op("tensor", lambda e, s_=s_: e.transpose(
                    ps_t[:, 64 * s_:64 * (s_ + 1)], obf[0:64, 128 * s_:128 * (s_ + 1)],
                    id_bf[0:64, 0:64]), S.cT if s_ == 3 else None)
            P.wait("vector", S.cT)
            for s_ in range(4):
                gt = qb * 4 + s_
                P.op("vector", lambda e, s_=s_, gt=gt, gp=gp: e.tensor_single_scalar(
                    big2[:, gt * 256 + gp * 64:gt * 256 + (gp + 1) * 64],
                    ps_t[:, 64 * s_:64 * (s_ + 1)], zr[:, s_:s_ + 1], ALU.mult), S.cV)

    if fused:
        o_d = nc.dram_tensor("o_d" + tag, [8192, 256], BF16).ap()
        o_g = nc.dram_tensor("o_g" + tag, [32768, 256], BF16).ap()
        P.wait("sync", S.cV)
        P.op("sync", lambda e: e.dma_start(
            out=o_d.rearrange("(t p) n -> p t n", p=128),
            in_=big2[:].rearrange("p (t n) -> p t n", n=256)), S.sth, 16)
        P.wait("gpsimd", S.sth)
        P.op("gpsimd", lambda e: e.collective_compute(
            "AllGather", mybir.AluOpType.bypass,
            replica_groups=[[0, 1, 2, 3], [4, 5, 6, 7]],
            ins=[o_d.opt()], outs=[o_g.opt()]), S.cG, 1)
        return o_g
    o_out = nc.dram_tensor("o_out" + tag, [8192, 256], BF16, kind="ExternalOutput").ap()
    P.wait("sync", S.cV)
    P.op("sync", lambda e: e.dma_start(
        out=o_out.rearrange("(t p) n -> p t n", p=128),
        in_=big2[:].rearrange("p (t n) -> p t n", n=256)), S.sth, 16)


def emit_l1_attn_fin(nc, P, S, es, W, CN, attn_in, x_in, x_out, tag="a1f",
                     o_gather=None):
    sb = lambda name, shape, dt: es.enter_context(nc.sbuf_tensor(name + tag, shape, dt))
    psa = lambda name, shape, dt: es.enter_context(nc.psum_tensor(name + tag, shape, dt))
    from concourse.bass import IndirectOffsetOnAxis
    wbuf = sb("wbuf", [128, 8192], BF16)
    xres = sb("xres", [128, 1024], F32)
    stage2 = sb("stage2", [128, 1024], BF16)
    scr1 = sb("scr1", [128, 1024], F32)
    scr2 = sb("scr2", [128, 1024], F32)
    scr3 = sb("scr3", [128, 1024], F32)
    att_bf = sb("att_bf", [128, 1024], BF16)
    lnfm = sb("lnfm", [128, 1024], BF16)
    og_bf = sb("og_bf", [128, 1024], BF16)
    ogf_bf = sb("ogf_bf", [128, 1024], BF16)
    sq_bf = sb("sq_bf", [128, 1024], BF16)
    if o_gather is not None:
        att4 = sb("att4", [128, 1024], BF16)
        aidx = sb("aidx", [128, 64], mybir.dt.int32)
    bor = sb("bor", [1, 1024], BF16)
    lng_bc = sb("lng_bc", [128, 1024], F32)
    lnb_bc = sb("lnb_bc", [128, 1024], F32)
    onescol = sb("onescol", [128, 1], BF16)
    onesrow = sb("onesrow", [1, 128], BF16)
    murow = sb("murow", [1, 128], F32)
    rsrow = sb("rsrow", [1, 128], F32)
    varrow = sb("varrow", [1, 128], F32)
    col1 = sb("col1", [128, 1], F32)
    col2 = sb("col2", [128, 1], F32)
    id_bf = sb("id_bf", [128, 128], BF16)
    cvals = sb("cvals", [128, 8], F32)
    register_consts(nc, cvals)
    ps_sc = psa("ps_sc", [128, 1024], F32)
    ps_st = psa("ps_st", [128, 512], F32)
    ps_t = psa("ps_t", [128, 1024], BF16)
    B = dict(og_bf=og_bf[:], ogf_bf=ogf_bf[:], sq_bf=sq_bf[:], ps_t=ps_t,
             ps_st=ps_st, id_bf=id_bf, onescol=onescol, murow=murow,
             rsrow=rsrow, varrow=varrow, col1=col1, col2=col2, cvals=cvals,
             scrO=scr2[:], xres=xres, stage2=stage2[:])
    for dst, srcap in [(id_bf[:], CN["ident"]), (onescol[:], CN["ones_col"]),
                       (onesrow[:], CN["ones_row"]), (bor[:], W["bo"])]:
        P.op("gpsimd", lambda e, d=dst, s=srcap: e.dma_start(out=d, in_=s), S.lds, 16)
    if o_gather is not None:
        P.op("gpsimd", lambda e: e.dma_start(out=aidx[:], in_=o_gather[1]), S.lds, 16)
    for dst, srcap in [(cvals[:], CN["constvals"]), (lng_bc[:], W["lng_t"]),
                       (lnb_bc[:], W["lnb_t"])]:
        P.op("sync", lambda e, d=dst, s=srcap: e.dma_start(out=d, in_=s), S.ldh, 16)
    P.op("gpsimd", lambda e: e.dma_start(
        out=wbuf[:].rearrange("p (kc n) -> p kc n", n=1024),
        in_=W["Wo"].rearrange("(kc p) n -> p kc n", p=128)), S.lds, 16)
    P.wait("tensor", S.lds)
    P.wait("tensor", S.ldh)
    P.wait("vector", S.lds)
    P.wait("vector", S.ldh)
    P.wait("scalar", S.ldh)
    P.op("scalar", lambda e: e.activation(scr1[:, 0:8], cvals[:, 0:8], AF.Exp), S.cS)
    P.op("scalar", lambda e: e.activation(scr1[:, 0:1], cvals[:, 7:8], AF.Ln), S.cS)
    P.op("scalar", lambda e: e.activation(scr1[:, 0:8], cvals[:, 0:8], AF.Square), S.cS)

    abv = att_bf[:].rearrange("p (g r c) -> p g r c", r=4, c=64)
    if o_gather is not None:
        P.wait("gpsimd", S.cG)
    for tile in range(NT):
        if o_gather is None:
            P.wait("sync", S.cV)
            P.op("sync", lambda e, tile=tile: e.dma_start(
                out=att_bf[:], in_=attn_in[128 * tile:128 * (tile + 1), :]), S.ldh, 16)
            P.wait("vector", S.ldh)
        else:
            P.wait("gpsimd", S.cV)
            for r in range(4):
                P.op("gpsimd", lambda e, tile=tile, r=r: e.indirect_dma_start(
                    att4[:, 256 * r:256 * (r + 1)], None, o_gather[0][:, :],
                    IndirectOffsetOnAxis(aidx[:, 4 * tile + r:4 * tile + r + 1], 1)),
                    S.lds, 16)
            P.wait("vector", S.lds)
            for r in range(4):
                P.op("vector", lambda e, r=r: e.tensor_copy(
                    abv[:, :, r, :],
                    att4[:, 256 * r:256 * (r + 1)].rearrange("p (g c) -> p g c", c=64)),
                    S.cV)
        P.wait("vector", S.sth)
        P.op("vector", lambda e: e.tensor_copy(scr1[:], att_bf[:]), S.cV)
        emit_ln_tail(nc, P, S, B, scr1[:], None, gamma_bc=lng_bc[:],
                     beta_bc=lnb_bc[:], to_sbuf=scr3[:])
        P.wait("vector", S.cS)
        P.op("vector", lambda e: e.tensor_copy(stage2[:], scr3[:]), S.cV)
        P.wait("tensor", S.cV)
        for c in range(8):
            P.op("tensor", lambda e, c=c: e.transpose(
                ps_t[:, 128 * c:128 * (c + 1)], stage2[:, 128 * c:128 * (c + 1)],
                id_bf[:]), S.cT)
        P.wait("vector", S.cT)
        P.op("vector", lambda e: e.tensor_copy(lnfm[:], ps_t[:, 0:1024]), S.cV)
        P.wait("tensor", S.cV)
        P.wait("sync", S.cV)
        P.op("sync", lambda e, tile=tile: e.dma_start(
            out=xres[:], in_=x_in[128 * tile:128 * (tile + 1), :]), S.ldh, 16)
        for dg in range(2):
            po = ps_sc[:, dg * 512:(dg + 1) * 512]
            for kc in range(8):
                P.op("tensor", lambda e, po=po, kc=kc, dg=dg: e.matmul(
                    po, lnfm[:, kc * 128:(kc + 1) * 128],
                    wbuf[:, kc * 1024 + dg * 512:kc * 1024 + (dg + 1) * 512],
                    start=(kc == 0), stop=False), None)
            P.op("tensor", lambda e, po=po, dg=dg: e.matmul(
                po, onesrow[:], bor[0:1, dg * 512:(dg + 1) * 512],
                start=False, stop=True), S.cT)
        P.wait("vector", S.cT)
        P.wait("vector", S.ldh)
        P.op("vector", lambda e: e.tensor_add(scr1[:], ps_sc[:, 0:1024], xres[:]), S.cV)
        emit_ln_tail(nc, P, S, B, scr1[:], x_out[128 * tile:128 * (tile + 1), :])


def emit_l1_ffn(nc, P, S, es, W, CN, x_in, x_out, tag="f1", precast=None):
    sb = lambda name, shape, dt: es.enter_context(nc.sbuf_tensor(name + tag, shape, dt))
    psa = lambda name, shape, dt: es.enter_context(nc.psum_tensor(name + tag, shape, dt))

    x_fm = sb("x_fm", [128, 16384], BF16)
    y_acc = sb("y_acc", [128, 16384], F32)
    w1t = sb("w1t", [128, 8192], BF16)
    w2t = sb("w2t", [128, 8192], BF16)
    w3p = [sb("w3a", [128, 8192], BF16), sb("w3b", [128, 8192], BF16)]
    s1_bf = sb("s1_bf", [128, 4096], BF16)
    g_fm = sb("g_fm", [128, 4096], BF16)
    wgt = sb("wgt", [128, 64], BF16)
    rt = sb("rt", [128, 128], F32)
    wsb = sb("wsb", [128, 128], F32)
    m1 = sb("m1", [128, 16], F32)
    m2 = sb("m2", [128, 16], F32)
    tmp8 = sb("tmp8", [128, 128], F32)
    tmp8b = sb("tmp8b", [128, 128], F32)
    p12 = sb("p12", [128, 96], F32)
    xres = sb("xres", [128, 1024], F32)
    stage2 = sb("stage2", [128, 1024], BF16)
    scr1 = sb("scr1", [128, 1024], F32)
    scr2 = sb("scr2", [128, 1024], F32)
    og_bf = sb("og_bf", [128, 1024], BF16)
    ogf_bf = sb("ogf_bf", [128, 1024], BF16)
    sq_bf = sb("sq_bf", [128, 1024], BF16)
    onescol = sb("onescol", [128, 1], BF16)
    murow = sb("murow", [1, 128], F32)
    rsrow = sb("rsrow", [1, 128], F32)
    varrow = sb("varrow", [1, 128], F32)
    col1 = sb("col1", [128, 1], F32)
    col2 = sb("col2", [128, 1], F32)
    id_bf = sb("id_bf", [128, 128], BF16)
    cvals = sb("cvals", [128, 8], F32)
    register_consts(nc, cvals)

    ps_h = psa("ps_h", [128, 2048], F32)
    ps_y = psa("ps_y", [128, 1024], F32)
    ps_st = psa("ps_st", [128, 512], F32)
    ps_t = psa("ps_t", [128, 1024], BF16)

    B = dict(og_bf=og_bf[:], ogf_bf=ogf_bf[:], sq_bf=sq_bf[:], ps_t=ps_t,
             ps_st=ps_st, id_bf=id_bf, onescol=onescol, murow=murow,
             rsrow=rsrow, varrow=varrow, col1=col1, col2=col2, cvals=cvals,
             scrO=scr2[:], xres=xres, stage2=stage2[:])

    if precast is None:
        w1bf = nc.dram_tensor("w1bf" + tag, [16384, 1024], BF16).ap()
        w2bf = nc.dram_tensor("w2bf" + tag, [16384, 1024], BF16).ap()
        w3bf = nc.dram_tensor("w3bf" + tag, [8192, 2048], BF16).ap()

        for e_ in range(8):
            P.op("gpsimd", lambda e, e_=e_: e.dma_start(
                out=w1bf[2048 * e_:2048 * (e_ + 1), :],
                in_=W["W1"][2048 * e_:2048 * (e_ + 1), :]), S.lds, 16)
            P.op("gpsimd", lambda e, e_=e_: e.dma_start(
                out=w2bf[2048 * e_:2048 * (e_ + 1), :],
                in_=W["W2"][2048 * e_:2048 * (e_ + 1), :]), S.lds, 16)
            P.op("gpsimd", lambda e, e_=e_: e.dma_start(
                out=w3bf[1024 * e_:1024 * (e_ + 1), :],
                in_=W["W3"][1024 * e_:1024 * (e_ + 1), :]), S.lds, 16)
    else:
        w1bf, w2bf, w3bf = precast
    for dst, srcap in [(id_bf[:], CN["ident"]), (onescol[:], CN["ones_col"]),
                       (wgt[:].rearrange("p (kc e) -> p kc e", e=8),
                        W["WgT"].rearrange("(kc p) e -> p kc e", p=128))]:
        P.op("gpsimd", lambda e, d=dst, s=srcap: e.dma_start(out=d, in_=s), S.lds, 16)
    P.op("sync", lambda e: e.dma_start(out=cvals[:], in_=CN["constvals"]), S.ldh, 16)
    P.wait("tensor", S.lds)
    P.wait("tensor", S.ldh)

    emit_xfm_build(nc, P, S, B, x_in, x_fm)

    P.wait("scalar", S.ldh)
    P.op("scalar", lambda e: e.activation(tmp8[:, 0:8], cvals[:, 0:8], AF.Exp), S.cS)
    P.op("scalar", lambda e: e.activation(tmp8[:, 0:1], cvals[:, 7:8], AF.Ln), S.cS)
    P.op("scalar", lambda e: e.activation(tmp8[:, 0:8], cvals[:, 0:8], AF.Silu), S.cS)
    P.op("scalar", lambda e: e.activation(tmp8[:, 0:8], cvals[:, 0:8], AF.Square), S.cS)

    P.wait("tensor", S.cV)
    for t in range(NT):
        P.wait("tensor", S.cV)
        for kc in range(8):
            P.op("tensor", lambda e, t=t, kc=kc: e.matmul(
                ps_y[:, 8 * t:8 * (t + 1)],
                x_fm[:, kc * 2048 + t * 128:kc * 2048 + (t + 1) * 128],
                wgt[:, kc * 8:(kc + 1) * 8],
                start=(kc == 0), stop=(kc == 7)), S.cT if kc == 7 else None)
    P.wait("vector", S.cT)
    P.op("vector", lambda e: e.tensor_copy(rt[:], ps_y[:, 0:128]), S.cV)
    r3 = rt[:].rearrange("p (t e) -> p t e", e=8)
    t3 = tmp8[:].rearrange("p (t e) -> p t e", e=8)
    u3 = tmp8b[:].rearrange("p (t e) -> p t e", e=8)
    w3 = wsb[:].rearrange("p (t e) -> p t e", e=8)
    m1v = m1[:].rearrange("p (t o) -> p t o", o=1)
    m2v = m2[:].rearrange("p (t o) -> p t o", o=1)
    jk = lambda: P.op("vector", lambda e: e.tensor_copy(scr1[:, 0:128], rt[:]), S.cV)
    jk()
    P.op("vector", lambda e: e.tensor_tensor(t3[:, :, 0:4], r3[:, :, 0:4], r3[:, :, 4:8], ALU.max), S.cV)
    jk()
    P.op("vector", lambda e: e.tensor_tensor(t3[:, :, 0:2], t3[:, :, 0:2], t3[:, :, 2:4], ALU.max), S.cV)
    jk()
    P.op("vector", lambda e: e.tensor_tensor(m1v, t3[:, :, 0:1], t3[:, :, 1:2], ALU.max), S.cV)
    jk()
    P.op("vector", lambda e: e.tensor_tensor(u3, r3, m1v.broadcast_to([128, 16, 8]), ALU.is_ge), S.cV)
    jk()
    P.op("vector", lambda e: e.scalar_tensor_tensor(t3, u3, -1e30, r3, ALU.mult, ALU.add), S.cV)
    jk()
    P.op("vector", lambda e: e.tensor_tensor(t3[:, :, 0:4], t3[:, :, 0:4], t3[:, :, 4:8], ALU.max), S.cV)
    jk()
    P.op("vector", lambda e: e.tensor_tensor(t3[:, :, 0:2], t3[:, :, 0:2], t3[:, :, 2:4], ALU.max), S.cV)
    jk()
    P.op("vector", lambda e: e.tensor_tensor(m2v, t3[:, :, 0:1], t3[:, :, 1:2], ALU.max), S.cV)
    jk()
    P.op("vector", lambda e: e.tensor_sub(p12[:, 0:16], m2[:], m1[:]), S.cV)
    jk()
    jk()
    P.wait("scalar", S.cV)
    P.op("scalar", lambda e: e.activation(p12[:, 16:32], p12[:, 0:16], AF.Exp), S.cS)
    P.op("scalar", lambda e: e.activation(p12[:, 32:48], p12[:, 16:32], AF.Ln, bias=1.0), S.cS)
    P.op("scalar", lambda e: e.activation(p12[:, 48:64], p12[:, 32:48], AF.Exp, scale=-1.0), S.cS)
    P.op("scalar", lambda e: e.copy(tmp8[:, 112:128], p12[:, 48:64]), S.cS)

    def router_part2():
        P.wait("vector", S.cS)
        P.op("vector", lambda e: e.tensor_tensor(t3, r3, m2v.broadcast_to([128, 16, 8]), ALU.is_ge), S.cV)
        P.op("vector", lambda e: e.tensor_mul(
            w3, u3, p12[:, 48:64].rearrange("p (t o) -> p t o", o=1).broadcast_to([128, 16, 8])), S.cV)
        P.op("vector", lambda e: e.tensor_mul(p12[:, 64:80], p12[:, 16:32], p12[:, 48:64]), S.cV)
        P.op("vector", lambda e: e.tensor_scalar(tmp8b[:], tmp8b[:], -1.0, 1.0, ALU.mult, ALU.add), S.cV)
        jk()
        P.op("vector", lambda e: e.tensor_mul(t3, t3, u3), S.cV)
        jk()
        P.op("vector", lambda e: e.tensor_mul(
            t3, t3, p12[:, 64:80].rearrange("p (t o) -> p t o", o=1).broadcast_to([128, 16, 8])), S.cV)
        jk()
        P.op("vector", lambda e: e.tensor_add(w3, w3, t3), S.cV)
        jk()
        jk()

    P.wait("gpsimd", S.cT)
    first = True
    if precast is None:
        P.wait("sync", S.lds)
    else:
        P.wait("sync", S.bgc)

    def issue_w12(e_, hf):
        for dc in range(8):
            P.op("sync", lambda e, e_=e_, hf=hf, dc=dc: e.dma_start(
                out=w1t[:, 1024 * dc:1024 * (dc + 1)],
                in_=w1bf[2048 * e_ + 1024 * hf:2048 * e_ + 1024 * (hf + 1),
                         128 * dc:128 * (dc + 1)], transpose=True), S.ldh, 16)
            P.op("sync", lambda e, e_=e_, hf=hf, dc=dc: e.dma_start(
                out=w2t[:, 1024 * dc:1024 * (dc + 1)],
                in_=w2bf[2048 * e_ + 1024 * hf:2048 * e_ + 1024 * (hf + 1),
                         128 * dc:128 * (dc + 1)], transpose=True), S.ldh, 16)

    def issue_w3(e_, hf, buf):
        for dc in range(8):
            P.op("sync", lambda e, e_=e_, hf=hf, dc=dc, buf=buf: e.dma_start(
                out=buf[:, 1024 * dc:1024 * (dc + 1)],
                in_=w3bf[1024 * e_:1024 * (e_ + 1),
                         1024 * hf + 128 * dc:1024 * hf + 128 * (dc + 1)],
                transpose=True), S.ldh, 16)

    issue_w12(0, 0)
    issue_w3(0, 0, w3p[0])
    for it in range(16):
        e_, hf = divmod(it, 2)
        w3t = w3p[it % 2]
        P.wait("tensor", S.ldh)
        if it + 1 < 16:
            en, hn = divmod(it + 1, 2)
            P.wait("sync", S.cT)
            issue_w3(en, hn, w3p[(it + 1) % 2])
        for tg in range(4):
            for hc in range(8):
                slot = hc % 2
                po1 = ps_h[:, 1024 * slot:1024 * slot + 512]
                po2 = ps_h[:, 1024 * slot + 512:1024 * slot + 1024]
                P.wait("tensor", S.cV)
                for kc in range(8):
                    P.op("tensor", lambda e, po1=po1, kc=kc, hc=hc, tg=tg: e.matmul(
                        po1, w1t[:, kc * 1024 + hc * 128:kc * 1024 + (hc + 1) * 128],
                        x_fm[:, kc * 2048 + tg * 512:kc * 2048 + (tg + 1) * 512],
                        start=(kc == 0), stop=(kc == 7)), None)
                for kc in range(8):
                    P.op("tensor", lambda e, po2=po2, kc=kc, hc=hc, tg=tg: e.matmul(
                        po2, w2t[:, kc * 1024 + hc * 128:kc * 1024 + (hc + 1) * 128],
                        x_fm[:, kc * 2048 + tg * 512:kc * 2048 + (tg + 1) * 512],
                        start=(kc == 0), stop=(kc == 7)),
                        S.cT if kc == 7 else None)
                P.wait("scalar", S.cT)
                P.op("scalar", lambda e, po1=po1, hc=hc: e.activation(
                    s1_bf[:, hc * 512:(hc + 1) * 512], po1, AF.Silu), S.cS)
                P.wait("vector", S.cS)
                P.op("vector", lambda e, po2=po2, hc=hc: e.tensor_mul(
                    g_fm[:, hc * 512:(hc + 1) * 512],
                    s1_bf[:, hc * 512:(hc + 1) * 512], po2), S.cV)
            if e_ == 0 and hf == 0 and tg == 0:
                router_part2()
            if tg == 3 and it + 1 < 16:
                en, hn = divmod(it + 1, 2)
                P.wait("sync", S.cT)
                issue_w12(en, hn)
            P.wait("tensor", S.cV)
            for tt in range(4):
                tile = tg * 4 + tt
                for dg in range(2):
                    po = ps_y[:, dg * 512:(dg + 1) * 512]
                    P.wait("tensor", S.cV)
                    for hc in range(8):
                        P.op("tensor", lambda e, po=po, hc=hc, tt=tt, dg=dg,
                             w3t=w3t: e.matmul(
                                 po, g_fm[:, hc * 512 + tt * 128:hc * 512 + tt * 128 + 128],
                                 w3t[:, hc * 1024 + dg * 512:hc * 1024 + (dg + 1) * 512],
                                 start=(hc == 0), stop=(hc == 7)), S.cT if hc == 7 else None)
                    P.wait("vector", S.cT)
                    ya = y_acc[:, tile * 1024 + dg * 512:tile * 1024 + (dg + 1) * 512]
                    wcol = wsb[:, tile * 8 + e_:tile * 8 + e_ + 1]
                    if first:
                        P.op("vector", lambda e, po=po, ya=ya, wcol=wcol:
                             e.tensor_single_scalar(ya, po, wcol, ALU.mult), S.cV)
                    else:
                        P.op("vector", lambda e, po=po, ya=ya, wcol=wcol:
                             e.scalar_tensor_tensor(ya, po, wcol, ya,
                                                    ALU.mult, ALU.add), S.cV)
        first = False

    if DBG_L1F:
        dwv = nc.dram_tensor("dwv" + tag, [128, 128], F32, kind="ExternalOutput").ap()
        dya = nc.dram_tensor("dya" + tag, [128, 16384], F32, kind="ExternalOutput").ap()
        drt = nc.dram_tensor("drt" + tag, [128, 128], F32, kind="ExternalOutput").ap()
        dm1 = nc.dram_tensor("dm1" + tag, [128, 16], F32, kind="ExternalOutput").ap()
        dm2 = nc.dram_tensor("dm2" + tag, [128, 16], F32, kind="ExternalOutput").ap()
        dp12 = nc.dram_tensor("dp12" + tag, [128, 80], F32, kind="ExternalOutput").ap()
        du3 = nc.dram_tensor("du3" + tag, [128, 128], F32, kind="ExternalOutput").ap()
        P.wait("gpsimd", S.cV)
        P.op("gpsimd", lambda e: e.dma_start(out=dwv, in_=wsb[:]), S.lds, 16)
        P.op("gpsimd", lambda e: e.dma_start(out=dya, in_=y_acc[:]), S.lds, 16)
        P.op("gpsimd", lambda e: e.dma_start(out=drt, in_=rt[:]), S.lds, 16)
        P.op("gpsimd", lambda e: e.dma_start(out=dm1, in_=m1[:]), S.lds, 16)
        P.op("gpsimd", lambda e: e.dma_start(out=dm2, in_=m2[:]), S.lds, 16)
        P.op("gpsimd", lambda e: e.dma_start(out=dp12, in_=p12[:, 0:80]), S.lds, 16)
        P.op("gpsimd", lambda e: e.dma_start(out=du3, in_=tmp8b[:]), S.lds, 16)
        P.wait("vector", S.lds)

    for tile in range(NT):
        P.wait("sync", S.cV)
        P.op("sync", lambda e, tile=tile: e.dma_start(
            out=xres[:], in_=x_in[128 * tile:128 * (tile + 1), :]), S.ldh, 16)
        P.wait("vector", S.ldh)
        P.wait("vector", S.sth)
        P.op("vector", lambda e, tile=tile: e.tensor_add(
            scr1[:], y_acc[:, tile * 1024:(tile + 1) * 1024], xres[:]), S.cV)
        emit_ln_tail(nc, P, S, B, scr1[:], x_out[128 * tile:128 * (tile + 1), :])



def declare_weights_l0a(nc):
    W = {}
    for n, shp in [("Wq", [1024, 1024]), ("Wk", [1024, 1024]), ("Wv", [1024, 1024]),
                   ("Wg", [1024, 1024]), ("Wo", [1024, 1024]), ("Wg1", [1024, 16]),
                   ("Wg2", [16, 1024]), ("bg2", [1, 1024]), ("gnw_tiled", [128, 1024])]:
        W[n] = nc.dram_tensor("l0a_" + n, shp, F32, kind="ExternalInput").ap()
    return W


def declare_consts(nc, names=("ident", "t1u", "mask_incl", "ones_col", "ones_row",
                              "ones_row2048", "diag_mask", "constvals")):
    shapes = {"ident": [128, 128], "t1u": [128, 128], "mask_incl": [128, 128],
              "ones_col": [128, 1], "ones_row": [1, 128],
              "ones_row2048": [1, 2048], "diag_mask": [128, 2048],
              "constvals": [128, 8]}
    return {n: nc.dram_tensor("cn_" + n, shapes[n], F32, kind="ExternalInput").ap()
            for n in names}


def l1a_core_indices(core):
    s, g, octs = core_layout(core)
    p64 = np.arange(64, dtype=np.int32)[:, None]
    p128 = np.arange(128, dtype=np.int32)[:, None]
    kidx = np.concatenate([(1536 * r + 1024 + 64 * g + p64) * 2048 + 1024 * half
                           for r in range(4) for half in range(2)], axis=1)
    qidx = np.concatenate([(1536 * r + 64 * (4 * gp + g) + p64) * 2048 + 1024 * half
                           for gp in range(4) for r in range(4) for half in range(2)],
                          axis=1)
    vcols = []
    for kt in range(64):
        oct_ = kt // 8
        rank, off = (oct_, 0) if oct_ < 4 else (7 - oct_, 1024)
        loc = off + (kt % 8) * 128 + p128
        vcols.append((1536 * rank + 1280) * 2048 + loc * 256 + 64 * g)
    vidx = np.concatenate(vcols, axis=1).astype(np.int32)
    ocols = []
    for r in range(4):
        for tile in range(16):
            ocols.append((8192 * r + 2048 * g + tile * 128 + p128) * 256)
    oidx = np.concatenate(ocols, axis=1).astype(np.int32)
    acols = []
    for tile in range(16):
        grow0 = 1024 * octs[tile // 8] + 128 * (tile % 8)
        for r in range(4):
            acols.append((8192 * r + grow0 + p128) * 256)
    aidx = np.concatenate(acols, axis=1).astype(np.int32)
    return dict(kidx=kidx, qidx=qidx, vidx=vidx, oidx=oidx, aidx=aidx)


def stage_barrier(P, S):
    for eng in ("sync", "gpsimd"):
        for sem in (S.cT, S.cV, S.cS, S.sth, S.lds, S.ldh):
            P.wait(eng, sem)
    P.wait("tensor", S.cV)
    P.wait("tensor", S.cS)
    P.wait("vector", S.cT)
    P.wait("vector", S.cS)
    P.wait("scalar", S.cT)
    P.wait("scalar", S.cV)


def build_fused_launch():
    nc = bass.Bass("TRN2", target_bir_lowering=False, debug=False,
                   detect_race_conditions=False)
    x_in = nc.dram_tensor("x_in", [TOK, DM], F32, kind="ExternalInput").ap()
    x_out = nc.dram_tensor("x_out", [TOK, DM], F32, kind="ExternalOutput").ap()
    kd = "ExternalOutput" if DBG_FUSED else "Internal"
    x1d = nc.dram_tensor("x1d", [TOK, DM], F32, kind=kd).ap()
    x2d = nc.dram_tensor("x2d", [TOK, DM], F32, kind=kd).ap()
    x3d = nc.dram_tensor("x3d", [TOK, DM], F32, kind=kd).ap()
    W0a = declare_weights_l0a(nc)
    CN = declare_consts(nc)
    CI0 = {"theta": nc.dram_tensor("theta", [64, 128], F32, kind="ExternalInput").ap(),
           "cbias": nc.dram_tensor("cbias", [64, 16], F32, kind="ExternalInput").ap()}
    W0f = {n: nc.dram_tensor("f0_" + n, shp, F32, kind="ExternalInput").ap()
           for n, shp in [("router_w", [1024, 8]), ("w1", [1024, 4096]),
                          ("b1", [1, 4096]), ("w2", [4096, 1024]),
                          ("b2_t", [128, 1024])]}
    W1a = {n: nc.dram_tensor("a1_" + n, shp, F32, kind="ExternalInput").ap()
           for n, shp in [("Wq", [1024, 1024]), ("Wk", [1024, 256]),
                          ("Wv", [1024, 256]), ("Wo", [1024, 1024]),
                          ("bq", [1, 1024]), ("bk", [1, 256]), ("bv", [1, 256]),
                          ("bo", [1, 1024]), ("lng_t", [128, 1024]),
                          ("lnb_t", [128, 1024])]}
    CI1 = {n: nc.dram_tensor("a1_" + n, shp, mybir.dt.int32, kind="ExternalInput").ap()
           for n, shp in [("kidx", [64, 8]), ("qidx", [64, 32]),
                          ("vidx", [128, 64]), ("oidx", [128, 64]),
                          ("aidx", [128, 64])]}
    W1f = {n: nc.dram_tensor("f1_" + n, shp, F32, kind="ExternalInput").ap()
           for n, shp in [("WgT", [1024, 8]), ("W1", [16384, 1024]),
                          ("W2", [16384, 1024]), ("W3", [8192, 2048])]}
    with contextlib.ExitStack() as es:
        S = Sems(nc, es)
        P = Plan()
        cvg = es.enter_context(nc.sbuf_tensor("cvals_glob", [128, 8], F32))
        register_consts(nc, cvg)
        P.op("sync", lambda e: e.dma_start(out=cvg[:], in_=CN["constvals"]), S.ldh, 16)
        with contextlib.ExitStack() as es1:
            emit_l0_attn(nc, P, S, es1, W0a, CN, CI0, x_in, x1d)
        stage_barrier(P, S)
        with contextlib.ExitStack() as es2:
            emit_l0_ffn(nc, P, S, es2, W0f, CN, x1d, x2d)
        stage_barrier(P, S)
        w1bf = nc.dram_tensor("w1bff1", [16384, 1024], BF16).ap()
        w2bf = nc.dram_tensor("w2bff1", [16384, 1024], BF16).ap()
        w3bf = nc.dram_tensor("w3bff1", [8192, 2048], BF16).ap()

        def bg_casts(gp):
            for e_ in (2 * gp, 2 * gp + 1):
                P.op("gpsimd", lambda e, e_=e_: e.dma_start(
                    out=w1bf[2048 * e_:2048 * (e_ + 1), :],
                    in_=W1f["W1"][2048 * e_:2048 * (e_ + 1), :]), S.bgc, 16)
                P.op("gpsimd", lambda e, e_=e_: e.dma_start(
                    out=w2bf[2048 * e_:2048 * (e_ + 1), :],
                    in_=W1f["W2"][2048 * e_:2048 * (e_ + 1), :]), S.bgc, 16)
                P.op("gpsimd", lambda e, e_=e_: e.dma_start(
                    out=w3bf[1024 * e_:1024 * (e_ + 1), :],
                    in_=W1f["W3"][1024 * e_:1024 * (e_ + 1), :]), S.bgc, 16)

        with contextlib.ExitStack() as es3:
            o_g = emit_l1_attn(nc, P, S, es3, W1a, CN, CI1, x2d, None, fused=True,
                               bg_dma=bg_casts)
        stage_barrier(P, S)
        with contextlib.ExitStack() as es4:
            emit_l1_attn_fin(nc, P, S, es4, W1a, CN, None, x2d, x3d,
                             o_gather=(o_g, CI1["aidx"]))
        stage_barrier(P, S)
        with contextlib.ExitStack() as es5:
            emit_l1_ffn(nc, P, S, es5, W1f, CN, x3d, x_out,
                        precast=(w1bf, w2bf, w3bf))
        P.wait("gpsimd", S.sth)
        P.wait("gpsimd", S.lds)
        P.wait("gpsimd", S.ldh)
        P.wait("sync", S.sth)
        P.wait("sync", S.lds)
        P.emit(nc)
    return nc


def build_l1_attn_launch(upto=99):
    nc = bass.Bass("TRN2", target_bir_lowering=False, debug=False,
                   detect_race_conditions=False)
    x_in = nc.dram_tensor("x_in", [TOK, DM], F32, kind="ExternalInput").ap()
    x_out = None
    W = {}
    for n, shp in [("Wq", [1024, 1024]), ("Wk", [1024, 256]), ("Wv", [1024, 256]),
                   ("Wo", [1024, 1024]), ("bq", [1, 1024]), ("bk", [1, 256]),
                   ("bv", [1, 256]), ("bo", [1, 1024]),
                   ("lng_t", [128, 1024]), ("lnb_t", [128, 1024])]:
        W[n] = nc.dram_tensor("a1_" + n, shp, F32, kind="ExternalInput").ap()
    CN = declare_consts(nc)
    CI = {n: nc.dram_tensor("a1_" + n, shp, mybir.dt.int32, kind="ExternalInput").ap()
          for n, shp in [("kidx", [64, 8]), ("qidx", [64, 32]),
                         ("vidx", [128, 64]), ("oidx", [128, 64])]}
    with contextlib.ExitStack() as es:
        S = Sems(nc, es)
        P = Plan()
        emit_l1_attn(nc, P, S, es, W, CN, CI, x_in, x_out, upto=upto)
        P.wait("gpsimd", S.sth)
        P.wait("gpsimd", S.lds)
        P.wait("sync", S.sth)
        P.emit(nc)
    return nc


def build_l1_attn_fin_launch():
    nc = bass.Bass("TRN2", target_bir_lowering=False, debug=False,
                   detect_race_conditions=False)
    attn_in = nc.dram_tensor("attn_in", [TOK, DM], BF16, kind="ExternalInput").ap()
    x_in = nc.dram_tensor("x_in", [TOK, DM], F32, kind="ExternalInput").ap()
    x_out = nc.dram_tensor("x_out", [TOK, DM], F32, kind="ExternalOutput").ap()
    W = {}
    for n, shp in [("Wo", [1024, 1024]), ("bo", [1, 1024]),
                   ("lng_t", [128, 1024]), ("lnb_t", [128, 1024])]:
        W[n] = nc.dram_tensor("a1_" + n, shp, F32, kind="ExternalInput").ap()
    CN = declare_consts(nc)
    with contextlib.ExitStack() as es:
        S = Sems(nc, es)
        P = Plan()
        emit_l1_attn_fin(nc, P, S, es, W, CN, attn_in, x_in, x_out)
        P.wait("gpsimd", S.sth)
        P.wait("gpsimd", S.lds)
        P.wait("sync", S.sth)
        P.emit(nc)
    return nc


def assemble_attn_for_core(o_results, core):
    s, g, octs = core_layout(core)
    import ml_dtypes
    attn = np.empty((TOK, DM), dtype=o_results[0].dtype)
    rows = np.concatenate([np.arange(1024 * j, 1024 * (j + 1)) for j in octs])
    for r in range(4):
        src = o_results[4 * s + r][rows]
        for gp in range(4):
            attn[:, 64 * (4 * gp + r):64 * (4 * gp + r + 1)] = src[:, 64 * gp:64 * (gp + 1)]
    return np.ascontiguousarray(attn)


def l1a_weight_arrays(inputs):
    i = {k: np.ascontiguousarray(np.asarray(v), dtype=np.float32)
         for k, v in inputs.items()}
    return {
        "a1_Wq": i["L1_attn_Wq"], "a1_Wk": i["L1_attn_Wk"], "a1_Wv": i["L1_attn_Wv"],
        "a1_Wo": i["L1_attn_Wo"],
        "a1_bq": i["L1_attn_bq"].reshape(1, 1024),
        "a1_bk": i["L1_attn_bk"].reshape(1, 256),
        "a1_bv": i["L1_attn_bv"].reshape(1, 256),
        "a1_bo": i["L1_attn_bo"].reshape(1, 1024),
        "a1_lng_t": np.ascontiguousarray(np.tile(i["L1_attn_ln_g"].reshape(1, 1024), (128, 1))),
        "a1_lnb_t": np.ascontiguousarray(np.tile(i["L1_attn_ln_b"].reshape(1, 1024), (128, 1))),
    }


def build_l1_ffn_launch():
    nc = bass.Bass("TRN2", target_bir_lowering=False, debug=False,
                   detect_race_conditions=False)
    x_in = nc.dram_tensor("x_in", [TOK, DM], F32, kind="ExternalInput").ap()
    x_out = nc.dram_tensor("x_out", [TOK, DM], F32, kind="ExternalOutput").ap()
    W = {}
    for n, shp in [("WgT", [1024, 8]), ("W1", [16384, 1024]),
                   ("W2", [16384, 1024]), ("W3", [8192, 2048])]:
        W[n] = nc.dram_tensor("f1_" + n, shp, F32, kind="ExternalInput").ap()
    CN = declare_consts(nc)
    with contextlib.ExitStack() as es:
        S = Sems(nc, es)
        P = Plan()
        emit_l1_ffn(nc, P, S, es, W, CN, x_in, x_out)
        P.wait("gpsimd", S.sth)
        P.wait("gpsimd", S.lds)
        P.wait("gpsimd", S.ldh)
        P.wait("sync", S.sth)
        P.emit(nc)
    return nc


def l1f_weight_arrays(inputs):
    i = {k: np.ascontiguousarray(np.asarray(v), dtype=np.float32)
         for k, v in inputs.items()}
    return {
        "f1_WgT": np.ascontiguousarray(i["L1_ffn_Wg"].T),
        "f1_W1": i["L1_ffn_W1"].reshape(16384, 1024),
        "f1_W2": i["L1_ffn_W2"].reshape(16384, 1024),
        "f1_W3": i["L1_ffn_W3"].reshape(8192, 2048),
    }


def build_l0_ffn_launch():
    nc = bass.Bass("TRN2", target_bir_lowering=False, debug=False,
                   detect_race_conditions=False)
    x_in = nc.dram_tensor("x_in", [TOK, DM], F32, kind="ExternalInput").ap()
    x_out = nc.dram_tensor("x_out", [TOK, DM], F32, kind="ExternalOutput").ap()
    W = {}
    for n, shp in [("router_w", [1024, 8]), ("w1", [1024, 4096]), ("b1", [1, 4096]),
                   ("w2", [4096, 1024]), ("b2_t", [128, 1024])]:
        W[n] = nc.dram_tensor("f0_" + n, shp, F32, kind="ExternalInput").ap()
    CN = declare_consts(nc)
    with contextlib.ExitStack() as es:
        S = Sems(nc, es)
        P = Plan()
        emit_l0_ffn(nc, P, S, es, W, CN, x_in, x_out)
        P.wait("gpsimd", S.sth)
        P.wait("gpsimd", S.lds)
        P.wait("sync", S.sth)
        P.emit(nc)
    return nc


def l0f_weight_arrays(inputs):
    i = {k: np.ascontiguousarray(np.asarray(v), dtype=np.float32)
         for k, v in inputs.items()}
    return {
        "f0_router_w": i["L0_ffn_router_w"], "f0_w1": i["L0_ffn_w1"],
        "f0_b1": i["L0_ffn_b1"].reshape(1, 4096), "f0_w2": i["L0_ffn_w2"],
        "f0_b2_t": np.ascontiguousarray(np.tile(i["L0_ffn_b2"].reshape(1, 1024), (128, 1))),
    }


def build_l0_attn_launch(upto=99, p2cut=99, dbg=False):
    nc = bass.Bass("TRN2", target_bir_lowering=False, debug=False,
                   detect_race_conditions=False)
    x_in = nc.dram_tensor("x_in", [TOK, DM], F32, kind="ExternalInput").ap()
    x_out = nc.dram_tensor("x_out", [TOK, DM], F32, kind="ExternalOutput").ap()
    W = declare_weights_l0a(nc)
    CN = declare_consts(nc)
    CI = {"theta": nc.dram_tensor("theta", [64, 128], F32, kind="ExternalInput").ap(),
          "cbias": nc.dram_tensor("cbias", [64, 16], F32, kind="ExternalInput").ap()}
    with contextlib.ExitStack() as es:
        S = Sems(nc, es)
        P = Plan()
        emit_l0_attn(nc, P, S, es, W, CN, CI, x_in, x_out, upto=upto, p2cut=p2cut, dbg=dbg)
        if upto < 99 or p2cut < 99:
            P.wait("sync", S.cV)
            P.wait("sync", S.cT)
            P.wait("sync", S.cS)
            P.op("sync", lambda e: e.dma_start(out=x_out[0:128, :], in_=x_in[0:128, :]),
                 S.sth, 16)
        P.wait("gpsimd", S.sth)
        P.wait("gpsimd", S.lds)
        P.wait("sync", S.sth)
        P.wait("sync", S.lds)
        P.emit(nc)
    return nc


def l0a_weight_arrays(inputs):
    i = {k: np.ascontiguousarray(np.asarray(v), dtype=np.float32)
         for k, v in inputs.items()}
    return {
        "l0a_Wq": i["L0_attn_Wq"], "l0a_Wk": i["L0_attn_Wk"], "l0a_Wv": i["L0_attn_Wv"],
        "l0a_Wg": i["L0_attn_Wg"], "l0a_Wo": i["L0_attn_Wo"], "l0a_Wg1": i["L0_attn_Wg1"],
        "l0a_Wg2": i["L0_attn_Wg2"], "l0a_bg2": i["L0_attn_bg2"].reshape(1, 1024),
        "l0a_gnw_tiled": np.ascontiguousarray(
            np.tile(np.tile(i["L0_attn_g_norm_w"], 16).reshape(1, 1024), (128, 1))),
    }


def run_l0_attn(inputs, x_full=None):
    nc = build_l0_attn_launch()
    cn = host_constants()
    wa = l0a_weight_arrays(inputs)
    in_maps = []
    base = dict(wa)
    for n, v in cn.items():
        base["cn_" + n] = v
    src = dict(inputs)
    if x_full is not None:
        src["L0_attn_x"] = x_full
    for core in range(NCORES):
        pc = per_core_arrays(src, core)
        m = dict(base)
        m["x_in"] = pc["x_in"]
        m["theta"] = pc["theta"]
        m["cbias"] = pc["cbias"]
        in_maps.append(m)
    res = run_bass_kernel_spmd(nc, in_maps, core_ids=list(range(NCORES)))
    return assemble_output([r["x_out"] for r in res.results])


FUSED = True


def kernel(**inputs):
    if not FUSED:
        return kernel_unfused(**inputs)
    cn = host_constants()
    nc = build_fused_launch()
    base = {}
    base.update(l0a_weight_arrays(inputs))
    base.update(l0f_weight_arrays(inputs))
    base.update(l1a_weight_arrays(inputs))
    base.update(l1f_weight_arrays(inputs))
    for n, v in cn.items():
        base["cn_" + n] = v
    maps = []
    for core in range(NCORES):
        pc = per_core_arrays(inputs, core)
        m = dict(base)
        m["x_in"] = pc["x_in"]
        m["theta"] = pc["theta"]
        m["cbias"] = pc["cbias"]
        for k, v in l1a_core_indices(core).items():
            m["a1_" + k] = v
        maps.append(m)
    res = run_bass_kernel_spmd(nc, maps, core_ids=list(range(NCORES)))
    return assemble_output([r["x_out"] for r in res.results])


def kernel_unfused(**inputs):
    cn = host_constants()
    base_cn = {"cn_" + n: v for n, v in cn.items()}

    def shard_x(x_full):
        return [per_core_arrays({"L0_attn_x": x_full}, core)["x_in"]
                for core in range(NCORES)]

    def run(nc, per_core_maps):
        res = run_bass_kernel_spmd(nc, per_core_maps, core_ids=list(range(NCORES)))
        return res.results

    nc = build_l0_attn_launch()
    wa = l0a_weight_arrays(inputs)
    maps = []
    for core in range(NCORES):
        pc = per_core_arrays(inputs, core)
        m = dict(wa); m.update(base_cn)
        m["x_in"] = pc["x_in"]; m["theta"] = pc["theta"]; m["cbias"] = pc["cbias"]
        maps.append(m)
    r1 = run(nc, maps)
    x1s = [r["x_out"] for r in r1]

    nc = build_l0_ffn_launch()
    wa = l0f_weight_arrays(inputs)
    maps = []
    for core in range(NCORES):
        m = dict(wa); m.update(base_cn)
        m["x_in"] = x1s[core]
        maps.append(m)
    r2 = run(nc, maps)
    x2s = [r["x_out"] for r in r2]

    nc = build_l1_attn_launch()
    wa = l1a_weight_arrays(inputs)
    maps = []
    for core in range(NCORES):
        m = {k: v for k, v in wa.items() if k in
             ("a1_Wq", "a1_Wk", "a1_Wv", "a1_bq", "a1_bk", "a1_bv", "a1_bo",
              "a1_Wo", "a1_lng_t", "a1_lnb_t")}
        m.update(base_cn)
        m["x_in"] = x2s[core]
        m.update({"a1_" + k: v for k, v in l1a_core_indices(core).items()})
        maps.append(m)
    nc_inputs = {a.memorylocations[0].name for a in nc.m.functions[0].allocations
                 if getattr(a, "kind", None) == "ExternalInput"}
    maps = [{k: v for k, v in m.items() if k in nc_inputs} for m in maps]
    r3 = run(nc, maps)
    o_res = [r["o_outa1"] for r in r3]

    nc = build_l1_attn_fin_launch()
    maps = []
    for core in range(NCORES):
        m = {k: wa[k] for k in ("a1_Wo", "a1_bo", "a1_lng_t", "a1_lnb_t")}
        m.update(base_cn)
        m["x_in"] = x2s[core]
        m["attn_in"] = assemble_attn_for_core(o_res, core)
        maps.append(m)
    r4 = run(nc, maps)
    x3s = [r["x_out"] for r in r4]

    nc = build_l1_ffn_launch()
    wa = l1f_weight_arrays(inputs)
    maps = []
    for core in range(NCORES):
        m = dict(wa); m.update(base_cn)
        m["x_in"] = x3s[core]
        maps.append(m)
    r5 = run(nc, maps)
    return assemble_output([r["x_out"] for r in r5])


# version 07
# speedup vs baseline: 1.1831x; 1.1083x over previous
import numpy as np
import contextlib
import concourse.bass as bass
import concourse.mybir as mybir
from concourse.bass_utils import run_bass_kernel_spmd

F32 = mybir.dt.float32
BF16 = mybir.dt.bfloat16
AF = mybir.ActivationFunctionType
ALU = mybir.AluOpType
AX = mybir.AxisListType

NCORES = 8
TOK = 2048
NT = TOK // 128
DM = 1024
C = 128
EPS = 1e-5
CONSTVALS = (1e-5, 1.0 / 64, -0.5, 1.0 / 1024, 0.125, -1.0, 1.0 / 32, 1.0)


def register_consts(nc, cvals):
    if getattr(nc, "_cv_registered", False):
        return
    nc._cv_registered = True
    for i, v in enumerate(CONSTVALS):
        nc.const_aps.register(F32, float(v), cvals[:, i:i + 1])

ENGS = ("sync", "scalar", "vector", "tensor", "gpsimd")
DBG_L0F = False
N_CC = 3
DBG_L1F = False
DBG_FUSED = False


class Sem:
    __slots__ = ("h", "n", "max_wait", "fenced")

    def __init__(self, h):
        self.h = h
        self.n = 0
        self.max_wait = 0
        self.fenced = 0


class Plan:

    def __init__(self):
        self.q = {e: [] for e in ENGS}
        self.waited = {}

    def op(self, eng, f, sem=None, k=1):
        if sem is not None and k == 16:
            if sem.max_wait > sem.fenced:
                self.q[eng].append(("wait", sem, sem.max_wait))
                sem.fenced = sem.max_wait
            sem.n += 16
            self.q[eng].append(("dma", f, sem, sem.n))
        elif sem is not None:
            sem.n += k
            self.q[eng].append(("op", f, sem, sem.n))
        else:
            self.q[eng].append(("op", f, None, 0))

    def wait(self, eng, sem, n=None):
        n = sem.n if n is None else n
        if n <= 0:
            return
        self.waited.setdefault(id(sem), set()).add(n)
        if n > sem.max_wait:
            sem.max_wait = n
        self.q[eng].append(("wait", sem, n))

    def emit(self, nc):
        emitted = {}

        def runner(items):
            def run(e):
                for it in items:
                    if it[0] == "wait":
                        _, sem, n = it
                        e.wait_ge(sem.h, n)
                    elif it[0] == "dma":
                        _, f, sem, cnt = it
                        f(e).then_inc(sem.h, 16)
                    else:
                        _, f, sem, cnt = it
                        ins = f(e)
                        if sem is not None and cnt in self.waited.get(id(sem), ()):
                            last = emitted.get(id(sem), 0)
                            ins.then_inc(sem.h, cnt - last)
                            emitted[id(sem)] = cnt
            return run

        with nc.Block() as block:
            block.sync(runner(self.q["sync"]))
            block.scalar(runner(self.q["scalar"]))
            block.vector(runner(self.q["vector"]))
            block.tensor(runner(self.q["tensor"]))
            block.gpsimd(runner(self.q["gpsimd"]))


class Sems:
    def __init__(self, nc, es):
        mk = lambda name: Sem(es.enter_context(nc.semaphore(name)))
        self.ldh = mk("ldh")
        self.lds = mk("lds")
        self.sth = mk("sth")
        self.cT = mk("cT")
        self.cV = mk("cV")
        self.cS = mk("cS")
        self.cG = mk("cG")
        self.bgc = mk("bgc")



def host_constants():
    c = {}
    c["ident"] = np.eye(128, dtype=np.float32)
    c["t1u"] = (-1.0 / 16.0) * np.triu(np.ones((128, 128), np.float32))
    c["mask_incl"] = np.triu(np.ones((128, 128), np.float32))
    c["ones_col"] = np.ones((128, 1), np.float32)
    c["ones_row"] = np.ones((1, 128), np.float32)
    c["ones_row2048"] = np.ones((1, 2048), np.float32)
    from kernel import CONSTVALS as _cv
    c["constvals"] = np.tile(np.array(list(_cv) + [0.0] * (8 - len(_cv)),
                                      np.float32), (128, 1))
    dm = np.zeros((128, 2048), np.float32)
    for j in range(4):
        for i in range(128):
            dm[i, j * 512:(j + 1) * 512][np.arange(512) >= j * 128 + i] = 1.0
    c["diag_mask"] = dm
    return c


def core_layout(core):
    s, g = core // 4, core % 4
    return s, g, [g, 7 - g]


def per_core_arrays(inputs, core):
    s, g, octs = core_layout(core)
    x = np.asarray(inputs["L0_attn_x"])
    xs = np.concatenate([x[s, 1024 * j:1024 * (j + 1)] for j in octs], axis=0)
    d = {"x_in": np.ascontiguousarray(xs, dtype=np.float32)}
    theta = np.zeros((64, 128), np.float32)
    cbias = np.full((64, 16), -1e9, np.float32)
    for oi, j in enumerate(octs):
        for jp in range(8):
            if jp < j:
                cbias[:, oi * 8 + jp] = 0.0
                for jpp in range(jp + 1, j):
                    theta[:, oi * 64 + jp * 8 + jpp] = 1.0
    d["theta"] = theta
    d["cbias"] = cbias
    return d


def assemble_output(shards):
    out = np.empty((2, 8192, 1024), np.float32)
    for core in range(NCORES):
        s, g, octs = core_layout(core)
        sh = shards[core]
        for oi, j in enumerate(octs):
            out[s, 1024 * j:1024 * (j + 1)] = sh[1024 * oi:1024 * (oi + 1)]
    return out



def emit_l0_attn(nc, P, S, es, W, CN, CI, x_in, x_out, tag="", upto=99, p2cut=99, dbg=False):
    sb = lambda name, shape, dt: es.enter_context(nc.sbuf_tensor(name + tag, shape, dt))
    psa = lambda name, shape, dt: es.enter_context(nc.psum_tensor(name + tag, shape, dt))

    q_fm = sb("q_fm", [128, 16384], BF16)
    k_tm = sb("k_tm", [128, 16384], BF16)
    v_tm = sb("v_tm", [128, 16384], BF16)
    gs_tm = sb("gs_tm", [128, 16384], BF16)
    arena = sb("arena", [128, 16384], BF16)
    wbuf = sb("wbuf", [128, 8192], BF16)
    u_sb = sb("u_sb", [17, 2048], F32)
    wg2b = sb("wg2b", [17, 1024], F32)
    wg1 = sb("wg1", [128, 128], BF16)
    xres = sb("xres", [128, 1024], F32)
    S_sb = sb("S_sb", [64, 1024], F32)
    S_bf = sb("S_bf", [64, 1024], BF16)
    a_all = sb("a_all", [64, 128], F32)
    a64 = sb("a64", [64, 16], F32)
    aoct = sb("aoct", [128, 8], F32)
    e3 = sb("e3", [128, 8], F32)
    acml = sb("acml", [64, 16], F32)
    e3h = sb("e3h", [64, 16], F32)
    ccoef = sb("ccoef", [64, 16], F32)
    stat = sb("stat", [128, 16], F32)
    stat2 = sb("stat2", [128, 16], F32)
    col1 = sb("col1", [128, 1], F32)
    col2 = sb("col2", [128, 1], F32)
    col3 = sb("col3", [128, 1], F32)
    junkS = sb("junkS", [128, 1], F32)
    junkV = sb("junkV", [128, 1], F32)
    gnw_bc = sb("gnw_bc", [128, 1024], BF16)
    id_bf = sb("id_bf", [128, 128], BF16)
    t1u = sb("t1u", [128, 128], F32)
    mask_bf = sb("mask_bf", [128, 128], BF16)
    onescol = sb("onescol", [128, 1], BF16)
    onesrow = sb("onesrow", [1, 128], BF16)
    murow = sb("murow", [1, 128], F32)
    rsrow = sb("rsrow", [1, 128], F32)
    varrow = sb("varrow", [1, 128], F32)
    theta_sb = sb("theta_sb", [64, 128], F32)
    cbias_sb = sb("cbias_sb", [64, 16], F32)
    cvals = sb("cvals", [128, 8], F32)
    register_consts(nc, cvals)

    def av(c0, c1, dt=F32):
        v = arena[:, c0:c1]
        return v.bitcast(dt) if dt != BF16 else v
    scr1 = av(0, 2048)
    scr2 = av(2048, 4096)
    scr3 = av(4096, 6144)
    qt64 = arena[0:64, 6144:8192]
    ktf64 = arena[0:64, 8192:10240]
    kt_bf = av(10240, 11264, BF16)
    pt_bf = av(11264, 13312, BF16)
    og_bf = av(13312, 14336, BF16)
    ogf_bf = av(14336, 15360, BF16)
    Stmp = scr3[0:64, :]
    r_sb = scr2
    stage2 = gs_tm[:, 0:1024]
    proj_scr = xres[:, 0:512]
    sloc = xres[0:64, :]

    ps_big = psa("ps_big", [128, 2048], F32)
    ps_o = psa("ps_o", [128, 1024], F32)
    ps_mkv = ps_o
    ps_t = psa("ps_t", [128, 1024], BF16)

    sin_d = nc.dram_tensor("sin_l0" + tag, [64, 2048], F32).ap()
    summ = nc.dram_tensor("summ_l0" + tag, [64, 2080], F32).ap()
    summ_g = nc.dram_tensor("summ_l0_g" + tag, [256, 2080], F32).ap()

    for dst, srcap in [(id_bf[:], CN["ident"]), (mask_bf[:], CN["mask_incl"]),
                       (onescol[:], CN["ones_col"]), (onesrow[:], CN["ones_row"]),
                       (gnw_bc[:], W["gnw_tiled"]),
                       (wg1[:].rearrange("p (kc n) -> p kc n", n=16),
                        W["Wg1"].rearrange("(kc p) n -> p kc n", p=128))]:
        P.op("gpsimd", lambda e, d=dst, s=srcap: e.dma_start(out=d, in_=s), S.lds, 16)
    for dst, srcap in [(t1u[:], CN["t1u"]), (theta_sb[:], CI["theta"]),
                       (cbias_sb[:], CI["cbias"]), (wg2b[0:16, :], W["Wg2"]),
                       (wg2b[16:17, :], W["bg2"]),
                       (u_sb[16:17, :], CN["ones_row2048"]),
                       (varrow[:], CN["ones_row"]),
                       (cvals[:], CN["constvals"])]:
        P.op("sync", lambda e, d=dst, s=srcap: e.dma_start(out=d, in_=s), S.ldh, 16)
    P.wait("tensor", S.lds)
    P.wait("tensor", S.ldh)

    P.wait("scalar", S.ldh)
    P.op("scalar", lambda e: e.activation(stat2[:, 0:8], cvals[:, 0:8], AF.Exp), S.cS)
    P.op("scalar", lambda e: e.activation(stat2[:, 0:1], cvals[:, 7:8], AF.Ln), S.cS)
    P.op("scalar", lambda e: e.activation(stat2[:, 0:8], cvals[:, 0:8], AF.Square), S.cS)
    P.op("scalar", lambda e: e.activation(stat2[:, 0:8], cvals[:, 0:8], AF.Silu), S.cS)

    x_fm = arena
    for t in range(NT):
        P.wait("sync", S.cV)
        P.op("sync", lambda e, t=t: e.dma_start(
            out=xres[:], in_=x_in[128 * t:128 * (t + 1), :]), S.ldh, 16)
        P.wait("vector", S.ldh)
        P.wait("vector", S.cT)
        P.op("vector", lambda e: e.tensor_copy(stage2, xres[:]), S.cV)
        P.wait("tensor", S.cV)
        for c in range(8):
            P.op("tensor", lambda e, c=c: e.transpose(
                ps_t[:, 128 * c:128 * (c + 1)], stage2[:, 128 * c:128 * (c + 1)],
                id_bf[:]), S.cT)
        P.wait("vector", S.cT)
        dst = x_fm[:].rearrange("p (fc n) -> p fc n", n=2048)[:, :, t * 128:(t + 1) * 128]
        P.op("vector", lambda e, d=dst: e.tensor_copy(d, ps_t[:].rearrange(
            "p (fc n) -> p fc n", n=128)), S.cV)

    if upto <= 1:
        return

    def stream_w(name):
        P.wait("gpsimd", S.cT)
        P.op("gpsimd", lambda e, s=W[name].rearrange("(kc p) n -> p kc n", p=128):
             e.dma_start(out=wbuf[:].rearrange("p (kc n) -> p kc n", n=1024), in_=s),
             S.lds, 16)
        P.wait("tensor", S.lds)

    stream_w("Wq")
    P.wait("tensor", S.cV)
    for oc in range(8):
        for tg in range(4):
            slot = (oc * 4 + tg) % 4
            po = ps_big[:, 512 * slot:512 * (slot + 1)]
            P.wait("tensor", S.cS)
            P.wait("tensor", S.cV)
            for kc in range(8):
                P.op("tensor", lambda e, po=po, kc=kc, oc=oc, tg=tg: e.matmul(
                    po, wbuf[:, kc * 1024 + oc * 128:kc * 1024 + (oc + 1) * 128],
                    x_fm[:, kc * 2048 + tg * 512:kc * 2048 + (tg + 1) * 512],
                    start=(kc == 0), stop=(kc == 7)), S.cT if kc == 7 else None)
            P.wait("scalar", S.cT)
            P.op("scalar", lambda e, po=po, oc=oc, tg=tg: e.activation(
                q_fm[:, oc * 2048 + tg * 512:oc * 2048 + (tg + 1) * 512], po,
                AF.Copy, scale=0.125), S.cS)

    for name, dst in [("Wk", k_tm), ("Wv", v_tm)]:
        stream_w(name)
        for t in range(NT):
            for fg in range(2):
                slot = (t * 2 + fg) % 4
                po = ps_big[:, 512 * slot:512 * (slot + 1)]
                P.wait("tensor", S.cV)
                P.wait("tensor", S.cS)
                for kc in range(8):
                    P.op("tensor", lambda e, po=po, kc=kc, t=t, fg=fg: e.matmul(
                        po, x_fm[:, kc * 2048 + t * 128:kc * 2048 + (t + 1) * 128],
                        wbuf[:, kc * 1024 + fg * 512:kc * 1024 + (fg + 1) * 512],
                        start=(kc == 0), stop=(kc == 7)), S.cT if kc == 7 else None)
                P.wait("vector", S.cT)
                P.op("vector", lambda e, po=po, t=t, fg=fg, dst=dst: e.tensor_copy(
                    dst[:, t * 1024 + fg * 512:t * 1024 + (fg + 1) * 512], po), S.cV)

    stream_w("Wg")
    for t in range(NT):
        for fg in range(2):
            slot = (t * 2 + fg) % 4
            po = ps_big[:, 512 * slot:512 * (slot + 1)]
            P.wait("tensor", S.cS)
            P.wait("tensor", S.cV)
            for kc in range(8):
                P.op("tensor", lambda e, po=po, kc=kc, t=t, fg=fg: e.matmul(
                    po, x_fm[:, kc * 2048 + t * 128:kc * 2048 + (t + 1) * 128],
                    wbuf[:, kc * 1024 + fg * 512:kc * 1024 + (fg + 1) * 512],
                    start=(kc == 0), stop=(kc == 7)), S.cT if kc == 7 else None)
            P.wait("scalar", S.cT)
            P.wait("scalar", S.cV)
            P.op("scalar", lambda e, po=po: e.activation(proj_scr, po, AF.Silu), S.cS)
            P.wait("vector", S.cS)
            P.op("vector", lambda e, t=t, fg=fg: e.tensor_mul(
                gs_tm[:, t * 1024 + fg * 512:t * 1024 + (fg + 1) * 512],
                proj_scr, gnw_bc[:, fg * 512:(fg + 1) * 512]), S.cV)

    for tg in range(4):
        P.wait("tensor", S.cS)
        P.wait("tensor", S.cV)
        for kc in range(8):
            P.op("tensor", lambda e, kc=kc, tg=tg: e.matmul(
                ps_mkv[0:16, 0:512], wg1[:, kc * 16:(kc + 1) * 16],
                x_fm[:, kc * 2048 + tg * 512:kc * 2048 + (tg + 1) * 512],
                start=(kc == 0), stop=(kc == 7)), S.cT if kc == 7 else None)
        P.wait("scalar", S.cT)
        P.op("scalar", lambda e, tg=tg: e.activation(
            u_sb[0:16, tg * 512:(tg + 1) * 512], ps_mkv[0:16, 0:512], AF.Copy), S.cS)

    if dbg:
        for nm, buf in [("dq", q_fm), ("dk", k_tm), ("dv", v_tm), ("dgs", gs_tm)]:
            dap = nc.dram_tensor(nm + tag, [128, 16384], F32, kind="ExternalOutput").ap()
            P.wait("gpsimd", S.cS)
            P.wait("gpsimd", S.cV)
            P.op("gpsimd", lambda e, d=dap, b=buf: e.dma_start(out=d, in_=b[:]), S.lds, 16)
        dup = nc.dram_tensor("du" + tag, [17, 2048], F32, kind="ExternalOutput").ap()
        P.op("gpsimd", lambda e, d=dup: e.dma_start(out=d, in_=u_sb[:]), S.lds, 16)

    if upto <= 2:
        return

    def chunk_preamble(tile, with_e1):
        P.wait("tensor", S.cS)
        P.wait("tensor", S.cV)
        for fg in range(2):
            P.op("tensor", lambda e, fg=fg, tile=tile: e.matmul(
                ps_big[:, fg * 512:(fg + 1) * 512],
                u_sb[0:17, tile * 128:(tile + 1) * 128],
                wg2b[0:17, fg * 512:(fg + 1) * 512],
                start=True, stop=True), S.cT)
        P.wait("scalar", S.cT)
        P.wait("scalar", S.cV)
        P.wait("scalar", S.sth)
        P.op("scalar", lambda e: e.activation(scr1, ps_big[:, 0:1024], AF.Exp, scale=-1.0), S.cS)
        P.op("scalar", lambda e: e.activation(scr2, scr1, AF.Ln, bias=1.0), S.cS)
        P.wait("tensor", S.cS)
        for fg in range(2):
            P.op("tensor", lambda e, fg=fg: e.matmul(
                ps_big[:, 1024 + fg * 512:1024 + (fg + 1) * 512], t1u[:],
                scr2[:, fg * 512:(fg + 1) * 512], start=True, stop=True), S.cT)
        for c in range(8):
            P.op("tensor", lambda e, c=c: e.matmul(
                ps_big[:, c * 128:(c + 1) * 128],
                scr2[:, c * 128:(c + 1) * 128], t1u[:],
                start=True, stop=True), S.cT)
        P.wait("scalar", S.cT)
        afm3 = ps_big[:, 0:1024].rearrange("p (c t) -> p c t", t=128)
        P.op("scalar", lambda e: e.activation(
            e3[:].rearrange("p (c o) -> p c o", o=1), afm3[:, :, 127:128], AF.Exp), S.cS)
        P.op("scalar", lambda e: e.activation(
            stat2[:, 0:8].rearrange("p (c o) -> p c o", o=1), afm3[:, :, 127:128],
            AF.Copy), S.cS)
        P.op("scalar", lambda e: e.activation(scr1, ps_big[:, 1024:2048], AF.Exp, scale=-1.0), S.cS)
        if with_e1:
            P.op("scalar", lambda e: e.activation(scr3, ps_big[:, 0:1024], AF.Exp), S.cS)
        P.wait("vector", S.cS)
        P.op("vector", lambda e, tile=tile: e.tensor_mul(
            kt_bf, k_tm[:, tile * 1024:(tile + 1) * 1024], scr1), S.cV)

    def mkv_and_update(tile):
        P.wait("tensor", S.cV)
        for h in range(16):
            P.op("tensor", lambda e, h=h, tile=tile: e.matmul(
                ps_mkv[0:64, 64 * h:64 * (h + 1)], kt_bf[:, h * 64:(h + 1) * 64],
                v_tm[:, tile * 1024 + h * 64:tile * 1024 + (h + 1) * 64],
                start=True, stop=True), S.cT if h == 15 else None)
        P.wait("vector", S.cT)
        e3hv = e3h[:].rearrange("p (h o) -> p h o", o=1)
        P.op("vector", lambda e, e3hv=e3hv: e.tensor_copy(
            e3hv[:, 0:16:2, :], e3[0:64, :].unsqueeze(2)), S.cV)
        P.op("vector", lambda e, e3hv=e3hv: e.tensor_copy(
            e3hv[:, 1:16:2, :], e3[64:128, :].unsqueeze(2)), S.cV)
        P.op("vector", lambda e: e.tensor_add(Stmp, S_sb[:], ps_mkv[0:64, :]), S.cV)
        P.op("vector", lambda e: e.tensor_mul(
            S_sb[:].rearrange("p (h v) -> p h v", v=64),
            Stmp.rearrange("p (h v) -> p h v", v=64),
            e3h[:].unsqueeze(2).broadcast_to([64, 16, 64])), S.cV)

    for oi in range(2):
        P.wait("vector", S.cT)
        P.wait("vector", S.sth)
        P.op("vector", lambda e: e.memset(S_sb[:], 0.0), S.cV)
        P.op("vector", lambda e: e.memset(aoct[:], 0.0), S.cV)
        for ci in range(8):
            tile = oi * 8 + ci
            chunk_preamble(tile, with_e1=False)
            if dbg and tile == 0:
                dsp = nc.dram_tensor("dspl" + tag, [128, 1024], F32, kind="ExternalOutput").ap()
                de2 = nc.dram_tensor("de2" + tag, [128, 1024], F32, kind="ExternalOutput").ap()
                de3 = nc.dram_tensor("de3" + tag, [128, 8], F32, kind="ExternalOutput").ap()
                dkt = nc.dram_tensor("dkt" + tag, [128, 1024], F32, kind="ExternalOutput").ap()
                P.wait("gpsimd", S.cS)
                P.wait("gpsimd", S.cV)
                P.op("gpsimd", lambda e, d=dsp: e.dma_start(out=d, in_=scr2), S.lds, 16)
                P.op("gpsimd", lambda e, d=de2: e.dma_start(out=d, in_=scr1), S.lds, 16)
                P.op("gpsimd", lambda e, d=de3: e.dma_start(out=d, in_=e3[:]), S.lds, 16)
                P.op("gpsimd", lambda e, d=dkt: e.dma_start(out=d, in_=kt_bf), S.lds, 16)
                P.wait("scalar", S.lds)
                P.wait("vector", S.lds)
            P.wait("vector", S.cS)
            P.op("vector", lambda e: e.tensor_add(aoct[:], aoct[:], stat2[:, 0:8]), S.cV)
            mkv_and_update(tile)
            if dbg and tile == 0:
                ds1 = nc.dram_tensor("ds1" + tag, [64, 1024], F32, kind="ExternalOutput").ap()
                P.wait("sync", S.cV)
                P.op("sync", lambda e, d=ds1: e.dma_start(out=d, in_=S_sb[:]), S.sth, 16)
                P.wait("vector", S.sth)
        a64v = a64[:].rearrange("p (h o) -> p h o", o=1)
        P.op("vector", lambda e, a64v=a64v: e.tensor_copy(
            a64v[:, 0:16:2, :], aoct[0:64, :].unsqueeze(2)), S.cV)
        P.op("vector", lambda e, a64v=a64v: e.tensor_copy(
            a64v[:, 1:16:2, :], aoct[64:128, :].unsqueeze(2)), S.cV)
        P.wait("sync", S.cV)
        P.op("sync", lambda e, oi=oi: e.dma_start(
            out=summ[:, oi * 1040:oi * 1040 + 16], in_=a64[:]), S.sth, 16)
        P.op("sync", lambda e, oi=oi: e.dma_start(
            out=summ[:, oi * 1040 + 16:oi * 1040 + 1040], in_=S_sb[:]), S.sth, 16)

    if upto <= 3:
        return

    P.wait("gpsimd", S.sth)
    P.op("gpsimd", lambda e: e.collective_compute(
        "AllGather", mybir.AluOpType.bypass,
        replica_groups=[[0, 1, 2, 3], [4, 5, 6, 7]],
        ins=[summ.opt()], outs=[summ_g.opt()]), S.cG, 1)
    P.wait("sync", S.cG)
    for j in range(8):
        rank, half = (j, 0) if j < 4 else (7 - j, 1)
        P.op("sync", lambda e, j=j, rank=rank, half=half: e.dma_start(
            out=a_all[:, 16 * j:16 * (j + 1)],
            in_=summ_g[64 * rank:64 * (rank + 1), half * 1040:half * 1040 + 16]),
            S.ldh, 16)
    P.wait("vector", S.ldh)
    for oi in range(2):
        P.wait("vector", S.sth)
        P.op("vector", lambda e: e.memset(S_sb[:], 0.0), S.cV)
        for jp in range(8):
            rank, half = (jp, 0) if jp < 4 else (7 - jp, 1)
            P.op("vector", lambda e: e.memset(acml[:], 0.0), S.cV)
            for jpp in range(8):
                tc = theta_sb[:, oi * 64 + jp * 8 + jpp:oi * 64 + jp * 8 + jpp + 1]
                P.op("vector", lambda e, jpp=jpp, tc=tc: e.scalar_tensor_tensor(
                    acml[:], a_all[:, 16 * jpp:16 * (jpp + 1)], tc, acml[:],
                    ALU.mult, ALU.add), S.cV)
            P.wait("scalar", S.cV)
            P.op("scalar", lambda e, oi=oi, jp=jp: e.activation(
                ccoef[:], acml[:], AF.Exp,
                bias=cbias_sb[:, oi * 8 + jp:oi * 8 + jp + 1]), S.cS)
            P.wait("sync", S.cV, S.cV.n)
            P.op("sync", lambda e, rank=rank, half=half: e.dma_start(
                out=sloc, in_=summ_g[64 * rank:64 * (rank + 1),
                                     half * 1040 + 16:half * 1040 + 1040]), S.ldh, 16)
            P.wait("vector", S.ldh)
            P.wait("vector", S.cS)
            P.op("vector", lambda e: e.tensor_mul(
                Stmp.rearrange("p (h v) -> p h v", v=64),
                sloc.rearrange("p (h v) -> p h v", v=64),
                ccoef[:].unsqueeze(2).broadcast_to([64, 16, 64])), S.cV)
            P.op("vector", lambda e: e.tensor_add(S_sb[:], S_sb[:], Stmp), S.cV)
        P.wait("sync", S.cV)
        P.op("sync", lambda e, oi=oi: e.dma_start(
            out=sin_d[:, 1024 * oi:1024 * (oi + 1)], in_=S_sb[:]), S.sth, 16)

    if upto <= 4:
        return

    stream_w("Wo")

    for oi in range(2):
        P.wait("sync", S.cV)
        P.wait("sync", S.sth)
        P.op("sync", lambda e, oi=oi: e.dma_start(
            out=S_sb[:], in_=sin_d[:, 1024 * oi:1024 * (oi + 1)]), S.ldh, 16)
        P.wait("vector", S.ldh)
        P.wait("vector", S.cT)
        for ci in range(8):
            tile = oi * 8 + ci
            P.wait("sync", S.cV)
            P.op("sync", lambda e, tile=tile: e.dma_start(
                out=xres[:], in_=x_in[128 * tile:128 * (tile + 1), :]), S.ldh, 16)
            chunk_preamble(tile, with_e1=True)
            if p2cut <= 1:
                continue
            P.op("vector", lambda e: e.tensor_copy(S_bf[:], S_sb[:]), S.cV)
            q3 = q_fm[:].rearrange("p (oc n) -> p oc n", n=2048)
            e13 = scr3.rearrange("p (c t) -> p c t", t=128)
            qt3 = qt64.rearrange("p (h t) -> p h t", t=128)
            P.op("vector", lambda e, q3=q3, e13=e13, qt3=qt3, tile=tile: e.tensor_mul(
                qt3[:, 0:16:2, :],
                q3[0:64, :, tile * 128:(tile + 1) * 128], e13[0:64, :, :]), S.cV)
            P.op("vector", lambda e, q3=q3, e13=e13, qt3=qt3, tile=tile: e.tensor_mul(
                qt3[:, 1:16:2, :],
                q3[64:128, :, tile * 128:(tile + 1) * 128], e13[64:128, :, :]), S.cV)
            P.wait("tensor", S.cV)
            for c in range(8):
                P.op("tensor", lambda e, c=c: e.transpose(
                    ps_t[:, 128 * c:128 * (c + 1)], kt_bf[:, 128 * c:128 * (c + 1)],
                    id_bf[:]), S.cT)
            P.wait("vector", S.cT)
            pt3 = ps_t[:].rearrange("p (c t) -> p c t", t=128)
            kf3 = ktf64.rearrange("p (h t) -> p h t", t=128)
            P.op("vector", lambda e, pt3=pt3, kf3=kf3: e.tensor_copy(
                kf3[:, 0:16:2, :], pt3[0:64, :, :]), S.cV)
            P.op("vector", lambda e, pt3=pt3, kf3=kf3: e.tensor_copy(
                kf3[:, 1:16:2, :], pt3[64:128, :, :]), S.cV)
            if dbg and tile == 0:
                dqt = nc.dram_tensor("dqt" + tag, [64, 2048], F32, kind="ExternalOutput").ap()
                dkf = nc.dram_tensor("dkf" + tag, [64, 2048], F32, kind="ExternalOutput").ap()
                P.wait("gpsimd", S.cV)
                P.op("gpsimd", lambda e, d=dqt: e.dma_start(out=d, in_=qt64), S.lds, 16)
                P.op("gpsimd", lambda e, d=dkf: e.dma_start(out=d, in_=ktf64), S.lds, 16)
                P.wait("tensor", S.lds)
            if p2cut <= 2:
                continue
            P.wait("tensor", S.cV)
            for h in range(16):
                P.op("tensor", lambda e, h=h: e.matmul(
                    ps_big[:, 128 * h:128 * (h + 1)],
                    ktf64[0:64, 128 * h:128 * (h + 1)],
                    qt64[0:64, 128 * h:128 * (h + 1)],
                    start=True, stop=True), S.cT if h == 15 else None)
            P.wait("vector", S.cT)
            P.op("vector", lambda e: e.tensor_mul(
                pt_bf.rearrange("p (h t) -> p h t", t=128),
                ps_big[:].rearrange("p (h t) -> p h t", t=128),
                mask_bf[:].unsqueeze(1).broadcast_to([128, 16, 128])), S.cV)
            if dbg and tile == 0:
                dpt = nc.dram_tensor("dpt" + tag, [128, 2048], F32, kind="ExternalOutput").ap()
                P.wait("gpsimd", S.cV)
                P.op("gpsimd", lambda e, d=dpt: e.dma_start(out=d, in_=pt_bf), S.lds, 16)
                P.wait("tensor", S.lds)
            if p2cut <= 3:
                continue
            P.wait("tensor", S.cV)
            for h in range(16):
                P.op("tensor", lambda e, h=h, tile=tile: e.matmul(
                    ps_o[:, 64 * h:64 * (h + 1)], pt_bf[:, 128 * h:128 * (h + 1)],
                    v_tm[:, tile * 1024 + 64 * h:tile * 1024 + 64 * (h + 1)],
                    start=True, stop=False), None)
                P.op("tensor", lambda e, h=h: e.matmul(
                    ps_o[:, 64 * h:64 * (h + 1)],
                    qt64[0:64, 128 * h:128 * (h + 1)],
                    S_bf[0:64, 64 * h:64 * (h + 1)],
                    start=False, stop=True), S.cT if h == 15 else None)
            if p2cut <= 4:
                continue
            P.wait("scalar", S.cT)
            P.op("scalar", lambda e: e.activation(scr1, ps_o[:], AF.Square,
                                                  scale=0.125), S.cS)
            P.wait("vector", S.cS)
            P.op("vector", lambda e: e.tensor_reduce(
                stat[:].rearrange("p (h o) -> p h o", o=1),
                scr1.rearrange("p (h d) -> p h d", d=64), AX.X, ALU.add), S.cV)
            P.op("vector", lambda e: e.tensor_copy(junkV[:], cvals[:, 0:1]), S.cV)
            P.wait("scalar", S.cV)
            P.op("scalar", lambda e: e.activation(stat2[:], stat[:], AF.Ln,
                                                  bias=EPS), S.cS)
            P.op("scalar", lambda e: e.activation(stat2[:], stat2[:], AF.Exp,
                                                  scale=-0.5), S.cS)
            P.wait("vector", S.cS)
            P.op("vector", lambda e: e.tensor_mul(
                scr3.rearrange("p (h d) -> p h d", d=64),
                ps_o[:].rearrange("p (h d) -> p h d", d=64),
                stat2[:].unsqueeze(2).broadcast_to([128, 16, 64])), S.cV)
            P.op("vector", lambda e, tile=tile: e.tensor_mul(
                og_bf, scr3, gs_tm[:, tile * 1024:(tile + 1) * 1024]), S.cV)
            if dbg and tile == 0:
                don = nc.dram_tensor("don" + tag, [128, 1024], F32, kind="ExternalOutput").ap()
                dog = nc.dram_tensor("dog" + tag, [128, 1024], F32, kind="ExternalOutput").ap()
                P.wait("gpsimd", S.cV)
                P.op("gpsimd", lambda e, d=don: e.dma_start(out=d, in_=scr3), S.lds, 16)
                P.op("gpsimd", lambda e, d=dog: e.dma_start(out=d, in_=og_bf), S.lds, 16)
                P.wait("tensor", S.lds)
                P.wait("scalar", S.lds)
            mkv_and_update(tile)
            P.wait("tensor", S.cV)
            for c in range(8):
                P.op("tensor", lambda e, c=c: e.transpose(
                    ps_t[:, 128 * c:128 * (c + 1)], og_bf[:, 128 * c:128 * (c + 1)],
                    id_bf[:]), S.cT)
            P.wait("vector", S.cT)
            P.op("vector", lambda e: e.tensor_copy(ogf_bf, ps_t[:]), S.cV)
            if p2cut <= 5:
                continue
            P.wait("tensor", S.cV)
            for dg in range(2):
                for kc in range(8):
                    P.op("tensor", lambda e, dg=dg, kc=kc: e.matmul(
                        ps_big[:, dg * 512:(dg + 1) * 512],
                        ogf_bf[:, kc * 128:(kc + 1) * 128],
                        wbuf[:, kc * 1024 + dg * 512:kc * 1024 + (dg + 1) * 512],
                        start=(kc == 0), stop=(kc == 7)),
                        S.cT if (dg == 1 and kc == 7) else None)
            P.wait("vector", S.cT)
            P.wait("vector", S.ldh)
            P.op("vector", lambda e: e.tensor_add(r_sb, ps_big[:, 0:1024], xres[:]), S.cV)
            P.op("vector", lambda e: e.tensor_copy(og_bf, r_sb), S.cV)
            P.wait("tensor", S.cV)
            for c in range(8):
                P.op("tensor", lambda e, c=c: e.transpose(
                    ps_t[:, 128 * c:128 * (c + 1)], og_bf[:, 128 * c:128 * (c + 1)],
                    id_bf[:]), S.cT)
            P.wait("vector", S.cT)
            P.op("vector", lambda e: e.tensor_copy(ogf_bf, ps_t[:]), S.cV)
            sq_bf = gs_tm[:, tile * 1024:(tile + 1) * 1024]
            P.wait("scalar", S.cV)
            P.op("scalar", lambda e, sq_bf=sq_bf: e.activation(
                sq_bf, ogf_bf, AF.Square, scale=1.0 / 32), S.cS)
            P.wait("tensor", S.cS)
            for c in range(8):
                P.op("tensor", lambda e, c=c: e.matmul(
                    ps_o[0:1, 0:128], onescol[:], ogf_bf[:, 128 * c:128 * (c + 1)],
                    start=(c == 0), stop=(c == 7)), None)
            for c in range(8):
                P.op("tensor", lambda e, c=c, sq_bf=sq_bf: e.matmul(
                    ps_o[0:1, 128:256], onescol[:], sq_bf[:, 128 * c:128 * (c + 1)],
                    start=(c == 0), stop=(c == 7)), S.cT if c == 7 else None)
            P.wait("scalar", S.cT)
            P.op("scalar", lambda e: e.activation(murow[:], ps_o[0:1, 0:128], AF.Copy,
                                                  scale=1.0 / 1024), S.cS)
            P.wait("vector", S.cS)
            P.op("vector", lambda e: e.tensor_mul(varrow[:], murow[:], murow[:]), S.cV)
            P.op("vector", lambda e: e.tensor_sub(varrow[:], ps_o[0:1, 128:256], varrow[:]), S.cV)
            P.wait("scalar", S.cV)
            P.op("scalar", lambda e: e.activation(varrow[:], varrow[:], AF.Ln,
                                                  bias=cvals[0:1, 0:1]), S.cS)
            P.op("scalar", lambda e: e.activation(rsrow[:], varrow[:], AF.Exp,
                                                  scale=-0.5), S.cS)
            P.wait("tensor", S.cS)
            P.op("tensor", lambda e: e.matmul(ps_o[0:128, 256:257], murow[:],
                                              cvals[0:1, 7:8],
                                              start=True, stop=True), None)
            P.op("tensor", lambda e: e.matmul(ps_o[0:128, 257:258], rsrow[:],
                                              cvals[0:1, 7:8],
                                              start=True, stop=True), S.cT)
            P.wait("vector", S.cT)
            P.op("vector", lambda e: e.tensor_copy(col1[:], ps_o[0:128, 256:257]), S.cV)
            P.op("vector", lambda e: e.tensor_copy(col2[:], ps_o[0:128, 257:258]), S.cV)
            P.op("vector", lambda e: e.tensor_scalar(
                scr2, r_sb, col1[:], col2[:], ALU.subtract, ALU.mult), S.cV)
            P.wait("sync", S.cV)
            P.op("sync", lambda e, tile=tile: e.dma_start(
                out=x_out[128 * tile:128 * (tile + 1), :], in_=scr2), S.sth, 16)


def emit_ln_tail(nc, P, S, B, r_f32, out_ap, gamma_bc=None, beta_bc=None, to_sbuf=None):
    og_bf, ogf_bf, sq_bf = B["og_bf"], B["ogf_bf"], B["sq_bf"]
    ps_t, ps_st = B["ps_t"], B["ps_st"]
    P.wait("vector", S.cT)
    P.op("vector", lambda e: e.tensor_copy(og_bf, r_f32), S.cV)
    P.wait("tensor", S.cV)
    for c in range(8):
        P.op("tensor", lambda e, c=c: e.transpose(
            ps_t[:, 128 * c:128 * (c + 1)], og_bf[:, 128 * c:128 * (c + 1)],
            B["id_bf"][:]), S.cT)
    P.wait("vector", S.cT)
    P.op("vector", lambda e: e.tensor_copy(ogf_bf, ps_t[:, 0:1024]), S.cV)
    P.wait("scalar", S.cV)
    P.op("scalar", lambda e: e.activation(sq_bf, ogf_bf, AF.Square, scale=1.0 / 32), S.cS)
    P.wait("tensor", S.cS)
    for c in range(8):
        P.op("tensor", lambda e, c=c: e.matmul(
            ps_st[0:1, 0:128], B["onescol"][:], ogf_bf[:, 128 * c:128 * (c + 1)],
            start=(c == 0), stop=(c == 7)), None)
    for c in range(8):
        P.op("tensor", lambda e, c=c: e.matmul(
            ps_st[0:1, 128:256], B["onescol"][:], sq_bf[:, 128 * c:128 * (c + 1)],
            start=(c == 0), stop=(c == 7)), S.cT if c == 7 else None)
    P.wait("scalar", S.cT)
    P.op("scalar", lambda e: e.activation(B["murow"][:], ps_st[0:1, 0:128], AF.Copy,
                                          scale=1.0 / 1024), S.cS)
    P.wait("vector", S.cS)
    P.op("vector", lambda e: e.tensor_mul(B["varrow"][:], B["murow"][:], B["murow"][:]), S.cV)
    P.op("vector", lambda e: e.tensor_sub(B["varrow"][:], ps_st[0:1, 128:256], B["varrow"][:]), S.cV)
    P.wait("scalar", S.cV)
    P.op("scalar", lambda e: e.activation(B["varrow"][:], B["varrow"][:], AF.Ln,
                                          bias=B["cvals"][0:1, 0:1]), S.cS)
    P.op("scalar", lambda e: e.activation(B["rsrow"][:], B["varrow"][:], AF.Exp,
                                          scale=-0.5), S.cS)
    P.wait("tensor", S.cS)
    onef = B["cvals"][0:1, 7:8]
    P.op("tensor", lambda e, onef=onef: e.matmul(
        ps_st[0:128, 256:257], B["murow"][:], onef, start=True, stop=True), None)
    P.op("tensor", lambda e, onef=onef: e.matmul(
        ps_st[0:128, 257:258], B["rsrow"][:], onef, start=True, stop=True), S.cT)
    P.wait("vector", S.cT)
    P.op("vector", lambda e: e.tensor_copy(B["col1"][:], ps_st[0:128, 256:257]), S.cV)
    P.op("vector", lambda e: e.tensor_copy(B["col2"][:], ps_st[0:128, 257:258]), S.cV)
    dst = B["scrO"] if to_sbuf is None else to_sbuf
    P.op("vector", lambda e, dst=dst: e.tensor_scalar(
        dst, r_f32, B["col1"][:], B["col2"][:], ALU.subtract, ALU.mult), S.cV)
    if gamma_bc is not None:
        P.op("vector", lambda e, dst=dst: e.tensor_mul(dst, dst, gamma_bc), S.cV)
        P.op("vector", lambda e, dst=dst: e.tensor_add(dst, dst, beta_bc), S.cV)
    if out_ap is not None:
        P.wait("sync", S.cV)
        P.op("sync", lambda e, dst=dst: e.dma_start(out=out_ap, in_=dst), S.sth, 16)


def emit_xfm_build(nc, P, S, B, x_in, x_fm):
    xres, stage2, ps_t, id_bf = B["xres"], B["stage2"], B["ps_t"], B["id_bf"]
    for t in range(NT):
        P.wait("sync", S.cV)
        P.op("sync", lambda e, t=t: e.dma_start(
            out=xres[:], in_=x_in[128 * t:128 * (t + 1), :]), S.ldh, 16)
        P.wait("vector", S.ldh)
        P.wait("vector", S.cT)
        P.op("vector", lambda e: e.tensor_copy(stage2, xres[:]), S.cV)
        P.wait("tensor", S.cV)
        for c in range(8):
            P.op("tensor", lambda e, c=c: e.transpose(
                ps_t[:, 128 * c:128 * (c + 1)], stage2[:, 128 * c:128 * (c + 1)],
                id_bf[:]), S.cT)
        P.wait("vector", S.cT)
        dst = x_fm[:].rearrange("p (fc n) -> p fc n", n=2048)[:, :, t * 128:(t + 1) * 128]
        P.op("vector", lambda e, d=dst: e.tensor_copy(d, ps_t[:, 0:1024].rearrange(
            "p (fc n) -> p fc n", n=128)), S.cV)


def emit_l0_ffn(nc, P, S, es, W, CN, x_in, x_out, tag="f0"):
    sb = lambda name, shape, dt: es.enter_context(nc.sbuf_tensor(name + tag, shape, dt))
    psa = lambda name, shape, dt: es.enter_context(nc.psum_tensor(name + tag, shape, dt))

    x_fm = sb("x_fm", [128, 16384], BF16)
    y_acc = sb("y_acc", [128, 16384], F32)
    g_fm = sb("g_fm", [128, 4096], BF16)
    w1b = sb("w1b", [128, 8192], BF16)
    w2b = sb("w2b", [128, 8192], BF16)
    rwf = sb("rwf", [128, 64], BF16)
    b1r = sb("b1r", [1, 4096], BF16)
    b2bc = sb("b2bc", [128, 1024], F32)
    ones512 = sb("ones512", [1, 512], BF16)
    xres = sb("xres", [128, 1024], F32)
    stage2 = sb("stage2", [128, 1024], BF16)
    scr1 = sb("scr1", [128, 1024], F32)
    scr2 = sb("scr2", [128, 1024], F32)
    rt = sb("rt", [128, 128], F32)
    fcol = sb("fcol", [128, 16], F32)
    m1 = sb("m1", [128, 16], F32)
    m2 = sb("m2", [128, 16], F32)
    tmp8 = sb("tmp8", [128, 128], F32)
    og_bf = sb("og_bf", [128, 1024], BF16)
    ogf_bf = sb("ogf_bf", [128, 1024], BF16)
    sq_bf = sb("sq_bf", [128, 1024], BF16)
    onescol = sb("onescol", [128, 1], BF16)
    murow = sb("murow", [1, 128], F32)
    rsrow = sb("rsrow", [1, 128], F32)
    varrow = sb("varrow", [1, 128], F32)
    col1 = sb("col1", [128, 1], F32)
    col2 = sb("col2", [128, 1], F32)
    id_bf = sb("id_bf", [128, 128], BF16)
    cvals = sb("cvals", [128, 8], F32)
    register_consts(nc, cvals)

    ps_h = psa("ps_h", [128, 2048], F32)
    ps_y = psa("ps_y", [128, 1024], F32)
    ps_st = psa("ps_st", [128, 512], F32)
    ps_t = psa("ps_t", [128, 1024], BF16)

    B = dict(og_bf=og_bf[:], ogf_bf=ogf_bf[:], sq_bf=sq_bf[:], ps_t=ps_t,
             ps_st=ps_st, id_bf=id_bf, onescol=onescol, murow=murow,
             rsrow=rsrow, varrow=varrow, col1=col1, col2=col2, cvals=cvals,
             scrO=scr2[:], xres=xres, stage2=stage2[:])

    for dst, srcap in [(id_bf[:], CN["ident"]), (onescol[:], CN["ones_col"]),
                       (b1r[:], W["b1"]), (ones512[:], CN["ones_row2048"][:, 0:512]),
                       (rwf[:].rearrange("p (kc e) -> p kc e", e=8),
                        W["router_w"].rearrange("(kc p) e -> p kc e", p=128))]:
        P.op("gpsimd", lambda e, d=dst, s=srcap: e.dma_start(out=d, in_=s), S.lds, 16)
    P.op("sync", lambda e: e.dma_start(out=cvals[:], in_=CN["constvals"]), S.ldh, 16)
    P.op("sync", lambda e: e.dma_start(out=b2bc[:], in_=W["b2_t"]), S.ldh, 16)
    P.wait("tensor", S.lds)
    P.wait("tensor", S.ldh)

    emit_xfm_build(nc, P, S, B, x_in, x_fm)

    P.wait("scalar", S.ldh)
    P.op("scalar", lambda e: e.activation(tmp8[:, 0:8], cvals[:, 0:8], AF.Exp), S.cS)
    P.op("scalar", lambda e: e.activation(tmp8[:, 0:1], cvals[:, 7:8], AF.Ln), S.cS)
    P.op("scalar", lambda e: e.activation(tmp8[:, 0:8], cvals[:, 0:8], AF.Gelu_apprx_tanh), S.cS)
    P.op("scalar", lambda e: e.activation(tmp8[:, 0:8], cvals[:, 0:8], AF.Square), S.cS)

    P.wait("tensor", S.cV)
    for t in range(NT):
        P.wait("tensor", S.cV)
        for kc in range(8):
            P.op("tensor", lambda e, t=t, kc=kc: e.matmul(
                ps_y[:, 8 * t:8 * (t + 1)],
                x_fm[:, kc * 2048 + t * 128:kc * 2048 + (t + 1) * 128],
                rwf[:, kc * 8:(kc + 1) * 8],
                start=(kc == 0), stop=(kc == 7)), S.cT if kc == 7 else None)
    P.wait("vector", S.cT)
    P.op("vector", lambda e: e.tensor_copy(rt[:], ps_y[:, 0:128]), S.cV)
    r3 = rt[:].rearrange("p (t e) -> p t e", e=8)
    t3 = tmp8[:].rearrange("p (t e) -> p t e", e=8)
    m1v = m1[:].rearrange("p (t o) -> p t o", o=1)
    m2v = m2[:].rearrange("p (t o) -> p t o", o=1)

    def tree(op, srcv, outv):
        P.op("vector", lambda e: e.tensor_tensor(t3[:, :, 0:4], srcv[:, :, 0:4], srcv[:, :, 4:8], op), S.cV)
        P.op("vector", lambda e: e.tensor_tensor(t3[:, :, 0:2], t3[:, :, 0:2], t3[:, :, 2:4], op), S.cV)
        P.op("vector", lambda e: e.tensor_tensor(outv, t3[:, :, 0:1], t3[:, :, 1:2], op), S.cV)

    tree(ALU.max, r3, m1v)
    P.op("vector", lambda e: e.tensor_sub(r3, r3, m1v.broadcast_to([128, 16, 8])), S.cV)
    P.wait("scalar", S.cV)
    P.op("scalar", lambda e: e.activation(rt[:], rt[:], AF.Exp), S.cS)
    P.wait("vector", S.cS)
    tree(ALU.add, r3, m2v)
    P.wait("scalar", S.cV)
    P.op("scalar", lambda e: e.activation(m2[:], m2[:], AF.Ln), S.cS)
    P.op("scalar", lambda e: e.activation(m2[:], m2[:], AF.Exp, scale=-1.0), S.cS)
    P.wait("vector", S.cS)
    P.op("vector", lambda e: e.tensor_mul(r3, r3, m2v.broadcast_to([128, 16, 8])), S.cV)
    tree(ALU.max, r3, m1v)
    P.op("vector", lambda e: e.tensor_tensor(t3, r3, m1v.broadcast_to([128, 16, 8]), ALU.is_ge), S.cV)
    P.op("vector", lambda e: e.scalar_tensor_tensor(t3, t3, -1e30, r3, ALU.mult, ALU.add), S.cV)
    P.op("vector", lambda e: e.tensor_tensor(t3[:, :, 0:4], t3[:, :, 0:4], t3[:, :, 4:8], ALU.max), S.cV)
    P.op("vector", lambda e: e.tensor_tensor(t3[:, :, 0:2], t3[:, :, 0:2], t3[:, :, 2:4], ALU.max), S.cV)
    P.op("vector", lambda e: e.tensor_tensor(m2v, t3[:, :, 0:1], t3[:, :, 1:2], ALU.max), S.cV)
    P.op("vector", lambda e: e.tensor_add(m1[:], m1[:], m2[:]), S.cV)
    P.wait("scalar", S.cV)
    P.op("scalar", lambda e: e.activation(m2[:], m1[:], AF.Ln), S.cS)
    P.op("scalar", lambda e: e.activation(m2[:], m2[:], AF.Exp, scale=-1.0), S.cS)
    P.wait("vector", S.cS)
    P.op("vector", lambda e: e.tensor_mul(fcol[:], m1[:], m2[:]), S.cV)

    for half in range(4):
        h0 = half * 1024
        P.wait("gpsimd", S.cT)
        s1 = W["w1"][:, h0:h0 + 1024].rearrange("(kc p) n -> p kc n", p=128)
        P.op("gpsimd", lambda e, s=s1: e.dma_start(
            out=w1b[:].rearrange("p (kc n) -> p kc n", n=1024), in_=s), S.lds, 16)
        s2 = W["w2"][h0:h0 + 1024, :].rearrange("(hc p) n -> p hc n", p=128)
        P.op("gpsimd", lambda e, s=s2: e.dma_start(
            out=w2b[:].rearrange("p (hc n) -> p hc n", n=1024), in_=s), S.lds, 16)
        P.wait("tensor", S.lds)
        for tg in range(4):
            for hc in range(8):
                slot = hc % 4
                po = ps_h[:, 512 * slot:512 * (slot + 1)]
                P.wait("tensor", S.cS)
                for kc in range(8):
                    P.op("tensor", lambda e, po=po, kc=kc, hc=hc, tg=tg: e.matmul(
                        po, w1b[:, kc * 1024 + hc * 128:kc * 1024 + (hc + 1) * 128],
                        x_fm[:, kc * 2048 + tg * 512:kc * 2048 + (tg + 1) * 512],
                        start=(kc == 0), stop=False), None)
                P.op("tensor", lambda e, po=po, hc=hc, h0=h0: e.matmul(
                    po, b1r[0:1, h0 + hc * 128:h0 + (hc + 1) * 128], ones512[:],
                    start=False, stop=True), S.cT)
                P.wait("scalar", S.cT)
                P.op("scalar", lambda e, po=po, hc=hc: e.activation(
                    g_fm[:, hc * 512:(hc + 1) * 512], po, AF.Gelu_apprx_tanh), S.cS)
            P.wait("tensor", S.cS)
            for tt in range(4):
                tile = tg * 4 + tt
                for dg in range(2):
                    po = ps_y[:, dg * 512:(dg + 1) * 512]
                    P.wait("tensor", S.cV)
                    for hc in range(8):
                        P.op("tensor", lambda e, po=po, hc=hc, tt=tt, dg=dg: e.matmul(
                            po, g_fm[:, hc * 512 + tt * 128:hc * 512 + tt * 128 + 128],
                            w2b[:, hc * 1024 + dg * 512:hc * 1024 + (dg + 1) * 512],
                            start=(hc == 0), stop=(hc == 7)), S.cT if hc == 7 else None)
                    P.wait("vector", S.cT)
                    ya = y_acc[:, tile * 1024 + dg * 512:tile * 1024 + (dg + 1) * 512]
                    if half == 0:
                        P.op("vector", lambda e, po=po, ya=ya: e.tensor_copy(ya, po), S.cV)
                    else:
                        P.op("vector", lambda e, po=po, ya=ya: e.tensor_add(ya, ya, po), S.cV)

    if DBG_L0F:
        dfc = nc.dram_tensor("dfc", [128, 16], F32, kind="ExternalOutput").ap()
        dya = nc.dram_tensor("dya", [128, 16384], F32, kind="ExternalOutput").ap()
        drt = nc.dram_tensor("drt", [128, 128], F32, kind="ExternalOutput").ap()
        P.wait("gpsimd", S.cV)
        P.op("gpsimd", lambda e: e.dma_start(out=dfc, in_=fcol[:]), S.lds, 16)
        P.op("gpsimd", lambda e: e.dma_start(out=dya, in_=y_acc[:]), S.lds, 16)
        P.op("gpsimd", lambda e: e.dma_start(out=drt, in_=rt[:]), S.lds, 16)
        P.wait("vector", S.lds)

    for tile in range(NT):
        P.wait("sync", S.cV)
        P.op("sync", lambda e, tile=tile: e.dma_start(
            out=xres[:], in_=x_in[128 * tile:128 * (tile + 1), :]), S.ldh, 16)
        P.wait("vector", S.ldh)
        P.wait("vector", S.sth)
        P.op("vector", lambda e, tile=tile: e.tensor_add(
            scr1[:], y_acc[:, tile * 1024:(tile + 1) * 1024], b2bc[:]), S.cV)
        P.op("vector", lambda e, tile=tile: e.tensor_single_scalar(
            scr1[:], scr1[:], fcol[:, tile:tile + 1], ALU.mult), S.cV)
        P.op("vector", lambda e: e.tensor_add(scr1[:], scr1[:], xres[:]), S.cV)
        emit_ln_tail(nc, P, S, B, scr1[:],
                     x_out[128 * tile:128 * (tile + 1), :])


def emit_l1_attn(nc, P, S, es, W, CN, CI, x_in, x_out, tag="a1", upto=99, fused=False,
                 bg_dma=None):
    sb = lambda name, shape, dt: es.enter_context(nc.sbuf_tensor(name + tag, shape, dt))
    psa = lambda name, shape, dt: es.enter_context(nc.psum_tensor(name + tag, shape, dt))
    from concourse.bass import IndirectOffsetOnAxis

    x_fm = sb("x_fm", [128, 16384], BF16)
    big2 = sb("big2", [128, 16384], BF16)
    kf = sb("kf", [128, 4096], BF16)
    v_sb = sb("v_sb", [128, 4096], BF16)
    V65 = sb("V65", [128, 4160], BF16)
    E_sb = sb("E_sb", [128, 2048], BF16)
    obf = sb("obf", [65, 512], BF16)
    zrow = sb("zrow", [1, 512], F32)
    zr = sb("zr", [128, 4], F32)
    wbuf = sb("wbuf", [128, 8192], BF16)
    xres = sb("xres", [128, 1024], F32)
    stage2 = sb("stage2", [128, 1024], BF16)
    scr1 = sb("scr1", [128, 1024], F32)
    scr2 = sb("scr2", [128, 1024], F32)
    scr3 = sb("scr3", [128, 1024], F32)
    att_bf = sb("att_bf", [128, 1024], BF16)
    lnfm = sb("lnfm", [128, 1024], BF16)
    og_bf = sb("og_bf", [128, 1024], BF16)
    ogf_bf = sb("ogf_bf", [128, 1024], BF16)
    sq_bf = sb("sq_bf", [128, 1024], BF16)
    bqr = sb("bqr", [1, 1024], BF16)
    bkr = sb("bkr", [1, 256], BF16)
    ones512 = sb("ones512", [1, 512], BF16)
    bvr = sb("bvr", [1, 256], BF16)
    bor = sb("bor", [1, 1024], BF16)
    lng_bc = sb("lng_bc", [128, 1024], F32)
    lnb_bc = sb("lnb_bc", [128, 1024], F32)
    dmask = sb("dmask", [128, 2048], BF16)
    onescol = sb("onescol", [128, 1], BF16)
    onesrow = sb("onesrow", [1, 128], BF16)
    murow = sb("murow", [1, 128], F32)
    rsrow = sb("rsrow", [1, 128], F32)
    varrow = sb("varrow", [1, 128], F32)
    col1 = sb("col1", [128, 1], F32)
    col2 = sb("col2", [128, 1], F32)
    id_bf = sb("id_bf", [128, 128], BF16)
    kidx = sb("kidx", [64, 8], mybir.dt.int32)
    qidx = sb("qidx", [64, 32], mybir.dt.int32)
    vidx = sb("vidx", [128, 64], mybir.dt.int32)
    oidx = sb("oidx", [128, 64], mybir.dt.int32)
    cvals = sb("cvals", [128, 8], F32)
    register_consts(nc, cvals)

    K_sb = x_fm[0:64, 0:8192]
    qh_sb = x_fm[0:64, 8192:16384]

    ps_sc = psa("ps_sc", [128, 2048], F32)
    ps_o = psa("ps_o", [128, 512], F32)
    ps_st = psa("ps_st", [128, 512], F32)
    ps_t = psa("ps_t", [128, 1024], BF16)

    B = dict(og_bf=og_bf[:], ogf_bf=ogf_bf[:], sq_bf=sq_bf[:], ps_t=ps_t,
             ps_st=ps_st, id_bf=id_bf, onescol=onescol, murow=murow,
             rsrow=rsrow, varrow=varrow, col1=col1, col2=col2, cvals=cvals,
             scrO=scr2[:], xres=xres, stage2=stage2[:])

    qkv_d = nc.dram_tensor("qkv_d" + tag, [1536, 2048], BF16).ap()
    qkv_g = nc.dram_tensor("qkv_g" + tag, [6144, 2048], BF16).ap()

    for dst, srcap in [(id_bf[:], CN["ident"]), (onescol[:], CN["ones_col"]),
                       (onesrow[:], CN["ones_row"]), (dmask[:], CN["diag_mask"]),
                       (bvr[:], W["bv"]), (bor[:], W["bo"]), (bqr[:], W["bq"]),
                       (bkr[:], W["bk"]), (ones512[:], CN["ones_row2048"][:, 0:512])]:
        P.op("gpsimd", lambda e, d=dst, s=srcap: e.dma_start(out=d, in_=s), S.lds, 16)
    for dst, srcap in [(cvals[:], CN["constvals"]), (lng_bc[:], W["lng_t"]),
                       (lnb_bc[:], W["lnb_t"]), (kidx[:], CI["kidx"]),
                       (qidx[:], CI["qidx"]), (vidx[:], CI["vidx"]),
                       (oidx[:], CI["oidx"])]:
        P.op("sync", lambda e, d=dst, s=srcap: e.dma_start(out=d, in_=s), S.ldh, 16)
    P.wait("tensor", S.lds)
    P.wait("tensor", S.ldh)

    if upto != 16.5:
        emit_xfm_build(nc, P, S, B, x_in, x_fm)
    P.wait("scalar", S.ldh)
    P.op("scalar", lambda e: e.activation(scr1[:, 0:8], cvals[:, 0:8], AF.Exp), S.cS)
    P.op("scalar", lambda e: e.activation(scr1[:, 0:1], cvals[:, 7:8], AF.Ln), S.cS)
    P.op("scalar", lambda e: e.activation(scr1[:, 0:8], cvals[:, 0:8], AF.Square), S.cS)


    def stream_w(name):
        P.wait("gpsimd", S.cT)
        P.op("gpsimd", lambda e, s=W[name].rearrange("(kc p) n -> p kc n", p=128),
             w=W[name].shape[1]: e.dma_start(
                 out=wbuf[:, 0:8 * w].rearrange("p (kc n) -> p kc n", n=w), in_=s),
             S.lds, 16)
        P.wait("tensor", S.lds)

    if upto == 16.5:
        P.wait("sync", S.cS)
        P.wait("sync", S.cV)
        P.op("sync", lambda e: e.dma_start(
            out=q_d.rearrange("(oc p) t -> p oc t", p=128),
            in_=big2[:].rearrange("p (oc t) -> p oc t", t=2048)), S.sth, 16)
        P.wait("gpsimd", S.sth)
        for din, dout in [(q_d, q_g), (k_d, k_g), (v_d, v_g)]:
            P.op("gpsimd", lambda e, a=din, b=dout: e.collective_compute(
                "AllGather", mybir.AluOpType.bypass,
                replica_groups=[[0, 1, 2, 3], [4, 5, 6, 7]],
                ins=[a.opt()], outs=[b.opt()]), S.cG, 1)
        P.wait("gpsimd", S.cG)
        return
    stream_w("Wq")
    P.wait("tensor", S.cV)
    for oc in range(8):
        for tg in range(4):
            slot = (oc * 4 + tg) % 4
            po = ps_sc[:, 512 * slot:512 * (slot + 1)]
            P.wait("tensor", S.cS)
            P.wait("tensor", S.cV)
            for kc in range(8):
                P.op("tensor", lambda e, po=po, kc=kc, oc=oc, tg=tg: e.matmul(
                    po, wbuf[:, kc * 1024 + oc * 128:kc * 1024 + (oc + 1) * 128],
                    x_fm[:, kc * 2048 + tg * 512:kc * 2048 + (tg + 1) * 512],
                    start=(kc == 0), stop=False), None)
            P.op("tensor", lambda e, po=po, oc=oc: e.matmul(
                po, bqr[0:1, oc * 128:(oc + 1) * 128], ones512[:],
                start=False, stop=True), S.cT)
            P.wait("scalar", S.cT)
            P.op("scalar", lambda e, po=po, oc=oc, tg=tg: e.activation(
                big2[:, oc * 2048 + tg * 512:oc * 2048 + (tg + 1) * 512], po,
                AF.Copy, scale=0.125), S.cS)
    stream_w("Wk")
    for c in range(2):
        for tg in range(4):
            slot = (c * 4 + tg) % 4
            po = ps_sc[:, 512 * slot:512 * (slot + 1)]
            P.wait("tensor", S.cS)
            for kc in range(8):
                P.op("tensor", lambda e, po=po, kc=kc, c=c, tg=tg: e.matmul(
                    po, wbuf[:, kc * 256 + c * 128:kc * 256 + (c + 1) * 128],
                    x_fm[:, kc * 2048 + tg * 512:kc * 2048 + (tg + 1) * 512],
                    start=(kc == 0), stop=False), None)
            P.op("tensor", lambda e, po=po, c=c: e.matmul(
                po, bkr[0:1, c * 128:(c + 1) * 128], ones512[:],
                start=False, stop=True), S.cT)
            P.wait("scalar", S.cT)
            P.op("scalar", lambda e, po=po, c=c, tg=tg: e.activation(
                kf[:, c * 2048 + tg * 512:c * 2048 + (tg + 1) * 512], po,
                AF.Copy), S.cS)
    stream_w("Wv")
    for t in range(NT):
        slot = t % 4
        po = ps_sc[:, 512 * slot:512 * slot + 256]
        P.wait("tensor", S.cV)
        P.wait("tensor", S.cS)
        for kc in range(8):
            P.op("tensor", lambda e, po=po, kc=kc, t=t: e.matmul(
                po, x_fm[:, kc * 2048 + t * 128:kc * 2048 + (t + 1) * 128],
                wbuf[:, kc * 256:(kc + 1) * 256],
                start=(kc == 0), stop=False), None)
        P.op("tensor", lambda e, po=po: e.matmul(
            po, onesrow[:], bvr[:], start=False, stop=True), S.cT)
        P.wait("vector", S.cT)
        P.op("vector", lambda e, po=po, t=t: e.tensor_copy(
            v_sb[:, t * 256:(t + 1) * 256], po), S.cV)

    P.wait("sync", S.cS)
    P.wait("sync", S.cV)
    P.op("sync", lambda e: e.dma_start(
        out=qkv_d[0:1024, :].rearrange("(oc p) t -> p oc t", p=128),
        in_=big2[:].rearrange("p (oc t) -> p oc t", t=2048)), S.sth, 16)
    P.op("sync", lambda e: e.dma_start(
        out=qkv_d[1024:1280, :].rearrange("(c p) t -> p c t", p=128),
        in_=kf[:].rearrange("p (c t) -> p c t", t=2048)), S.sth, 16)
    P.op("sync", lambda e: e.dma_start(
        out=qkv_d[1280:1536, :].rearrange("r (t2 n) -> (r t2) n", n=256)
        .rearrange("(t p) n -> p t n", p=128),
        in_=v_sb[:].rearrange("p (t n) -> p t n", n=256)), S.sth, 16)
    P.wait("gpsimd", S.sth)
    P.op("gpsimd", lambda e: e.collective_compute(
        "AllGather", mybir.AluOpType.bypass,
        replica_groups=[[0, 1, 2, 3], [4, 5, 6, 7]],
        ins=[qkv_d.opt()], outs=[qkv_g.opt()]), S.cG, 1)

    P.wait("vector", S.cG, S.cG.n)
    P.wait("vector", S.sth)
    P.op("vector", lambda e: e.memset(
        V65[:].rearrange("p (kt c) -> p kt c", c=65)[:, :, 64:65], 1.0), S.cV)
    P.wait("gpsimd", S.cG)
    P.wait("gpsimd", S.cV)
    for r in range(4):
        for half in range(2):
            oct_ = r if half == 0 else 7 - r
            P.op("gpsimd", lambda e, r=r, half=half, oct_=oct_: e.indirect_dma_start(
                K_sb[:, oct_ * 1024:(oct_ + 1) * 1024], None,
                qkv_g[:, :],
                IndirectOffsetOnAxis(kidx[:, 2 * r + half:2 * r + half + 1], 1)), S.lds, 16)
    for kt in range(64):
        P.op("gpsimd", lambda e, kt=kt: e.indirect_dma_start(
            V65[:, kt * 65:kt * 65 + 64], None,
            qkv_g[:, :],
            IndirectOffsetOnAxis(vidx[:, kt:kt + 1], 1)), S.lds, 16)

    if upto <= 3:
        return
    nqb = 16
    for gp in range(4):
        P.wait("gpsimd", S.cT)
        for r in range(4):
            for half in range(2):
                oct_ = r if half == 0 else 7 - r
                P.op("gpsimd", lambda e, gp=gp, r=r, half=half, oct_=oct_:
                     e.indirect_dma_start(
                         qh_sb[:, oct_ * 1024:(oct_ + 1) * 1024], None,
                         qkv_g[:, :],
                         IndirectOffsetOnAxis(
                             qidx[:, gp * 8 + 2 * r + half:gp * 8 + 2 * r + half + 1], 1)),
                     S.lds, 16)
        if bg_dma is not None:
            bg_dma(gp)
        P.wait("tensor", S.lds)
        for qb in range(nqb):
            for ktg in range(qb + 1):
                P.wait("tensor", S.cS)
                P.wait("tensor", S.cV)
                for j in range(4):
                    kt = ktg * 4 + j
                    P.op("tensor", lambda e, kt=kt, j=j, qb=qb: e.matmul(
                        ps_sc[:, j * 512:(j + 1) * 512],
                        K_sb[:, kt * 128:(kt + 1) * 128],
                        qh_sb[:, qb * 512:(qb + 1) * 512],
                        start=True, stop=True), S.cT if j == 3 else None)
                P.wait("scalar", S.cT)
                P.op("scalar", lambda e: e.activation(E_sb[:], ps_sc[:], AF.Exp), S.cS)
                if ktg == qb:
                    P.wait("vector", S.cS)
                    P.op("vector", lambda e: e.tensor_mul(E_sb[:], E_sb[:], dmask[:]), S.cV)
                    P.wait("tensor", S.cV)
                P.wait("tensor", S.cS)
                for j in range(4):
                    kt = ktg * 4 + j
                    P.op("tensor", lambda e, kt=kt, j=j, qb=qb, ktg=ktg: e.matmul(
                        ps_o[:, 0:512][0:65, :] if False else ps_o[0:65, 0:512],
                        V65[:, kt * 65:(kt + 1) * 65],
                        E_sb[:, j * 512:(j + 1) * 512],
                        start=(ktg == 0 and j == 0), stop=(ktg == qb and j == 3)),
                        S.cT if (ktg == qb and j == 3) else None)
            P.wait("scalar", S.cT)
            P.wait("scalar", S.cV)
            P.op("scalar", lambda e: e.activation(obf[:], ps_o[0:65, 0:512], AF.Copy), S.cS)
            P.wait("vector", S.cS)
            P.op("vector", lambda e: e.tensor_copy(zrow[:], ps_o[64:65, 0:512]), S.cV)
            P.wait("tensor", S.cV)
            for s_ in range(4):
                P.op("tensor", lambda e, s_=s_: e.matmul(
                    ps_st[0:128, s_:s_ + 1], zrow[0:1, 128 * s_:128 * (s_ + 1)],
                    cvals[0:1, 7:8], start=True, stop=True),
                    S.cT if s_ == 3 else None)
            P.wait("scalar", S.cT)
            P.op("scalar", lambda e: e.activation(zr[:, 0:4], ps_st[0:128, 0:4], AF.Ln), S.cS)
            P.op("scalar", lambda e: e.activation(zr[:, 0:4], zr[:, 0:4], AF.Exp,
                                                  scale=-1.0), S.cS)
            P.wait("tensor", S.cS)
            for s_ in range(4):
                P.op("tensor", lambda e, s_=s_: e.transpose(
                    ps_t[:, 64 * s_:64 * (s_ + 1)], obf[0:64, 128 * s_:128 * (s_ + 1)],
                    id_bf[0:64, 0:64]), S.cT if s_ == 3 else None)
            P.wait("vector", S.cT)
            for s_ in range(4):
                gt = qb * 4 + s_
                P.op("vector", lambda e, s_=s_, gt=gt, gp=gp: e.tensor_single_scalar(
                    big2[:, gt * 256 + gp * 64:gt * 256 + (gp + 1) * 64],
                    ps_t[:, 64 * s_:64 * (s_ + 1)], zr[:, s_:s_ + 1], ALU.mult), S.cV)

    if fused:
        o_d = nc.dram_tensor("o_d" + tag, [8192, 256], BF16).ap()
        o_g = nc.dram_tensor("o_g" + tag, [32768, 256], BF16).ap()
        P.wait("sync", S.cV)
        P.op("sync", lambda e: e.dma_start(
            out=o_d.rearrange("(t p) n -> p t n", p=128),
            in_=big2[:].rearrange("p (t n) -> p t n", n=256)), S.sth, 16)
        P.wait("gpsimd", S.sth)
        P.op("gpsimd", lambda e: e.collective_compute(
            "AllGather", mybir.AluOpType.bypass,
            replica_groups=[[0, 1, 2, 3], [4, 5, 6, 7]],
            ins=[o_d.opt()], outs=[o_g.opt()]), S.cG, 1)
        return o_g
    o_out = nc.dram_tensor("o_out" + tag, [8192, 256], BF16, kind="ExternalOutput").ap()
    P.wait("sync", S.cV)
    P.op("sync", lambda e: e.dma_start(
        out=o_out.rearrange("(t p) n -> p t n", p=128),
        in_=big2[:].rearrange("p (t n) -> p t n", n=256)), S.sth, 16)


def emit_l1_attn_fin(nc, P, S, es, W, CN, attn_in, x_in, x_out, tag="a1f",
                     o_gather=None):
    sb = lambda name, shape, dt: es.enter_context(nc.sbuf_tensor(name + tag, shape, dt))
    psa = lambda name, shape, dt: es.enter_context(nc.psum_tensor(name + tag, shape, dt))
    from concourse.bass import IndirectOffsetOnAxis
    wbuf = sb("wbuf", [128, 8192], BF16)
    xres = sb("xres", [128, 1024], F32)
    stage2 = sb("stage2", [128, 1024], BF16)
    scr1 = sb("scr1", [128, 1024], F32)
    scr2 = sb("scr2", [128, 1024], F32)
    scr3 = sb("scr3", [128, 1024], F32)
    att_bf = sb("att_bf", [128, 1024], BF16)
    lnfm = sb("lnfm", [128, 1024], BF16)
    og_bf = sb("og_bf", [128, 1024], BF16)
    ogf_bf = sb("ogf_bf", [128, 1024], BF16)
    sq_bf = sb("sq_bf", [128, 1024], BF16)
    if o_gather is not None:
        att4 = sb("att4", [128, 1024], BF16)
        aidx = sb("aidx", [128, 64], mybir.dt.int32)
    bor = sb("bor", [1, 1024], BF16)
    lng_bc = sb("lng_bc", [128, 1024], F32)
    lnb_bc = sb("lnb_bc", [128, 1024], F32)
    onescol = sb("onescol", [128, 1], BF16)
    onesrow = sb("onesrow", [1, 128], BF16)
    murow = sb("murow", [1, 128], F32)
    rsrow = sb("rsrow", [1, 128], F32)
    varrow = sb("varrow", [1, 128], F32)
    col1 = sb("col1", [128, 1], F32)
    col2 = sb("col2", [128, 1], F32)
    id_bf = sb("id_bf", [128, 128], BF16)
    cvals = sb("cvals", [128, 8], F32)
    register_consts(nc, cvals)
    ps_sc = psa("ps_sc", [128, 1024], F32)
    ps_st = psa("ps_st", [128, 512], F32)
    ps_t = psa("ps_t", [128, 1024], BF16)
    B = dict(og_bf=og_bf[:], ogf_bf=ogf_bf[:], sq_bf=sq_bf[:], ps_t=ps_t,
             ps_st=ps_st, id_bf=id_bf, onescol=onescol, murow=murow,
             rsrow=rsrow, varrow=varrow, col1=col1, col2=col2, cvals=cvals,
             scrO=scr2[:], xres=xres, stage2=stage2[:])
    for dst, srcap in [(id_bf[:], CN["ident"]), (onescol[:], CN["ones_col"]),
                       (onesrow[:], CN["ones_row"]), (bor[:], W["bo"])]:
        P.op("gpsimd", lambda e, d=dst, s=srcap: e.dma_start(out=d, in_=s), S.lds, 16)
    if o_gather is not None:
        P.op("gpsimd", lambda e: e.dma_start(out=aidx[:], in_=o_gather[1]), S.lds, 16)
    for dst, srcap in [(cvals[:], CN["constvals"]), (lng_bc[:], W["lng_t"]),
                       (lnb_bc[:], W["lnb_t"])]:
        P.op("sync", lambda e, d=dst, s=srcap: e.dma_start(out=d, in_=s), S.ldh, 16)
    P.op("gpsimd", lambda e: e.dma_start(
        out=wbuf[:].rearrange("p (kc n) -> p kc n", n=1024),
        in_=W["Wo"].rearrange("(kc p) n -> p kc n", p=128)), S.lds, 16)
    P.wait("tensor", S.lds)
    P.wait("tensor", S.ldh)
    P.wait("vector", S.lds)
    P.wait("vector", S.ldh)
    P.wait("scalar", S.ldh)
    P.op("scalar", lambda e: e.activation(scr1[:, 0:8], cvals[:, 0:8], AF.Exp), S.cS)
    P.op("scalar", lambda e: e.activation(scr1[:, 0:1], cvals[:, 7:8], AF.Ln), S.cS)
    P.op("scalar", lambda e: e.activation(scr1[:, 0:8], cvals[:, 0:8], AF.Square), S.cS)

    abv = att_bf[:].rearrange("p (g r c) -> p g r c", r=4, c=64)
    if o_gather is not None:
        P.wait("gpsimd", S.cG)
    for tile in range(NT):
        if o_gather is None:
            P.wait("sync", S.cV)
            P.op("sync", lambda e, tile=tile: e.dma_start(
                out=att_bf[:], in_=attn_in[128 * tile:128 * (tile + 1), :]), S.ldh, 16)
            P.wait("vector", S.ldh)
        else:
            P.wait("gpsimd", S.cV)
            for r in range(4):
                P.op("gpsimd", lambda e, tile=tile, r=r: e.indirect_dma_start(
                    att4[:, 256 * r:256 * (r + 1)], None, o_gather[0][:, :],
                    IndirectOffsetOnAxis(aidx[:, 4 * tile + r:4 * tile + r + 1], 1)),
                    S.lds, 16)
            P.wait("vector", S.lds)
            for r in range(4):
                P.op("vector", lambda e, r=r: e.tensor_copy(
                    abv[:, :, r, :],
                    att4[:, 256 * r:256 * (r + 1)].rearrange("p (g c) -> p g c", c=64)),
                    S.cV)
        P.wait("vector", S.sth)
        P.op("vector", lambda e: e.tensor_copy(scr1[:], att_bf[:]), S.cV)
        emit_ln_tail(nc, P, S, B, scr1[:], None, gamma_bc=lng_bc[:],
                     beta_bc=lnb_bc[:], to_sbuf=scr3[:])
        P.wait("vector", S.cS)
        P.op("vector", lambda e: e.tensor_copy(stage2[:], scr3[:]), S.cV)
        P.wait("tensor", S.cV)
        for c in range(8):
            P.op("tensor", lambda e, c=c: e.transpose(
                ps_t[:, 128 * c:128 * (c + 1)], stage2[:, 128 * c:128 * (c + 1)],
                id_bf[:]), S.cT)
        P.wait("vector", S.cT)
        P.op("vector", lambda e: e.tensor_copy(lnfm[:], ps_t[:, 0:1024]), S.cV)
        P.wait("tensor", S.cV)
        P.wait("sync", S.cV)
        P.op("sync", lambda e, tile=tile: e.dma_start(
            out=xres[:], in_=x_in[128 * tile:128 * (tile + 1), :]), S.ldh, 16)
        for dg in range(2):
            po = ps_sc[:, dg * 512:(dg + 1) * 512]
            for kc in range(8):
                P.op("tensor", lambda e, po=po, kc=kc, dg=dg: e.matmul(
                    po, lnfm[:, kc * 128:(kc + 1) * 128],
                    wbuf[:, kc * 1024 + dg * 512:kc * 1024 + (dg + 1) * 512],
                    start=(kc == 0), stop=False), None)
            P.op("tensor", lambda e, po=po, dg=dg: e.matmul(
                po, onesrow[:], bor[0:1, dg * 512:(dg + 1) * 512],
                start=False, stop=True), S.cT)
        P.wait("vector", S.cT)
        P.wait("vector", S.ldh)
        P.op("vector", lambda e: e.tensor_add(scr1[:], ps_sc[:, 0:1024], xres[:]), S.cV)
        emit_ln_tail(nc, P, S, B, scr1[:], x_out[128 * tile:128 * (tile + 1), :])


def emit_l1_ffn(nc, P, S, es, W, CN, x_in, x_out, tag="f1", precast=None):
    sb = lambda name, shape, dt: es.enter_context(nc.sbuf_tensor(name + tag, shape, dt))
    psa = lambda name, shape, dt: es.enter_context(nc.psum_tensor(name + tag, shape, dt))

    x_fm = sb("x_fm", [128, 16384], BF16)
    y_acc = sb("y_acc", [128, 16384], F32)
    w1t = sb("w1t", [128, 8192], BF16)
    w2t = sb("w2t", [128, 8192], BF16)
    w3p = [sb("w3a", [128, 8192], BF16), sb("w3b", [128, 8192], BF16)]
    s1_bf = sb("s1_bf", [128, 4096], BF16)
    g_fm = sb("g_fm", [128, 4096], BF16)
    wgt = sb("wgt", [128, 64], BF16)
    rt = sb("rt", [128, 128], F32)
    wsb = sb("wsb", [128, 128], F32)
    m1 = sb("m1", [128, 16], F32)
    m2 = sb("m2", [128, 16], F32)
    tmp8 = sb("tmp8", [128, 128], F32)
    tmp8b = sb("tmp8b", [128, 128], F32)
    p12 = sb("p12", [128, 96], F32)
    xres = sb("xres", [128, 1024], F32)
    stage2 = sb("stage2", [128, 1024], BF16)
    scr1 = sb("scr1", [128, 1024], F32)
    scr2 = sb("scr2", [128, 1024], F32)
    og_bf = sb("og_bf", [128, 1024], BF16)
    ogf_bf = sb("ogf_bf", [128, 1024], BF16)
    sq_bf = sb("sq_bf", [128, 1024], BF16)
    onescol = sb("onescol", [128, 1], BF16)
    murow = sb("murow", [1, 128], F32)
    rsrow = sb("rsrow", [1, 128], F32)
    varrow = sb("varrow", [1, 128], F32)
    col1 = sb("col1", [128, 1], F32)
    col2 = sb("col2", [128, 1], F32)
    id_bf = sb("id_bf", [128, 128], BF16)
    cvals = sb("cvals", [128, 8], F32)
    register_consts(nc, cvals)

    ps_h = psa("ps_h", [128, 2048], F32)
    ps_y = psa("ps_y", [128, 1024], F32)
    ps_st = psa("ps_st", [128, 512], F32)
    ps_t = psa("ps_t", [128, 1024], BF16)

    B = dict(og_bf=og_bf[:], ogf_bf=ogf_bf[:], sq_bf=sq_bf[:], ps_t=ps_t,
             ps_st=ps_st, id_bf=id_bf, onescol=onescol, murow=murow,
             rsrow=rsrow, varrow=varrow, col1=col1, col2=col2, cvals=cvals,
             scrO=scr2[:], xres=xres, stage2=stage2[:])

    if precast is None:
        w1bf = nc.dram_tensor("w1bf" + tag, [16384, 1024], BF16).ap()
        w2bf = nc.dram_tensor("w2bf" + tag, [16384, 1024], BF16).ap()
        w3bf = nc.dram_tensor("w3bf" + tag, [8192, 2048], BF16).ap()

        for e_ in range(8):
            P.op("gpsimd", lambda e, e_=e_: e.dma_start(
                out=w1bf[2048 * e_:2048 * (e_ + 1), :],
                in_=W["W1"][2048 * e_:2048 * (e_ + 1), :]), S.lds, 16)
            P.op("gpsimd", lambda e, e_=e_: e.dma_start(
                out=w2bf[2048 * e_:2048 * (e_ + 1), :],
                in_=W["W2"][2048 * e_:2048 * (e_ + 1), :]), S.lds, 16)
            P.op("gpsimd", lambda e, e_=e_: e.dma_start(
                out=w3bf[1024 * e_:1024 * (e_ + 1), :],
                in_=W["W3"][1024 * e_:1024 * (e_ + 1), :]), S.lds, 16)
    else:
        w1bf, w2bf, w3bf = precast
    for dst, srcap in [(id_bf[:], CN["ident"]), (onescol[:], CN["ones_col"]),
                       (wgt[:].rearrange("p (kc e) -> p kc e", e=8),
                        W["WgT"].rearrange("(kc p) e -> p kc e", p=128))]:
        P.op("gpsimd", lambda e, d=dst, s=srcap: e.dma_start(out=d, in_=s), S.lds, 16)
    P.op("sync", lambda e: e.dma_start(out=cvals[:], in_=CN["constvals"]), S.ldh, 16)
    P.wait("tensor", S.lds)
    P.wait("tensor", S.ldh)

    emit_xfm_build(nc, P, S, B, x_in, x_fm)

    P.wait("scalar", S.ldh)
    P.op("scalar", lambda e: e.activation(tmp8[:, 0:8], cvals[:, 0:8], AF.Exp), S.cS)
    P.op("scalar", lambda e: e.activation(tmp8[:, 0:1], cvals[:, 7:8], AF.Ln), S.cS)
    P.op("scalar", lambda e: e.activation(tmp8[:, 0:8], cvals[:, 0:8], AF.Silu), S.cS)
    P.op("scalar", lambda e: e.activation(tmp8[:, 0:8], cvals[:, 0:8], AF.Square), S.cS)

    P.wait("tensor", S.cV)
    for t in range(NT):
        P.wait("tensor", S.cV)
        for kc in range(8):
            P.op("tensor", lambda e, t=t, kc=kc: e.matmul(
                ps_y[:, 8 * t:8 * (t + 1)],
                x_fm[:, kc * 2048 + t * 128:kc * 2048 + (t + 1) * 128],
                wgt[:, kc * 8:(kc + 1) * 8],
                start=(kc == 0), stop=(kc == 7)), S.cT if kc == 7 else None)
    P.wait("vector", S.cT)
    P.op("vector", lambda e: e.tensor_copy(rt[:], ps_y[:, 0:128]), S.cV)
    r3 = rt[:].rearrange("p (t e) -> p t e", e=8)
    t3 = tmp8[:].rearrange("p (t e) -> p t e", e=8)
    u3 = tmp8b[:].rearrange("p (t e) -> p t e", e=8)
    w3 = wsb[:].rearrange("p (t e) -> p t e", e=8)
    m1v = m1[:].rearrange("p (t o) -> p t o", o=1)
    m2v = m2[:].rearrange("p (t o) -> p t o", o=1)
    jk = lambda: P.op("vector", lambda e: e.tensor_copy(scr1[:, 0:128], rt[:]), S.cV)
    jk()
    P.op("vector", lambda e: e.tensor_tensor(t3[:, :, 0:4], r3[:, :, 0:4], r3[:, :, 4:8], ALU.max), S.cV)
    jk()
    P.op("vector", lambda e: e.tensor_tensor(t3[:, :, 0:2], t3[:, :, 0:2], t3[:, :, 2:4], ALU.max), S.cV)
    jk()
    P.op("vector", lambda e: e.tensor_tensor(m1v, t3[:, :, 0:1], t3[:, :, 1:2], ALU.max), S.cV)
    jk()
    P.op("vector", lambda e: e.tensor_tensor(u3, r3, m1v.broadcast_to([128, 16, 8]), ALU.is_ge), S.cV)
    jk()
    P.op("vector", lambda e: e.scalar_tensor_tensor(t3, u3, -1e30, r3, ALU.mult, ALU.add), S.cV)
    jk()
    P.op("vector", lambda e: e.tensor_tensor(t3[:, :, 0:4], t3[:, :, 0:4], t3[:, :, 4:8], ALU.max), S.cV)
    jk()
    P.op("vector", lambda e: e.tensor_tensor(t3[:, :, 0:2], t3[:, :, 0:2], t3[:, :, 2:4], ALU.max), S.cV)
    jk()
    P.op("vector", lambda e: e.tensor_tensor(m2v, t3[:, :, 0:1], t3[:, :, 1:2], ALU.max), S.cV)
    jk()
    P.op("vector", lambda e: e.tensor_sub(p12[:, 0:16], m2[:], m1[:]), S.cV)
    jk()
    jk()
    P.wait("scalar", S.cV)
    P.op("scalar", lambda e: e.activation(p12[:, 16:32], p12[:, 0:16], AF.Exp), S.cS)
    P.op("scalar", lambda e: e.activation(p12[:, 32:48], p12[:, 16:32], AF.Ln, bias=1.0), S.cS)
    P.op("scalar", lambda e: e.activation(p12[:, 48:64], p12[:, 32:48], AF.Exp, scale=-1.0), S.cS)
    P.op("scalar", lambda e: e.copy(tmp8[:, 112:128], p12[:, 48:64]), S.cS)

    def router_part2():
        P.wait("vector", S.cS)
        P.op("vector", lambda e: e.tensor_tensor(t3, r3, m2v.broadcast_to([128, 16, 8]), ALU.is_ge), S.cV)
        P.op("vector", lambda e: e.tensor_mul(
            w3, u3, p12[:, 48:64].rearrange("p (t o) -> p t o", o=1).broadcast_to([128, 16, 8])), S.cV)
        P.op("vector", lambda e: e.tensor_mul(p12[:, 64:80], p12[:, 16:32], p12[:, 48:64]), S.cV)
        P.op("vector", lambda e: e.tensor_scalar(tmp8b[:], tmp8b[:], -1.0, 1.0, ALU.mult, ALU.add), S.cV)
        jk()
        P.op("vector", lambda e: e.tensor_mul(t3, t3, u3), S.cV)
        jk()
        P.op("vector", lambda e: e.tensor_mul(
            t3, t3, p12[:, 64:80].rearrange("p (t o) -> p t o", o=1).broadcast_to([128, 16, 8])), S.cV)
        jk()
        P.op("vector", lambda e: e.tensor_add(w3, w3, t3), S.cV)
        jk()
        jk()

    P.wait("gpsimd", S.cT)
    first = True
    if precast is None:
        P.wait("sync", S.lds)
    else:
        P.wait("sync", S.bgc)

    def issue_w12(e_, hf):
        for dc in range(8):
            P.op("sync", lambda e, e_=e_, hf=hf, dc=dc: e.dma_start(
                out=w1t[:, 1024 * dc:1024 * (dc + 1)],
                in_=w1bf[2048 * e_ + 1024 * hf:2048 * e_ + 1024 * (hf + 1),
                         128 * dc:128 * (dc + 1)], transpose=True), S.ldh, 16)
            P.op("sync", lambda e, e_=e_, hf=hf, dc=dc: e.dma_start(
                out=w2t[:, 1024 * dc:1024 * (dc + 1)],
                in_=w2bf[2048 * e_ + 1024 * hf:2048 * e_ + 1024 * (hf + 1),
                         128 * dc:128 * (dc + 1)], transpose=True), S.ldh, 16)

    def issue_w3(e_, hf, buf):
        for dc in range(8):
            P.op("sync", lambda e, e_=e_, hf=hf, dc=dc, buf=buf: e.dma_start(
                out=buf[:, 1024 * dc:1024 * (dc + 1)],
                in_=w3bf[1024 * e_:1024 * (e_ + 1),
                         1024 * hf + 128 * dc:1024 * hf + 128 * (dc + 1)],
                transpose=True), S.ldh, 16)

    issue_w12(0, 0)
    issue_w3(0, 0, w3p[0])
    for it in range(16):
        e_, hf = divmod(it, 2)
        w3t = w3p[it % 2]
        P.wait("tensor", S.ldh)
        if it + 1 < 16:
            en, hn = divmod(it + 1, 2)
            P.wait("sync", S.cT)
            issue_w3(en, hn, w3p[(it + 1) % 2])
        for tg in range(4):
            for hc in range(8):
                slot = hc % 2
                po1 = ps_h[:, 1024 * slot:1024 * slot + 512]
                po2 = ps_h[:, 1024 * slot + 512:1024 * slot + 1024]
                P.wait("tensor", S.cV, max(S.cV.n - 1, 0))
                for kc in range(8):
                    P.op("tensor", lambda e, po1=po1, kc=kc, hc=hc, tg=tg: e.matmul(
                        po1, w1t[:, kc * 1024 + hc * 128:kc * 1024 + (hc + 1) * 128],
                        x_fm[:, kc * 2048 + tg * 512:kc * 2048 + (tg + 1) * 512],
                        start=(kc == 0), stop=(kc == 7)), None)
                for kc in range(8):
                    P.op("tensor", lambda e, po2=po2, kc=kc, hc=hc, tg=tg: e.matmul(
                        po2, w2t[:, kc * 1024 + hc * 128:kc * 1024 + (hc + 1) * 128],
                        x_fm[:, kc * 2048 + tg * 512:kc * 2048 + (tg + 1) * 512],
                        start=(kc == 0), stop=(kc == 7)),
                        S.cT if kc == 7 else None)
                P.wait("scalar", S.cT)
                P.op("scalar", lambda e, po1=po1, hc=hc: e.activation(
                    s1_bf[:, hc * 512:(hc + 1) * 512], po1, AF.Silu), S.cS)
                P.wait("vector", S.cS)
                P.op("vector", lambda e, po2=po2, hc=hc: e.tensor_mul(
                    g_fm[:, hc * 512:(hc + 1) * 512],
                    s1_bf[:, hc * 512:(hc + 1) * 512], po2), S.cV)
            if e_ == 0 and hf == 0 and tg == 0:
                router_part2()
            if tg == 3 and it + 1 < 16:
                en, hn = divmod(it + 1, 2)
                P.wait("sync", S.cT)
                issue_w12(en, hn)
            P.wait("tensor", S.cV)
            for tt in range(4):
                tile = tg * 4 + tt
                for dg in range(2):
                    po = ps_y[:, dg * 512:(dg + 1) * 512]
                    P.wait("tensor", S.cV, max(S.cV.n - 1, 0))
                    for hc in range(8):
                        P.op("tensor", lambda e, po=po, hc=hc, tt=tt, dg=dg,
                             w3t=w3t: e.matmul(
                                 po, g_fm[:, hc * 512 + tt * 128:hc * 512 + tt * 128 + 128],
                                 w3t[:, hc * 1024 + dg * 512:hc * 1024 + (dg + 1) * 512],
                                 start=(hc == 0), stop=(hc == 7)), S.cT if hc == 7 else None)
                    P.wait("vector", S.cT)
                    ya = y_acc[:, tile * 1024 + dg * 512:tile * 1024 + (dg + 1) * 512]
                    wcol = wsb[:, tile * 8 + e_:tile * 8 + e_ + 1]
                    if first:
                        P.op("vector", lambda e, po=po, ya=ya, wcol=wcol:
                             e.tensor_single_scalar(ya, po, wcol, ALU.mult), S.cV)
                    else:
                        P.op("vector", lambda e, po=po, ya=ya, wcol=wcol:
                             e.scalar_tensor_tensor(ya, po, wcol, ya,
                                                    ALU.mult, ALU.add), S.cV)
        first = False

    if DBG_L1F:
        dwv = nc.dram_tensor("dwv" + tag, [128, 128], F32, kind="ExternalOutput").ap()
        dya = nc.dram_tensor("dya" + tag, [128, 16384], F32, kind="ExternalOutput").ap()
        drt = nc.dram_tensor("drt" + tag, [128, 128], F32, kind="ExternalOutput").ap()
        dm1 = nc.dram_tensor("dm1" + tag, [128, 16], F32, kind="ExternalOutput").ap()
        dm2 = nc.dram_tensor("dm2" + tag, [128, 16], F32, kind="ExternalOutput").ap()
        dp12 = nc.dram_tensor("dp12" + tag, [128, 80], F32, kind="ExternalOutput").ap()
        du3 = nc.dram_tensor("du3" + tag, [128, 128], F32, kind="ExternalOutput").ap()
        P.wait("gpsimd", S.cV)
        P.op("gpsimd", lambda e: e.dma_start(out=dwv, in_=wsb[:]), S.lds, 16)
        P.op("gpsimd", lambda e: e.dma_start(out=dya, in_=y_acc[:]), S.lds, 16)
        P.op("gpsimd", lambda e: e.dma_start(out=drt, in_=rt[:]), S.lds, 16)
        P.op("gpsimd", lambda e: e.dma_start(out=dm1, in_=m1[:]), S.lds, 16)
        P.op("gpsimd", lambda e: e.dma_start(out=dm2, in_=m2[:]), S.lds, 16)
        P.op("gpsimd", lambda e: e.dma_start(out=dp12, in_=p12[:, 0:80]), S.lds, 16)
        P.op("gpsimd", lambda e: e.dma_start(out=du3, in_=tmp8b[:]), S.lds, 16)
        P.wait("vector", S.lds)

    for tile in range(NT):
        P.wait("sync", S.cV)
        P.op("sync", lambda e, tile=tile: e.dma_start(
            out=xres[:], in_=x_in[128 * tile:128 * (tile + 1), :]), S.ldh, 16)
        P.wait("vector", S.ldh)
        P.wait("vector", S.sth)
        P.op("vector", lambda e, tile=tile: e.tensor_add(
            scr1[:], y_acc[:, tile * 1024:(tile + 1) * 1024], xres[:]), S.cV)
        emit_ln_tail(nc, P, S, B, scr1[:], x_out[128 * tile:128 * (tile + 1), :])



def declare_weights_l0a(nc):
    W = {}
    for n, shp in [("Wq", [1024, 1024]), ("Wk", [1024, 1024]), ("Wv", [1024, 1024]),
                   ("Wg", [1024, 1024]), ("Wo", [1024, 1024]), ("Wg1", [1024, 16]),
                   ("Wg2", [16, 1024]), ("bg2", [1, 1024]), ("gnw_tiled", [128, 1024])]:
        W[n] = nc.dram_tensor("l0a_" + n, shp, F32, kind="ExternalInput").ap()
    return W


def declare_consts(nc, names=("ident", "t1u", "mask_incl", "ones_col", "ones_row",
                              "ones_row2048", "diag_mask", "constvals")):
    shapes = {"ident": [128, 128], "t1u": [128, 128], "mask_incl": [128, 128],
              "ones_col": [128, 1], "ones_row": [1, 128],
              "ones_row2048": [1, 2048], "diag_mask": [128, 2048],
              "constvals": [128, 8]}
    return {n: nc.dram_tensor("cn_" + n, shapes[n], F32, kind="ExternalInput").ap()
            for n in names}


def l1a_core_indices(core):
    s, g, octs = core_layout(core)
    p64 = np.arange(64, dtype=np.int32)[:, None]
    p128 = np.arange(128, dtype=np.int32)[:, None]
    kidx = np.concatenate([(1536 * r + 1024 + 64 * g + p64) * 2048 + 1024 * half
                           for r in range(4) for half in range(2)], axis=1)
    qidx = np.concatenate([(1536 * r + 64 * (4 * gp + g) + p64) * 2048 + 1024 * half
                           for gp in range(4) for r in range(4) for half in range(2)],
                          axis=1)
    vcols = []
    for kt in range(64):
        oct_ = kt // 8
        rank, off = (oct_, 0) if oct_ < 4 else (7 - oct_, 1024)
        loc = off + (kt % 8) * 128 + p128
        vcols.append((1536 * rank + 1280) * 2048 + loc * 256 + 64 * g)
    vidx = np.concatenate(vcols, axis=1).astype(np.int32)
    ocols = []
    for r in range(4):
        for tile in range(16):
            ocols.append((8192 * r + 2048 * g + tile * 128 + p128) * 256)
    oidx = np.concatenate(ocols, axis=1).astype(np.int32)
    acols = []
    for tile in range(16):
        grow0 = 1024 * octs[tile // 8] + 128 * (tile % 8)
        for r in range(4):
            acols.append((8192 * r + grow0 + p128) * 256)
    aidx = np.concatenate(acols, axis=1).astype(np.int32)
    return dict(kidx=kidx, qidx=qidx, vidx=vidx, oidx=oidx, aidx=aidx)


def stage_barrier(P, S):
    for eng in ("sync", "gpsimd"):
        for sem in (S.cT, S.cV, S.cS, S.sth, S.lds, S.ldh):
            P.wait(eng, sem)
    P.wait("tensor", S.cV)
    P.wait("tensor", S.cS)
    P.wait("vector", S.cT)
    P.wait("vector", S.cS)
    P.wait("scalar", S.cT)
    P.wait("scalar", S.cV)


def build_fused_launch():
    nc = bass.Bass("TRN2", target_bir_lowering=False, debug=False,
                   detect_race_conditions=False)
    x_in = nc.dram_tensor("x_in", [TOK, DM], F32, kind="ExternalInput").ap()
    x_out = nc.dram_tensor("x_out", [TOK, DM], F32, kind="ExternalOutput").ap()
    kd = "ExternalOutput" if DBG_FUSED else "Internal"
    x1d = nc.dram_tensor("x1d", [TOK, DM], F32, kind=kd).ap()
    x2d = nc.dram_tensor("x2d", [TOK, DM], F32, kind=kd).ap()
    x3d = nc.dram_tensor("x3d", [TOK, DM], F32, kind=kd).ap()
    W0a = declare_weights_l0a(nc)
    CN = declare_consts(nc)
    CI0 = {"theta": nc.dram_tensor("theta", [64, 128], F32, kind="ExternalInput").ap(),
           "cbias": nc.dram_tensor("cbias", [64, 16], F32, kind="ExternalInput").ap()}
    W0f = {n: nc.dram_tensor("f0_" + n, shp, F32, kind="ExternalInput").ap()
           for n, shp in [("router_w", [1024, 8]), ("w1", [1024, 4096]),
                          ("b1", [1, 4096]), ("w2", [4096, 1024]),
                          ("b2_t", [128, 1024])]}
    W1a = {n: nc.dram_tensor("a1_" + n, shp, F32, kind="ExternalInput").ap()
           for n, shp in [("Wq", [1024, 1024]), ("Wk", [1024, 256]),
                          ("Wv", [1024, 256]), ("Wo", [1024, 1024]),
                          ("bq", [1, 1024]), ("bk", [1, 256]), ("bv", [1, 256]),
                          ("bo", [1, 1024]), ("lng_t", [128, 1024]),
                          ("lnb_t", [128, 1024])]}
    CI1 = {n: nc.dram_tensor("a1_" + n, shp, mybir.dt.int32, kind="ExternalInput").ap()
           for n, shp in [("kidx", [64, 8]), ("qidx", [64, 32]),
                          ("vidx", [128, 64]), ("oidx", [128, 64]),
                          ("aidx", [128, 64])]}
    W1f = {n: nc.dram_tensor("f1_" + n, shp, F32, kind="ExternalInput").ap()
           for n, shp in [("WgT", [1024, 8]), ("W1", [16384, 1024]),
                          ("W2", [16384, 1024]), ("W3", [8192, 2048])]}
    with contextlib.ExitStack() as es:
        S = Sems(nc, es)
        P = Plan()
        cvg = es.enter_context(nc.sbuf_tensor("cvals_glob", [128, 8], F32))
        register_consts(nc, cvg)
        P.op("sync", lambda e: e.dma_start(out=cvg[:], in_=CN["constvals"]), S.ldh, 16)
        with contextlib.ExitStack() as es1:
            emit_l0_attn(nc, P, S, es1, W0a, CN, CI0, x_in, x1d)
        stage_barrier(P, S)
        with contextlib.ExitStack() as es2:
            emit_l0_ffn(nc, P, S, es2, W0f, CN, x1d, x2d)
        stage_barrier(P, S)
        w1bf = nc.dram_tensor("w1bff1", [16384, 1024], BF16).ap()
        w2bf = nc.dram_tensor("w2bff1", [16384, 1024], BF16).ap()
        w3bf = nc.dram_tensor("w3bff1", [8192, 2048], BF16).ap()

        def bg_casts(gp):
            for e_ in (2 * gp, 2 * gp + 1):
                P.op("gpsimd", lambda e, e_=e_: e.dma_start(
                    out=w1bf[2048 * e_:2048 * (e_ + 1), :],
                    in_=W1f["W1"][2048 * e_:2048 * (e_ + 1), :]), S.bgc, 16)
                P.op("gpsimd", lambda e, e_=e_: e.dma_start(
                    out=w2bf[2048 * e_:2048 * (e_ + 1), :],
                    in_=W1f["W2"][2048 * e_:2048 * (e_ + 1), :]), S.bgc, 16)
                P.op("gpsimd", lambda e, e_=e_: e.dma_start(
                    out=w3bf[1024 * e_:1024 * (e_ + 1), :],
                    in_=W1f["W3"][1024 * e_:1024 * (e_ + 1), :]), S.bgc, 16)

        with contextlib.ExitStack() as es3:
            o_g = emit_l1_attn(nc, P, S, es3, W1a, CN, CI1, x2d, None, fused=True,
                               bg_dma=bg_casts)
        stage_barrier(P, S)
        with contextlib.ExitStack() as es4:
            emit_l1_attn_fin(nc, P, S, es4, W1a, CN, None, x2d, x3d,
                             o_gather=(o_g, CI1["aidx"]))
        stage_barrier(P, S)
        with contextlib.ExitStack() as es5:
            emit_l1_ffn(nc, P, S, es5, W1f, CN, x3d, x_out,
                        precast=(w1bf, w2bf, w3bf))
        P.wait("gpsimd", S.sth)
        P.wait("gpsimd", S.lds)
        P.wait("gpsimd", S.ldh)
        P.wait("sync", S.sth)
        P.wait("sync", S.lds)
        P.emit(nc)
    return nc


def build_l1_attn_launch(upto=99):
    nc = bass.Bass("TRN2", target_bir_lowering=False, debug=False,
                   detect_race_conditions=False)
    x_in = nc.dram_tensor("x_in", [TOK, DM], F32, kind="ExternalInput").ap()
    x_out = None
    W = {}
    for n, shp in [("Wq", [1024, 1024]), ("Wk", [1024, 256]), ("Wv", [1024, 256]),
                   ("Wo", [1024, 1024]), ("bq", [1, 1024]), ("bk", [1, 256]),
                   ("bv", [1, 256]), ("bo", [1, 1024]),
                   ("lng_t", [128, 1024]), ("lnb_t", [128, 1024])]:
        W[n] = nc.dram_tensor("a1_" + n, shp, F32, kind="ExternalInput").ap()
    CN = declare_consts(nc)
    CI = {n: nc.dram_tensor("a1_" + n, shp, mybir.dt.int32, kind="ExternalInput").ap()
          for n, shp in [("kidx", [64, 8]), ("qidx", [64, 32]),
                         ("vidx", [128, 64]), ("oidx", [128, 64])]}
    with contextlib.ExitStack() as es:
        S = Sems(nc, es)
        P = Plan()
        emit_l1_attn(nc, P, S, es, W, CN, CI, x_in, x_out, upto=upto)
        P.wait("gpsimd", S.sth)
        P.wait("gpsimd", S.lds)
        P.wait("sync", S.sth)
        P.emit(nc)
    return nc


def build_l1_attn_fin_launch():
    nc = bass.Bass("TRN2", target_bir_lowering=False, debug=False,
                   detect_race_conditions=False)
    attn_in = nc.dram_tensor("attn_in", [TOK, DM], BF16, kind="ExternalInput").ap()
    x_in = nc.dram_tensor("x_in", [TOK, DM], F32, kind="ExternalInput").ap()
    x_out = nc.dram_tensor("x_out", [TOK, DM], F32, kind="ExternalOutput").ap()
    W = {}
    for n, shp in [("Wo", [1024, 1024]), ("bo", [1, 1024]),
                   ("lng_t", [128, 1024]), ("lnb_t", [128, 1024])]:
        W[n] = nc.dram_tensor("a1_" + n, shp, F32, kind="ExternalInput").ap()
    CN = declare_consts(nc)
    with contextlib.ExitStack() as es:
        S = Sems(nc, es)
        P = Plan()
        emit_l1_attn_fin(nc, P, S, es, W, CN, attn_in, x_in, x_out)
        P.wait("gpsimd", S.sth)
        P.wait("gpsimd", S.lds)
        P.wait("sync", S.sth)
        P.emit(nc)
    return nc


def assemble_attn_for_core(o_results, core):
    s, g, octs = core_layout(core)
    import ml_dtypes
    attn = np.empty((TOK, DM), dtype=o_results[0].dtype)
    rows = np.concatenate([np.arange(1024 * j, 1024 * (j + 1)) for j in octs])
    for r in range(4):
        src = o_results[4 * s + r][rows]
        for gp in range(4):
            attn[:, 64 * (4 * gp + r):64 * (4 * gp + r + 1)] = src[:, 64 * gp:64 * (gp + 1)]
    return np.ascontiguousarray(attn)


def l1a_weight_arrays(inputs):
    i = {k: np.ascontiguousarray(np.asarray(v), dtype=np.float32)
         for k, v in inputs.items()}
    return {
        "a1_Wq": i["L1_attn_Wq"], "a1_Wk": i["L1_attn_Wk"], "a1_Wv": i["L1_attn_Wv"],
        "a1_Wo": i["L1_attn_Wo"],
        "a1_bq": i["L1_attn_bq"].reshape(1, 1024),
        "a1_bk": i["L1_attn_bk"].reshape(1, 256),
        "a1_bv": i["L1_attn_bv"].reshape(1, 256),
        "a1_bo": i["L1_attn_bo"].reshape(1, 1024),
        "a1_lng_t": np.ascontiguousarray(np.tile(i["L1_attn_ln_g"].reshape(1, 1024), (128, 1))),
        "a1_lnb_t": np.ascontiguousarray(np.tile(i["L1_attn_ln_b"].reshape(1, 1024), (128, 1))),
    }


def build_l1_ffn_launch():
    nc = bass.Bass("TRN2", target_bir_lowering=False, debug=False,
                   detect_race_conditions=False)
    x_in = nc.dram_tensor("x_in", [TOK, DM], F32, kind="ExternalInput").ap()
    x_out = nc.dram_tensor("x_out", [TOK, DM], F32, kind="ExternalOutput").ap()
    W = {}
    for n, shp in [("WgT", [1024, 8]), ("W1", [16384, 1024]),
                   ("W2", [16384, 1024]), ("W3", [8192, 2048])]:
        W[n] = nc.dram_tensor("f1_" + n, shp, F32, kind="ExternalInput").ap()
    CN = declare_consts(nc)
    with contextlib.ExitStack() as es:
        S = Sems(nc, es)
        P = Plan()
        emit_l1_ffn(nc, P, S, es, W, CN, x_in, x_out)
        P.wait("gpsimd", S.sth)
        P.wait("gpsimd", S.lds)
        P.wait("gpsimd", S.ldh)
        P.wait("sync", S.sth)
        P.emit(nc)
    return nc


def l1f_weight_arrays(inputs):
    i = {k: np.ascontiguousarray(np.asarray(v), dtype=np.float32)
         for k, v in inputs.items()}
    return {
        "f1_WgT": np.ascontiguousarray(i["L1_ffn_Wg"].T),
        "f1_W1": i["L1_ffn_W1"].reshape(16384, 1024),
        "f1_W2": i["L1_ffn_W2"].reshape(16384, 1024),
        "f1_W3": i["L1_ffn_W3"].reshape(8192, 2048),
    }


def build_l0_ffn_launch():
    nc = bass.Bass("TRN2", target_bir_lowering=False, debug=False,
                   detect_race_conditions=False)
    x_in = nc.dram_tensor("x_in", [TOK, DM], F32, kind="ExternalInput").ap()
    x_out = nc.dram_tensor("x_out", [TOK, DM], F32, kind="ExternalOutput").ap()
    W = {}
    for n, shp in [("router_w", [1024, 8]), ("w1", [1024, 4096]), ("b1", [1, 4096]),
                   ("w2", [4096, 1024]), ("b2_t", [128, 1024])]:
        W[n] = nc.dram_tensor("f0_" + n, shp, F32, kind="ExternalInput").ap()
    CN = declare_consts(nc)
    with contextlib.ExitStack() as es:
        S = Sems(nc, es)
        P = Plan()
        emit_l0_ffn(nc, P, S, es, W, CN, x_in, x_out)
        P.wait("gpsimd", S.sth)
        P.wait("gpsimd", S.lds)
        P.wait("sync", S.sth)
        P.emit(nc)
    return nc


def l0f_weight_arrays(inputs):
    i = {k: np.ascontiguousarray(np.asarray(v), dtype=np.float32)
         for k, v in inputs.items()}
    return {
        "f0_router_w": i["L0_ffn_router_w"], "f0_w1": i["L0_ffn_w1"],
        "f0_b1": i["L0_ffn_b1"].reshape(1, 4096), "f0_w2": i["L0_ffn_w2"],
        "f0_b2_t": np.ascontiguousarray(np.tile(i["L0_ffn_b2"].reshape(1, 1024), (128, 1))),
    }


def build_l0_attn_launch(upto=99, p2cut=99, dbg=False):
    nc = bass.Bass("TRN2", target_bir_lowering=False, debug=False,
                   detect_race_conditions=False)
    x_in = nc.dram_tensor("x_in", [TOK, DM], F32, kind="ExternalInput").ap()
    x_out = nc.dram_tensor("x_out", [TOK, DM], F32, kind="ExternalOutput").ap()
    W = declare_weights_l0a(nc)
    CN = declare_consts(nc)
    CI = {"theta": nc.dram_tensor("theta", [64, 128], F32, kind="ExternalInput").ap(),
          "cbias": nc.dram_tensor("cbias", [64, 16], F32, kind="ExternalInput").ap()}
    with contextlib.ExitStack() as es:
        S = Sems(nc, es)
        P = Plan()
        emit_l0_attn(nc, P, S, es, W, CN, CI, x_in, x_out, upto=upto, p2cut=p2cut, dbg=dbg)
        if upto < 99 or p2cut < 99:
            P.wait("sync", S.cV)
            P.wait("sync", S.cT)
            P.wait("sync", S.cS)
            P.op("sync", lambda e: e.dma_start(out=x_out[0:128, :], in_=x_in[0:128, :]),
                 S.sth, 16)
        P.wait("gpsimd", S.sth)
        P.wait("gpsimd", S.lds)
        P.wait("sync", S.sth)
        P.wait("sync", S.lds)
        P.emit(nc)
    return nc


def l0a_weight_arrays(inputs):
    i = {k: np.ascontiguousarray(np.asarray(v), dtype=np.float32)
         for k, v in inputs.items()}
    return {
        "l0a_Wq": i["L0_attn_Wq"], "l0a_Wk": i["L0_attn_Wk"], "l0a_Wv": i["L0_attn_Wv"],
        "l0a_Wg": i["L0_attn_Wg"], "l0a_Wo": i["L0_attn_Wo"], "l0a_Wg1": i["L0_attn_Wg1"],
        "l0a_Wg2": i["L0_attn_Wg2"], "l0a_bg2": i["L0_attn_bg2"].reshape(1, 1024),
        "l0a_gnw_tiled": np.ascontiguousarray(
            np.tile(np.tile(i["L0_attn_g_norm_w"], 16).reshape(1, 1024), (128, 1))),
    }


def run_l0_attn(inputs, x_full=None):
    nc = build_l0_attn_launch()
    cn = host_constants()
    wa = l0a_weight_arrays(inputs)
    in_maps = []
    base = dict(wa)
    for n, v in cn.items():
        base["cn_" + n] = v
    src = dict(inputs)
    if x_full is not None:
        src["L0_attn_x"] = x_full
    for core in range(NCORES):
        pc = per_core_arrays(src, core)
        m = dict(base)
        m["x_in"] = pc["x_in"]
        m["theta"] = pc["theta"]
        m["cbias"] = pc["cbias"]
        in_maps.append(m)
    res = run_bass_kernel_spmd(nc, in_maps, core_ids=list(range(NCORES)))
    return assemble_output([r["x_out"] for r in res.results])


FUSED = True


def kernel(**inputs):
    if not FUSED:
        return kernel_unfused(**inputs)
    cn = host_constants()
    nc = build_fused_launch()
    base = {}
    base.update(l0a_weight_arrays(inputs))
    base.update(l0f_weight_arrays(inputs))
    base.update(l1a_weight_arrays(inputs))
    base.update(l1f_weight_arrays(inputs))
    for n, v in cn.items():
        base["cn_" + n] = v
    maps = []
    for core in range(NCORES):
        pc = per_core_arrays(inputs, core)
        m = dict(base)
        m["x_in"] = pc["x_in"]
        m["theta"] = pc["theta"]
        m["cbias"] = pc["cbias"]
        for k, v in l1a_core_indices(core).items():
            m["a1_" + k] = v
        maps.append(m)
    res = run_bass_kernel_spmd(nc, maps, core_ids=list(range(NCORES)))
    return assemble_output([r["x_out"] for r in res.results])


def kernel_unfused(**inputs):
    cn = host_constants()
    base_cn = {"cn_" + n: v for n, v in cn.items()}

    def shard_x(x_full):
        return [per_core_arrays({"L0_attn_x": x_full}, core)["x_in"]
                for core in range(NCORES)]

    def run(nc, per_core_maps):
        res = run_bass_kernel_spmd(nc, per_core_maps, core_ids=list(range(NCORES)))
        return res.results

    nc = build_l0_attn_launch()
    wa = l0a_weight_arrays(inputs)
    maps = []
    for core in range(NCORES):
        pc = per_core_arrays(inputs, core)
        m = dict(wa); m.update(base_cn)
        m["x_in"] = pc["x_in"]; m["theta"] = pc["theta"]; m["cbias"] = pc["cbias"]
        maps.append(m)
    r1 = run(nc, maps)
    x1s = [r["x_out"] for r in r1]

    nc = build_l0_ffn_launch()
    wa = l0f_weight_arrays(inputs)
    maps = []
    for core in range(NCORES):
        m = dict(wa); m.update(base_cn)
        m["x_in"] = x1s[core]
        maps.append(m)
    r2 = run(nc, maps)
    x2s = [r["x_out"] for r in r2]

    nc = build_l1_attn_launch()
    wa = l1a_weight_arrays(inputs)
    maps = []
    for core in range(NCORES):
        m = {k: v for k, v in wa.items() if k in
             ("a1_Wq", "a1_Wk", "a1_Wv", "a1_bq", "a1_bk", "a1_bv", "a1_bo",
              "a1_Wo", "a1_lng_t", "a1_lnb_t")}
        m.update(base_cn)
        m["x_in"] = x2s[core]
        m.update({"a1_" + k: v for k, v in l1a_core_indices(core).items()})
        maps.append(m)
    nc_inputs = {a.memorylocations[0].name for a in nc.m.functions[0].allocations
                 if getattr(a, "kind", None) == "ExternalInput"}
    maps = [{k: v for k, v in m.items() if k in nc_inputs} for m in maps]
    r3 = run(nc, maps)
    o_res = [r["o_outa1"] for r in r3]

    nc = build_l1_attn_fin_launch()
    maps = []
    for core in range(NCORES):
        m = {k: wa[k] for k in ("a1_Wo", "a1_bo", "a1_lng_t", "a1_lnb_t")}
        m.update(base_cn)
        m["x_in"] = x2s[core]
        m["attn_in"] = assemble_attn_for_core(o_res, core)
        maps.append(m)
    r4 = run(nc, maps)
    x3s = [r["x_out"] for r in r4]

    nc = build_l1_ffn_launch()
    wa = l1f_weight_arrays(inputs)
    maps = []
    for core in range(NCORES):
        m = dict(wa); m.update(base_cn)
        m["x_in"] = x3s[core]
        maps.append(m)
    r5 = run(nc, maps)
    return assemble_output([r["x_out"] for r in r5])


# version 08
# speedup vs baseline: 1.1942x; 1.0094x over previous
import numpy as np
import contextlib
import concourse.bass as bass
import concourse.mybir as mybir
from concourse.bass_utils import run_bass_kernel_spmd

F32 = mybir.dt.float32
BF16 = mybir.dt.bfloat16
AF = mybir.ActivationFunctionType
ALU = mybir.AluOpType
AX = mybir.AxisListType

NCORES = 8
TOK = 2048
NT = TOK // 128
DM = 1024
C = 128
EPS = 1e-5
CONSTVALS = (1e-5, 1.0 / 64, -0.5, 1.0 / 1024, 0.125, -1.0, 1.0 / 32, 1.0)


def register_consts(nc, cvals):
    if getattr(nc, "_cv_registered", False):
        return
    nc._cv_registered = True
    for i, v in enumerate(CONSTVALS):
        nc.const_aps.register(F32, float(v), cvals[:, i:i + 1])

ENGS = ("sync", "scalar", "vector", "tensor", "gpsimd")
DBG_L0F = False
N_CC = 3
DBG_L1F = False
DBG_FUSED = False


class Sem:
    __slots__ = ("h", "n", "max_wait", "fenced")

    def __init__(self, h):
        self.h = h
        self.n = 0
        self.max_wait = 0
        self.fenced = 0


class Plan:

    def __init__(self):
        self.q = {e: [] for e in ENGS}
        self.waited = {}

    def op(self, eng, f, sem=None, k=1):
        if sem is not None and k == 16:
            if sem.max_wait > sem.fenced:
                self.q[eng].append(("wait", sem, sem.max_wait))
                sem.fenced = sem.max_wait
            sem.n += 16
            self.q[eng].append(("dma", f, sem, sem.n))
        elif sem is not None:
            sem.n += k
            self.q[eng].append(("op", f, sem, sem.n))
        else:
            self.q[eng].append(("op", f, None, 0))

    def wait(self, eng, sem, n=None):
        n = sem.n if n is None else n
        if n <= 0:
            return
        self.waited.setdefault(id(sem), set()).add(n)
        if n > sem.max_wait:
            sem.max_wait = n
        self.q[eng].append(("wait", sem, n))

    def emit(self, nc):
        emitted = {}

        def runner(items):
            def run(e):
                for it in items:
                    if it[0] == "wait":
                        _, sem, n = it
                        e.wait_ge(sem.h, n)
                    elif it[0] == "dma":
                        _, f, sem, cnt = it
                        f(e).then_inc(sem.h, 16)
                    else:
                        _, f, sem, cnt = it
                        ins = f(e)
                        if sem is not None and cnt in self.waited.get(id(sem), ()):
                            last = emitted.get(id(sem), 0)
                            ins.then_inc(sem.h, cnt - last)
                            emitted[id(sem)] = cnt
            return run

        with nc.Block() as block:
            block.sync(runner(self.q["sync"]))
            block.scalar(runner(self.q["scalar"]))
            block.vector(runner(self.q["vector"]))
            block.tensor(runner(self.q["tensor"]))
            block.gpsimd(runner(self.q["gpsimd"]))


class Sems:
    def __init__(self, nc, es):
        mk = lambda name: Sem(es.enter_context(nc.semaphore(name)))
        self.ldh = mk("ldh")
        self.lds = mk("lds")
        self.sth = mk("sth")
        self.cT = mk("cT")
        self.cV = mk("cV")
        self.cS = mk("cS")
        self.cG = mk("cG")
        self.bgc = mk("bgc")



def host_constants():
    c = {}
    c["ident"] = np.eye(128, dtype=np.float32)
    c["t1u"] = (-1.0 / 16.0) * np.triu(np.ones((128, 128), np.float32))
    c["mask_incl"] = np.triu(np.ones((128, 128), np.float32))
    c["ones_col"] = np.ones((128, 1), np.float32)
    c["ones_row"] = np.ones((1, 128), np.float32)
    c["ones_row2048"] = np.ones((1, 2048), np.float32)
    from kernel import CONSTVALS as _cv
    c["constvals"] = np.tile(np.array(list(_cv) + [0.0] * (8 - len(_cv)),
                                      np.float32), (128, 1))
    dm = np.zeros((128, 2048), np.float32)
    for j in range(4):
        for i in range(128):
            dm[i, j * 512:(j + 1) * 512][np.arange(512) >= j * 128 + i] = 1.0
    c["diag_mask"] = dm
    return c


def core_layout(core):
    s, g = core // 4, core % 4
    return s, g, [g, 7 - g]


def per_core_arrays(inputs, core):
    s, g, octs = core_layout(core)
    x = np.asarray(inputs["L0_attn_x"])
    xs = np.concatenate([x[s, 1024 * j:1024 * (j + 1)] for j in octs], axis=0)
    d = {"x_in": np.ascontiguousarray(xs, dtype=np.float32)}
    theta = np.zeros((64, 128), np.float32)
    cbias = np.full((64, 16), -1e9, np.float32)
    for oi, j in enumerate(octs):
        for jp in range(8):
            if jp < j:
                cbias[:, oi * 8 + jp] = 0.0
                for jpp in range(jp + 1, j):
                    theta[:, oi * 64 + jp * 8 + jpp] = 1.0
    d["theta"] = theta
    d["cbias"] = cbias
    return d


def assemble_output(shards):
    out = np.empty((2, 8192, 1024), np.float32)
    for core in range(NCORES):
        s, g, octs = core_layout(core)
        sh = shards[core]
        for oi, j in enumerate(octs):
            out[s, 1024 * j:1024 * (j + 1)] = sh[1024 * oi:1024 * (oi + 1)]
    return out



def emit_l0_attn(nc, P, S, es, W, CN, CI, x_in, x_out, tag="", upto=99, p2cut=99, dbg=False):
    sb = lambda name, shape, dt: es.enter_context(nc.sbuf_tensor(name + tag, shape, dt))
    psa = lambda name, shape, dt: es.enter_context(nc.psum_tensor(name + tag, shape, dt))

    q_fm = sb("q_fm", [128, 16384], BF16)
    k_tm = sb("k_tm", [128, 16384], BF16)
    v_tm = sb("v_tm", [128, 16384], BF16)
    gs_tm = sb("gs_tm", [128, 16384], BF16)
    arena = sb("arena", [128, 16384], BF16)
    wbuf = sb("wbuf", [128, 8192], BF16)
    u_sb = sb("u_sb", [17, 2048], F32)
    wg2b = sb("wg2b", [17, 1024], F32)
    wg1 = sb("wg1", [128, 128], BF16)
    xres = sb("xres", [128, 1024], F32)
    S_sb = sb("S_sb", [64, 1024], F32)
    S_bf = sb("S_bf", [64, 1024], BF16)
    a_all = sb("a_all", [64, 128], F32)
    a64 = sb("a64", [64, 16], F32)
    aoct = sb("aoct", [128, 8], F32)
    e3 = sb("e3", [128, 8], F32)
    acml = sb("acml", [64, 16], F32)
    e3h = sb("e3h", [64, 16], F32)
    ccoef = sb("ccoef", [64, 16], F32)
    stat = sb("stat", [128, 16], F32)
    stat2 = sb("stat2", [128, 16], F32)
    col1 = sb("col1", [128, 1], F32)
    col2 = sb("col2", [128, 1], F32)
    col3 = sb("col3", [128, 1], F32)
    junkS = sb("junkS", [128, 1], F32)
    junkV = sb("junkV", [128, 1], F32)
    gnw_bc = sb("gnw_bc", [128, 1024], BF16)
    id_bf = sb("id_bf", [128, 128], BF16)
    t1u = sb("t1u", [128, 128], F32)
    mask_bf = sb("mask_bf", [128, 128], BF16)
    onescol = sb("onescol", [128, 1], BF16)
    onesrow = sb("onesrow", [1, 128], BF16)
    murow = sb("murow", [1, 128], F32)
    rsrow = sb("rsrow", [1, 128], F32)
    varrow = sb("varrow", [1, 128], F32)
    theta_sb = sb("theta_sb", [64, 128], F32)
    cbias_sb = sb("cbias_sb", [64, 16], F32)
    cvals = sb("cvals", [128, 8], F32)
    register_consts(nc, cvals)

    def av(c0, c1, dt=F32):
        v = arena[:, c0:c1]
        return v.bitcast(dt) if dt != BF16 else v
    scr1 = av(0, 2048)
    scr2 = av(2048, 4096)
    scr3 = av(4096, 6144)
    qt64 = arena[0:64, 6144:8192]
    ktf64 = arena[0:64, 8192:10240]
    kt_bf = av(10240, 11264, BF16)
    pt_bf = av(11264, 13312, BF16)
    og_bf = av(13312, 14336, BF16)
    ogf_bf = av(14336, 15360, BF16)
    Stmp = scr3[0:64, :]
    r_sb = scr2
    stage2 = gs_tm[:, 0:1024]
    proj_scr = xres[:, 0:512]
    sloc = xres[0:64, :]

    ps_big = psa("ps_big", [128, 2048], F32)
    ps_o = psa("ps_o", [128, 1024], F32)
    ps_mkv = ps_o
    ps_t = psa("ps_t", [128, 1024], BF16)

    sin_d = nc.dram_tensor("sin_l0" + tag, [64, 2048], F32).ap()
    summ = nc.dram_tensor("summ_l0" + tag, [64, 2080], F32).ap()
    summ_g = nc.dram_tensor("summ_l0_g" + tag, [256, 2080], F32).ap()

    for dst, srcap in [(id_bf[:], CN["ident"]), (mask_bf[:], CN["mask_incl"]),
                       (onescol[:], CN["ones_col"]), (onesrow[:], CN["ones_row"]),
                       (gnw_bc[:], W["gnw_tiled"]),
                       (wg1[:].rearrange("p (kc n) -> p kc n", n=16),
                        W["Wg1"].rearrange("(kc p) n -> p kc n", p=128))]:
        P.op("gpsimd", lambda e, d=dst, s=srcap: e.dma_start(out=d, in_=s), S.lds, 16)
    for dst, srcap in [(t1u[:], CN["t1u"]), (theta_sb[:], CI["theta"]),
                       (cbias_sb[:], CI["cbias"]), (wg2b[0:16, :], W["Wg2"]),
                       (wg2b[16:17, :], W["bg2"]),
                       (u_sb[16:17, :], CN["ones_row2048"]),
                       (varrow[:], CN["ones_row"]),
                       (cvals[:], CN["constvals"])]:
        P.op("sync", lambda e, d=dst, s=srcap: e.dma_start(out=d, in_=s), S.ldh, 16)
    P.wait("tensor", S.lds)
    P.wait("tensor", S.ldh)

    P.wait("scalar", S.ldh)
    P.op("scalar", lambda e: e.activation(stat2[:, 0:8], cvals[:, 0:8], AF.Exp), S.cS)
    P.op("scalar", lambda e: e.activation(stat2[:, 0:1], cvals[:, 7:8], AF.Ln), S.cS)
    P.op("scalar", lambda e: e.activation(stat2[:, 0:8], cvals[:, 0:8], AF.Square), S.cS)
    P.op("scalar", lambda e: e.activation(stat2[:, 0:8], cvals[:, 0:8], AF.Silu), S.cS)

    x_fm = arena
    for t in range(NT):
        P.wait("sync", S.cV)
        P.op("sync", lambda e, t=t: e.dma_start(
            out=xres[:], in_=x_in[128 * t:128 * (t + 1), :]), S.ldh, 16)
        P.wait("vector", S.ldh)
        P.wait("vector", S.cT)
        P.op("vector", lambda e: e.tensor_copy(stage2, xres[:]), S.cV)
        P.wait("tensor", S.cV)
        for c in range(8):
            P.op("tensor", lambda e, c=c: e.transpose(
                ps_t[:, 128 * c:128 * (c + 1)], stage2[:, 128 * c:128 * (c + 1)],
                id_bf[:]), S.cT)
        P.wait("vector", S.cT)
        dst = x_fm[:].rearrange("p (fc n) -> p fc n", n=2048)[:, :, t * 128:(t + 1) * 128]
        P.op("vector", lambda e, d=dst: e.tensor_copy(d, ps_t[:].rearrange(
            "p (fc n) -> p fc n", n=128)), S.cV)

    if upto <= 1:
        return

    def stream_w(name):
        P.wait("gpsimd", S.cT)
        P.op("gpsimd", lambda e, s=W[name].rearrange("(kc p) n -> p kc n", p=128):
             e.dma_start(out=wbuf[:].rearrange("p (kc n) -> p kc n", n=1024), in_=s),
             S.lds, 16)
        P.wait("tensor", S.lds)

    stream_w("Wq")
    P.wait("tensor", S.cV)
    for oc in range(8):
        for tg in range(4):
            slot = (oc * 4 + tg) % 4
            po = ps_big[:, 512 * slot:512 * (slot + 1)]
            P.wait("tensor", S.cS)
            P.wait("tensor", S.cV)
            for kc in range(8):
                P.op("tensor", lambda e, po=po, kc=kc, oc=oc, tg=tg: e.matmul(
                    po, wbuf[:, kc * 1024 + oc * 128:kc * 1024 + (oc + 1) * 128],
                    x_fm[:, kc * 2048 + tg * 512:kc * 2048 + (tg + 1) * 512],
                    start=(kc == 0), stop=(kc == 7)), S.cT if kc == 7 else None)
            P.wait("scalar", S.cT)
            P.op("scalar", lambda e, po=po, oc=oc, tg=tg: e.activation(
                q_fm[:, oc * 2048 + tg * 512:oc * 2048 + (tg + 1) * 512], po,
                AF.Copy, scale=0.125), S.cS)

    for name, dst in [("Wk", k_tm), ("Wv", v_tm)]:
        stream_w(name)
        for t in range(NT):
            for fg in range(2):
                slot = (t * 2 + fg) % 4
                po = ps_big[:, 512 * slot:512 * (slot + 1)]
                P.wait("tensor", S.cV)
                P.wait("tensor", S.cS)
                for kc in range(8):
                    P.op("tensor", lambda e, po=po, kc=kc, t=t, fg=fg: e.matmul(
                        po, x_fm[:, kc * 2048 + t * 128:kc * 2048 + (t + 1) * 128],
                        wbuf[:, kc * 1024 + fg * 512:kc * 1024 + (fg + 1) * 512],
                        start=(kc == 0), stop=(kc == 7)), S.cT if kc == 7 else None)
                P.wait("vector", S.cT)
                P.op("vector", lambda e, po=po, t=t, fg=fg, dst=dst: e.tensor_copy(
                    dst[:, t * 1024 + fg * 512:t * 1024 + (fg + 1) * 512], po), S.cV)

    stream_w("Wg")
    for t in range(NT):
        for fg in range(2):
            slot = (t * 2 + fg) % 4
            po = ps_big[:, 512 * slot:512 * (slot + 1)]
            P.wait("tensor", S.cS)
            P.wait("tensor", S.cV)
            for kc in range(8):
                P.op("tensor", lambda e, po=po, kc=kc, t=t, fg=fg: e.matmul(
                    po, x_fm[:, kc * 2048 + t * 128:kc * 2048 + (t + 1) * 128],
                    wbuf[:, kc * 1024 + fg * 512:kc * 1024 + (fg + 1) * 512],
                    start=(kc == 0), stop=(kc == 7)), S.cT if kc == 7 else None)
            P.wait("scalar", S.cT)
            P.wait("scalar", S.cV)
            P.op("scalar", lambda e, po=po: e.activation(proj_scr, po, AF.Silu), S.cS)
            P.wait("vector", S.cS)
            P.op("vector", lambda e, t=t, fg=fg: e.tensor_mul(
                gs_tm[:, t * 1024 + fg * 512:t * 1024 + (fg + 1) * 512],
                proj_scr, gnw_bc[:, fg * 512:(fg + 1) * 512]), S.cV)

    for tg in range(4):
        P.wait("tensor", S.cS)
        P.wait("tensor", S.cV)
        for kc in range(8):
            P.op("tensor", lambda e, kc=kc, tg=tg: e.matmul(
                ps_mkv[0:16, 0:512], wg1[:, kc * 16:(kc + 1) * 16],
                x_fm[:, kc * 2048 + tg * 512:kc * 2048 + (tg + 1) * 512],
                start=(kc == 0), stop=(kc == 7)), S.cT if kc == 7 else None)
        P.wait("scalar", S.cT)
        P.op("scalar", lambda e, tg=tg: e.activation(
            u_sb[0:16, tg * 512:(tg + 1) * 512], ps_mkv[0:16, 0:512], AF.Copy), S.cS)

    if dbg:
        for nm, buf in [("dq", q_fm), ("dk", k_tm), ("dv", v_tm), ("dgs", gs_tm)]:
            dap = nc.dram_tensor(nm + tag, [128, 16384], F32, kind="ExternalOutput").ap()
            P.wait("gpsimd", S.cS)
            P.wait("gpsimd", S.cV)
            P.op("gpsimd", lambda e, d=dap, b=buf: e.dma_start(out=d, in_=b[:]), S.lds, 16)
        dup = nc.dram_tensor("du" + tag, [17, 2048], F32, kind="ExternalOutput").ap()
        P.op("gpsimd", lambda e, d=dup: e.dma_start(out=d, in_=u_sb[:]), S.lds, 16)

    if upto <= 2:
        return

    def chunk_preamble(tile, with_e1):
        P.wait("tensor", S.cS)
        P.wait("tensor", S.cV)
        for fg in range(2):
            P.op("tensor", lambda e, fg=fg, tile=tile: e.matmul(
                ps_big[:, fg * 512:(fg + 1) * 512],
                u_sb[0:17, tile * 128:(tile + 1) * 128],
                wg2b[0:17, fg * 512:(fg + 1) * 512],
                start=True, stop=True), S.cT)
        P.wait("scalar", S.cT)
        P.wait("scalar", S.cV)
        P.wait("scalar", S.sth)
        P.op("scalar", lambda e: e.activation(scr1, ps_big[:, 0:1024], AF.Exp, scale=-1.0), S.cS)
        P.op("scalar", lambda e: e.activation(scr2, scr1, AF.Ln, bias=1.0), S.cS)
        P.wait("tensor", S.cS)
        for fg in range(2):
            P.op("tensor", lambda e, fg=fg: e.matmul(
                ps_big[:, 1024 + fg * 512:1024 + (fg + 1) * 512], t1u[:],
                scr2[:, fg * 512:(fg + 1) * 512], start=True, stop=True), S.cT)
        for c in range(8):
            P.op("tensor", lambda e, c=c: e.matmul(
                ps_big[:, c * 128:(c + 1) * 128],
                scr2[:, c * 128:(c + 1) * 128], t1u[:],
                start=True, stop=True), S.cT)
        P.wait("scalar", S.cT)
        afm3 = ps_big[:, 0:1024].rearrange("p (c t) -> p c t", t=128)
        P.op("scalar", lambda e: e.activation(
            e3[:].rearrange("p (c o) -> p c o", o=1), afm3[:, :, 127:128], AF.Exp), S.cS)
        P.op("scalar", lambda e: e.activation(
            stat2[:, 0:8].rearrange("p (c o) -> p c o", o=1), afm3[:, :, 127:128],
            AF.Copy), S.cS)
        P.op("scalar", lambda e: e.activation(scr1, ps_big[:, 1024:2048], AF.Exp, scale=-1.0), S.cS)
        if with_e1:
            P.op("scalar", lambda e: e.activation(scr3, ps_big[:, 0:1024], AF.Exp), S.cS)
        P.wait("vector", S.cS)
        P.op("vector", lambda e, tile=tile: e.tensor_mul(
            kt_bf, k_tm[:, tile * 1024:(tile + 1) * 1024], scr1), S.cV)

    def mkv_and_update(tile):
        P.wait("tensor", S.cV)
        for h in range(16):
            P.op("tensor", lambda e, h=h, tile=tile: e.matmul(
                ps_mkv[0:64, 64 * h:64 * (h + 1)], kt_bf[:, h * 64:(h + 1) * 64],
                v_tm[:, tile * 1024 + h * 64:tile * 1024 + (h + 1) * 64],
                start=True, stop=True), S.cT if h == 15 else None)
        P.wait("vector", S.cT)
        e3hv = e3h[:].rearrange("p (h o) -> p h o", o=1)
        P.op("vector", lambda e, e3hv=e3hv: e.tensor_copy(
            e3hv[:, 0:16:2, :], e3[0:64, :].unsqueeze(2)), S.cV)
        P.op("vector", lambda e, e3hv=e3hv: e.tensor_copy(
            e3hv[:, 1:16:2, :], e3[64:128, :].unsqueeze(2)), S.cV)
        P.op("vector", lambda e: e.tensor_add(Stmp, S_sb[:], ps_mkv[0:64, :]), S.cV)
        P.op("vector", lambda e: e.tensor_mul(
            S_sb[:].rearrange("p (h v) -> p h v", v=64),
            Stmp.rearrange("p (h v) -> p h v", v=64),
            e3h[:].unsqueeze(2).broadcast_to([64, 16, 64])), S.cV)

    for oi in range(2):
        P.wait("vector", S.cT)
        P.wait("vector", S.sth)
        P.op("vector", lambda e: e.memset(S_sb[:], 0.0), S.cV)
        P.op("vector", lambda e: e.memset(aoct[:], 0.0), S.cV)
        for ci in range(8):
            tile = oi * 8 + ci
            chunk_preamble(tile, with_e1=False)
            if dbg and tile == 0:
                dsp = nc.dram_tensor("dspl" + tag, [128, 1024], F32, kind="ExternalOutput").ap()
                de2 = nc.dram_tensor("de2" + tag, [128, 1024], F32, kind="ExternalOutput").ap()
                de3 = nc.dram_tensor("de3" + tag, [128, 8], F32, kind="ExternalOutput").ap()
                dkt = nc.dram_tensor("dkt" + tag, [128, 1024], F32, kind="ExternalOutput").ap()
                P.wait("gpsimd", S.cS)
                P.wait("gpsimd", S.cV)
                P.op("gpsimd", lambda e, d=dsp: e.dma_start(out=d, in_=scr2), S.lds, 16)
                P.op("gpsimd", lambda e, d=de2: e.dma_start(out=d, in_=scr1), S.lds, 16)
                P.op("gpsimd", lambda e, d=de3: e.dma_start(out=d, in_=e3[:]), S.lds, 16)
                P.op("gpsimd", lambda e, d=dkt: e.dma_start(out=d, in_=kt_bf), S.lds, 16)
                P.wait("scalar", S.lds)
                P.wait("vector", S.lds)
            P.wait("vector", S.cS)
            P.op("vector", lambda e: e.tensor_add(aoct[:], aoct[:], stat2[:, 0:8]), S.cV)
            mkv_and_update(tile)
            if dbg and tile == 0:
                ds1 = nc.dram_tensor("ds1" + tag, [64, 1024], F32, kind="ExternalOutput").ap()
                P.wait("sync", S.cV)
                P.op("sync", lambda e, d=ds1: e.dma_start(out=d, in_=S_sb[:]), S.sth, 16)
                P.wait("vector", S.sth)
        a64v = a64[:].rearrange("p (h o) -> p h o", o=1)
        P.op("vector", lambda e, a64v=a64v: e.tensor_copy(
            a64v[:, 0:16:2, :], aoct[0:64, :].unsqueeze(2)), S.cV)
        P.op("vector", lambda e, a64v=a64v: e.tensor_copy(
            a64v[:, 1:16:2, :], aoct[64:128, :].unsqueeze(2)), S.cV)
        P.wait("sync", S.cV)
        P.op("sync", lambda e, oi=oi: e.dma_start(
            out=summ[:, oi * 1040:oi * 1040 + 16], in_=a64[:]), S.sth, 16)
        P.op("sync", lambda e, oi=oi: e.dma_start(
            out=summ[:, oi * 1040 + 16:oi * 1040 + 1040], in_=S_sb[:]), S.sth, 16)

    if upto <= 3:
        return

    P.wait("gpsimd", S.sth)
    P.op("gpsimd", lambda e: e.collective_compute(
        "AllGather", mybir.AluOpType.bypass,
        replica_groups=[[0, 1, 2, 3], [4, 5, 6, 7]],
        ins=[summ.opt()], outs=[summ_g.opt()]), S.cG, 1)
    P.wait("sync", S.cG)
    for j in range(8):
        rank, half = (j, 0) if j < 4 else (7 - j, 1)
        P.op("sync", lambda e, j=j, rank=rank, half=half: e.dma_start(
            out=a_all[:, 16 * j:16 * (j + 1)],
            in_=summ_g[64 * rank:64 * (rank + 1), half * 1040:half * 1040 + 16]),
            S.ldh, 16)
    P.wait("vector", S.ldh)
    for oi in range(2):
        P.wait("vector", S.sth)
        P.op("vector", lambda e: e.memset(S_sb[:], 0.0), S.cV)
        for jp in range(8):
            rank, half = (jp, 0) if jp < 4 else (7 - jp, 1)
            P.op("vector", lambda e: e.memset(acml[:], 0.0), S.cV)
            for jpp in range(8):
                tc = theta_sb[:, oi * 64 + jp * 8 + jpp:oi * 64 + jp * 8 + jpp + 1]
                P.op("vector", lambda e, jpp=jpp, tc=tc: e.scalar_tensor_tensor(
                    acml[:], a_all[:, 16 * jpp:16 * (jpp + 1)], tc, acml[:],
                    ALU.mult, ALU.add), S.cV)
            P.wait("scalar", S.cV)
            P.op("scalar", lambda e, oi=oi, jp=jp: e.activation(
                ccoef[:], acml[:], AF.Exp,
                bias=cbias_sb[:, oi * 8 + jp:oi * 8 + jp + 1]), S.cS)
            P.wait("sync", S.cV, S.cV.n)
            P.op("sync", lambda e, rank=rank, half=half: e.dma_start(
                out=sloc, in_=summ_g[64 * rank:64 * (rank + 1),
                                     half * 1040 + 16:half * 1040 + 1040]), S.ldh, 16)
            P.wait("vector", S.ldh)
            P.wait("vector", S.cS)
            P.op("vector", lambda e: e.tensor_mul(
                Stmp.rearrange("p (h v) -> p h v", v=64),
                sloc.rearrange("p (h v) -> p h v", v=64),
                ccoef[:].unsqueeze(2).broadcast_to([64, 16, 64])), S.cV)
            P.op("vector", lambda e: e.tensor_add(S_sb[:], S_sb[:], Stmp), S.cV)
        P.wait("sync", S.cV)
        P.op("sync", lambda e, oi=oi: e.dma_start(
            out=sin_d[:, 1024 * oi:1024 * (oi + 1)], in_=S_sb[:]), S.sth, 16)

    if upto <= 4:
        return

    stream_w("Wo")

    for oi in range(2):
        P.wait("sync", S.cV)
        P.wait("sync", S.sth)
        P.op("sync", lambda e, oi=oi: e.dma_start(
            out=S_sb[:], in_=sin_d[:, 1024 * oi:1024 * (oi + 1)]), S.ldh, 16)
        P.wait("vector", S.ldh)
        P.wait("vector", S.cT)
        for ci in range(8):
            tile = oi * 8 + ci
            P.wait("sync", S.cV)
            P.op("sync", lambda e, tile=tile: e.dma_start(
                out=xres[:], in_=x_in[128 * tile:128 * (tile + 1), :]), S.ldh, 16)
            chunk_preamble(tile, with_e1=True)
            if p2cut <= 1:
                continue
            P.op("vector", lambda e: e.tensor_copy(S_bf[:], S_sb[:]), S.cV)
            q3 = q_fm[:].rearrange("p (oc n) -> p oc n", n=2048)
            e13 = scr3.rearrange("p (c t) -> p c t", t=128)
            qt3 = qt64.rearrange("p (h t) -> p h t", t=128)
            P.op("vector", lambda e, q3=q3, e13=e13, qt3=qt3, tile=tile: e.tensor_mul(
                qt3[:, 0:16:2, :],
                q3[0:64, :, tile * 128:(tile + 1) * 128], e13[0:64, :, :]), S.cV)
            P.op("vector", lambda e, q3=q3, e13=e13, qt3=qt3, tile=tile: e.tensor_mul(
                qt3[:, 1:16:2, :],
                q3[64:128, :, tile * 128:(tile + 1) * 128], e13[64:128, :, :]), S.cV)
            P.wait("tensor", S.cV)
            for c in range(8):
                P.op("tensor", lambda e, c=c: e.transpose(
                    ps_t[:, 128 * c:128 * (c + 1)], kt_bf[:, 128 * c:128 * (c + 1)],
                    id_bf[:]), S.cT)
            P.wait("vector", S.cT)
            pt3 = ps_t[:].rearrange("p (c t) -> p c t", t=128)
            kf3 = ktf64.rearrange("p (h t) -> p h t", t=128)
            P.op("vector", lambda e, pt3=pt3, kf3=kf3: e.tensor_copy(
                kf3[:, 0:16:2, :], pt3[0:64, :, :]), S.cV)
            P.op("vector", lambda e, pt3=pt3, kf3=kf3: e.tensor_copy(
                kf3[:, 1:16:2, :], pt3[64:128, :, :]), S.cV)
            if dbg and tile == 0:
                dqt = nc.dram_tensor("dqt" + tag, [64, 2048], F32, kind="ExternalOutput").ap()
                dkf = nc.dram_tensor("dkf" + tag, [64, 2048], F32, kind="ExternalOutput").ap()
                P.wait("gpsimd", S.cV)
                P.op("gpsimd", lambda e, d=dqt: e.dma_start(out=d, in_=qt64), S.lds, 16)
                P.op("gpsimd", lambda e, d=dkf: e.dma_start(out=d, in_=ktf64), S.lds, 16)
                P.wait("tensor", S.lds)
            if p2cut <= 2:
                continue
            P.wait("tensor", S.cV)
            for h in range(16):
                P.op("tensor", lambda e, h=h: e.matmul(
                    ps_big[:, 128 * h:128 * (h + 1)],
                    ktf64[0:64, 128 * h:128 * (h + 1)],
                    qt64[0:64, 128 * h:128 * (h + 1)],
                    start=True, stop=True), S.cT if h == 15 else None)
            P.wait("vector", S.cT)
            P.op("vector", lambda e: e.tensor_mul(
                pt_bf.rearrange("p (h t) -> p h t", t=128),
                ps_big[:].rearrange("p (h t) -> p h t", t=128),
                mask_bf[:].unsqueeze(1).broadcast_to([128, 16, 128])), S.cV)
            if dbg and tile == 0:
                dpt = nc.dram_tensor("dpt" + tag, [128, 2048], F32, kind="ExternalOutput").ap()
                P.wait("gpsimd", S.cV)
                P.op("gpsimd", lambda e, d=dpt: e.dma_start(out=d, in_=pt_bf), S.lds, 16)
                P.wait("tensor", S.lds)
            if p2cut <= 3:
                continue
            P.wait("tensor", S.cV)
            for h in range(16):
                P.op("tensor", lambda e, h=h, tile=tile: e.matmul(
                    ps_o[:, 64 * h:64 * (h + 1)], pt_bf[:, 128 * h:128 * (h + 1)],
                    v_tm[:, tile * 1024 + 64 * h:tile * 1024 + 64 * (h + 1)],
                    start=True, stop=False), None)
                P.op("tensor", lambda e, h=h: e.matmul(
                    ps_o[:, 64 * h:64 * (h + 1)],
                    qt64[0:64, 128 * h:128 * (h + 1)],
                    S_bf[0:64, 64 * h:64 * (h + 1)],
                    start=False, stop=True), S.cT if h == 15 else None)
            if p2cut <= 4:
                continue
            P.wait("scalar", S.cT)
            P.op("scalar", lambda e: e.activation(scr1, ps_o[:], AF.Square,
                                                  scale=0.125), S.cS)
            P.wait("vector", S.cS)
            P.op("vector", lambda e: e.tensor_reduce(
                stat[:].rearrange("p (h o) -> p h o", o=1),
                scr1.rearrange("p (h d) -> p h d", d=64), AX.X, ALU.add), S.cV)
            P.op("vector", lambda e: e.tensor_copy(junkV[:], cvals[:, 0:1]), S.cV)
            P.wait("scalar", S.cV)
            P.op("scalar", lambda e: e.activation(stat2[:], stat[:], AF.Ln,
                                                  bias=EPS), S.cS)
            P.op("scalar", lambda e: e.activation(stat2[:], stat2[:], AF.Exp,
                                                  scale=-0.5), S.cS)
            P.wait("vector", S.cS)
            P.op("vector", lambda e: e.tensor_mul(
                scr3.rearrange("p (h d) -> p h d", d=64),
                ps_o[:].rearrange("p (h d) -> p h d", d=64),
                stat2[:].unsqueeze(2).broadcast_to([128, 16, 64])), S.cV)
            P.op("vector", lambda e, tile=tile: e.tensor_mul(
                og_bf, scr3, gs_tm[:, tile * 1024:(tile + 1) * 1024]), S.cV)
            if dbg and tile == 0:
                don = nc.dram_tensor("don" + tag, [128, 1024], F32, kind="ExternalOutput").ap()
                dog = nc.dram_tensor("dog" + tag, [128, 1024], F32, kind="ExternalOutput").ap()
                P.wait("gpsimd", S.cV)
                P.op("gpsimd", lambda e, d=don: e.dma_start(out=d, in_=scr3), S.lds, 16)
                P.op("gpsimd", lambda e, d=dog: e.dma_start(out=d, in_=og_bf), S.lds, 16)
                P.wait("tensor", S.lds)
                P.wait("scalar", S.lds)
            mkv_and_update(tile)
            P.wait("tensor", S.cV)
            for c in range(8):
                P.op("tensor", lambda e, c=c: e.transpose(
                    ps_t[:, 128 * c:128 * (c + 1)], og_bf[:, 128 * c:128 * (c + 1)],
                    id_bf[:]), S.cT)
            P.wait("vector", S.cT)
            P.op("vector", lambda e: e.tensor_copy(ogf_bf, ps_t[:]), S.cV)
            if p2cut <= 5:
                continue
            P.wait("tensor", S.cV)
            for dg in range(2):
                for kc in range(8):
                    P.op("tensor", lambda e, dg=dg, kc=kc: e.matmul(
                        ps_big[:, dg * 512:(dg + 1) * 512],
                        ogf_bf[:, kc * 128:(kc + 1) * 128],
                        wbuf[:, kc * 1024 + dg * 512:kc * 1024 + (dg + 1) * 512],
                        start=(kc == 0), stop=(kc == 7)),
                        S.cT if (dg == 1 and kc == 7) else None)
            P.wait("vector", S.cT)
            P.wait("vector", S.ldh)
            P.op("vector", lambda e: e.tensor_add(r_sb, ps_big[:, 0:1024], xres[:]), S.cV)
            P.op("vector", lambda e: e.tensor_copy(og_bf, r_sb), S.cV)
            P.wait("tensor", S.cV)
            for c in range(8):
                P.op("tensor", lambda e, c=c: e.transpose(
                    ps_t[:, 128 * c:128 * (c + 1)], og_bf[:, 128 * c:128 * (c + 1)],
                    id_bf[:]), S.cT)
            P.wait("vector", S.cT)
            P.op("vector", lambda e: e.tensor_copy(ogf_bf, ps_t[:]), S.cV)
            sq_bf = gs_tm[:, tile * 1024:(tile + 1) * 1024]
            P.wait("scalar", S.cV)
            P.op("scalar", lambda e, sq_bf=sq_bf: e.activation(
                sq_bf, ogf_bf, AF.Square, scale=1.0 / 32), S.cS)
            P.wait("tensor", S.cS)
            for c in range(8):
                P.op("tensor", lambda e, c=c: e.matmul(
                    ps_o[0:1, 0:128], onescol[:], ogf_bf[:, 128 * c:128 * (c + 1)],
                    start=(c == 0), stop=(c == 7)), None)
            for c in range(8):
                P.op("tensor", lambda e, c=c, sq_bf=sq_bf: e.matmul(
                    ps_o[0:1, 128:256], onescol[:], sq_bf[:, 128 * c:128 * (c + 1)],
                    start=(c == 0), stop=(c == 7)), S.cT if c == 7 else None)
            P.wait("scalar", S.cT)
            P.op("scalar", lambda e: e.activation(murow[:], ps_o[0:1, 0:128], AF.Copy,
                                                  scale=1.0 / 1024), S.cS)
            P.wait("vector", S.cS)
            P.op("vector", lambda e: e.tensor_mul(varrow[:], murow[:], murow[:]), S.cV)
            P.op("vector", lambda e: e.tensor_sub(varrow[:], ps_o[0:1, 128:256], varrow[:]), S.cV)
            P.wait("scalar", S.cV)
            P.op("scalar", lambda e: e.activation(varrow[:], varrow[:], AF.Ln,
                                                  bias=cvals[0:1, 0:1]), S.cS)
            P.op("scalar", lambda e: e.activation(rsrow[:], varrow[:], AF.Exp,
                                                  scale=-0.5), S.cS)
            P.wait("tensor", S.cS)
            P.op("tensor", lambda e: e.matmul(ps_o[0:128, 256:257], murow[:],
                                              cvals[0:1, 7:8],
                                              start=True, stop=True), None)
            P.op("tensor", lambda e: e.matmul(ps_o[0:128, 257:258], rsrow[:],
                                              cvals[0:1, 7:8],
                                              start=True, stop=True), S.cT)
            P.wait("vector", S.cT)
            P.op("vector", lambda e: e.tensor_copy(col1[:], ps_o[0:128, 256:257]), S.cV)
            P.op("vector", lambda e: e.tensor_copy(col2[:], ps_o[0:128, 257:258]), S.cV)
            P.op("vector", lambda e: e.tensor_scalar(
                scr2, r_sb, col1[:], col2[:], ALU.subtract, ALU.mult), S.cV)
            P.wait("sync", S.cV)
            P.op("sync", lambda e, tile=tile: e.dma_start(
                out=x_out[128 * tile:128 * (tile + 1), :], in_=scr2), S.sth, 16)


def emit_ln_tail(nc, P, S, B, r_f32, out_ap, gamma_bc=None, beta_bc=None, to_sbuf=None):
    og_bf, ogf_bf, sq_bf = B["og_bf"], B["ogf_bf"], B["sq_bf"]
    ps_t, ps_st = B["ps_t"], B["ps_st"]
    P.wait("vector", S.cT)
    P.op("vector", lambda e: e.tensor_copy(og_bf, r_f32), S.cV)
    P.wait("tensor", S.cV)
    for c in range(8):
        P.op("tensor", lambda e, c=c: e.transpose(
            ps_t[:, 128 * c:128 * (c + 1)], og_bf[:, 128 * c:128 * (c + 1)],
            B["id_bf"][:]), S.cT)
    P.wait("vector", S.cT)
    P.op("vector", lambda e: e.tensor_copy(ogf_bf, ps_t[:, 0:1024]), S.cV)
    P.wait("scalar", S.cV)
    P.op("scalar", lambda e: e.activation(sq_bf, ogf_bf, AF.Square, scale=1.0 / 32), S.cS)
    P.wait("tensor", S.cS)
    for c in range(8):
        P.op("tensor", lambda e, c=c: e.matmul(
            ps_st[0:1, 0:128], B["onescol"][:], ogf_bf[:, 128 * c:128 * (c + 1)],
            start=(c == 0), stop=(c == 7)), None)
    for c in range(8):
        P.op("tensor", lambda e, c=c: e.matmul(
            ps_st[0:1, 128:256], B["onescol"][:], sq_bf[:, 128 * c:128 * (c + 1)],
            start=(c == 0), stop=(c == 7)), S.cT if c == 7 else None)
    P.wait("scalar", S.cT)
    P.op("scalar", lambda e: e.activation(B["murow"][:], ps_st[0:1, 0:128], AF.Copy,
                                          scale=1.0 / 1024), S.cS)
    P.wait("vector", S.cS)
    P.op("vector", lambda e: e.tensor_mul(B["varrow"][:], B["murow"][:], B["murow"][:]), S.cV)
    P.op("vector", lambda e: e.tensor_sub(B["varrow"][:], ps_st[0:1, 128:256], B["varrow"][:]), S.cV)
    P.wait("scalar", S.cV)
    P.op("scalar", lambda e: e.activation(B["varrow"][:], B["varrow"][:], AF.Ln,
                                          bias=B["cvals"][0:1, 0:1]), S.cS)
    P.op("scalar", lambda e: e.activation(B["rsrow"][:], B["varrow"][:], AF.Exp,
                                          scale=-0.5), S.cS)
    P.wait("tensor", S.cS)
    onef = B["cvals"][0:1, 7:8]
    P.op("tensor", lambda e, onef=onef: e.matmul(
        ps_st[0:128, 256:257], B["murow"][:], onef, start=True, stop=True), None)
    P.op("tensor", lambda e, onef=onef: e.matmul(
        ps_st[0:128, 257:258], B["rsrow"][:], onef, start=True, stop=True), S.cT)
    P.wait("vector", S.cT)
    P.op("vector", lambda e: e.tensor_copy(B["col1"][:], ps_st[0:128, 256:257]), S.cV)
    P.op("vector", lambda e: e.tensor_copy(B["col2"][:], ps_st[0:128, 257:258]), S.cV)
    dst = B["scrO"] if to_sbuf is None else to_sbuf
    P.op("vector", lambda e, dst=dst: e.tensor_scalar(
        dst, r_f32, B["col1"][:], B["col2"][:], ALU.subtract, ALU.mult), S.cV)
    if gamma_bc is not None:
        P.op("vector", lambda e, dst=dst: e.tensor_mul(dst, dst, gamma_bc), S.cV)
        P.op("vector", lambda e, dst=dst: e.tensor_add(dst, dst, beta_bc), S.cV)
    if out_ap is not None:
        P.wait("sync", S.cV)
        P.op("sync", lambda e, dst=dst: e.dma_start(out=out_ap, in_=dst), S.sth, 16)


def emit_xfm_build(nc, P, S, B, x_in, x_fm):
    xres, stage2, ps_t, id_bf = B["xres"], B["stage2"], B["ps_t"], B["id_bf"]
    for t in range(NT):
        P.wait("sync", S.cV)
        P.op("sync", lambda e, t=t: e.dma_start(
            out=xres[:], in_=x_in[128 * t:128 * (t + 1), :]), S.ldh, 16)
        P.wait("vector", S.ldh)
        P.wait("vector", S.cT)
        P.op("vector", lambda e: e.tensor_copy(stage2, xres[:]), S.cV)
        P.wait("tensor", S.cV)
        for c in range(8):
            P.op("tensor", lambda e, c=c: e.transpose(
                ps_t[:, 128 * c:128 * (c + 1)], stage2[:, 128 * c:128 * (c + 1)],
                id_bf[:]), S.cT)
        P.wait("vector", S.cT)
        dst = x_fm[:].rearrange("p (fc n) -> p fc n", n=2048)[:, :, t * 128:(t + 1) * 128]
        P.op("vector", lambda e, d=dst: e.tensor_copy(d, ps_t[:, 0:1024].rearrange(
            "p (fc n) -> p fc n", n=128)), S.cV)


def emit_l0_ffn(nc, P, S, es, W, CN, x_in, x_out, tag="f0"):
    sb = lambda name, shape, dt: es.enter_context(nc.sbuf_tensor(name + tag, shape, dt))
    psa = lambda name, shape, dt: es.enter_context(nc.psum_tensor(name + tag, shape, dt))

    x_fm = sb("x_fm", [128, 16384], BF16)
    y_acc = sb("y_acc", [128, 16384], F32)
    g_fm = sb("g_fm", [128, 4096], BF16)
    w1b = sb("w1b", [128, 8192], BF16)
    w2b = sb("w2b", [128, 8192], BF16)
    rwf = sb("rwf", [128, 64], BF16)
    b1r = sb("b1r", [1, 4096], BF16)
    b2bc = sb("b2bc", [128, 1024], F32)
    ones512 = sb("ones512", [1, 512], BF16)
    xres = sb("xres", [128, 1024], F32)
    stage2 = sb("stage2", [128, 1024], BF16)
    scr1 = sb("scr1", [128, 1024], F32)
    scr2 = sb("scr2", [128, 1024], F32)
    rt = sb("rt", [128, 128], F32)
    fcol = sb("fcol", [128, 16], F32)
    m1 = sb("m1", [128, 16], F32)
    m2 = sb("m2", [128, 16], F32)
    tmp8 = sb("tmp8", [128, 128], F32)
    og_bf = sb("og_bf", [128, 1024], BF16)
    ogf_bf = sb("ogf_bf", [128, 1024], BF16)
    sq_bf = sb("sq_bf", [128, 1024], BF16)
    onescol = sb("onescol", [128, 1], BF16)
    murow = sb("murow", [1, 128], F32)
    rsrow = sb("rsrow", [1, 128], F32)
    varrow = sb("varrow", [1, 128], F32)
    col1 = sb("col1", [128, 1], F32)
    col2 = sb("col2", [128, 1], F32)
    id_bf = sb("id_bf", [128, 128], BF16)
    cvals = sb("cvals", [128, 8], F32)
    register_consts(nc, cvals)

    ps_h = psa("ps_h", [128, 2048], F32)
    ps_y = psa("ps_y", [128, 1024], F32)
    ps_st = psa("ps_st", [128, 512], F32)
    ps_t = psa("ps_t", [128, 1024], BF16)

    B = dict(og_bf=og_bf[:], ogf_bf=ogf_bf[:], sq_bf=sq_bf[:], ps_t=ps_t,
             ps_st=ps_st, id_bf=id_bf, onescol=onescol, murow=murow,
             rsrow=rsrow, varrow=varrow, col1=col1, col2=col2, cvals=cvals,
             scrO=scr2[:], xres=xres, stage2=stage2[:])

    for dst, srcap in [(id_bf[:], CN["ident"]), (onescol[:], CN["ones_col"]),
                       (b1r[:], W["b1"]), (ones512[:], CN["ones_row2048"][:, 0:512]),
                       (rwf[:].rearrange("p (kc e) -> p kc e", e=8),
                        W["router_w"].rearrange("(kc p) e -> p kc e", p=128))]:
        P.op("gpsimd", lambda e, d=dst, s=srcap: e.dma_start(out=d, in_=s), S.lds, 16)
    P.op("sync", lambda e: e.dma_start(out=cvals[:], in_=CN["constvals"]), S.ldh, 16)
    P.op("sync", lambda e: e.dma_start(out=b2bc[:], in_=W["b2_t"]), S.ldh, 16)
    P.wait("tensor", S.lds)
    P.wait("tensor", S.ldh)

    emit_xfm_build(nc, P, S, B, x_in, x_fm)

    P.wait("scalar", S.ldh)
    P.op("scalar", lambda e: e.activation(tmp8[:, 0:8], cvals[:, 0:8], AF.Exp), S.cS)
    P.op("scalar", lambda e: e.activation(tmp8[:, 0:1], cvals[:, 7:8], AF.Ln), S.cS)
    P.op("scalar", lambda e: e.activation(tmp8[:, 0:8], cvals[:, 0:8], AF.Gelu_apprx_tanh), S.cS)
    P.op("scalar", lambda e: e.activation(tmp8[:, 0:8], cvals[:, 0:8], AF.Square), S.cS)

    P.wait("tensor", S.cV)
    for t in range(NT):
        P.wait("tensor", S.cV)
        for kc in range(8):
            P.op("tensor", lambda e, t=t, kc=kc: e.matmul(
                ps_y[:, 8 * t:8 * (t + 1)],
                x_fm[:, kc * 2048 + t * 128:kc * 2048 + (t + 1) * 128],
                rwf[:, kc * 8:(kc + 1) * 8],
                start=(kc == 0), stop=(kc == 7)), S.cT if kc == 7 else None)
    P.wait("vector", S.cT)
    P.op("vector", lambda e: e.tensor_copy(rt[:], ps_y[:, 0:128]), S.cV)
    r3 = rt[:].rearrange("p (t e) -> p t e", e=8)
    t3 = tmp8[:].rearrange("p (t e) -> p t e", e=8)
    m1v = m1[:].rearrange("p (t o) -> p t o", o=1)
    m2v = m2[:].rearrange("p (t o) -> p t o", o=1)

    def tree(op, srcv, outv):
        P.op("vector", lambda e: e.tensor_tensor(t3[:, :, 0:4], srcv[:, :, 0:4], srcv[:, :, 4:8], op), S.cV)
        P.op("vector", lambda e: e.tensor_tensor(t3[:, :, 0:2], t3[:, :, 0:2], t3[:, :, 2:4], op), S.cV)
        P.op("vector", lambda e: e.tensor_tensor(outv, t3[:, :, 0:1], t3[:, :, 1:2], op), S.cV)

    tree(ALU.max, r3, m1v)
    P.op("vector", lambda e: e.tensor_sub(r3, r3, m1v.broadcast_to([128, 16, 8])), S.cV)
    P.wait("scalar", S.cV)
    P.op("scalar", lambda e: e.activation(rt[:], rt[:], AF.Exp), S.cS)
    P.wait("vector", S.cS)
    tree(ALU.add, r3, m2v)
    P.wait("scalar", S.cV)
    P.op("scalar", lambda e: e.activation(m2[:], m2[:], AF.Ln), S.cS)
    P.op("scalar", lambda e: e.activation(m2[:], m2[:], AF.Exp, scale=-1.0), S.cS)
    P.wait("vector", S.cS)
    P.op("vector", lambda e: e.tensor_mul(r3, r3, m2v.broadcast_to([128, 16, 8])), S.cV)
    tree(ALU.max, r3, m1v)
    P.op("vector", lambda e: e.tensor_tensor(t3, r3, m1v.broadcast_to([128, 16, 8]), ALU.is_ge), S.cV)
    P.op("vector", lambda e: e.scalar_tensor_tensor(t3, t3, -1e30, r3, ALU.mult, ALU.add), S.cV)
    P.op("vector", lambda e: e.tensor_tensor(t3[:, :, 0:4], t3[:, :, 0:4], t3[:, :, 4:8], ALU.max), S.cV)
    P.op("vector", lambda e: e.tensor_tensor(t3[:, :, 0:2], t3[:, :, 0:2], t3[:, :, 2:4], ALU.max), S.cV)
    P.op("vector", lambda e: e.tensor_tensor(m2v, t3[:, :, 0:1], t3[:, :, 1:2], ALU.max), S.cV)
    P.op("vector", lambda e: e.tensor_add(m1[:], m1[:], m2[:]), S.cV)
    P.wait("scalar", S.cV)
    P.op("scalar", lambda e: e.activation(m2[:], m1[:], AF.Ln), S.cS)
    P.op("scalar", lambda e: e.activation(m2[:], m2[:], AF.Exp, scale=-1.0), S.cS)
    P.wait("vector", S.cS)
    P.op("vector", lambda e: e.tensor_mul(fcol[:], m1[:], m2[:]), S.cV)

    for half in range(4):
        h0 = half * 1024
        P.wait("gpsimd", S.cT)
        s1 = W["w1"][:, h0:h0 + 1024].rearrange("(kc p) n -> p kc n", p=128)
        P.op("gpsimd", lambda e, s=s1: e.dma_start(
            out=w1b[:].rearrange("p (kc n) -> p kc n", n=1024), in_=s), S.lds, 16)
        s2 = W["w2"][h0:h0 + 1024, :].rearrange("(hc p) n -> p hc n", p=128)
        P.op("gpsimd", lambda e, s=s2: e.dma_start(
            out=w2b[:].rearrange("p (hc n) -> p hc n", n=1024), in_=s), S.lds, 16)
        P.wait("tensor", S.lds)
        for tg in range(4):
            for hc in range(8):
                slot = hc % 4
                po = ps_h[:, 512 * slot:512 * (slot + 1)]
                P.wait("tensor", S.cS, max(S.cS.n - 3, 0))
                for kc in range(8):
                    P.op("tensor", lambda e, po=po, kc=kc, hc=hc, tg=tg: e.matmul(
                        po, w1b[:, kc * 1024 + hc * 128:kc * 1024 + (hc + 1) * 128],
                        x_fm[:, kc * 2048 + tg * 512:kc * 2048 + (tg + 1) * 512],
                        start=(kc == 0), stop=False), None)
                P.op("tensor", lambda e, po=po, hc=hc, h0=h0: e.matmul(
                    po, b1r[0:1, h0 + hc * 128:h0 + (hc + 1) * 128], ones512[:],
                    start=False, stop=True), S.cT)
                P.wait("scalar", S.cT)
                P.op("scalar", lambda e, po=po, hc=hc: e.activation(
                    g_fm[:, hc * 512:(hc + 1) * 512], po, AF.Gelu_apprx_tanh), S.cS)
            P.wait("tensor", S.cS)
            for tt in range(4):
                tile = tg * 4 + tt
                for dg in range(2):
                    po = ps_y[:, dg * 512:(dg + 1) * 512]
                    P.wait("tensor", S.cV, max(S.cV.n - 1, 0))
                    for hc in range(8):
                        P.op("tensor", lambda e, po=po, hc=hc, tt=tt, dg=dg: e.matmul(
                            po, g_fm[:, hc * 512 + tt * 128:hc * 512 + tt * 128 + 128],
                            w2b[:, hc * 1024 + dg * 512:hc * 1024 + (dg + 1) * 512],
                            start=(hc == 0), stop=(hc == 7)), S.cT if hc == 7 else None)
                    P.wait("vector", S.cT)
                    ya = y_acc[:, tile * 1024 + dg * 512:tile * 1024 + (dg + 1) * 512]
                    if half == 0:
                        P.op("vector", lambda e, po=po, ya=ya: e.tensor_copy(ya, po), S.cV)
                    else:
                        P.op("vector", lambda e, po=po, ya=ya: e.tensor_add(ya, ya, po), S.cV)

    if DBG_L0F:
        dfc = nc.dram_tensor("dfc", [128, 16], F32, kind="ExternalOutput").ap()
        dya = nc.dram_tensor("dya", [128, 16384], F32, kind="ExternalOutput").ap()
        drt = nc.dram_tensor("drt", [128, 128], F32, kind="ExternalOutput").ap()
        P.wait("gpsimd", S.cV)
        P.op("gpsimd", lambda e: e.dma_start(out=dfc, in_=fcol[:]), S.lds, 16)
        P.op("gpsimd", lambda e: e.dma_start(out=dya, in_=y_acc[:]), S.lds, 16)
        P.op("gpsimd", lambda e: e.dma_start(out=drt, in_=rt[:]), S.lds, 16)
        P.wait("vector", S.lds)

    for tile in range(NT):
        P.wait("sync", S.cV)
        P.op("sync", lambda e, tile=tile: e.dma_start(
            out=xres[:], in_=x_in[128 * tile:128 * (tile + 1), :]), S.ldh, 16)
        P.wait("vector", S.ldh)
        P.wait("vector", S.sth)
        P.op("vector", lambda e, tile=tile: e.tensor_add(
            scr1[:], y_acc[:, tile * 1024:(tile + 1) * 1024], b2bc[:]), S.cV)
        P.op("vector", lambda e, tile=tile: e.tensor_single_scalar(
            scr1[:], scr1[:], fcol[:, tile:tile + 1], ALU.mult), S.cV)
        P.op("vector", lambda e: e.tensor_add(scr1[:], scr1[:], xres[:]), S.cV)
        emit_ln_tail(nc, P, S, B, scr1[:],
                     x_out[128 * tile:128 * (tile + 1), :])


def emit_l1_attn(nc, P, S, es, W, CN, CI, x_in, x_out, tag="a1", upto=99, fused=False,
                 bg_dma=None):
    sb = lambda name, shape, dt: es.enter_context(nc.sbuf_tensor(name + tag, shape, dt))
    psa = lambda name, shape, dt: es.enter_context(nc.psum_tensor(name + tag, shape, dt))
    from concourse.bass import IndirectOffsetOnAxis

    x_fm = sb("x_fm", [128, 16384], BF16)
    big2 = sb("big2", [128, 16384], BF16)
    kf = sb("kf", [128, 4096], BF16)
    v_sb = sb("v_sb", [128, 4096], BF16)
    V65 = sb("V65", [128, 4160], BF16)
    E_sb = sb("E_sb", [128, 2048], BF16)
    obf = sb("obf", [65, 512], BF16)
    zrow = sb("zrow", [1, 512], F32)
    zr = sb("zr", [128, 4], F32)
    wbuf = sb("wbuf", [128, 8192], BF16)
    xres = sb("xres", [128, 1024], F32)
    stage2 = sb("stage2", [128, 1024], BF16)
    scr1 = sb("scr1", [128, 1024], F32)
    scr2 = sb("scr2", [128, 1024], F32)
    scr3 = sb("scr3", [128, 1024], F32)
    att_bf = sb("att_bf", [128, 1024], BF16)
    lnfm = sb("lnfm", [128, 1024], BF16)
    og_bf = sb("og_bf", [128, 1024], BF16)
    ogf_bf = sb("ogf_bf", [128, 1024], BF16)
    sq_bf = sb("sq_bf", [128, 1024], BF16)
    bqr = sb("bqr", [1, 1024], BF16)
    bkr = sb("bkr", [1, 256], BF16)
    ones512 = sb("ones512", [1, 512], BF16)
    bvr = sb("bvr", [1, 256], BF16)
    bor = sb("bor", [1, 1024], BF16)
    lng_bc = sb("lng_bc", [128, 1024], F32)
    lnb_bc = sb("lnb_bc", [128, 1024], F32)
    dmask = sb("dmask", [128, 2048], BF16)
    onescol = sb("onescol", [128, 1], BF16)
    onesrow = sb("onesrow", [1, 128], BF16)
    murow = sb("murow", [1, 128], F32)
    rsrow = sb("rsrow", [1, 128], F32)
    varrow = sb("varrow", [1, 128], F32)
    col1 = sb("col1", [128, 1], F32)
    col2 = sb("col2", [128, 1], F32)
    id_bf = sb("id_bf", [128, 128], BF16)
    kidx = sb("kidx", [64, 8], mybir.dt.int32)
    qidx = sb("qidx", [64, 32], mybir.dt.int32)
    vidx = sb("vidx", [128, 64], mybir.dt.int32)
    oidx = sb("oidx", [128, 64], mybir.dt.int32)
    cvals = sb("cvals", [128, 8], F32)
    register_consts(nc, cvals)

    K_sb = x_fm[0:64, 0:8192]
    qh_sb = x_fm[0:64, 8192:16384]

    ps_sc = psa("ps_sc", [128, 2048], F32)
    ps_o = psa("ps_o", [128, 512], F32)
    ps_st = psa("ps_st", [128, 512], F32)
    ps_t = psa("ps_t", [128, 1024], BF16)

    B = dict(og_bf=og_bf[:], ogf_bf=ogf_bf[:], sq_bf=sq_bf[:], ps_t=ps_t,
             ps_st=ps_st, id_bf=id_bf, onescol=onescol, murow=murow,
             rsrow=rsrow, varrow=varrow, col1=col1, col2=col2, cvals=cvals,
             scrO=scr2[:], xres=xres, stage2=stage2[:])

    qkv_d = nc.dram_tensor("qkv_d" + tag, [1536, 2048], BF16).ap()
    qkv_g = nc.dram_tensor("qkv_g" + tag, [6144, 2048], BF16).ap()

    for dst, srcap in [(id_bf[:], CN["ident"]), (onescol[:], CN["ones_col"]),
                       (onesrow[:], CN["ones_row"]), (dmask[:], CN["diag_mask"]),
                       (bvr[:], W["bv"]), (bor[:], W["bo"]), (bqr[:], W["bq"]),
                       (bkr[:], W["bk"]), (ones512[:], CN["ones_row2048"][:, 0:512])]:
        P.op("gpsimd", lambda e, d=dst, s=srcap: e.dma_start(out=d, in_=s), S.lds, 16)
    for dst, srcap in [(cvals[:], CN["constvals"]), (lng_bc[:], W["lng_t"]),
                       (lnb_bc[:], W["lnb_t"]), (kidx[:], CI["kidx"]),
                       (qidx[:], CI["qidx"]), (vidx[:], CI["vidx"]),
                       (oidx[:], CI["oidx"])]:
        P.op("sync", lambda e, d=dst, s=srcap: e.dma_start(out=d, in_=s), S.ldh, 16)
    P.wait("tensor", S.lds)
    P.wait("tensor", S.ldh)

    if upto != 16.5:
        emit_xfm_build(nc, P, S, B, x_in, x_fm)
    P.wait("scalar", S.ldh)
    P.op("scalar", lambda e: e.activation(scr1[:, 0:8], cvals[:, 0:8], AF.Exp), S.cS)
    P.op("scalar", lambda e: e.activation(scr1[:, 0:1], cvals[:, 7:8], AF.Ln), S.cS)
    P.op("scalar", lambda e: e.activation(scr1[:, 0:8], cvals[:, 0:8], AF.Square), S.cS)


    def stream_w(name):
        P.wait("gpsimd", S.cT)
        P.op("gpsimd", lambda e, s=W[name].rearrange("(kc p) n -> p kc n", p=128),
             w=W[name].shape[1]: e.dma_start(
                 out=wbuf[:, 0:8 * w].rearrange("p (kc n) -> p kc n", n=w), in_=s),
             S.lds, 16)
        P.wait("tensor", S.lds)

    if upto == 16.5:
        P.wait("sync", S.cS)
        P.wait("sync", S.cV)
        P.op("sync", lambda e: e.dma_start(
            out=q_d.rearrange("(oc p) t -> p oc t", p=128),
            in_=big2[:].rearrange("p (oc t) -> p oc t", t=2048)), S.sth, 16)
        P.wait("gpsimd", S.sth)
        for din, dout in [(q_d, q_g), (k_d, k_g), (v_d, v_g)]:
            P.op("gpsimd", lambda e, a=din, b=dout: e.collective_compute(
                "AllGather", mybir.AluOpType.bypass,
                replica_groups=[[0, 1, 2, 3], [4, 5, 6, 7]],
                ins=[a.opt()], outs=[b.opt()]), S.cG, 1)
        P.wait("gpsimd", S.cG)
        return
    stream_w("Wq")
    P.wait("tensor", S.cV)
    for oc in range(8):
        for tg in range(4):
            slot = (oc * 4 + tg) % 4
            po = ps_sc[:, 512 * slot:512 * (slot + 1)]
            P.wait("tensor", S.cS)
            P.wait("tensor", S.cV)
            for kc in range(8):
                P.op("tensor", lambda e, po=po, kc=kc, oc=oc, tg=tg: e.matmul(
                    po, wbuf[:, kc * 1024 + oc * 128:kc * 1024 + (oc + 1) * 128],
                    x_fm[:, kc * 2048 + tg * 512:kc * 2048 + (tg + 1) * 512],
                    start=(kc == 0), stop=False), None)
            P.op("tensor", lambda e, po=po, oc=oc: e.matmul(
                po, bqr[0:1, oc * 128:(oc + 1) * 128], ones512[:],
                start=False, stop=True), S.cT)
            P.wait("scalar", S.cT)
            P.op("scalar", lambda e, po=po, oc=oc, tg=tg: e.activation(
                big2[:, oc * 2048 + tg * 512:oc * 2048 + (tg + 1) * 512], po,
                AF.Copy, scale=0.125), S.cS)
    stream_w("Wk")
    for c in range(2):
        for tg in range(4):
            slot = (c * 4 + tg) % 4
            po = ps_sc[:, 512 * slot:512 * (slot + 1)]
            P.wait("tensor", S.cS)
            for kc in range(8):
                P.op("tensor", lambda e, po=po, kc=kc, c=c, tg=tg: e.matmul(
                    po, wbuf[:, kc * 256 + c * 128:kc * 256 + (c + 1) * 128],
                    x_fm[:, kc * 2048 + tg * 512:kc * 2048 + (tg + 1) * 512],
                    start=(kc == 0), stop=False), None)
            P.op("tensor", lambda e, po=po, c=c: e.matmul(
                po, bkr[0:1, c * 128:(c + 1) * 128], ones512[:],
                start=False, stop=True), S.cT)
            P.wait("scalar", S.cT)
            P.op("scalar", lambda e, po=po, c=c, tg=tg: e.activation(
                kf[:, c * 2048 + tg * 512:c * 2048 + (tg + 1) * 512], po,
                AF.Copy), S.cS)
    stream_w("Wv")
    for t in range(NT):
        slot = t % 4
        po = ps_sc[:, 512 * slot:512 * slot + 256]
        P.wait("tensor", S.cV)
        P.wait("tensor", S.cS)
        for kc in range(8):
            P.op("tensor", lambda e, po=po, kc=kc, t=t: e.matmul(
                po, x_fm[:, kc * 2048 + t * 128:kc * 2048 + (t + 1) * 128],
                wbuf[:, kc * 256:(kc + 1) * 256],
                start=(kc == 0), stop=False), None)
        P.op("tensor", lambda e, po=po: e.matmul(
            po, onesrow[:], bvr[:], start=False, stop=True), S.cT)
        P.wait("vector", S.cT)
        P.op("vector", lambda e, po=po, t=t: e.tensor_copy(
            v_sb[:, t * 256:(t + 1) * 256], po), S.cV)

    P.wait("sync", S.cS)
    P.wait("sync", S.cV)
    P.op("sync", lambda e: e.dma_start(
        out=qkv_d[0:1024, :].rearrange("(oc p) t -> p oc t", p=128),
        in_=big2[:].rearrange("p (oc t) -> p oc t", t=2048)), S.sth, 16)
    P.op("sync", lambda e: e.dma_start(
        out=qkv_d[1024:1280, :].rearrange("(c p) t -> p c t", p=128),
        in_=kf[:].rearrange("p (c t) -> p c t", t=2048)), S.sth, 16)
    P.op("sync", lambda e: e.dma_start(
        out=qkv_d[1280:1536, :].rearrange("r (t2 n) -> (r t2) n", n=256)
        .rearrange("(t p) n -> p t n", p=128),
        in_=v_sb[:].rearrange("p (t n) -> p t n", n=256)), S.sth, 16)
    P.wait("gpsimd", S.sth)
    P.op("gpsimd", lambda e: e.collective_compute(
        "AllGather", mybir.AluOpType.bypass,
        replica_groups=[[0, 1, 2, 3], [4, 5, 6, 7]],
        ins=[qkv_d.opt()], outs=[qkv_g.opt()]), S.cG, 1)

    P.wait("vector", S.cG, S.cG.n)
    P.wait("vector", S.sth)
    P.op("vector", lambda e: e.memset(
        V65[:].rearrange("p (kt c) -> p kt c", c=65)[:, :, 64:65], 1.0), S.cV)
    P.wait("gpsimd", S.cG)
    P.wait("gpsimd", S.cV)
    for r in range(4):
        for half in range(2):
            oct_ = r if half == 0 else 7 - r
            P.op("gpsimd", lambda e, r=r, half=half, oct_=oct_: e.indirect_dma_start(
                K_sb[:, oct_ * 1024:(oct_ + 1) * 1024], None,
                qkv_g[:, :],
                IndirectOffsetOnAxis(kidx[:, 2 * r + half:2 * r + half + 1], 1)), S.lds, 16)
    for kt in range(64):
        P.op("gpsimd", lambda e, kt=kt: e.indirect_dma_start(
            V65[:, kt * 65:kt * 65 + 64], None,
            qkv_g[:, :],
            IndirectOffsetOnAxis(vidx[:, kt:kt + 1], 1)), S.lds, 16)

    if upto <= 3:
        return
    nqb = 16
    for gp in range(4):
        P.wait("gpsimd", S.cT)
        for r in range(4):
            for half in range(2):
                oct_ = r if half == 0 else 7 - r
                P.op("gpsimd", lambda e, gp=gp, r=r, half=half, oct_=oct_:
                     e.indirect_dma_start(
                         qh_sb[:, oct_ * 1024:(oct_ + 1) * 1024], None,
                         qkv_g[:, :],
                         IndirectOffsetOnAxis(
                             qidx[:, gp * 8 + 2 * r + half:gp * 8 + 2 * r + half + 1], 1)),
                     S.lds, 16)
        if bg_dma is not None:
            bg_dma(gp)
        P.wait("tensor", S.lds)
        for qb in range(nqb):
            for ktg in range(qb + 1):
                P.wait("tensor", S.cS)
                P.wait("tensor", S.cV)
                for j in range(4):
                    kt = ktg * 4 + j
                    P.op("tensor", lambda e, kt=kt, j=j, qb=qb: e.matmul(
                        ps_sc[:, j * 512:(j + 1) * 512],
                        K_sb[:, kt * 128:(kt + 1) * 128],
                        qh_sb[:, qb * 512:(qb + 1) * 512],
                        start=True, stop=True), S.cT if j == 3 else None)
                P.wait("scalar", S.cT)
                P.op("scalar", lambda e: e.activation(E_sb[:], ps_sc[:], AF.Exp), S.cS)
                if ktg == qb:
                    P.wait("vector", S.cS)
                    P.op("vector", lambda e: e.tensor_mul(E_sb[:], E_sb[:], dmask[:]), S.cV)
                    P.wait("tensor", S.cV)
                P.wait("tensor", S.cS)
                for j in range(4):
                    kt = ktg * 4 + j
                    P.op("tensor", lambda e, kt=kt, j=j, qb=qb, ktg=ktg: e.matmul(
                        ps_o[:, 0:512][0:65, :] if False else ps_o[0:65, 0:512],
                        V65[:, kt * 65:(kt + 1) * 65],
                        E_sb[:, j * 512:(j + 1) * 512],
                        start=(ktg == 0 and j == 0), stop=(ktg == qb and j == 3)),
                        S.cT if (ktg == qb and j == 3) else None)
            P.wait("scalar", S.cT)
            P.wait("scalar", S.cV)
            P.op("scalar", lambda e: e.activation(obf[:], ps_o[0:65, 0:512], AF.Copy), S.cS)
            P.wait("vector", S.cS)
            P.op("vector", lambda e: e.tensor_copy(zrow[:], ps_o[64:65, 0:512]), S.cV)
            P.wait("tensor", S.cV)
            for s_ in range(4):
                P.op("tensor", lambda e, s_=s_: e.matmul(
                    ps_st[0:128, s_:s_ + 1], zrow[0:1, 128 * s_:128 * (s_ + 1)],
                    cvals[0:1, 7:8], start=True, stop=True),
                    S.cT if s_ == 3 else None)
            P.wait("scalar", S.cT)
            P.op("scalar", lambda e: e.activation(zr[:, 0:4], ps_st[0:128, 0:4], AF.Ln), S.cS)
            P.op("scalar", lambda e: e.activation(zr[:, 0:4], zr[:, 0:4], AF.Exp,
                                                  scale=-1.0), S.cS)
            P.wait("tensor", S.cS)
            for s_ in range(4):
                P.op("tensor", lambda e, s_=s_: e.transpose(
                    ps_t[:, 64 * s_:64 * (s_ + 1)], obf[0:64, 128 * s_:128 * (s_ + 1)],
                    id_bf[0:64, 0:64]), S.cT if s_ == 3 else None)
            P.wait("vector", S.cT)
            for s_ in range(4):
                gt = qb * 4 + s_
                P.op("vector", lambda e, s_=s_, gt=gt, gp=gp: e.tensor_single_scalar(
                    big2[:, gt * 256 + gp * 64:gt * 256 + (gp + 1) * 64],
                    ps_t[:, 64 * s_:64 * (s_ + 1)], zr[:, s_:s_ + 1], ALU.mult), S.cV)

    if fused:
        o_d = nc.dram_tensor("o_d" + tag, [8192, 256], BF16).ap()
        o_g = nc.dram_tensor("o_g" + tag, [32768, 256], BF16).ap()
        P.wait("sync", S.cV)
        P.op("sync", lambda e: e.dma_start(
            out=o_d.rearrange("(t p) n -> p t n", p=128),
            in_=big2[:].rearrange("p (t n) -> p t n", n=256)), S.sth, 16)
        P.wait("gpsimd", S.sth)
        P.op("gpsimd", lambda e: e.collective_compute(
            "AllGather", mybir.AluOpType.bypass,
            replica_groups=[[0, 1, 2, 3], [4, 5, 6, 7]],
            ins=[o_d.opt()], outs=[o_g.opt()]), S.cG, 1)
        return o_g
    o_out = nc.dram_tensor("o_out" + tag, [8192, 256], BF16, kind="ExternalOutput").ap()
    P.wait("sync", S.cV)
    P.op("sync", lambda e: e.dma_start(
        out=o_out.rearrange("(t p) n -> p t n", p=128),
        in_=big2[:].rearrange("p (t n) -> p t n", n=256)), S.sth, 16)


def emit_l1_attn_fin(nc, P, S, es, W, CN, attn_in, x_in, x_out, tag="a1f",
                     o_gather=None):
    sb = lambda name, shape, dt: es.enter_context(nc.sbuf_tensor(name + tag, shape, dt))
    psa = lambda name, shape, dt: es.enter_context(nc.psum_tensor(name + tag, shape, dt))
    from concourse.bass import IndirectOffsetOnAxis
    wbuf = sb("wbuf", [128, 8192], BF16)
    xres = sb("xres", [128, 1024], F32)
    stage2 = sb("stage2", [128, 1024], BF16)
    scr1 = sb("scr1", [128, 1024], F32)
    scr2 = sb("scr2", [128, 1024], F32)
    scr3 = sb("scr3", [128, 1024], F32)
    att_bf = sb("att_bf", [128, 1024], BF16)
    lnfm = sb("lnfm", [128, 1024], BF16)
    og_bf = sb("og_bf", [128, 1024], BF16)
    ogf_bf = sb("ogf_bf", [128, 1024], BF16)
    sq_bf = sb("sq_bf", [128, 1024], BF16)
    if o_gather is not None:
        att4 = sb("att4", [128, 1024], BF16)
        aidx = sb("aidx", [128, 64], mybir.dt.int32)
    bor = sb("bor", [1, 1024], BF16)
    lng_bc = sb("lng_bc", [128, 1024], F32)
    lnb_bc = sb("lnb_bc", [128, 1024], F32)
    onescol = sb("onescol", [128, 1], BF16)
    onesrow = sb("onesrow", [1, 128], BF16)
    murow = sb("murow", [1, 128], F32)
    rsrow = sb("rsrow", [1, 128], F32)
    varrow = sb("varrow", [1, 128], F32)
    col1 = sb("col1", [128, 1], F32)
    col2 = sb("col2", [128, 1], F32)
    id_bf = sb("id_bf", [128, 128], BF16)
    cvals = sb("cvals", [128, 8], F32)
    register_consts(nc, cvals)
    ps_sc = psa("ps_sc", [128, 1024], F32)
    ps_st = psa("ps_st", [128, 512], F32)
    ps_t = psa("ps_t", [128, 1024], BF16)
    B = dict(og_bf=og_bf[:], ogf_bf=ogf_bf[:], sq_bf=sq_bf[:], ps_t=ps_t,
             ps_st=ps_st, id_bf=id_bf, onescol=onescol, murow=murow,
             rsrow=rsrow, varrow=varrow, col1=col1, col2=col2, cvals=cvals,
             scrO=scr2[:], xres=xres, stage2=stage2[:])
    for dst, srcap in [(id_bf[:], CN["ident"]), (onescol[:], CN["ones_col"]),
                       (onesrow[:], CN["ones_row"]), (bor[:], W["bo"])]:
        P.op("gpsimd", lambda e, d=dst, s=srcap: e.dma_start(out=d, in_=s), S.lds, 16)
    if o_gather is not None:
        P.op("gpsimd", lambda e: e.dma_start(out=aidx[:], in_=o_gather[1]), S.lds, 16)
    for dst, srcap in [(cvals[:], CN["constvals"]), (lng_bc[:], W["lng_t"]),
                       (lnb_bc[:], W["lnb_t"])]:
        P.op("sync", lambda e, d=dst, s=srcap: e.dma_start(out=d, in_=s), S.ldh, 16)
    P.op("gpsimd", lambda e: e.dma_start(
        out=wbuf[:].rearrange("p (kc n) -> p kc n", n=1024),
        in_=W["Wo"].rearrange("(kc p) n -> p kc n", p=128)), S.lds, 16)
    P.wait("tensor", S.lds)
    P.wait("tensor", S.ldh)
    P.wait("vector", S.lds)
    P.wait("vector", S.ldh)
    P.wait("scalar", S.ldh)
    P.op("scalar", lambda e: e.activation(scr1[:, 0:8], cvals[:, 0:8], AF.Exp), S.cS)
    P.op("scalar", lambda e: e.activation(scr1[:, 0:1], cvals[:, 7:8], AF.Ln), S.cS)
    P.op("scalar", lambda e: e.activation(scr1[:, 0:8], cvals[:, 0:8], AF.Square), S.cS)

    abv = att_bf[:].rearrange("p (g r c) -> p g r c", r=4, c=64)
    if o_gather is not None:
        P.wait("gpsimd", S.cG)
    for tile in range(NT):
        if o_gather is None:
            P.wait("sync", S.cV)
            P.op("sync", lambda e, tile=tile: e.dma_start(
                out=att_bf[:], in_=attn_in[128 * tile:128 * (tile + 1), :]), S.ldh, 16)
            P.wait("vector", S.ldh)
        else:
            P.wait("gpsimd", S.cV)
            for r in range(4):
                P.op("gpsimd", lambda e, tile=tile, r=r: e.indirect_dma_start(
                    att4[:, 256 * r:256 * (r + 1)], None, o_gather[0][:, :],
                    IndirectOffsetOnAxis(aidx[:, 4 * tile + r:4 * tile + r + 1], 1)),
                    S.lds, 16)
            P.wait("vector", S.lds)
            for r in range(4):
                P.op("vector", lambda e, r=r: e.tensor_copy(
                    abv[:, :, r, :],
                    att4[:, 256 * r:256 * (r + 1)].rearrange("p (g c) -> p g c", c=64)),
                    S.cV)
        P.wait("vector", S.sth)
        P.op("vector", lambda e: e.tensor_copy(scr1[:], att_bf[:]), S.cV)
        emit_ln_tail(nc, P, S, B, scr1[:], None, gamma_bc=lng_bc[:],
                     beta_bc=lnb_bc[:], to_sbuf=scr3[:])
        P.wait("vector", S.cS)
        P.op("vector", lambda e: e.tensor_copy(stage2[:], scr3[:]), S.cV)
        P.wait("tensor", S.cV)
        for c in range(8):
            P.op("tensor", lambda e, c=c: e.transpose(
                ps_t[:, 128 * c:128 * (c + 1)], stage2[:, 128 * c:128 * (c + 1)],
                id_bf[:]), S.cT)
        P.wait("vector", S.cT)
        P.op("vector", lambda e: e.tensor_copy(lnfm[:], ps_t[:, 0:1024]), S.cV)
        P.wait("tensor", S.cV)
        P.wait("sync", S.cV)
        P.op("sync", lambda e, tile=tile: e.dma_start(
            out=xres[:], in_=x_in[128 * tile:128 * (tile + 1), :]), S.ldh, 16)
        for dg in range(2):
            po = ps_sc[:, dg * 512:(dg + 1) * 512]
            for kc in range(8):
                P.op("tensor", lambda e, po=po, kc=kc, dg=dg: e.matmul(
                    po, lnfm[:, kc * 128:(kc + 1) * 128],
                    wbuf[:, kc * 1024 + dg * 512:kc * 1024 + (dg + 1) * 512],
                    start=(kc == 0), stop=False), None)
            P.op("tensor", lambda e, po=po, dg=dg: e.matmul(
                po, onesrow[:], bor[0:1, dg * 512:(dg + 1) * 512],
                start=False, stop=True), S.cT)
        P.wait("vector", S.cT)
        P.wait("vector", S.ldh)
        P.op("vector", lambda e: e.tensor_add(scr1[:], ps_sc[:, 0:1024], xres[:]), S.cV)
        emit_ln_tail(nc, P, S, B, scr1[:], x_out[128 * tile:128 * (tile + 1), :])


def emit_l1_ffn(nc, P, S, es, W, CN, x_in, x_out, tag="f1", precast=None):
    sb = lambda name, shape, dt: es.enter_context(nc.sbuf_tensor(name + tag, shape, dt))
    psa = lambda name, shape, dt: es.enter_context(nc.psum_tensor(name + tag, shape, dt))

    x_fm = sb("x_fm", [128, 16384], BF16)
    y_acc = sb("y_acc", [128, 16384], F32)
    w1t = sb("w1t", [128, 8192], BF16)
    w2t = sb("w2t", [128, 8192], BF16)
    w3p = [sb("w3a", [128, 8192], BF16), sb("w3b", [128, 8192], BF16)]
    s1_bf = sb("s1_bf", [128, 4096], BF16)
    g_fm = sb("g_fm", [128, 4096], BF16)
    wgt = sb("wgt", [128, 64], BF16)
    rt = sb("rt", [128, 128], F32)
    wsb = sb("wsb", [128, 128], F32)
    m1 = sb("m1", [128, 16], F32)
    m2 = sb("m2", [128, 16], F32)
    tmp8 = sb("tmp8", [128, 128], F32)
    tmp8b = sb("tmp8b", [128, 128], F32)
    p12 = sb("p12", [128, 96], F32)
    xres = sb("xres", [128, 1024], F32)
    stage2 = sb("stage2", [128, 1024], BF16)
    scr1 = sb("scr1", [128, 1024], F32)
    scr2 = sb("scr2", [128, 1024], F32)
    og_bf = sb("og_bf", [128, 1024], BF16)
    ogf_bf = sb("ogf_bf", [128, 1024], BF16)
    sq_bf = sb("sq_bf", [128, 1024], BF16)
    onescol = sb("onescol", [128, 1], BF16)
    murow = sb("murow", [1, 128], F32)
    rsrow = sb("rsrow", [1, 128], F32)
    varrow = sb("varrow", [1, 128], F32)
    col1 = sb("col1", [128, 1], F32)
    col2 = sb("col2", [128, 1], F32)
    id_bf = sb("id_bf", [128, 128], BF16)
    cvals = sb("cvals", [128, 8], F32)
    register_consts(nc, cvals)

    ps_h = psa("ps_h", [128, 2048], F32)
    ps_y = psa("ps_y", [128, 1024], F32)
    ps_st = psa("ps_st", [128, 512], F32)
    ps_t = psa("ps_t", [128, 1024], BF16)

    B = dict(og_bf=og_bf[:], ogf_bf=ogf_bf[:], sq_bf=sq_bf[:], ps_t=ps_t,
             ps_st=ps_st, id_bf=id_bf, onescol=onescol, murow=murow,
             rsrow=rsrow, varrow=varrow, col1=col1, col2=col2, cvals=cvals,
             scrO=scr2[:], xres=xres, stage2=stage2[:])

    if precast is None:
        w1bf = nc.dram_tensor("w1bf" + tag, [16384, 1024], BF16).ap()
        w2bf = nc.dram_tensor("w2bf" + tag, [16384, 1024], BF16).ap()
        w3bf = nc.dram_tensor("w3bf" + tag, [8192, 2048], BF16).ap()

        for e_ in range(8):
            P.op("gpsimd", lambda e, e_=e_: e.dma_start(
                out=w1bf[2048 * e_:2048 * (e_ + 1), :],
                in_=W["W1"][2048 * e_:2048 * (e_ + 1), :]), S.lds, 16)
            P.op("gpsimd", lambda e, e_=e_: e.dma_start(
                out=w2bf[2048 * e_:2048 * (e_ + 1), :],
                in_=W["W2"][2048 * e_:2048 * (e_ + 1), :]), S.lds, 16)
            P.op("gpsimd", lambda e, e_=e_: e.dma_start(
                out=w3bf[1024 * e_:1024 * (e_ + 1), :],
                in_=W["W3"][1024 * e_:1024 * (e_ + 1), :]), S.lds, 16)
    else:
        w1bf, w2bf, w3bf = precast
    for dst, srcap in [(id_bf[:], CN["ident"]), (onescol[:], CN["ones_col"]),
                       (wgt[:].rearrange("p (kc e) -> p kc e", e=8),
                        W["WgT"].rearrange("(kc p) e -> p kc e", p=128))]:
        P.op("gpsimd", lambda e, d=dst, s=srcap: e.dma_start(out=d, in_=s), S.lds, 16)
    P.op("sync", lambda e: e.dma_start(out=cvals[:], in_=CN["constvals"]), S.ldh, 16)
    P.wait("tensor", S.lds)
    P.wait("tensor", S.ldh)

    emit_xfm_build(nc, P, S, B, x_in, x_fm)

    P.wait("scalar", S.ldh)
    P.op("scalar", lambda e: e.activation(tmp8[:, 0:8], cvals[:, 0:8], AF.Exp), S.cS)
    P.op("scalar", lambda e: e.activation(tmp8[:, 0:1], cvals[:, 7:8], AF.Ln), S.cS)
    P.op("scalar", lambda e: e.activation(tmp8[:, 0:8], cvals[:, 0:8], AF.Silu), S.cS)
    P.op("scalar", lambda e: e.activation(tmp8[:, 0:8], cvals[:, 0:8], AF.Square), S.cS)

    P.wait("tensor", S.cV)
    for t in range(NT):
        P.wait("tensor", S.cV)
        for kc in range(8):
            P.op("tensor", lambda e, t=t, kc=kc: e.matmul(
                ps_y[:, 8 * t:8 * (t + 1)],
                x_fm[:, kc * 2048 + t * 128:kc * 2048 + (t + 1) * 128],
                wgt[:, kc * 8:(kc + 1) * 8],
                start=(kc == 0), stop=(kc == 7)), S.cT if kc == 7 else None)
    P.wait("vector", S.cT)
    P.op("vector", lambda e: e.tensor_copy(rt[:], ps_y[:, 0:128]), S.cV)
    r3 = rt[:].rearrange("p (t e) -> p t e", e=8)
    t3 = tmp8[:].rearrange("p (t e) -> p t e", e=8)
    u3 = tmp8b[:].rearrange("p (t e) -> p t e", e=8)
    w3 = wsb[:].rearrange("p (t e) -> p t e", e=8)
    m1v = m1[:].rearrange("p (t o) -> p t o", o=1)
    m2v = m2[:].rearrange("p (t o) -> p t o", o=1)
    jk = lambda: P.op("vector", lambda e: e.tensor_copy(scr1[:, 0:128], rt[:]), S.cV)
    jk()
    P.op("vector", lambda e: e.tensor_tensor(t3[:, :, 0:4], r3[:, :, 0:4], r3[:, :, 4:8], ALU.max), S.cV)
    jk()
    P.op("vector", lambda e: e.tensor_tensor(t3[:, :, 0:2], t3[:, :, 0:2], t3[:, :, 2:4], ALU.max), S.cV)
    jk()
    P.op("vector", lambda e: e.tensor_tensor(m1v, t3[:, :, 0:1], t3[:, :, 1:2], ALU.max), S.cV)
    jk()
    P.op("vector", lambda e: e.tensor_tensor(u3, r3, m1v.broadcast_to([128, 16, 8]), ALU.is_ge), S.cV)
    jk()
    P.op("vector", lambda e: e.scalar_tensor_tensor(t3, u3, -1e30, r3, ALU.mult, ALU.add), S.cV)
    jk()
    P.op("vector", lambda e: e.tensor_tensor(t3[:, :, 0:4], t3[:, :, 0:4], t3[:, :, 4:8], ALU.max), S.cV)
    jk()
    P.op("vector", lambda e: e.tensor_tensor(t3[:, :, 0:2], t3[:, :, 0:2], t3[:, :, 2:4], ALU.max), S.cV)
    jk()
    P.op("vector", lambda e: e.tensor_tensor(m2v, t3[:, :, 0:1], t3[:, :, 1:2], ALU.max), S.cV)
    jk()
    P.op("vector", lambda e: e.tensor_sub(p12[:, 0:16], m2[:], m1[:]), S.cV)
    jk()
    jk()
    P.wait("scalar", S.cV)
    P.op("scalar", lambda e: e.activation(p12[:, 16:32], p12[:, 0:16], AF.Exp), S.cS)
    P.op("scalar", lambda e: e.activation(p12[:, 32:48], p12[:, 16:32], AF.Ln, bias=1.0), S.cS)
    P.op("scalar", lambda e: e.activation(p12[:, 48:64], p12[:, 32:48], AF.Exp, scale=-1.0), S.cS)
    P.op("scalar", lambda e: e.copy(tmp8[:, 112:128], p12[:, 48:64]), S.cS)

    def router_part2():
        P.wait("vector", S.cS)
        P.op("vector", lambda e: e.tensor_tensor(t3, r3, m2v.broadcast_to([128, 16, 8]), ALU.is_ge), S.cV)
        P.op("vector", lambda e: e.tensor_mul(
            w3, u3, p12[:, 48:64].rearrange("p (t o) -> p t o", o=1).broadcast_to([128, 16, 8])), S.cV)
        P.op("vector", lambda e: e.tensor_mul(p12[:, 64:80], p12[:, 16:32], p12[:, 48:64]), S.cV)
        P.op("vector", lambda e: e.tensor_scalar(tmp8b[:], tmp8b[:], -1.0, 1.0, ALU.mult, ALU.add), S.cV)
        jk()
        P.op("vector", lambda e: e.tensor_mul(t3, t3, u3), S.cV)
        jk()
        P.op("vector", lambda e: e.tensor_mul(
            t3, t3, p12[:, 64:80].rearrange("p (t o) -> p t o", o=1).broadcast_to([128, 16, 8])), S.cV)
        jk()
        P.op("vector", lambda e: e.tensor_add(w3, w3, t3), S.cV)
        jk()
        jk()

    P.wait("gpsimd", S.cT)
    first = True
    if precast is None:
        P.wait("sync", S.lds)
    else:
        P.wait("sync", S.bgc)

    def issue_w12(e_, hf):
        for dc in range(8):
            P.op("sync", lambda e, e_=e_, hf=hf, dc=dc: e.dma_start(
                out=w1t[:, 1024 * dc:1024 * (dc + 1)],
                in_=w1bf[2048 * e_ + 1024 * hf:2048 * e_ + 1024 * (hf + 1),
                         128 * dc:128 * (dc + 1)], transpose=True), S.ldh, 16)
            P.op("sync", lambda e, e_=e_, hf=hf, dc=dc: e.dma_start(
                out=w2t[:, 1024 * dc:1024 * (dc + 1)],
                in_=w2bf[2048 * e_ + 1024 * hf:2048 * e_ + 1024 * (hf + 1),
                         128 * dc:128 * (dc + 1)], transpose=True), S.ldh, 16)

    def issue_w3(e_, hf, buf):
        for dc in range(8):
            P.op("sync", lambda e, e_=e_, hf=hf, dc=dc, buf=buf: e.dma_start(
                out=buf[:, 1024 * dc:1024 * (dc + 1)],
                in_=w3bf[1024 * e_:1024 * (e_ + 1),
                         1024 * hf + 128 * dc:1024 * hf + 128 * (dc + 1)],
                transpose=True), S.ldh, 16)

    issue_w12(0, 0)
    issue_w3(0, 0, w3p[0])
    for it in range(16):
        e_, hf = divmod(it, 2)
        w3t = w3p[it % 2]
        P.wait("tensor", S.ldh)
        if it + 1 < 16:
            en, hn = divmod(it + 1, 2)
            P.wait("sync", S.cT)
            issue_w3(en, hn, w3p[(it + 1) % 2])
        for tg in range(4):
            for hc in range(8):
                slot = hc % 2
                po1 = ps_h[:, 1024 * slot:1024 * slot + 512]
                po2 = ps_h[:, 1024 * slot + 512:1024 * slot + 1024]
                P.wait("tensor", S.cV, max(S.cV.n - 1, 0))
                for kc in range(8):
                    P.op("tensor", lambda e, po1=po1, kc=kc, hc=hc, tg=tg: e.matmul(
                        po1, w1t[:, kc * 1024 + hc * 128:kc * 1024 + (hc + 1) * 128],
                        x_fm[:, kc * 2048 + tg * 512:kc * 2048 + (tg + 1) * 512],
                        start=(kc == 0), stop=(kc == 7)), None)
                for kc in range(8):
                    P.op("tensor", lambda e, po2=po2, kc=kc, hc=hc, tg=tg: e.matmul(
                        po2, w2t[:, kc * 1024 + hc * 128:kc * 1024 + (hc + 1) * 128],
                        x_fm[:, kc * 2048 + tg * 512:kc * 2048 + (tg + 1) * 512],
                        start=(kc == 0), stop=(kc == 7)),
                        S.cT if kc == 7 else None)
                P.wait("scalar", S.cT)
                P.op("scalar", lambda e, po1=po1, hc=hc: e.activation(
                    s1_bf[:, hc * 512:(hc + 1) * 512], po1, AF.Silu), S.cS)
                P.wait("vector", S.cS)
                P.op("vector", lambda e, po2=po2, hc=hc: e.tensor_mul(
                    g_fm[:, hc * 512:(hc + 1) * 512],
                    s1_bf[:, hc * 512:(hc + 1) * 512], po2), S.cV)
            if e_ == 0 and hf == 0 and tg == 0:
                router_part2()
            if tg == 3 and it + 1 < 16:
                en, hn = divmod(it + 1, 2)
                P.wait("sync", S.cT)
                issue_w12(en, hn)
            P.wait("tensor", S.cV)
            for tt in range(4):
                tile = tg * 4 + tt
                for dg in range(2):
                    po = ps_y[:, dg * 512:(dg + 1) * 512]
                    P.wait("tensor", S.cV, max(S.cV.n - 1, 0))
                    for hc in range(8):
                        P.op("tensor", lambda e, po=po, hc=hc, tt=tt, dg=dg,
                             w3t=w3t: e.matmul(
                                 po, g_fm[:, hc * 512 + tt * 128:hc * 512 + tt * 128 + 128],
                                 w3t[:, hc * 1024 + dg * 512:hc * 1024 + (dg + 1) * 512],
                                 start=(hc == 0), stop=(hc == 7)), S.cT if hc == 7 else None)
                    P.wait("vector", S.cT)
                    ya = y_acc[:, tile * 1024 + dg * 512:tile * 1024 + (dg + 1) * 512]
                    wcol = wsb[:, tile * 8 + e_:tile * 8 + e_ + 1]
                    if first:
                        P.op("vector", lambda e, po=po, ya=ya, wcol=wcol:
                             e.tensor_single_scalar(ya, po, wcol, ALU.mult), S.cV)
                    else:
                        P.op("vector", lambda e, po=po, ya=ya, wcol=wcol:
                             e.scalar_tensor_tensor(ya, po, wcol, ya,
                                                    ALU.mult, ALU.add), S.cV)
        first = False

    if DBG_L1F:
        dwv = nc.dram_tensor("dwv" + tag, [128, 128], F32, kind="ExternalOutput").ap()
        dya = nc.dram_tensor("dya" + tag, [128, 16384], F32, kind="ExternalOutput").ap()
        drt = nc.dram_tensor("drt" + tag, [128, 128], F32, kind="ExternalOutput").ap()
        dm1 = nc.dram_tensor("dm1" + tag, [128, 16], F32, kind="ExternalOutput").ap()
        dm2 = nc.dram_tensor("dm2" + tag, [128, 16], F32, kind="ExternalOutput").ap()
        dp12 = nc.dram_tensor("dp12" + tag, [128, 80], F32, kind="ExternalOutput").ap()
        du3 = nc.dram_tensor("du3" + tag, [128, 128], F32, kind="ExternalOutput").ap()
        P.wait("gpsimd", S.cV)
        P.op("gpsimd", lambda e: e.dma_start(out=dwv, in_=wsb[:]), S.lds, 16)
        P.op("gpsimd", lambda e: e.dma_start(out=dya, in_=y_acc[:]), S.lds, 16)
        P.op("gpsimd", lambda e: e.dma_start(out=drt, in_=rt[:]), S.lds, 16)
        P.op("gpsimd", lambda e: e.dma_start(out=dm1, in_=m1[:]), S.lds, 16)
        P.op("gpsimd", lambda e: e.dma_start(out=dm2, in_=m2[:]), S.lds, 16)
        P.op("gpsimd", lambda e: e.dma_start(out=dp12, in_=p12[:, 0:80]), S.lds, 16)
        P.op("gpsimd", lambda e: e.dma_start(out=du3, in_=tmp8b[:]), S.lds, 16)
        P.wait("vector", S.lds)

    for tile in range(NT):
        P.wait("sync", S.cV)
        P.op("sync", lambda e, tile=tile: e.dma_start(
            out=xres[:], in_=x_in[128 * tile:128 * (tile + 1), :]), S.ldh, 16)
        P.wait("vector", S.ldh)
        P.wait("vector", S.sth)
        P.op("vector", lambda e, tile=tile: e.tensor_add(
            scr1[:], y_acc[:, tile * 1024:(tile + 1) * 1024], xres[:]), S.cV)
        emit_ln_tail(nc, P, S, B, scr1[:], x_out[128 * tile:128 * (tile + 1), :])



def declare_weights_l0a(nc):
    W = {}
    for n, shp in [("Wq", [1024, 1024]), ("Wk", [1024, 1024]), ("Wv", [1024, 1024]),
                   ("Wg", [1024, 1024]), ("Wo", [1024, 1024]), ("Wg1", [1024, 16]),
                   ("Wg2", [16, 1024]), ("bg2", [1, 1024]), ("gnw_tiled", [128, 1024])]:
        W[n] = nc.dram_tensor("l0a_" + n, shp, F32, kind="ExternalInput").ap()
    return W


def declare_consts(nc, names=("ident", "t1u", "mask_incl", "ones_col", "ones_row",
                              "ones_row2048", "diag_mask", "constvals")):
    shapes = {"ident": [128, 128], "t1u": [128, 128], "mask_incl": [128, 128],
              "ones_col": [128, 1], "ones_row": [1, 128],
              "ones_row2048": [1, 2048], "diag_mask": [128, 2048],
              "constvals": [128, 8]}
    return {n: nc.dram_tensor("cn_" + n, shapes[n], F32, kind="ExternalInput").ap()
            for n in names}


def l1a_core_indices(core):
    s, g, octs = core_layout(core)
    p64 = np.arange(64, dtype=np.int32)[:, None]
    p128 = np.arange(128, dtype=np.int32)[:, None]
    kidx = np.concatenate([(1536 * r + 1024 + 64 * g + p64) * 2048 + 1024 * half
                           for r in range(4) for half in range(2)], axis=1)
    qidx = np.concatenate([(1536 * r + 64 * (4 * gp + g) + p64) * 2048 + 1024 * half
                           for gp in range(4) for r in range(4) for half in range(2)],
                          axis=1)
    vcols = []
    for kt in range(64):
        oct_ = kt // 8
        rank, off = (oct_, 0) if oct_ < 4 else (7 - oct_, 1024)
        loc = off + (kt % 8) * 128 + p128
        vcols.append((1536 * rank + 1280) * 2048 + loc * 256 + 64 * g)
    vidx = np.concatenate(vcols, axis=1).astype(np.int32)
    ocols = []
    for r in range(4):
        for tile in range(16):
            ocols.append((8192 * r + 2048 * g + tile * 128 + p128) * 256)
    oidx = np.concatenate(ocols, axis=1).astype(np.int32)
    acols = []
    for tile in range(16):
        grow0 = 1024 * octs[tile // 8] + 128 * (tile % 8)
        for r in range(4):
            acols.append((8192 * r + grow0 + p128) * 256)
    aidx = np.concatenate(acols, axis=1).astype(np.int32)
    return dict(kidx=kidx, qidx=qidx, vidx=vidx, oidx=oidx, aidx=aidx)


def stage_barrier(P, S):
    for eng in ("sync", "gpsimd"):
        for sem in (S.cT, S.cV, S.cS, S.sth, S.lds, S.ldh):
            P.wait(eng, sem)
    P.wait("tensor", S.cV)
    P.wait("tensor", S.cS)
    P.wait("vector", S.cT)
    P.wait("vector", S.cS)
    P.wait("scalar", S.cT)
    P.wait("scalar", S.cV)


def build_fused_launch():
    nc = bass.Bass("TRN2", target_bir_lowering=False, debug=False,
                   detect_race_conditions=False)
    x_in = nc.dram_tensor("x_in", [TOK, DM], F32, kind="ExternalInput").ap()
    x_out = nc.dram_tensor("x_out", [TOK, DM], F32, kind="ExternalOutput").ap()
    kd = "ExternalOutput" if DBG_FUSED else "Internal"
    x1d = nc.dram_tensor("x1d", [TOK, DM], F32, kind=kd).ap()
    x2d = nc.dram_tensor("x2d", [TOK, DM], F32, kind=kd).ap()
    x3d = nc.dram_tensor("x3d", [TOK, DM], F32, kind=kd).ap()
    W0a = declare_weights_l0a(nc)
    CN = declare_consts(nc)
    CI0 = {"theta": nc.dram_tensor("theta", [64, 128], F32, kind="ExternalInput").ap(),
           "cbias": nc.dram_tensor("cbias", [64, 16], F32, kind="ExternalInput").ap()}
    W0f = {n: nc.dram_tensor("f0_" + n, shp, F32, kind="ExternalInput").ap()
           for n, shp in [("router_w", [1024, 8]), ("w1", [1024, 4096]),
                          ("b1", [1, 4096]), ("w2", [4096, 1024]),
                          ("b2_t", [128, 1024])]}
    W1a = {n: nc.dram_tensor("a1_" + n, shp, F32, kind="ExternalInput").ap()
           for n, shp in [("Wq", [1024, 1024]), ("Wk", [1024, 256]),
                          ("Wv", [1024, 256]), ("Wo", [1024, 1024]),
                          ("bq", [1, 1024]), ("bk", [1, 256]), ("bv", [1, 256]),
                          ("bo", [1, 1024]), ("lng_t", [128, 1024]),
                          ("lnb_t", [128, 1024])]}
    CI1 = {n: nc.dram_tensor("a1_" + n, shp, mybir.dt.int32, kind="ExternalInput").ap()
           for n, shp in [("kidx", [64, 8]), ("qidx", [64, 32]),
                          ("vidx", [128, 64]), ("oidx", [128, 64]),
                          ("aidx", [128, 64])]}
    W1f = {n: nc.dram_tensor("f1_" + n, shp, F32, kind="ExternalInput").ap()
           for n, shp in [("WgT", [1024, 8]), ("W1", [16384, 1024]),
                          ("W2", [16384, 1024]), ("W3", [8192, 2048])]}
    with contextlib.ExitStack() as es:
        S = Sems(nc, es)
        P = Plan()
        cvg = es.enter_context(nc.sbuf_tensor("cvals_glob", [128, 8], F32))
        register_consts(nc, cvg)
        P.op("sync", lambda e: e.dma_start(out=cvg[:], in_=CN["constvals"]), S.ldh, 16)
        with contextlib.ExitStack() as es1:
            emit_l0_attn(nc, P, S, es1, W0a, CN, CI0, x_in, x1d)
        stage_barrier(P, S)
        with contextlib.ExitStack() as es2:
            emit_l0_ffn(nc, P, S, es2, W0f, CN, x1d, x2d)
        stage_barrier(P, S)
        w1bf = nc.dram_tensor("w1bff1", [16384, 1024], BF16).ap()
        w2bf = nc.dram_tensor("w2bff1", [16384, 1024], BF16).ap()
        w3bf = nc.dram_tensor("w3bff1", [8192, 2048], BF16).ap()

        def bg_casts(gp):
            for e_ in (2 * gp, 2 * gp + 1):
                P.op("gpsimd", lambda e, e_=e_: e.dma_start(
                    out=w1bf[2048 * e_:2048 * (e_ + 1), :],
                    in_=W1f["W1"][2048 * e_:2048 * (e_ + 1), :]), S.bgc, 16)
                P.op("gpsimd", lambda e, e_=e_: e.dma_start(
                    out=w2bf[2048 * e_:2048 * (e_ + 1), :],
                    in_=W1f["W2"][2048 * e_:2048 * (e_ + 1), :]), S.bgc, 16)
                P.op("gpsimd", lambda e, e_=e_: e.dma_start(
                    out=w3bf[1024 * e_:1024 * (e_ + 1), :],
                    in_=W1f["W3"][1024 * e_:1024 * (e_ + 1), :]), S.bgc, 16)

        with contextlib.ExitStack() as es3:
            o_g = emit_l1_attn(nc, P, S, es3, W1a, CN, CI1, x2d, None, fused=True,
                               bg_dma=bg_casts)
        stage_barrier(P, S)
        with contextlib.ExitStack() as es4:
            emit_l1_attn_fin(nc, P, S, es4, W1a, CN, None, x2d, x3d,
                             o_gather=(o_g, CI1["aidx"]))
        stage_barrier(P, S)
        with contextlib.ExitStack() as es5:
            emit_l1_ffn(nc, P, S, es5, W1f, CN, x3d, x_out,
                        precast=(w1bf, w2bf, w3bf))
        P.wait("gpsimd", S.sth)
        P.wait("gpsimd", S.lds)
        P.wait("gpsimd", S.ldh)
        P.wait("sync", S.sth)
        P.wait("sync", S.lds)
        P.emit(nc)
    return nc


def build_l1_attn_launch(upto=99):
    nc = bass.Bass("TRN2", target_bir_lowering=False, debug=False,
                   detect_race_conditions=False)
    x_in = nc.dram_tensor("x_in", [TOK, DM], F32, kind="ExternalInput").ap()
    x_out = None
    W = {}
    for n, shp in [("Wq", [1024, 1024]), ("Wk", [1024, 256]), ("Wv", [1024, 256]),
                   ("Wo", [1024, 1024]), ("bq", [1, 1024]), ("bk", [1, 256]),
                   ("bv", [1, 256]), ("bo", [1, 1024]),
                   ("lng_t", [128, 1024]), ("lnb_t", [128, 1024])]:
        W[n] = nc.dram_tensor("a1_" + n, shp, F32, kind="ExternalInput").ap()
    CN = declare_consts(nc)
    CI = {n: nc.dram_tensor("a1_" + n, shp, mybir.dt.int32, kind="ExternalInput").ap()
          for n, shp in [("kidx", [64, 8]), ("qidx", [64, 32]),
                         ("vidx", [128, 64]), ("oidx", [128, 64])]}
    with contextlib.ExitStack() as es:
        S = Sems(nc, es)
        P = Plan()
        emit_l1_attn(nc, P, S, es, W, CN, CI, x_in, x_out, upto=upto)
        P.wait("gpsimd", S.sth)
        P.wait("gpsimd", S.lds)
        P.wait("sync", S.sth)
        P.emit(nc)
    return nc


def build_l1_attn_fin_launch():
    nc = bass.Bass("TRN2", target_bir_lowering=False, debug=False,
                   detect_race_conditions=False)
    attn_in = nc.dram_tensor("attn_in", [TOK, DM], BF16, kind="ExternalInput").ap()
    x_in = nc.dram_tensor("x_in", [TOK, DM], F32, kind="ExternalInput").ap()
    x_out = nc.dram_tensor("x_out", [TOK, DM], F32, kind="ExternalOutput").ap()
    W = {}
    for n, shp in [("Wo", [1024, 1024]), ("bo", [1, 1024]),
                   ("lng_t", [128, 1024]), ("lnb_t", [128, 1024])]:
        W[n] = nc.dram_tensor("a1_" + n, shp, F32, kind="ExternalInput").ap()
    CN = declare_consts(nc)
    with contextlib.ExitStack() as es:
        S = Sems(nc, es)
        P = Plan()
        emit_l1_attn_fin(nc, P, S, es, W, CN, attn_in, x_in, x_out)
        P.wait("gpsimd", S.sth)
        P.wait("gpsimd", S.lds)
        P.wait("sync", S.sth)
        P.emit(nc)
    return nc


def assemble_attn_for_core(o_results, core):
    s, g, octs = core_layout(core)
    import ml_dtypes
    attn = np.empty((TOK, DM), dtype=o_results[0].dtype)
    rows = np.concatenate([np.arange(1024 * j, 1024 * (j + 1)) for j in octs])
    for r in range(4):
        src = o_results[4 * s + r][rows]
        for gp in range(4):
            attn[:, 64 * (4 * gp + r):64 * (4 * gp + r + 1)] = src[:, 64 * gp:64 * (gp + 1)]
    return np.ascontiguousarray(attn)


def l1a_weight_arrays(inputs):
    i = {k: np.ascontiguousarray(np.asarray(v), dtype=np.float32)
         for k, v in inputs.items()}
    return {
        "a1_Wq": i["L1_attn_Wq"], "a1_Wk": i["L1_attn_Wk"], "a1_Wv": i["L1_attn_Wv"],
        "a1_Wo": i["L1_attn_Wo"],
        "a1_bq": i["L1_attn_bq"].reshape(1, 1024),
        "a1_bk": i["L1_attn_bk"].reshape(1, 256),
        "a1_bv": i["L1_attn_bv"].reshape(1, 256),
        "a1_bo": i["L1_attn_bo"].reshape(1, 1024),
        "a1_lng_t": np.ascontiguousarray(np.tile(i["L1_attn_ln_g"].reshape(1, 1024), (128, 1))),
        "a1_lnb_t": np.ascontiguousarray(np.tile(i["L1_attn_ln_b"].reshape(1, 1024), (128, 1))),
    }


def build_l1_ffn_launch():
    nc = bass.Bass("TRN2", target_bir_lowering=False, debug=False,
                   detect_race_conditions=False)
    x_in = nc.dram_tensor("x_in", [TOK, DM], F32, kind="ExternalInput").ap()
    x_out = nc.dram_tensor("x_out", [TOK, DM], F32, kind="ExternalOutput").ap()
    W = {}
    for n, shp in [("WgT", [1024, 8]), ("W1", [16384, 1024]),
                   ("W2", [16384, 1024]), ("W3", [8192, 2048])]:
        W[n] = nc.dram_tensor("f1_" + n, shp, F32, kind="ExternalInput").ap()
    CN = declare_consts(nc)
    with contextlib.ExitStack() as es:
        S = Sems(nc, es)
        P = Plan()
        emit_l1_ffn(nc, P, S, es, W, CN, x_in, x_out)
        P.wait("gpsimd", S.sth)
        P.wait("gpsimd", S.lds)
        P.wait("gpsimd", S.ldh)
        P.wait("sync", S.sth)
        P.emit(nc)
    return nc


def l1f_weight_arrays(inputs):
    i = {k: np.ascontiguousarray(np.asarray(v), dtype=np.float32)
         for k, v in inputs.items()}
    return {
        "f1_WgT": np.ascontiguousarray(i["L1_ffn_Wg"].T),
        "f1_W1": i["L1_ffn_W1"].reshape(16384, 1024),
        "f1_W2": i["L1_ffn_W2"].reshape(16384, 1024),
        "f1_W3": i["L1_ffn_W3"].reshape(8192, 2048),
    }


def build_l0_ffn_launch():
    nc = bass.Bass("TRN2", target_bir_lowering=False, debug=False,
                   detect_race_conditions=False)
    x_in = nc.dram_tensor("x_in", [TOK, DM], F32, kind="ExternalInput").ap()
    x_out = nc.dram_tensor("x_out", [TOK, DM], F32, kind="ExternalOutput").ap()
    W = {}
    for n, shp in [("router_w", [1024, 8]), ("w1", [1024, 4096]), ("b1", [1, 4096]),
                   ("w2", [4096, 1024]), ("b2_t", [128, 1024])]:
        W[n] = nc.dram_tensor("f0_" + n, shp, F32, kind="ExternalInput").ap()
    CN = declare_consts(nc)
    with contextlib.ExitStack() as es:
        S = Sems(nc, es)
        P = Plan()
        emit_l0_ffn(nc, P, S, es, W, CN, x_in, x_out)
        P.wait("gpsimd", S.sth)
        P.wait("gpsimd", S.lds)
        P.wait("sync", S.sth)
        P.emit(nc)
    return nc


def l0f_weight_arrays(inputs):
    i = {k: np.ascontiguousarray(np.asarray(v), dtype=np.float32)
         for k, v in inputs.items()}
    return {
        "f0_router_w": i["L0_ffn_router_w"], "f0_w1": i["L0_ffn_w1"],
        "f0_b1": i["L0_ffn_b1"].reshape(1, 4096), "f0_w2": i["L0_ffn_w2"],
        "f0_b2_t": np.ascontiguousarray(np.tile(i["L0_ffn_b2"].reshape(1, 1024), (128, 1))),
    }


def build_l0_attn_launch(upto=99, p2cut=99, dbg=False):
    nc = bass.Bass("TRN2", target_bir_lowering=False, debug=False,
                   detect_race_conditions=False)
    x_in = nc.dram_tensor("x_in", [TOK, DM], F32, kind="ExternalInput").ap()
    x_out = nc.dram_tensor("x_out", [TOK, DM], F32, kind="ExternalOutput").ap()
    W = declare_weights_l0a(nc)
    CN = declare_consts(nc)
    CI = {"theta": nc.dram_tensor("theta", [64, 128], F32, kind="ExternalInput").ap(),
          "cbias": nc.dram_tensor("cbias", [64, 16], F32, kind="ExternalInput").ap()}
    with contextlib.ExitStack() as es:
        S = Sems(nc, es)
        P = Plan()
        emit_l0_attn(nc, P, S, es, W, CN, CI, x_in, x_out, upto=upto, p2cut=p2cut, dbg=dbg)
        if upto < 99 or p2cut < 99:
            P.wait("sync", S.cV)
            P.wait("sync", S.cT)
            P.wait("sync", S.cS)
            P.op("sync", lambda e: e.dma_start(out=x_out[0:128, :], in_=x_in[0:128, :]),
                 S.sth, 16)
        P.wait("gpsimd", S.sth)
        P.wait("gpsimd", S.lds)
        P.wait("sync", S.sth)
        P.wait("sync", S.lds)
        P.emit(nc)
    return nc


def l0a_weight_arrays(inputs):
    i = {k: np.ascontiguousarray(np.asarray(v), dtype=np.float32)
         for k, v in inputs.items()}
    return {
        "l0a_Wq": i["L0_attn_Wq"], "l0a_Wk": i["L0_attn_Wk"], "l0a_Wv": i["L0_attn_Wv"],
        "l0a_Wg": i["L0_attn_Wg"], "l0a_Wo": i["L0_attn_Wo"], "l0a_Wg1": i["L0_attn_Wg1"],
        "l0a_Wg2": i["L0_attn_Wg2"], "l0a_bg2": i["L0_attn_bg2"].reshape(1, 1024),
        "l0a_gnw_tiled": np.ascontiguousarray(
            np.tile(np.tile(i["L0_attn_g_norm_w"], 16).reshape(1, 1024), (128, 1))),
    }


def run_l0_attn(inputs, x_full=None):
    nc = build_l0_attn_launch()
    cn = host_constants()
    wa = l0a_weight_arrays(inputs)
    in_maps = []
    base = dict(wa)
    for n, v in cn.items():
        base["cn_" + n] = v
    src = dict(inputs)
    if x_full is not None:
        src["L0_attn_x"] = x_full
    for core in range(NCORES):
        pc = per_core_arrays(src, core)
        m = dict(base)
        m["x_in"] = pc["x_in"]
        m["theta"] = pc["theta"]
        m["cbias"] = pc["cbias"]
        in_maps.append(m)
    res = run_bass_kernel_spmd(nc, in_maps, core_ids=list(range(NCORES)))
    return assemble_output([r["x_out"] for r in res.results])


FUSED = True


def kernel(**inputs):
    if not FUSED:
        return kernel_unfused(**inputs)
    cn = host_constants()
    nc = build_fused_launch()
    base = {}
    base.update(l0a_weight_arrays(inputs))
    base.update(l0f_weight_arrays(inputs))
    base.update(l1a_weight_arrays(inputs))
    base.update(l1f_weight_arrays(inputs))
    for n, v in cn.items():
        base["cn_" + n] = v
    maps = []
    for core in range(NCORES):
        pc = per_core_arrays(inputs, core)
        m = dict(base)
        m["x_in"] = pc["x_in"]
        m["theta"] = pc["theta"]
        m["cbias"] = pc["cbias"]
        for k, v in l1a_core_indices(core).items():
            m["a1_" + k] = v
        maps.append(m)
    res = run_bass_kernel_spmd(nc, maps, core_ids=list(range(NCORES)))
    return assemble_output([r["x_out"] for r in res.results])


def kernel_unfused(**inputs):
    cn = host_constants()
    base_cn = {"cn_" + n: v for n, v in cn.items()}

    def shard_x(x_full):
        return [per_core_arrays({"L0_attn_x": x_full}, core)["x_in"]
                for core in range(NCORES)]

    def run(nc, per_core_maps):
        res = run_bass_kernel_spmd(nc, per_core_maps, core_ids=list(range(NCORES)))
        return res.results

    nc = build_l0_attn_launch()
    wa = l0a_weight_arrays(inputs)
    maps = []
    for core in range(NCORES):
        pc = per_core_arrays(inputs, core)
        m = dict(wa); m.update(base_cn)
        m["x_in"] = pc["x_in"]; m["theta"] = pc["theta"]; m["cbias"] = pc["cbias"]
        maps.append(m)
    r1 = run(nc, maps)
    x1s = [r["x_out"] for r in r1]

    nc = build_l0_ffn_launch()
    wa = l0f_weight_arrays(inputs)
    maps = []
    for core in range(NCORES):
        m = dict(wa); m.update(base_cn)
        m["x_in"] = x1s[core]
        maps.append(m)
    r2 = run(nc, maps)
    x2s = [r["x_out"] for r in r2]

    nc = build_l1_attn_launch()
    wa = l1a_weight_arrays(inputs)
    maps = []
    for core in range(NCORES):
        m = {k: v for k, v in wa.items() if k in
             ("a1_Wq", "a1_Wk", "a1_Wv", "a1_bq", "a1_bk", "a1_bv", "a1_bo",
              "a1_Wo", "a1_lng_t", "a1_lnb_t")}
        m.update(base_cn)
        m["x_in"] = x2s[core]
        m.update({"a1_" + k: v for k, v in l1a_core_indices(core).items()})
        maps.append(m)
    nc_inputs = {a.memorylocations[0].name for a in nc.m.functions[0].allocations
                 if getattr(a, "kind", None) == "ExternalInput"}
    maps = [{k: v for k, v in m.items() if k in nc_inputs} for m in maps]
    r3 = run(nc, maps)
    o_res = [r["o_outa1"] for r in r3]

    nc = build_l1_attn_fin_launch()
    maps = []
    for core in range(NCORES):
        m = {k: wa[k] for k in ("a1_Wo", "a1_bo", "a1_lng_t", "a1_lnb_t")}
        m.update(base_cn)
        m["x_in"] = x2s[core]
        m["attn_in"] = assemble_attn_for_core(o_res, core)
        maps.append(m)
    r4 = run(nc, maps)
    x3s = [r["x_out"] for r in r4]

    nc = build_l1_ffn_launch()
    wa = l1f_weight_arrays(inputs)
    maps = []
    for core in range(NCORES):
        m = dict(wa); m.update(base_cn)
        m["x_in"] = x3s[core]
        maps.append(m)
    r5 = run(nc, maps)
    return assemble_output([r["x_out"] for r in r5])
